# version 11
# baseline (speedup 1.0000x reference)
import numpy as np
from contextlib import ExitStack
import ml_dtypes
import concourse.bass as bass
import concourse.mybir as mybir
from concourse.bass_utils import run_bass_kernel_spmd

BFNP = ml_dtypes.bfloat16
F32 = mybir.dt.float32
BF16 = mybir.dt.bfloat16
AF = mybir.ActivationFunctionType
OP = mybir.AluOpType
AX = mybir.AxisListType.X

D = 1024
S = 4096
SQ = 2048
H = 16
HD = 64
R = 257
E = 8
HF = 4096
NQT = SQ // 128
NJT = S // 128
NG = 4


class Ctr:
    def __init__(self, handle):
        self.h = handle
        self.n = 0


class Rec:
    ENG = ("t", "v", "s", "g", "y")

    def __init__(self):
        self.q = {e: [] for e in self.ENG}

    def op(self, e, fn, wait=(), inc=None, inck=1, drain=False):
        w = tuple((c.h if isinstance(c, Ctr) else c, int(v)) for c, v in wait)
        i = None
        if inc is not None:
            inc.n += inck
            i = (inc.h, inck)
        self.q[e].append((w, fn, i, drain))
        return inc.n if inc is not None else None

    def wait(self, e, c, v):
        self.op(e, None, wait=[(c, v)])

    def drain(self, e):
        self.op(e, None, drain=True)

    def replay(self, nc, barrier=True):
        with nc.Block() as block:
            dec = dict(t=block.tensor, v=block.vector, s=block.scalar,
                       g=block.gpsimd, y=block.sync)
            for e in self.ENG:
                items = self.q[e]
                if not items:
                    continue

                def body(engine, items=items):
                    for w, fn, i, dr in items:
                        if dr:
                            engine.drain()
                        for (h, val) in w:
                            engine.wait_ge(h, val)
                        inst = fn(engine) if fn is not None else None
                        if i is not None:
                            inst.then_inc(i[0], i[1])
                dec[e](body)
            self.q = {e: [] for e in self.ENG}
        if barrier:
            nc.all_engine_barrier()


class Ring:
    def __init__(self, nslots):
        self.free_at = [None] * nslots
        self.i = 0

    def acquire(self):
        s = self.i % len(self.free_at)
        self.i += 1
        return s, self.free_at[s]

    def mark(self, slot, ctr, val):
        self.free_at[slot] = (ctr, int(val))


def build_program(dbg=()):
    nc = bass.Bass("TRN2", target_bir_lowering=False,
                   detect_race_conditions=False)
    dbg = set(dbg)

    def dram(name, shape, dt, kind):
        if kind == "Internal" and name in dbg:
            kind = "ExternalOutput"
        return nc.dram_tensor(name, shape, dt, kind=kind).ap()

    C = {}
    C["xb"] = dram("xb", [S, D], F32, "ExternalInput")
    C["wqkv"] = dram("wqkv", [3 * D, D], F32, "ExternalInput")
    C["bqkv"] = dram("bqkv", [1, 3 * D], F32, "ExternalInput")
    C["wr"] = dram("wr", [R, HD], F32, "ExternalInput")
    C["br"] = dram("br", [1, R], F32, "ExternalInput")
    C["w1"] = dram("w1", [E, D, HF], F32, "ExternalInput")
    C["w2"] = dram("w2", [E, HF, D], F32, "ExternalInput")
    C["wg"] = dram("wg", [D, E], F32, "ExternalInput")
    C["bg"] = dram("bg", [E, 1], F32, "ExternalInput")
    C["crow"] = dram("crow", [1, D], F32, "ExternalInput")
    C["wmod"] = dram("wmod", [D, 6 * D], F32, "ExternalInput")
    C["eye_bh"] = dram("eye_bh", [128, 128], BF16, "ExternalInput")
    C["eye_fh"] = dram("eye_fh", [128, 128], F32, "ExternalInput")
    C["tri_geh"] = dram("tri_geh", [128, 128], F32, "ExternalInput")
    C["tri_lth"] = dram("tri_lth", [128, 128], F32, "ExternalInput")
    C["tri_leh"] = dram("tri_leh", [128, 128], F32, "ExternalInput")
    C["tri_gth"] = dram("tri_gth", [128, 128], F32, "ExternalInput")
    C["ones_h"] = dram("ones_h", [128, 1], BF16, "ExternalInput")
    C["ones_fh"] = dram("ones_fh", [128, 1], F32, "ExternalInput")
    C["iota_h"] = dram("iota_h", [128, 16], mybir.dt.int32, "ExternalInput")
    C["dump_h"] = dram("dump_h", [128, 1], mybir.dt.int32, "ExternalInput")
    C["eoff_h"] = dram("eoff_h", [8, 1], F32, "ExternalInput")
    C["out"] = dram("out", [SQ, D], F32, "ExternalOutput")
    C["rp_hbm"] = dram("rp_hbm", [8, 128, R], F32, "Internal")
    C["attn_hbm"] = dram("attn_hbm", [SQ, D], BF16, "Internal")
    C["x2_hbm"] = dram("x2_hbm", [SQ, D], F32, "Internal")
    C["mod_hbm"] = dram("mod_hbm", [1, 6 * D], F32, "Internal")
    C["h2_hbm"] = dram("h2_hbm", [SQ + 1, D], BF16, "Internal")
    C["y_hbm"] = dram("y_hbm", [SQ + 1, D], BF16, "Internal")
    C["idx_hbm"] = dram("idx_hbm", [4096, 1], mybir.dt.int32, "Internal")

    with ExitStack() as _stk:
        eye_b = _stk.enter_context(nc.sbuf_tensor("eye_b", [128, 128], BF16))
        eye_f = _stk.enter_context(nc.sbuf_tensor("eye_f", [128, 128], F32))
        tri_ge = _stk.enter_context(nc.sbuf_tensor("tri_ge", [128, 128], F32))
        tri_lt = _stk.enter_context(nc.sbuf_tensor("tri_lt", [128, 128], F32))
        tri_le = _stk.enter_context(nc.sbuf_tensor("tri_le", [128, 128], F32))
        tri_gt = _stk.enter_context(nc.sbuf_tensor("tri_gt", [128, 128], F32))
        ones_b = _stk.enter_context(nc.sbuf_tensor("ones_b", [128, 1], BF16))
        wrt8 = _stk.enter_context(nc.sbuf_tensor("wrt8", [HD, R], BF16))
        br_b = _stk.enter_context(nc.sbuf_tensor("br_b", [128, R], F32))
        ca1b = _stk.enter_context(nc.sbuf_tensor("ca1b", [128, D], F32))
        sab = _stk.enter_context(nc.sbuf_tensor("sab", [128, D], F32))
        gab = _stk.enter_context(nc.sbuf_tensor("gab", [128, D], F32))
        cb1b = _stk.enter_context(nc.sbuf_tensor("cb1b", [128, D], F32))
        sbb = _stk.enter_context(nc.sbuf_tensor("sbb", [128, D], F32))
        gbt = _stk.enter_context(nc.sbuf_tensor("gbt", [128, E], F32))
        _tm = _stk.enter_context(nc.semaphore("t_mm"))
        _ve = _stk.enter_context(nc.semaphore("v_ev"))
        _sg = _stk.enter_context(nc.semaphore("s_ge"))
        SM = {}
        for name in ("c", "wr", "wm0", "wm1", "xt0", "xt1", "xt2",
                     "at0", "at1", "sp0", "sp1", "sp2", "wst", "bias",
                     "rpw", "sk0", "sk1", "ao0", "ao1", "gt", "mask",
                     "w1a", "w1b", "w2s", "h1d", "cc", "mb",
                     "h2s", "rt0", "rt1", "rt2", "rt3", "gx"):
            SM[name] = Ctr(_stk.enter_context(nc.semaphore("sm_" + name)))
        C.update(eye_b=eye_b, eye_f=eye_f, tri_ge=tri_ge, tri_lt=tri_lt, tri_le=tri_le,
                 tri_gt=tri_gt, ones_b=ones_b, wrt8=wrt8, br_b=br_b,
                 ca1b=ca1b, sab=sab, gab=gab, cb1b=cb1b, sbb=sbb, gbt=gbt,
                 tm=Ctr(_tm), ve=Ctr(_ve), sg=Ctr(_sg), SM=SM)

        if "DUMP_PT" in dbg:
            C["DUMP_PT"] = True
        if "NO_POS" in dbg:
            C["NO_POS"] = True
        with nc.Block() as _blk0:
            @_blk0.gpsimd
            def _(g):
                for s in list(SM.values()):
                    g.sem_clear(s.h)
                g.sem_clear(_tm)
                g.sem_clear(_ve)
                g.sem_clear(_sg)
        nc.all_engine_barrier()
        emit_p0(nc, C)
        with nc.sbuf_tensor("h1T", [128, 8 * S], BF16) as h1T, \
             nc.sbuf_tensor("stats_sb", [128, 32 * 8], F32) as _ssb:
            C["h1T"] = h1T
            if "DUMP_H1T" in dbg:
                C["stats_dump"] = _ssb
            emit_ln(nc, C, first=True)
            if "DUMP_H1T" in dbg:
                dmp = nc.dram_tensor("h1T_dump", [128, 8 * S], BF16,
                                     kind="ExternalOutput").ap()
                sdmp_h = nc.dram_tensor("stats_dump", [128, 32 * 8], F32,
                                        kind="ExternalOutput").ap()
                dmp2 = nc.dram_tensor("bc_dump", [128, 3 * D], F32,
                                      kind="ExternalOutput").ap()
                Pd = Rec()
                Pd.wait("y", C["ve"], C["p1_done"])
                Pd.op("y", lambda y: y.dma_start(dmp, h1T[:, :]),
                      inc=C["SM"]["h1d"], inck=16)
                Pd.op("y", lambda y: y.dma_start(dmp2[:, 0:D], C["ca1b"][:, :]),
                      inc=C["SM"]["h1d"], inck=16)
                Pd.op("y", lambda y: y.dma_start(dmp2[:, D:2 * D], C["sab"][:, :]),
                      inc=C["SM"]["h1d"], inck=16)
                Pd.op("y", lambda y: y.dma_start(dmp2[:, 2 * D:3 * D], C["gab"][:, :]),
                      inc=C["SM"]["h1d"], inck=16)
                Pd.op("y", lambda y: y.dma_start(sdmp_h, C["stats_dump"][:, :]),
                      inc=C["SM"]["h1d"], inck=16)
                Pd.wait("y", C["SM"]["h1d"], C["SM"]["h1d"].n)
                Pd.replay(nc)
            if "STOP_P1" not in dbg:
                emit_attn(nc, C)
                if "STOP_P2" in dbg:
                    Pd = Rec()
                    Pd.wait("y", C["SM"]["ao0"], C["SM"]["ao0"].n)
                    Pd.wait("y", C["SM"]["ao1"], C["SM"]["ao1"].n)
                    Pd.replay(nc)
        if "STOP_P1" in dbg or "STOP_P2" in dbg:
            return nc
        with nc.sbuf_tensor("h2T", [128, 8 * SQ], BF16) as h2T, \
                nc.sbuf_tensor("h2loT", [128, 8 * SQ], BF16) as h2loT:
            C["h2T"] = h2T
            C["h2loT"] = h2loT
            emit_ln(nc, C, first=False)
            emit_gate(nc, C)
            if "STOP_P3" in dbg:
                Pd = Rec()
                for s in C["SM"].values():
                    Pd.wait("y", s, s.n)
                Pd.replay(nc)
                return nc
        with nc.sbuf_tensor("ysb", [128, 8 * SQ], BF16) as ysb:
            C["ysb"] = ysb
            emit_ffn(nc, C)
            emit_out(nc, C)
    return nc


def emit_p0(nc, C):
    P = Rec()
    tm, ve, sg = C["tm"], C["ve"], C["sg"]
    sm = C["SM"]
    smc, smwr, wm = sm["c"], sm["wr"], (sm["wm0"], sm["wm1"])
    smcc, smmb = sm["cc"], sm["mb"]

    for dst, src in [(C["eye_b"], C["eye_bh"]), (C["eye_f"], C["eye_fh"]),
                     (C["tri_ge"], C["tri_geh"]),
                     (C["tri_lt"], C["tri_lth"]), (C["tri_le"], C["tri_leh"]),
                     (C["tri_gt"], C["tri_gth"]), (C["ones_b"], C["ones_h"])]:
        P.op("g", lambda g, d=dst, s=src: g.dma_start(d[:, :], s), inc=smc, inck=16)
    P.op("g", lambda g: g.dma_start(C["br_b"][:, :],
                                    C["br"].partition_broadcast(128)),
         inc=smc, inck=16)
    ld_consts = smc.n

    with ExitStack() as _stk:
        wr_st = _stk.enter_context(nc.sbuf_tensor("wr_st", [128, 128], BF16))
        c_sb = _stk.enter_context(nc.sbuf_tensor("c_sb", [1, D], F32))
        cs_b = _stk.enter_context(nc.sbuf_tensor("cs_b", [1, D], BF16))
        csT = _stk.enter_context(nc.sbuf_tensor("csT", [128, 8], BF16))
        wmod_sb = _stk.enter_context(nc.sbuf_tensor("wmod_sb", [128, 2 * 1536], BF16))
        modrow = _stk.enter_context(nc.sbuf_tensor("modrow", [1, 6 * D], F32))
        g_row = _stk.enter_context(nc.sbuf_tensor("g_row", [1, D], F32))
        pP0f = _stk.enter_context(nc.psum_tensor("pP0f", [128, 512], F32))
        pP0b = _stk.enter_context(nc.psum_tensor("pP0b", [128, 1024], BF16))
        pmod = _stk.enter_context(nc.psum_tensor("pmod", [1, 1536], F32))
        P.wait("t", smc, ld_consts)
        last_tp = None
        last_evw = None
        for ch in range(3):
            r0, r1 = [(0, 128), (128, 256), (256, 257)][ch]
            n = r1 - r0
            w = [(tm, last_tp)] if last_tp is not None else []
            ld = P.op("g", lambda g, r0=r0, r1=r1, n=n: g.dma_start(
                wr_st[0:n, 0:HD], C["wr"][r0:r1, :]), wait=w, inc=smwr, inck=16)
            wtp = [(smwr, ld)]
            if last_evw is not None:
                wtp.append((ve, last_evw))
            tp = P.op("t", lambda t, n=n, ch=ch: t.transpose(
                pP0b[0:HD, ch * 128:ch * 128 + n], wr_st[0:n, 0:HD],
                C["eye_b"][0:n, 0:n]), wait=wtp, inc=tm)
            last_evw = P.op("v", lambda v, ch=ch, n=n: v.tensor_scalar(
                C["wrt8"][:, ch * 128:ch * 128 + n],
                pP0b[0:HD, ch * 128:ch * 128 + n], 8.0, None, OP.mult),
                wait=[(tm, tp)], inc=ve)
            last_tp = tp

        ldc = P.op("g", lambda g: g.dma_start(c_sb[:, :], C["crow"]),
                   inc=smcc, inck=16)
        sil = P.op("s", lambda s: s.activation(cs_b[:, :], c_sb[:, :], AF.Silu),
                   wait=[(smcc, ldc)], inc=sg)
        tpc = None
        for dc in range(8):
            wtc = [(sg, sil), (ve, last_evw)] if dc == 0 else []
            tpc = P.op("t", lambda t, dc=dc: t.transpose(
                pP0b[:, 384 + 4 * dc:385 + 4 * dc],
                cs_b[:, dc * 128:(dc + 1) * 128],
                C["eye_b"][0:1, 0:1]), wait=wtc, inc=tm)
        evc = P.op("v", lambda v: v.tensor_copy(csT[:, :], pP0b[:, 384:416:4]),
                   wait=[(tm, tpc)], inc=ve)

        mm_done = {}
        ev_prev = None
        for q4 in range(4):
            for dc in range(8):
                k = q4 * 8 + dc
                bo = (k % 2) * 1536
                w = [(tm, mm_done[k - 2])] if k >= 2 else []
                ld = P.op("g", lambda g, q4=q4, dc=dc, bo=bo: g.dma_start(
                    wmod_sb[:, bo:bo + 1536],
                    C["wmod"][dc * 128:(dc + 1) * 128,
                              q4 * 1536:(q4 + 1) * 1536]),
                    wait=w, inc=wm[k % 2], inck=16)
                mm = None
                for ns in range(3):
                    wmm = []
                    if ns == 0:
                        wmm.append((wm[k % 2], ld))
                        if dc == 0:
                            wmm.append((ve, evc if ev_prev is None else ev_prev))
                    mm = P.op("t", lambda t, dc=dc, bo=bo, ns=ns: t.matmul(
                        pmod[:, ns * 512:(ns + 1) * 512], csT[:, dc:dc + 1],
                        wmod_sb[:, bo + ns * 512:bo + (ns + 1) * 512],
                        start=(dc == 0), stop=(dc == 7)),
                        wait=wmm, inc=(tm if ns == 2 else None))
                mm_done[k] = tm.n
            ev_prev = P.op("v", lambda v, q4=q4: v.tensor_copy(
                modrow[:, q4 * 1536:(q4 + 1) * 1536], pmod[:, :]),
                wait=[(tm, mm_done[q4 * 8 + 7])], inc=ve)

        def seg(i):
            return modrow[:, i * D:(i + 1) * D]
        P.op("v", lambda v: v.tensor_scalar(seg(1), seg(1), 1.0, None, OP.add),
             wait=[(ve, ev_prev)], inc=ve)
        P.op("v", lambda v: v.tensor_scalar(seg(4), seg(4), 1.0, None, OP.add),
             inc=ve)
        tlast = None
        for i in (2, 5):
            ab = P.op("s", lambda s, i=i: s.activation(
                g_row[:, :], seg(i), AF.Abs), wait=[(ve, ve.n)], inc=sg)
            P.op("v", lambda v, i=i: v.tensor_scalar(
                g_row[:, :], g_row[:, :], 0.25, None, OP.max),
                wait=[(sg, ab)], inc=ve)
            P.op("v", lambda v, i=i: v.tensor_scalar(
                seg(i), seg(i), 0.0, 2.0, OP.is_ge, OP.mult), inc=ve)
            P.op("v", lambda v, i=i: v.tensor_scalar(
                seg(i), seg(i), -1.0, None, OP.add), inc=ve)
            tlast = P.op("v", lambda v, i=i: v.tensor_tensor(
                seg(i), seg(i), g_row[:, :], OP.mult), inc=ve)
        st = P.op("g", lambda g: g.dma_start(C["mod_hbm"], modrow[:, :]),
                  wait=[(ve, tlast)], inc=smmb, inck=16)
        first_b = True
        for dst, i in [(C["sab"], 0), (C["ca1b"], 1), (C["gab"], 2),
                       (C["sbb"], 3), (C["cb1b"], 4)]:
            P.op("g", lambda g, d=dst, i=i: g.dma_start(
                d[:, :], C["mod_hbm"][:, i * D:(i + 1) * D].partition_broadcast(128)),
                wait=([(smmb, st)] if first_b else []), inc=smmb, inck=16)
            first_b = False
        C["p0_bcast_c"] = smmb.n
        tb = None
        for dc in range(8):
            tb = P.op("t", lambda t, dc=dc: t.transpose(
                pP0f[:, dc:dc + 1], seg(5)[:, dc * 128:(dc + 1) * 128],
                C["eye_f"][0:1, 0:1]),
                wait=[(ve, tlast)], inc=tm)
        P.op("v", lambda v: v.tensor_copy(C["gbt"][:, :], pP0f[:, 0:8]),
             wait=[(tm, tb)], inc=ve)
        P.wait("v", smc, ld_consts)
        P.replay(nc)


def emit_ln(nc, C, first):
    P = Rec()
    tm, ve, sg = C["tm"], C["ve"], C["sg"]
    sm = C["SM"]
    xts = (sm["xt0"], sm["xt1"], sm["xt2"])
    ats = (sm["at0"], sm["at1"])
    sps = (sm["sp0"], sm["sp1"], sm["sp2"])
    ntt = (S if first else SQ) // 128
    dstT = C["h1T"] if first else C["h2T"]
    seqw = S if first else SQ
    cab = C["ca1b"] if first else C["cb1b"]
    sab = C["sab"] if first else C["sbb"]
    stats_dmp = C.get("stats_dump") if first else None
    if first:
        P.wait("v", sm["mb"], C["p0_bcast_c"])
    else:
        P.wait("y", sm["ao0"], sm["ao0"].n)
        P.wait("y", sm["ao1"], sm["ao1"].n)

    sfx = "1" if first else "2"
    with ExitStack() as _stk:
        xt = _stk.enter_context(nc.sbuf_tensor("xt" + sfx, [128, 3 * D], F32))
        at = _stk.enter_context(nc.sbuf_tensor("at" + sfx, [128, 2 * D], BF16))
        xsq = _stk.enter_context(nc.sbuf_tensor("xsq" + sfx, [128, D], F32))
        stats = _stk.enter_context(nc.sbuf_tensor("stats" + sfx, [128, 8], F32))
        hst = _stk.enter_context(nc.sbuf_tensor("hst" + sfx, [128, 2 * D], BF16))
        hstF = _stk.enter_context(nc.sbuf_tensor("hstF" + sfx, [128, D], F32))
        hstLo = _stk.enter_context(nc.sbuf_tensor("hstLo" + sfx, [128, 2 * D], BF16))
        pL0 = _stk.enter_context(nc.psum_tensor("pL0" + sfx, [128, 1024], BF16))
        pL1 = _stk.enter_context(nc.psum_tensor("pL1" + sfx, [128, 1024], BF16))
        pL2 = _stk.enter_context(nc.psum_tensor("pL2" + sfx, [128, 1024], BF16))
        pL3 = _stk.enter_context(nc.psum_tensor("pL3" + sfx, [128, 1024], BF16))
        ev_done = {}
        tp_done = {}
        tp_last = {}
        spill_ds = {}
        sq2_prev = [None]
        h2sp = {}
        for tt in range(ntt):
            sl = tt % 3
            w = [(ve, ev_done[tt - 3])] if tt >= 3 else []
            if (not first) and tt >= 3:
                w.append((sps[sl], spill_ds[tt - 3]))
            ld = P.op("y", lambda y, tt=tt, sl=sl: y.dma_start(
                xt[:, sl * D:(sl + 1) * D], C["xb"][tt * 128:(tt + 1) * 128, :]),
                wait=w, inc=xts[sl], inck=16)
            if not first:
                w2 = [(ve, ev_done[tt - 2])] if tt >= 2 else []
                lda = P.op("y", lambda y, tt=tt: y.dma_start(
                    at[:, (tt % 2) * D:(tt % 2 + 1) * D],
                    C["attn_hbm"][tt * 128:(tt + 1) * 128, :]),
                    wait=w2, inc=ats[tt % 2], inck=16)
                add = P.op("v", lambda v, sl=sl, tt=tt: v.tensor_tensor(
                    xt[:, sl * D:(sl + 1) * D], xt[:, sl * D:(sl + 1) * D],
                    at[:, (tt % 2) * D:(tt % 2 + 1) * D], OP.add),
                    wait=[(xts[sl], ld), (ats[tt % 2], lda)], inc=ve)
                spill_ds[tt] = P.op("y", lambda y, tt=tt, sl=sl: y.dma_start(
                    C["x2_hbm"][tt * 128:(tt + 1) * 128, :],
                    xt[:, sl * D:(sl + 1) * D]),
                    wait=[(ve, add)], inc=sps[sl], inck=16)
                ready_v = [(ve, add)]
                ready_s = [(ve, add)]
            else:
                ready_v = [(xts[sl], ld)]
                ready_s = [(xts[sl], ld)]
            P.op("v", lambda v, sl=sl: v.tensor_reduce(
                stats[:, 0:1], xt[:, sl * D:(sl + 1) * D], AX, OP.add),
                wait=ready_v, inc=ve)
            sq = P.op("s", lambda s, sl=sl: s.activation(
                xsq[:, :], xt[:, sl * D:(sl + 1) * D], AF.Square),
                wait=ready_s + ([(ve, ev_done[tt - 1])] if tt >= 1 else []),
                inc=sg)
            ws2 = [(sg, sq)]
            if tt >= 1:
                ws2.append((sg, sq2_prev[0]))
            P.op("v", lambda v: v.tensor_reduce(
                stats[:, 1:2], xsq[:, :], AX, OP.add), wait=ws2, inc=ve)
            P.op("v", lambda v: v.tensor_scalar(
                stats[:, 0:1], stats[:, 0:1], 1.0 / D, None, OP.mult), inc=ve,
                drain=True)
            P.op("v", lambda v: v.tensor_scalar(
                stats[:, 1:2], stats[:, 1:2], 1.0 / D, None, OP.mult), inc=ve,
                drain=True)
            P.op("v", lambda v: v.tensor_tensor(
                stats[:, 2:3], stats[:, 0:1], stats[:, 0:1], OP.mult), inc=ve,
                drain=True)
            v5 = P.op("v", lambda v: v.tensor_scalar(
                stats[:, 1:2], stats[:, 1:2], stats[:, 2:3], 1e-5,
                OP.subtract, OP.add), inc=ve, drain=True)
            sq2 = P.op("s", lambda s: s.activation(
                stats[:, 3:4], stats[:, 1:2], AF.Sqrt), wait=[(ve, v5)], inc=sg)
            sq2_prev[0] = sq2
            v6 = P.op("v", lambda v: v.reciprocal(
                stats[:, 4:5], stats[:, 3:4]), wait=[(sg, sq2)], inc=ve)
            if stats_dmp is not None:
                P.op("v", lambda v, tt=tt: v.tensor_copy(
                    stats_dmp[:, tt * 8:tt * 8 + 8], stats[:, :]), inc=ve)
            P.op("v", lambda v, sl=sl: v.tensor_scalar(
                xsq[:, :], xt[:, sl * D:(sl + 1) * D], stats[:, 0:1],
                stats[:, 4:5], OP.subtract, OP.mult), wait=[(ve, v6)], inc=ve,
                drain=True)
            P.op("v", lambda v: v.tensor_tensor(
                xsq[:, :], xsq[:, :], cab[:, :], OP.mult), inc=ve)
            hsl = (tt % 2) * D
            wa3 = [(tm, tp_last[tt - 2])] if tt >= 2 else []
            if (not first) and tt >= 2:
                wa3.append((sm["h2s"], h2sp[tt - 2]))
            if first:
                a3 = P.op("v", lambda v, hsl=hsl: v.tensor_tensor(
                    hst[:, hsl:hsl + D], xsq[:, :], sab[:, :], OP.add),
                    wait=wa3, inc=ve)
            else:
                P.op("v", lambda v: v.tensor_tensor(
                    hstF[:, :], xsq[:, :], sab[:, :], OP.add),
                    wait=wa3, inc=ve)
                P.op("v", lambda v, hsl=hsl: v.tensor_copy(
                    hst[:, hsl:hsl + D], hstF[:, :]), inc=ve, drain=True)
                a3 = P.op("v", lambda v, hsl=hsl: v.scalar_tensor_tensor(
                    hstLo[:, hsl:hsl + D], hst[:, hsl:hsl + D], -1.0,
                    hstF[:, :], OP.mult, OP.add), inc=ve, drain=True)
            ev_done[tt] = a3
            if not first:
                h2sp[tt] = P.op("y", lambda y, tt=tt, hsl=hsl: y.dma_start(
                    C["h2_hbm"][tt * 128:(tt + 1) * 128, :],
                    hst[:, hsl:hsl + D]),
                    wait=[(ve, a3)], inc=sm["h2s"], inck=16)
            pps = [(pL0, pL1)[tt % 2], (pL2, pL3)[tt % 2]]
            tps = []
            srcs = [(hst, dstT)] if first else [(hst, dstT), (hstLo, C["h2loT"])]
            for si, (sbuf_src, _dst) in enumerate(srcs):
                for dc in range(8):
                    wt = [(ve, a3)] if (dc == 0 and si == 0) else []
                    if dc == 0 and si == 0 and tt >= 2:
                        wt.append((ve, tp_done[tt - 2]))
                    tp = P.op("t", lambda t, dc=dc, pp=pps[si], hsl=hsl,
                              s=sbuf_src: t.transpose(
                                  pp[:, dc * 128:(dc + 1) * 128],
                                  s[:, hsl + dc * 128:hsl + (dc + 1) * 128],
                                  C["eye_b"][:, :]),
                              wait=wt, inc=tm)
                    tps.append(tp)
            tp_last[tt] = tps[-1]
            e2 = None
            for si, (sbuf_src, dst_) in enumerate(srcs):
                for dc in range(8):
                    e2 = P.op("v", lambda v, dc=dc, tt=tt, pp=pps[si], d=dst_:
                              v.tensor_copy(
                                  d[:, dc * seqw + tt * 128:
                                    dc * seqw + (tt + 1) * 128],
                                  pp[:, dc * 128:(dc + 1) * 128]),
                              wait=[(tm, tps[(si + 1) * 8 - 1])], inc=ve)
            tp_done[tt] = e2
        C["p1_done" if first else "p3_ln_done"] = ve.n
        if not first:
            zr = P.op("v", lambda v: v.memset(hst[0:1, 0:D], 0.0),
                      wait=[(sm["h2s"], h2sp[ntt - 1]),
                            (sm["h2s"], h2sp[ntt - 2])], inc=ve)
            P.op("y", lambda y: y.dma_start(C["h2_hbm"][SQ:SQ + 1, :],
                                            hst[0:1, 0:D]),
                 wait=[(ve, zr)], inc=sm["h2s"], inck=16)
            C["h2s_all"] = sm["h2s"].n
        P.replay(nc)


def emit_attn(nc, C):
    P = Rec()
    tm, ve, sg = C["tm"], C["ve"], C["sg"]
    sm = C["SM"]
    smb, smw, smr = sm["bias"], sm["wst"], sm["rpw"]
    sks = (sm["sk0"], sm["sk1"])
    aos = (sm["ao0"], sm["ao1"])
    h1T, eye_b = C["h1T"], C["eye_b"]
    if "DUMP_PT" in C:
        C["pt_dump"] = nc.dram_tensor("pt_dump", [NJT, 128, 512], BF16,
                                      kind="ExternalOutput").ap()

    with ExitStack() as _stk:
        w_st = _stk.enter_context(nc.sbuf_tensor("w_st", [128, D], BF16))
        wqT = _stk.enter_context(nc.sbuf_tensor("wqT", [128, D], BF16))
        wkT = _stk.enter_context(nc.sbuf_tensor("wkT", [128, D], BF16))
        wvT = _stk.enter_context(nc.sbuf_tensor("wvT", [128, D], BF16))
        qb2 = _stk.enter_context(nc.sbuf_tensor("qb2", [128, 1], F32))
        kb2 = _stk.enter_context(nc.sbuf_tensor("kb2", [128, 1], F32))
        bvb = _stk.enter_context(nc.sbuf_tensor("bvb", [128, 128], F32))
        qN = _stk.enter_context(nc.sbuf_tensor("qN", [65, 2 * SQ], BF16))
        qA = _stk.enter_context(nc.sbuf_tensor("qA", [65, 2 * SQ], BF16))
        qB = _stk.enter_context(nc.sbuf_tensor("qB", [65, 2 * SQ], BF16))
        kt = _stk.enter_context(nc.sbuf_tensor("kt", [65, 2 * S], BF16))
        vsb = _stk.enter_context(nc.sbuf_tensor("vsb", [128, 2 * NJT * 65], BF16))
        qsq = _stk.enter_context(nc.sbuf_tensor("qsq", [64, SQ], BF16))
        mrow = _stk.enter_context(nc.sbuf_tensor("mrow", [1, 2 * SQ], BF16))
        nkrow = _stk.enter_context(nc.sbuf_tensor("nkrow", [1, 512], F32))
        maxk = _stk.enter_context(nc.sbuf_tensor("maxk", [1, 16], F32))
        skews = _stk.enter_context(nc.sbuf_tensor("skews", [128, 2 * 384], F32))
        poss = _stk.enter_context(nc.sbuf_tensor("poss", [128, 2 * 384], F32))
        posb = _stk.enter_context(nc.sbuf_tensor("posb", [128, 8 * 384], BF16))
        rp01 = _stk.enter_context(nc.sbuf_tensor("rp01", [128, 16], BF16))
        epT = _stk.enter_context(nc.sbuf_tensor("epT", [128, 4 * 12 * 128], BF16))
        rp_sb = _stk.enter_context(nc.sbuf_tensor("rp_sb", [128, 8 * R], F32))
        PT = _stk.enter_context(nc.sbuf_tensor("PT", [128, 6 * 512], BF16))
        avsb = _stk.enter_context(nc.sbuf_tensor("avsb", [65, 1024], BF16))
        avrc = _stk.enter_context(nc.sbuf_tensor("avrc", [128, 1], F32))
        avout = _stk.enter_context(nc.sbuf_tensor("avout", [128, 2 * 64], BF16))
        pS0 = _stk.enter_context(nc.psum_tensor("pS0", [128, 512], F32))
        pS1 = _stk.enter_context(nc.psum_tensor("pS1", [128, 512], F32))
        pS2 = _stk.enter_context(nc.psum_tensor("pS2", [128, 512], F32))
        pS3 = _stk.enter_context(nc.psum_tensor("pS3", [128, 512], F32))
        pAVa = _stk.enter_context(nc.psum_tensor("pAVa", [128, 512], F32))
        pAVb = _stk.enter_context(nc.psum_tensor("pAVb", [128, 512], F32))
        pBF0 = _stk.enter_context(nc.psum_tensor("pBF0", [128, 1024], BF16))
        pBF1 = _stk.enter_context(nc.psum_tensor("pBF1", [128, 1024], BF16))
        pS = [pS0, pS1, pS2, pS3]
        pBF = [pBF0, pBF1]
        rS = Ring(4)
        rBF = Ring(2)
        rRP = Ring(8)
        rPT = Ring(6)
        rSK = Ring(2)
        rAO = Ring(2)
        rHB = Ring(8)
        pAV = {0: pAVa, 1: pAVb}

        posT_last = [None] * 8
        posex_last = [None, None]
        tpr_last = [None] * 8
        prologue_mm = [None]
        P.wait("t", ve, C["p1_done"])
        P.op("v", lambda v: v.memset(kt[64:65, :], 1.0), inc=ve)
        ones_ev = P.op("v", lambda v: v.memset(
            vsb[:, 64:64 + 65 * (2 * NJT - 1) + 1:65], 1.0), inc=ve)
        C["p2_ones_ev"] = ones_ev

        def pair_setup(p):
            h0, h1 = 2 * p, 2 * p + 1
            bq = C["bqkv"]
            if p > 0:
                P.wait("g", tm, C[f"proj_mm_{p - 1}"])
            for dst, off0, off1 in ((qb2, 192 * h0, 192 * h1),
                                    (kb2, 192 * h0 + 64, 192 * h1 + 64)):
                P.op("g", lambda g, d=dst, o=off0: g.dma_start(
                    d[0:64, :], bq[:, o:o + 64].transpose((1, 0))),
                    inc=smb, inck=16)
                P.op("g", lambda g, d=dst, o=off1: g.dma_start(
                    d[64:128, :], bq[:, o:o + 64].transpose((1, 0))),
                    inc=smb, inck=16)
            P.op("g", lambda g: g.dma_start(
                bvb[:, 0:64],
                bq[:, 192 * h0 + 128:192 * h0 + 192].partition_broadcast(128)),
                inc=smb, inck=16)
            P.op("g", lambda g: g.dma_start(
                bvb[:, 64:128],
                bq[:, 192 * h1 + 128:192 * h1 + 192].partition_broadcast(128)),
                inc=smb, inck=16)
            bias_ld = smb.n
            P.op("v", lambda v: v.tensor_scalar(
                qb2[:, :], qb2[:, :], 0.125, None, OP.mult),
                wait=[(smb, bias_ld)], inc=ve)

            for kind, dstW, scale in (("q", wqT, 0.125), ("k", wkT, None),
                                      ("v", wvT, None)):
                off = {"q": 0, "k": 64, "v": 128}[kind]
                P.op("g", lambda g, o=off: g.dma_start(
                    w_st[0:64, :], C["wqkv"][192 * h0 + o:192 * h0 + o + 64, :]),
                    inc=smw, inck=16)
                lb = P.op("g", lambda g, o=off: g.dma_start(
                    w_st[64:128, :], C["wqkv"][192 * h1 + o:192 * h1 + o + 64, :]),
                    inc=smw, inck=16)
                sbf, wbf = rBF.acquire()
                tps = []
                for dc in range(8):
                    wt = []
                    if dc == 0:
                        wt.append((smw, lb))
                        if wbf is not None:
                            wt.append(wbf)
                    tp = P.op("t", lambda t, dc=dc, s=sbf: t.transpose(
                        pBF[s][:, dc * 128:(dc + 1) * 128],
                        w_st[:, dc * 128:(dc + 1) * 128], eye_b[:, :]),
                        wait=wt, inc=tm)
                    tps.append(tp)
                evl = None
                for dc in range(8):
                    if scale is None:
                        evl = P.op("v", lambda v, dc=dc, s=sbf, dW=dstW:
                                   v.tensor_copy(dW[:, dc * 128:(dc + 1) * 128],
                                                 pBF[s][:, dc * 128:(dc + 1) * 128]),
                                   wait=[(tm, tps[-1])], inc=ve)
                    else:
                        evl = P.op("v", lambda v, dc=dc, s=sbf, dW=dstW, sc=scale:
                                   v.tensor_scalar(dW[:, dc * 128:(dc + 1) * 128],
                                                   pBF[s][:, dc * 128:(dc + 1) * 128],
                                                   sc, None, OP.mult),
                                   wait=[(tm, tps[-1])], inc=ve)
                rBF.mark(sbf, ve, evl)
                P.wait("g", tm, tps[-1])
            wT_ev = ve.n

            for ns in range(4):
                sl, wfree = rS.acquire()
                wt = [(ve, wT_ev), (smb, bias_ld)] + ([wfree] if wfree else [])
                mm = None
                for dc in range(8):
                    mm = P.op("t", lambda t, dc=dc, ns=ns, sl=sl: t.matmul(
                        pS[sl][:, :], wqT[:, dc * 128:(dc + 1) * 128],
                        h1T[:, dc * S + ns * 512: dc * S + (ns + 1) * 512],
                        start=(dc == 0), stop=(dc == 7)),
                        wait=(wt if dc == 0 else []),
                        inc=(tm if dc == 7 else None))
                ev = None
                for dstq in (qN, qA, qB):
                    for pi in range(2):
                        ev = P.op("v", lambda v, d=dstq, pi=pi, ns=ns, sl=sl:
                                  v.tensor_scalar(
                                      d[0:64, pi * SQ + ns * 512:
                                        pi * SQ + (ns + 1) * 512],
                                      pS[sl][pi * 64:(pi + 1) * 64, :],
                                      qb2[pi * 64:(pi + 1) * 64, :],
                                      None, OP.add),
                                  wait=[(tm, mm)], inc=ve)
                rS.mark(sl, ve, ev)
            C[f"proj_q_{p}"] = ve.n
            for ns in range(8):
                sl, wfree = rS.acquire()
                wt = [(ve, wT_ev), (smb, bias_ld)] + ([wfree] if wfree else [])
                mm = None
                for dc in range(8):
                    mm = P.op("t", lambda t, dc=dc, ns=ns, sl=sl: t.matmul(
                        pS[sl][:, :], wkT[:, dc * 128:(dc + 1) * 128],
                        h1T[:, dc * S + ns * 512: dc * S + (ns + 1) * 512],
                        start=(dc == 0), stop=(dc == 7)),
                        wait=(wt if dc == 0 else []),
                        inc=(tm if dc == 7 else None))
                ev = None
                for pi in range(2):
                    ev = P.op("v", lambda v, pi=pi, ns=ns, sl=sl:
                              v.tensor_scalar(
                                  kt[0:64, pi * S + ns * 512:
                                     pi * S + (ns + 1) * 512],
                                  pS[sl][pi * 64:(pi + 1) * 64, :],
                                  kb2[pi * 64:(pi + 1) * 64, :], None, OP.add),
                              wait=[(tm, mm)], inc=ve)
                rS.mark(sl, ve, ev)
            C[f"proj_k_{p}"] = ve.n
            for jt in range(NJT):
                sl, wfree = rS.acquire()
                wt = [(ve, wT_ev), (smb, bias_ld)] + ([wfree] if wfree else [])
                mm = None
                for dc in range(8):
                    mm = P.op("t", lambda t, dc=dc, jt=jt, sl=sl: t.matmul(
                        pS[sl][:, 0:128],
                        h1T[:, dc * S + jt * 128: dc * S + (jt + 1) * 128],
                        wvT[:, dc * 128:(dc + 1) * 128],
                        start=(dc == 0), stop=(dc == 7)),
                        wait=(wt if dc == 0 else []),
                        inc=(tm if dc == 7 else None))
                ev = None
                for pi in range(2):
                    ev = P.op("v", lambda v, pi=pi, jt=jt, sl=sl:
                              v.tensor_tensor(
                                  vsb[:, pi * NJT * 65 + jt * 65:
                                      pi * NJT * 65 + jt * 65 + 64],
                                  pS[sl][:, pi * 64:(pi + 1) * 64],
                                  bvb[:, pi * 64:(pi + 1) * 64], OP.add),
                              wait=[(tm, mm)], inc=ve)
                rS.mark(sl, ve, ev)
            C[f"proj_ev_{p}"] = ve.n
            C[f"proj_mm_{p}"] = tm.n

        def prologue(h):
            pi = h % 2
            wsq1 = [(ve, C[f"proj_q_{h // 2}"])]
            if prologue_mm[0] is not None:
                wsq1.append((tm, prologue_mm[0]))
            sq1 = P.op("s", lambda s, pi=pi: s.activation(
                qsq[:, :], qN[0:64, pi * SQ:(pi + 1) * SQ], AF.Square),
                wait=wsq1, inc=sg)
            for ns in range(4):
                sl, wfree = rS.acquire()
                wt = [(sg, sq1)] if ns == 0 else []
                if wfree is not None:
                    wt.append(wfree)
                mm = P.op("t", lambda t, ns=ns, sl=sl: t.matmul(
                    pS[sl][0:1, :], C["ones_b"][0:64, :],
                    qsq[0:64, ns * 512:(ns + 1) * 512], start=True, stop=True),
                    wait=wt, inc=tm)
                sv = P.op("s", lambda s, pi=pi, ns=ns, sl=sl: s.activation(
                    mrow[0:1, pi * SQ + ns * 512:pi * SQ + (ns + 1) * 512],
                    pS[sl][0:1, :], AF.Sqrt), wait=[(tm, mm)], inc=sg)
                rS.mark(sl, sg, sv)
            prev_mm = None
            for ns in range(8):
                wsq = [(ve, C[f"proj_k_{h // 2}"])] if ns == 0 else [(tm, prev_mm)]
                sq2 = P.op("s", lambda s, pi=pi, ns=ns: s.activation(
                    qsq[:, 0:512], kt[0:64, pi * S + ns * 512:
                                      pi * S + (ns + 1) * 512], AF.Square),
                    wait=wsq, inc=sg)
                sl, wfree = rS.acquire()
                mm = P.op("t", lambda t, sl=sl: t.matmul(
                    pS[sl][0:1, :], C["ones_b"][0:64, :], qsq[0:64, 0:512],
                    start=True, stop=True),
                    wait=[(sg, sq2)] + ([wfree] if wfree else []), inc=tm)
                rv = P.op("v", lambda v, ns=ns, sl=sl: v.tensor_reduce(
                    maxk[:, ns:ns + 1], pS[sl][0:1, :], AX, OP.max),
                    wait=[(tm, mm)], inc=ve)
                rS.mark(sl, ve, rv)
                prev_mm = mm
            prologue_mm[0] = prev_mm
            red = P.op("v", lambda v: v.tensor_reduce(
                maxk[:, 8:9], maxk[:, 0:8], AX, OP.max), inc=ve, drain=True)
            sqm = P.op("s", lambda s: s.activation(
                maxk[:, 9:10], maxk[:, 8:9], AF.Sqrt), wait=[(ve, red)], inc=sg)
            m1 = P.op("v", lambda v, pi=pi: v.tensor_scalar(
                mrow[0:1, pi * SQ:(pi + 1) * SQ],
                mrow[0:1, pi * SQ:(pi + 1) * SQ], maxk[0:1, 9:10], -1.0,
                OP.mult, OP.mult), wait=[(sg, sqm)], inc=ve)
            r64 = P.op("v", lambda v, pi=pi: v.tensor_copy(
                qN[64:65, pi * SQ:(pi + 1) * SQ],
                mrow[0:1, pi * SQ:(pi + 1) * SQ]), wait=[(ve, m1)], inc=ve)
            C[f"row64_{h}"] = r64

        def rp_stage_a1(h, qt):
            pi = h % 2
            slr, wr_ = rRP.acquire()
            slh, wh_ = rHB.acquire()
            sl, wfree = rS.acquire()
            wt = [(ve, C[f"row64_{h}"])] + ([wfree] if wfree else [])
            mm = P.op("t", lambda t, pi=pi, qt=qt, sl=sl: t.matmul(
                pS[sl][:, 0:R], qN[0:64, pi * SQ + qt * 128:
                                   pi * SQ + (qt + 1) * 128],
                C["wrt8"][:, :], start=True, stop=True), wait=wt, inc=tm)
            evr = P.op("v", lambda v, slr=slr, sl=sl: v.tensor_tensor(
                rp_sb[:, slr * R:slr * R + R], pS[sl][:, 0:R], br_b_ap(C),
                OP.add), wait=[(tm, mm)] + ([wr_] if wr_ else []), inc=ve)
            rS.mark(sl, ve, evr)
            rslot = (qt % 4) + 4 * (h % 2)
            wcc = [(ve, evr)]
            if tpr_last[rslot] is not None:
                wcc.append((tm, tpr_last[rslot]))
            cc = P.op("v", lambda v, slr=slr, rslot=rslot: v.tensor_copy(
                rp01[:, 2 * rslot:2 * rslot + 2],
                rp_sb[:, slr * R:slr * R + 257:256]),
                wait=wcc, inc=ve, drain=True)
            wst = [(ve, evr)] + ([wh_] if wh_ else [])
            st = P.op("g", lambda g, slr=slr, slh=slh: g.dma_start(
                C["rp_hbm"][slh, :, :], rp_sb[:, slr * R:slr * R + R]),
                wait=wst, inc=smr, inck=16)
            return slr, slh, st, cc

        def rp_stage_a2(h, qt, slp_shared, cc, prev_e65):
            pi = h % 2
            rslot = (qt % 4) + 4 * (h % 2)
            tpr = None
            for j in range(2):
                wt = []
                if j == 0:
                    wt.append((ve, cc))
                    if prev_e65 is not None:
                        wt.append((ve, prev_e65))
                tpr = P.op("t", lambda t, rslot=rslot, j=j, slp=slp_shared:
                           t.transpose(
                               pBF[slp][0:1, 512 + j * 128:512 + (j + 1) * 128],
                               rp01[:, 2 * rslot + j:2 * rslot + j + 1],
                               eye_b[:, :]),
                           wait=wt, inc=tm)
            tpr_last[rslot] = tpr
            e65 = None
            for dstq, j in ((qA, 0), (qB, 1)):
                e65 = P.op("v", lambda v, d=dstq, j=j, pi=pi, qt=qt,
                           slp=slp_shared: v.tensor_tensor(
                               d[64:65, pi * SQ + qt * 128:pi * SQ + (qt + 1) * 128],
                               pBF[slp][0:1, 512 + j * 128:512 + (j + 1) * 128],
                               mrow[0:1, pi * SQ + qt * 128:pi * SQ + (qt + 1) * 128],
                               OP.add),
                           wait=[(tm, tpr), (ve, C[f"row64_{h}"])], inc=ve)
            C[f"row65_{h}_{qt}"] = e65
            return e65

        def rp_stage_b1(h, qt, slr, slh, st):
            ms = [0, 1, 2] if qt >= 1 else [1, 2]
            ssk, wsk = rSK.acquire()
            base = ssk * 384
            if posex_last[qt % 2] is not None:
                P.wait("v", sg, posex_last[qt % 2])
            lds = None
            for m in ms:
                lds = P.op("g", lambda g, m=m, slh=slh, base=base: g.dma_start(
                    skews[:, base + m * 128:base + (m + 1) * 128],
                    bass.AP(C["rp_hbm"].tensor, slh * 128 * R + m * 128,
                            [[256, 128], [1, 128]])),
                    wait=([(smr, st)] + ([wsk] if wsk else [])) if m == ms[0]
                    else [], inc=sks[ssk], inck=16)
            po = (qt % 2) * 384
            pb = ((qt % 4) + 4 * (h % 2)) * 384
            pslot = (qt % 4) + 4 * (h % 2)
            bl = []
            for m in ms:
                if m == 1:
                    bl.append(P.op("v", lambda v, base=base, po=po: v.tensor_copy(
                        poss[:, po + 128:po + 256],
                        skews[:, base + 128:base + 256]),
                        wait=[(sks[ssk], lds)], inc=ve))
                elif m == 0:
                    P.op("v", lambda v, base=base, po=po: v.tensor_tensor(
                        poss[:, po:po + 128], skews[:, base:base + 128],
                        C["tri_ge"][:, :], OP.mult), wait=[(sks[ssk], lds)], inc=ve)
                    bl.append(P.op("v", lambda v, slr=slr, po=po:
                                   v.scalar_tensor_tensor(
                                       poss[:, po:po + 128], C["tri_lt"][:, :],
                                       rp_sb[:, slr * R:slr * R + 1],
                                       poss[:, po:po + 128], OP.mult, OP.add),
                                   inc=ve))
                else:
                    P.op("v", lambda v, base=base, po=po: v.tensor_tensor(
                        poss[:, po + 256:po + 384],
                        skews[:, base + 256:base + 384],
                        C["tri_le"][:, :], OP.mult), wait=[(sks[ssk], lds)], inc=ve)
                    bl.append(P.op("v", lambda v, slr=slr, po=po:
                                   v.scalar_tensor_tensor(
                                       poss[:, po + 256:po + 384],
                                       C["tri_gt"][:, :],
                                       rp_sb[:, slr * R + 256:slr * R + 257],
                                       poss[:, po + 256:po + 384],
                                       OP.mult, OP.add), inc=ve))
            rSK.mark(ssk, ve, bl[-1])
            rRP.mark(slr, ve, bl[-1])
            lo = 0 if qt >= 1 else 128
            wex = [(ve, bl[-1])]
            if posT_last[pslot] is not None:
                wex.append((tm, posT_last[pslot]))
            ex = P.op("s", lambda s, pb=pb, po=po, lo=lo: s.activation(
                posb[:, pb + lo:pb + 384], poss[:, po + lo:po + 384], AF.Exp),
                wait=wex, inc=sg)
            posex_last[qt % 2] = ex
            rHB.mark(slh, sks[ssk], lds)
            return ex

        def rp_stage_b2(h, qt, g_, side, ex):
            ms = [0, 1, 2] if qt >= 1 else [1, 2]
            pb = ((qt % 4) + 4 * (h % 2)) * 384
            pslot = (qt % 4) + 4 * (h % 2)
            slot = (g_ % 2) * 2 + side
            sbf, wbf = rBF.acquire()
            tps = []
            for m in ms:
                wt = [(sg, ex)] if m == ms[0] else []
                if m == ms[0] and wbf is not None:
                    wt.append(wbf)
                tps.append(P.op("t", lambda t, m=m, pb=pb, s=sbf: t.transpose(
                    pBF[s][:, m * 128:(m + 1) * 128],
                    posb[:, pb + m * 128:pb + (m + 1) * 128], eye_b[:, :]),
                    wait=wt, inc=tm))
            posT_last[pslot] = tps[-1]
            eve = None
            for i, m in enumerate(ms):
                eve = P.op("v", lambda v, m=m, s=sbf, slot=slot, qt=qt: v.tensor_copy(
                    epT[:, slot * 1536 + (qt % 4) * 384 + m * 128:
                        slot * 1536 + (qt % 4) * 384 + (m + 1) * 128],
                    pBF[s][:, m * 128:(m + 1) * 128]),
                    wait=[(tm, tps[-1])], inc=ve)
            rBF.mark(sbf, ve, eve)
            C[f"ep_{h}_{qt}"] = eve

        def segments(g_, jt):
            its = [4 * g_ + l for l in range(4)]
            cls = []
            for it in its:
                if abs(jt - it) <= 1:
                    cls.append("N")
                elif it <= jt - 2:
                    cls.append("R")
                else:
                    cls.append("L")
            segs = []
            s0 = 0
            for i in range(1, 5):
                if i == 4 or cls[i] != cls[s0]:
                    segs.append((cls[s0], s0, i))
                    s0 = i
            return segs

        def body2(hA, hB, g_, rp_hooks):
            near = [jt for jt in range(max(0, 4 * g_ - 1), min(NJT, 4 * g_ + 5))]
            far = [jt for jt in range(NJT) if jt not in near]
            order = far + near
            heads = (hA, hB)
            nch = {0: 0, 1: 0}
            epslot = (g_ % 2) * 2
            units = [(jt, s) for jt in order for s in (0, 1)]
            pend_av = []
            LAG = 4
            for ui, (jt, side) in enumerate(units):
                if ui in rp_hooks:
                    rp_hooks[ui]()
                h = heads[side]
                pi = h % 2
                segs = segments(g_, jt)
                sl, wfree = rS.acquire()
                slt, wpt = rPT.acquire()
                mm = None
                for si, (cls, l0, l1) in enumerate(segs):
                    qsrc = {"N": qN, "L": qA, "R": qB}[cls]
                    wt = []
                    if si == 0 and wfree:
                        wt.append(wfree)
                    if cls != "N":
                        wt.append((ve, C[f"row65_{h}_{4 * g_ + l1 - 1}"]))
                    elif si == 0:
                        wt.append((ve, C[f"row64_{h}"]))
                    mm = P.op("t", lambda t, qsrc=qsrc, l0=l0, l1=l1, jt=jt,
                              sl=sl, si=si, nseg=len(segs), pi=pi, g_=g_:
                              t.matmul(
                                  pS[sl][:, l0 * 128:l1 * 128],
                                  kt[0:65, pi * S + jt * 128:
                                     pi * S + (jt + 1) * 128],
                                  qsrc[0:65, pi * SQ + g_ * 512 + l0 * 128:
                                       pi * SQ + g_ * 512 + l1 * 128],
                                  start=(si == 0), stop=(si == nseg - 1)),
                              wait=wt, inc=(tm if si == len(segs) - 1 else None))
                ex = P.op("s", lambda s, sl=sl, slt=slt: s.activation(
                    PT[:, slt * 512:(slt + 1) * 512], pS[sl][:, :], AF.Exp),
                    wait=[(tm, mm)] + ([wpt] if wpt else []), inc=sg)
                rS.mark(sl, sg, ex)
                av_wait = [(sg, ex)]
                for (cls, l0, l1) in (segs if not C.get("NO_POS") else []):
                    if cls != "N":
                        continue
                    for l in range(l0, l1):
                        it = 4 * g_ + l
                        m = jt - it + 1
                        mu = P.op("v", lambda v, slt=slt, l=l, it=it, m=m,
                                  side=side, epslot=epslot:
                                  v.tensor_tensor(
                                      PT[:, slt * 512 + l * 128:
                                         slt * 512 + (l + 1) * 128],
                                      PT[:, slt * 512 + l * 128:
                                         slt * 512 + (l + 1) * 128],
                                      epT[:, (epslot + side) * 1536 +
                                          (it % 4) * 384 + m * 128:
                                          (epslot + side) * 1536 +
                                          (it % 4) * 384 + (m + 1) * 128],
                                      OP.mult),
                                  wait=[(sg, ex), (ve, C[f"ep_{h}_{it}"])],
                                  inc=ve)
                        av_wait = [(ve, mu)]
                wt = list(av_wait)
                if nch[side] == 0:
                    aw = C.get(f"av_free_{side}")
                    if aw is not None:
                        wt.append(aw)
                    wt.append((ve, C["p2_ones_ev"]))

                def emit_av(jt=jt, slt=slt, pi=pi, side=side, wt=tuple(wt),
                            first=(nch[side] == 0), last=(nch[side] == NJT - 1)):
                    av = P.op("t", lambda t, jt=jt, slt=slt, pi=pi, side=side,
                              first=first, last=last: t.matmul(
                                  pAV[side][0:65, :],
                                  vsb[:, pi * NJT * 65 + jt * 65:
                                      pi * NJT * 65 + (jt + 1) * 65],
                                  PT[:, slt * 512:(slt + 1) * 512],
                                  start=first, stop=last),
                              wait=list(wt), inc=tm)
                    rPT.mark(slt, tm, av)
                    C[f"av_done_{side}"] = av
                pend_av.append(emit_av)
                nch[side] += 1
                if len(pend_av) > LAG:
                    pend_av.pop(0)()
            for f in pend_av:
                f()
            cps = {}
            for side in (0, 1):
                cp = P.op("v", lambda v, side=side: v.tensor_copy(
                    avsb[:, side * 512:(side + 1) * 512], pAV[side][0:65, :]),
                    wait=[(tm, C[f"av_done_{side}"])], inc=ve)
                C[f"av_free_{side}"] = (ve, cp)
                cps[side] = cp

            def finish(g_=g_, cps=cps, heads=heads):
                for side in (0, 1):
                    h = heads[side]
                    cp = cps[side]
                    sbf, wbf = rBF.acquire()
                    tps = []
                    for l in range(4):
                        wt = [(ve, cp)] if l == 0 else []
                        if l == 0 and wbf is not None:
                            wt.append(wbf)
                        tps.append(P.op("t", lambda t, l=l, s=sbf, side=side:
                                        t.transpose(
                                            pBF[s][:, l * 256:l * 256 + 65],
                                            avsb[:, side * 512 + l * 128:
                                                 side * 512 + (l + 1) * 128],
                                            eye_b[0:65, 0:65]), wait=wt, inc=tm))
                    evf = None
                    for l in range(4):
                        qt = 4 * g_ + l
                        so, wao = rAO.acquire()
                        rc = P.op("v", lambda v, l=l, s=sbf: v.reciprocal(
                            avrc[:, :], pBF[s][:, l * 256 + 64:l * 256 + 65]),
                            wait=[(tm, tps[-1])] + ([wao] if wao else []), inc=ve)
                        evf = P.op("v", lambda v, l=l, s=sbf, so=so, h=h:
                                   v.scalar_tensor_tensor(
                                       avout[:, so * 64:(so + 1) * 64],
                                       pBF[s][:, l * 256:l * 256 + 64], avrc[:, :],
                                       gab_slice(C, h), OP.mult, OP.mult),
                                   wait=[(ve, rc)], inc=ve, drain=True)
                        st = P.op("y", lambda y, so=so, qt=qt, h=h: y.dma_start(
                            C["attn_hbm"][qt * 128:(qt + 1) * 128,
                                          h * 64:(h + 1) * 64],
                            avout[:, so * 64:(so + 1) * 64]),
                            wait=[(ve, evf)], inc=aos[so], inck=16)
                        rAO.mark(so, aos[so], st)
                    rBF.mark(sbf, ve, evf)
            return finish

        def rp_a1_round(heads, g_):
            out = []
            for h in heads:
                for qt in range(4 * g_, 4 * g_ + 4):
                    out.append((h, qt) + rp_stage_a1(h, qt))
            return out

        def rp_a2_round(out):
            slp, wp_ = rBF.acquire()
            if wp_ is not None:
                P.wait("t", wp_[0], wp_[1])
            e65 = None
            for (h, qt, slr, slh, st, cc) in out:
                e65 = rp_stage_a2(h, qt, slp, cc, e65)
            rBF.mark(slp, ve, e65)
            return [(h, qt, slr, slh, st) for (h, qt, slr, slh, st, cc) in out]

        def rp_b1_round(infos):
            return [(h, qt, rp_stage_b1(h, qt, slr, slh, st))
                    for (h, qt, slr, slh, st) in infos]

        def rp_b2_round(infos, heads, g_):
            for (h, qt, ex) in infos:
                rp_stage_b2(h, qt, g_, 0 if h == heads[0] else 1, ex)

        for p in range(H // 2):
            heads = (2 * p, 2 * p + 1)
            pair_setup(p)
            prologue(2 * p)
            prologue(2 * p + 1)
            ia = rp_a1_round(heads, 0)
            ia = rp_a2_round(ia)
            ib = rp_b1_round(ia)
            rp_b2_round(ib, heads, 0)
            fin = C.pop("p2_fin", None)
            for g_ in range(NG):
                hooks = {}
                if fin is not None:
                    hooks[6] = fin
                if g_ < NG - 1:
                    state = {}

                    def hk1(g_=g_, state=state):
                        state["ia"] = rp_a1_round(heads, g_ + 1)

                    def hk1b(state=state):
                        ia = rp_a2_round(state["ia"])
                        state["ib"] = rp_b1_round(ia)

                    def hk2(g_=g_, state=state):
                        rp_b2_round(state["ib"], heads, g_ + 1)
                    hooks[12] = hk1
                    hooks[24] = hk1b
                    hooks[42] = hk2
                fin = body2(heads[0], heads[1], g_, hooks)
            C["p2_fin"] = fin
        C.pop("p2_fin")()
        P.replay(nc)


def br_b_ap(C):
    return C["br_b"][:, :]


def gab_slice(C, h):
    return C["gab"][:, h * 64:(h + 1) * 64]


def emit_gate(nc, C):
    P = Rec()
    tm, ve, sg = C["tm"], C["ve"], C["sg"]
    sm = C["SM"]
    smg = sm["gt"]
    h2T = C["h2T"]
    CAP = 512
    I32 = mybir.dt.int32
    with ExitStack() as _stk:
        wg_sb = _stk.enter_context(nc.sbuf_tensor("wg_sb", [128, 64], BF16))
        wg_lo = _stk.enter_context(nc.sbuf_tensor("wg_lo", [128, 64], BF16))
        wgf = _stk.enter_context(nc.sbuf_tensor("wgf", [128, 64], F32))
        bgc = _stk.enter_context(nc.sbuf_tensor("bgc", [E, 1], F32))
        lgT = _stk.enter_context(nc.sbuf_tensor("lgT", [E, SQ], F32))
        lgtok = _stk.enter_context(nc.sbuf_tensor("lgtok", [128, 128], F32))
        mx16 = _stk.enter_context(nc.sbuf_tensor("mx16", [128, 16], F32))
        onehot = _stk.enter_context(nc.sbuf_tensor("onehot", [128, 128], F32))
        onehT = _stk.enter_context(nc.sbuf_tensor("onehT", [E, SQ], F32))
        cum0 = _stk.enter_context(nc.sbuf_tensor("cum0", [E, SQ], F32))
        cum1 = _stk.enter_context(nc.sbuf_tensor("cum1", [E, SQ], F32))
        eoffc = _stk.enter_context(nc.sbuf_tensor("eoffc", [E, 1], F32))
        onesf = _stk.enter_context(nc.sbuf_tensor("onesf", [128, 1], F32))
        slotr = _stk.enter_context(nc.sbuf_tensor("slotr", [1, SQ], F32))
        slotf = _stk.enter_context(nc.sbuf_tensor("slotf", [128, 16], F32))
        sloti = _stk.enter_context(nc.sbuf_tensor("sloti", [128, 16], I32))
        iotas = _stk.enter_context(nc.sbuf_tensor("iotas", [128, 16], I32))
        dumps = _stk.enter_context(nc.sbuf_tensor("dumps", [128, 1], I32))
        _ps1 = ExitStack()
        pG = _ps1.enter_context(nc.psum_tensor("pG", [8, 2048], F32))
        pGT = _ps1.enter_context(nc.psum_tensor("pGT", [128, 512], F32))

        ldw = P.op("g", lambda g: g.dma_start(
            wgf[:, :], bass.AP(C["wg"].tensor, 0, [[8, 128], [1024, 8], [1, 8]])),
            inc=smg, inck=16)
        ldb = P.op("g", lambda g: g.dma_start(bgc[:, :], C["bg"]), inc=smg, inck=16)
        lde = P.op("g", lambda g: g.dma_start(eoffc[:, :], C["eoff_h"]),
                   inc=smg, inck=16)
        ldo = P.op("g", lambda g: g.dma_start(onesf[:, :], C["ones_fh"]),
                   inc=smg, inck=16)
        ldi = P.op("g", lambda g: g.dma_start(iotas[:, :], C["iota_h"]),
                   inc=smg, inck=16)
        ldd = P.op("g", lambda g: g.dma_start(dumps[:, :], C["dump_h"]),
                   inc=smg, inck=16)
        ld_all = smg.n
        lnv = C["p3_ln_done"]
        P.op("v", lambda v: v.tensor_copy(wg_sb[:, :], wgf[:, :]),
             wait=[(smg, ld_all)], inc=ve, drain=True)
        wglo_ev = P.op("v", lambda v: v.scalar_tensor_tensor(
            wg_lo[:, :], wg_sb[:, :], -1.0, wgf[:, :], OP.mult, OP.add),
            inc=ve, drain=True)
        h2lT = C["h2loT"]
        mm = None
        for ns in range(4):
            ops = [(wg_sb, h2T), (wg_lo, h2T), (wg_sb, h2lT)]
            nmm = len(ops) * 8
            i = 0
            for wop, hop in ops:
                for dc in range(8):
                    wt = [(smg, ld_all), (ve, lnv), (ve, wglo_ev)]                         if (ns == 0 and i == 0) else []
                    mm = P.op("t", lambda t, ns=ns, dc=dc, wop=wop, hop=hop,
                              st=(i == 0), sp=(i == nmm - 1): t.matmul(
                        pG[:, ns * 512:(ns + 1) * 512], wop[:, dc * 8:(dc + 1) * 8],
                        hop[:, dc * SQ + ns * 512: dc * SQ + (ns + 1) * 512],
                        start=st, stop=sp),
                        wait=wt, inc=(tm if i == nmm - 1 else None))
                    i += 1
        evg = P.op("v", lambda v: v.tensor_scalar(
            lgT[:, :], pG[:, :], bgc[:, :], None, OP.add),
            wait=[(tm, mm)], inc=ve)
        tps = []
        for tt in range(16):
            tp = P.op("t", lambda t, tt=tt: t.transpose(
                pGT[:, tt * 8:(tt + 1) * 8], lgT[:, tt * 128:(tt + 1) * 128],
                C["eye_f"][0:8, 0:8]), wait=([(ve, evg)] if tt == 0 else []),
                inc=tm)
            tps.append(tp)
        ev2 = P.op("v", lambda v: v.tensor_copy(lgtok[:, :], pGT[:, 0:128]),
                   wait=[(tm, tps[-1])], inc=ve)
        for tt in range(16):
            P.op("v", lambda v, tt=tt: v.tensor_reduce(
                mx16[:, tt:tt + 1], lgtok[:, tt * 8:(tt + 1) * 8], AX, OP.max),
                wait=([(ve, ev2)] if tt == 0 else []), inc=ve, drain=(tt == 0))
        P.drain("v")
        oh = None
        for tt in range(16):
            oh = P.op("v", lambda v, tt=tt: v.tensor_tensor(
                onehot[:, tt * 8:(tt + 1) * 8], lgtok[:, tt * 8:(tt + 1) * 8],
                mx16[:, tt:tt + 1].to_broadcast([128, 8]), OP.is_equal), inc=ve)
        _ps1.close()
        _ps2 = ExitStack()
        pOT = _ps2.enter_context(nc.psum_tensor("pOT", [8, 2048], F32))
        tps2 = []
        for tt in range(16):
            tp = P.op("t", lambda t, tt=tt: t.transpose(
                pOT[:, tt * 128:(tt + 1) * 128], onehot[:, tt * 8:(tt + 1) * 8],
                C["eye_f"][:, :]), wait=([(ve, oh)] if tt == 0 else []), inc=tm)
            tps2.append(tp)
        ev3 = P.op("v", lambda v: v.tensor_copy(onehT[:, :], pOT[:, :]),
                   wait=[(tm, tps2[-1])], inc=ve)
        a, b = onehT, cum0
        prev = ev3
        k = 1
        while k < SQ:
            P.op("v", lambda v, a=a, b=b, k=k: v.tensor_copy(
                b[:, 0:k], a[:, 0:k]), wait=[(ve, prev)], inc=ve)
            prev = P.op("v", lambda v, a=a, b=b, k=k: v.tensor_tensor(
                b[:, k:SQ], a[:, k:SQ], a[:, 0:SQ - k], OP.add), inc=ve)
            a, b = b, (cum1 if b is cum0 else cum0)
            if a is onehT:
                a = cum1
            k *= 2
        cum = a
        c1 = P.op("v", lambda v, cum=cum: v.tensor_scalar(
            cum[:, :], cum[:, :], float(CAP), eoffc[:, :], OP.min, OP.add),
            wait=[(ve, prev), (smg, ld_all)], inc=ve, drain=True)
        c2 = P.op("v", lambda v, cum=cum: v.tensor_tensor(
            cum[:, :], cum[:, :], onehT[:, :], OP.mult), inc=ve, drain=True)
        _ps2.close()
        _ps3 = ExitStack()
        pSL = _ps3.enter_context(nc.psum_tensor("pSL", [1, 2048], F32))
        pST = _ps3.enter_context(nc.psum_tensor("pST", [128, 512], F32))
        mms = None
        for ns in range(4):
            mms = P.op("t", lambda t, ns=ns, cum=cum: t.matmul(
                pSL[:, ns * 512:(ns + 1) * 512], onesf[0:8, :],
                cum[0:8, ns * 512:(ns + 1) * 512], start=True, stop=True),
                wait=([(ve, c2)] if ns == 0 else []), inc=tm)
        evs = P.op("v", lambda v: v.tensor_copy(slotr[:, :], pSL[:, :]),
                   wait=[(tm, mms)], inc=ve)
        tps3 = []
        for tt in range(16):
            tp = P.op("t", lambda t, tt=tt: t.transpose(
                pST[:, tt * 4:tt * 4 + 1] if False else pST[:, tt:tt + 1],
                slotr[:, tt * 128:(tt + 1) * 128], C["eye_f"][0:1, 0:1]),
                wait=([(ve, evs)] if tt == 0 else []), inc=tm)
            tps3.append(tp)
        cl = P.op("v", lambda v: v.tensor_scalar(
            slotf[:, :], pST[:, 0:16], 4095.0, 0.0, OP.min, OP.max),
            wait=[(tm, tps3[-1])], inc=ve)
        ci = P.op("v", lambda v: v.tensor_copy(sloti[:, :], slotf[:, :]),
                  wait=[(ve, cl)], inc=ve, drain=True)
        pf = None
        for k2 in range(32):
            pf = P.op("g", lambda g, k2=k2: g.dma_start(
                C["idx_hbm"][k2 * 128:(k2 + 1) * 128, :], dumps[:, :]),
                wait=([(smg, ld_all)] if k2 == 0 else []),
                inc=sm["rt0"], inck=16)
        P.wait("g", sm["rt0"], pf)
        sc = None
        for tt in range(16):
            sc = P.op("g", lambda g, tt=tt: g.indirect_dma_start(
                C["idx_hbm"],
                bass.IndirectOffsetOnAxis(ap=sloti[:, tt:tt + 1], axis=0),
                iotas[:, tt:tt + 1], None),
                wait=([(ve, ci)] if tt == 0 else []), inc=sm["rt1"], inck=16)
        C["idx_ready"] = sc
        _ps3.close()
        P.replay(nc)


def emit_ffn(nc, C):
    P = Rec()
    tm, ve, sg = C["tm"], C["ve"], C["sg"]
    sm = C["SM"]
    w1sm = (sm["w1a"], sm["w1b"])
    w2sm = sm["w2s"]
    smgx = sm["gx"]
    I32 = mybir.dt.int32
    CAP = 512
    NSL = 8 * CAP
    HTILES = NSL // 128
    with ExitStack() as _stk:
        w1s = _stk.enter_context(nc.sbuf_tensor("w1s", [128, 2 * 8192], BF16))
        w2s = _stk.enter_context(nc.sbuf_tensor("w2s", [128, 8192], BF16))
        h2pT = _stk.enter_context(nc.sbuf_tensor("h2pT", [128, 8 * 2048], BF16))
        hid = _stk.enter_context(nc.sbuf_tensor("hid", [128, 8 * 512], BF16))
        gst = _stk.enter_context(nc.sbuf_tensor("gst", [128, 2 * D], BF16))
        yst = _stk.enter_context(nc.sbuf_tensor("yst", [128, 2 * D], BF16))
        idxs = _stk.enter_context(nc.sbuf_tensor("idxs", [128, 32], I32))
        zt = _stk.enter_context(nc.sbuf_tensor("zt", [128, D], BF16))
        ysb = C["ysb"]
        ph0 = _stk.enter_context(nc.psum_tensor("ph0", [128, 512], F32))
        ph1 = _stk.enter_context(nc.psum_tensor("ph1", [128, 512], F32))
        po0 = _stk.enter_context(nc.psum_tensor("po0", [128, 512], F32))
        po1 = _stk.enter_context(nc.psum_tensor("po1", [128, 512], F32))
        pT0 = _stk.enter_context(nc.psum_tensor("pT0", [128, 1024], BF16))
        pT1 = _stk.enter_context(nc.psum_tensor("pT1", [128, 1024], BF16))
        pT = (pT0, pT1)

        ldx = None
        for k in range(32):
            ldx = P.op("g", lambda g, k=k: g.dma_start(
                idxs[:, k:k + 1], C["idx_hbm"][k * 128:(k + 1) * 128, :]),
                wait=([(sm["rt1"], C["idx_ready"])] if k == 0 else []),
                inc=smgx, inck=16)
        mz = P.op("v", lambda v: v.memset(zt[:, :], 0.0), inc=ve)
        pz = None
        for k in range(17):
            r0 = min(k * 128, SQ)
            pz = P.op("y", lambda y, r0=r0, k=k: y.dma_start(
                C["y_hbm"][r0:r0 + 128 if k < 16 else SQ + 1, :],
                zt[:, :] if k < 16 else zt[0:1, :]),
                wait=([(ve, mz)] if k == 0 else []), inc=sm["rt3"], inck=16)
        yzero = sm["rt3"].n
        ms = P.op("v", lambda v: v.memset(ysb[:, :], 0.0), inc=ve)

        gi_count = [0]
        w1ld, w2ld = {}, {}

        def stream_w(e, gg):
            gi = gi_count[0]
            sl1 = gi % 2
            w = [(tm, C[f"w1_free_{gi - 2}"])] if gi >= 2 else []
            w1ld[gi] = P.op("g", lambda g, e=e, gg=gg, sl1=sl1: g.dma_start(
                w1s[:, sl1 * 8192:(sl1 + 1) * 8192],
                bass.AP(C["w1"].tensor, e * D * HF + gg * 1024,
                        [[HF, 128], [128 * HF, 8], [1, 1024]])),
                wait=w, inc=w1sm[sl1], inck=16)
            w = [(tm, C[f"w2_free_{gi - 1}"])] if gi >= 1 else []
            w2ld[gi] = P.op("g", lambda g, e=e, gg=gg: g.dma_start(
                w2s[:, :],
                bass.AP(C["w2"].tensor, e * HF * D + gg * 1024 * D,
                        [[D, 128], [128 * D, 8], [1, 1024]])),
                wait=w, inc=w2sm, inck=16)
            gi_count[0] += 1
            return gi, sl1

        last_gelu = {}
        last_evac = {}
        for half in range(2):
            for k in range(16):
                tile = half * 16 + k
                wt = [(smgx, ldx), (sm["h2s"], C["h2s_all"])] if (k == 0 and half == 0) else []
                if tile >= 2:
                    wt.append((tm, C[f"gtp_{tile - 2}"]))
                gx = P.op("g", lambda g, tile=tile, k=k: g.indirect_dma_start(
                    gst[:, (k % 2) * D:(k % 2 + 1) * D], None, C["h2_hbm"],
                    bass.IndirectOffsetOnAxis(ap=idxs[:, tile:tile + 1], axis=0)),
                    wait=wt, inc=smgx, inck=16)
                tps = []
                for dc in range(8):
                    wt2 = [(smgx, gx)] if dc == 0 else []
                    if dc == 0 and tile >= 2:
                        wt2.append((ve, C[f"gx_ev_{tile - 2}"]))
                    tps.append(P.op("t", lambda t, dc=dc, k=k: t.transpose(
                        pT[k % 2][:, dc * 128:(dc + 1) * 128],
                        gst[:, (k % 2) * D + dc * 128:(k % 2) * D + (dc + 1) * 128],
                        C["eye_b"][:, :]), wait=wt2, inc=tm))
                C[f"gtp_{tile}"] = tps[-1]
                e2 = None
                for dc in range(8):
                    e2 = P.op("v", lambda v, dc=dc, k=k: v.tensor_copy(
                        h2pT[:, dc * 2048 + k * 128: dc * 2048 + (k + 1) * 128],
                        pT[k % 2][:, dc * 128:(dc + 1) * 128]),
                        wait=[(tm, tps[-1])], inc=ve)
                C[f"gx_ev_{tile}"] = e2
            gather_ev = ve.n
            for er in range(4):
                e = half * 4 + er
                for gg in range(4):
                    gi, sl1 = stream_w(e, gg)
                    for ht in range(8):
                        kk = gi * 8 + ht
                        wt = []
                        if ht == 0:
                            wt.append((w1sm[sl1], w1ld[gi]))
                            if gg == 0:
                                wt.append((ve, gather_ev))
                        if kk >= 2:
                            wt.append((sg, last_gelu[kk - 2]))
                        mm = None
                        for dc in range(8):
                            mm = P.op("t", lambda t, dc=dc, ht=ht, sl1=sl1,
                                      er=er: t.matmul(
                                          (ph0, ph1)[ht % 2][:, :],
                                          w1s[:, sl1 * 8192 + dc * 1024 +
                                              ht * 128:sl1 * 8192 + dc * 1024 +
                                              ht * 128 + 128],
                                          h2pT[:, dc * 2048 + er * 512:
                                               dc * 2048 + (er + 1) * 512],
                                          start=(dc == 0), stop=(dc == 7)),
                                      wait=(wt if dc == 0 else []),
                                      inc=(tm if dc == 7 else None))
                        if ht == 7:
                            C[f"w1_free_{gi}"] = tm.n
                        gl = P.op("s", lambda s, ht=ht: s.activation(
                            hid[:, ht * 512:(ht + 1) * 512],
                            (ph0, ph1)[ht % 2][:, :], AF.Gelu_apprx_tanh),
                            wait=[(tm, tm.n)], inc=sg)
                        last_gelu[kk] = gl
                    for dt in range(8):
                        kk2 = gi * 8 + dt
                        wt = [(sg, last_gelu[gi * 8 + 7])]
                        if dt == 0:
                            wt.append((w2sm, w2ld[gi]))
                        if kk2 >= 2:
                            wt.append((ve, last_evac[kk2 - 2]))
                        mm = None
                        for ht in range(8):
                            mm = P.op("t", lambda t, ht=ht, dt=dt: t.matmul(
                                (po0, po1)[dt % 2][:, :],
                                w2s[:, ht * 1024 + dt * 128:
                                    ht * 1024 + dt * 128 + 128],
                                hid[:, ht * 512:(ht + 1) * 512],
                                start=(ht == 0), stop=(ht == 7)),
                                wait=(wt if ht == 0 else []),
                                inc=(tm if ht == 7 else None))
                        if dt == 7:
                            C[f"w2_free_{gi}"] = tm.n
                        evv = P.op("v", lambda v, dt=dt, er=er: v.scalar_tensor_tensor(
                            ysb[:, dt * 2048 + er * 512: dt * 2048 + (er + 1) * 512],
                            (po0, po1)[dt % 2][:, :], C["gbt"][:, dt:dt + 1],
                            ysb[:, dt * 2048 + er * 512: dt * 2048 + (er + 1) * 512],
                            OP.mult, OP.add),
                            wait=[(tm, tm.n)] + ([(ve, ms)] if gi % 16 == 0 and dt == 0
                                                 else []), inc=ve)
                        last_evac[kk2] = evv
            ffn_half_ev = ve.n
            sev = {}
            ysc = {}
            last_ytp = None
            for k in range(16):
                tile = half * 16 + k
                tps = []
                for dc in range(8):
                    wt2 = []
                    if dc == 0:
                        wt2.append((ve, ffn_half_ev))
                        if k >= 2:
                            wt2.append((ve, sev[k - 2]))
                    tps.append(P.op("t", lambda t, dc=dc, k=k: t.transpose(
                        pT[k % 2][:, dc * 128:(dc + 1) * 128],
                        ysb[:, dc * 2048 + k * 128: dc * 2048 + (k + 1) * 128],
                        C["eye_b"][:, :]), wait=wt2, inc=tm))
                last_ytp = tps[-1]
                we2 = [(tm, tps[-1])]
                if k >= 2:
                    we2.append((sm["rt2"], ysc[k - 2]))
                e2 = P.op("v", lambda v, k=k: v.tensor_copy(
                    yst[:, (k % 2) * D:(k % 2 + 1) * D], pT[k % 2][:, :]),
                    wait=we2, inc=ve)
                sev[k] = e2
                wt3 = [(ve, e2)]
                if half == 0 and k == 0:
                    wt3.append((sm["rt3"], yzero))
                ysc[k] = P.op("g", lambda g, tile=tile, k=k: g.indirect_dma_start(
                    C["y_hbm"],
                    bass.IndirectOffsetOnAxis(ap=idxs[:, tile:tile + 1], axis=0),
                    yst[:, (k % 2) * D:(k % 2 + 1) * D], None),
                    wait=wt3, inc=sm["rt2"], inck=16)
            if half == 0:
                ms = P.op("v", lambda v: v.memset(ysb[:, :], 0.0),
                          wait=[(tm, last_ytp)], inc=ve)
        C["ysc_all"] = sm["rt2"].n
        P.replay(nc)


def emit_out(nc, C):
    P = Rec()
    tm, ve, sg = C["tm"], C["ve"], C["sg"]
    sm = C["SM"]
    xts = (sm["xt0"], sm["xt1"])
    sps = (sm["sp0"], sm["sp1"])
    yls = (sm["at0"], sm["at1"])
    with ExitStack() as _stk:
        x2t = _stk.enter_context(nc.sbuf_tensor("x2t", [128, 2 * D], F32))
        yt = _stk.enter_context(nc.sbuf_tensor("yt", [128, 2 * D], BF16))
        outt = _stk.enter_context(nc.sbuf_tensor("outt", [128, 2 * D], F32))
        for s in (sm["sp0"], sm["sp1"], sm["sp2"]):
            P.wait("y", s, s.n)
        P.wait("y", sm["rt2"], C["ysc_all"])
        ev_done = {}
        st_done = {}
        for tt in range(SQ // 128):
            sl = tt % 2
            w = [(ve, ev_done[tt - 2])] if tt >= 2 else []
            ld = P.op("y", lambda y, tt=tt, sl=sl: y.dma_start(
                x2t[:, sl * D:(sl + 1) * D],
                C["x2_hbm"][tt * 128:(tt + 1) * 128, :]),
                wait=w, inc=xts[sl], inck=16)
            ldy = P.op("y", lambda y, tt=tt, sl=sl: y.dma_start(
                yt[:, sl * D:(sl + 1) * D],
                C["y_hbm"][tt * 128:(tt + 1) * 128, :]),
                wait=[], inc=yls[sl], inck=16)
            wad = [(xts[sl], ld), (yls[sl], ldy)]
            if tt >= 2:
                wad.append((sps[sl], st_done[tt - 2]))
            ad = P.op("v", lambda v, sl=sl: v.tensor_tensor(
                outt[:, sl * D:(sl + 1) * D], yt[:, sl * D:(sl + 1) * D],
                x2t[:, sl * D:(sl + 1) * D], OP.add), wait=wad, inc=ve)
            ev_done[tt] = ad
            st_done[tt] = P.op("y", lambda y, tt=tt, sl=sl: y.dma_start(
                C["out"][tt * 128:(tt + 1) * 128, :],
                outt[:, sl * D:(sl + 1) * D]),
                wait=[(ve, ad)], inc=sps[sl], inck=16)
        for s in sm.values():
            P.wait("y", s, s.n)
        P.replay(nc)


_CACHED = {}


def kernel(L0_attn_x, L0_attn_Wqkv, L0_attn_bqkv, L0_attn_Wr, L0_attn_br,
           L0_ffn_W1, L0_ffn_W2, L0_ffn_Wg, L0_ffn_bg, L0_c, L0_wmod):
    if "nc" not in _CACHED:
        _CACHED["nc"] = build_program()
    nc = _CACHED["nc"]

    eye = np.eye(128)
    di = np.arange(128)
    tri = dict(
        eye_bh=eye.astype(BFNP),
        eye_fh=eye.astype(np.float32),
        ones_fh=np.ones((128, 1), dtype=np.float32),
        iota_h=np.ascontiguousarray(
            np.arange(SQ, dtype=np.int32).reshape(16, 128).T),
        dump_h=np.full((128, 1), SQ, dtype=np.int32),
        eoff_h=(np.arange(8, dtype=np.float32) * 512 - 1).reshape(8, 1),
        tri_geh=(di[None, :] >= di[:, None]).astype(np.float32),
        tri_lth=(di[None, :] < di[:, None]).astype(np.float32),
        tri_leh=(di[None, :] <= di[:, None]).astype(np.float32),
        tri_gth=(di[None, :] > di[:, None]).astype(np.float32),
        ones_h=np.ones((128, 1), dtype=BFNP),
    )
    wr_f = np.ascontiguousarray(L0_attn_Wr)
    br_f = np.ascontiguousarray(L0_attn_br).reshape(1, R)
    wr_r = np.ascontiguousarray(L0_attn_Wr[::-1])
    br_r = np.ascontiguousarray(L0_attn_br[::-1]).reshape(1, R)
    shared = dict(
        wqkv=np.ascontiguousarray(L0_attn_Wqkv),
        bqkv=np.ascontiguousarray(L0_attn_bqkv).reshape(1, 3 * D),
        w1=np.ascontiguousarray(L0_ffn_W1),
        w2=np.ascontiguousarray(L0_ffn_W2),
        wg=np.ascontiguousarray(L0_ffn_Wg),
        bg=np.ascontiguousarray(L0_ffn_bg).reshape(E, 1),
        wmod=np.ascontiguousarray(L0_wmod),
        **tri,
    )
    in_maps = []
    for c in range(8):
        b = c // 2
        rev = c % 2 == 1
        xbm = L0_attn_x[b] if not rev else L0_attn_x[b][::-1]
        m = dict(shared)
        m["xb"] = np.ascontiguousarray(xbm)
        m["wr"] = wr_r if rev else wr_f
        m["br"] = br_r if rev else br_f
        m["crow"] = np.ascontiguousarray(L0_c[b]).reshape(1, D)
        in_maps.append(m)

    res = run_bass_kernel_spmd(nc, in_maps, core_ids=list(range(8)))
    out = np.empty((4, S, D), dtype=np.float32)
    for c in range(8):
        b = c // 2
        o = np.asarray(res.results[c]["out"])
        if c % 2 == 0:
            out[b, 0:SQ] = o
        else:
            out[b, SQ:S] = o[::-1]
    return out


# version 12
# speedup vs baseline: 1.0868x; 1.0868x over previous
import numpy as np
from contextlib import ExitStack
import ml_dtypes
import concourse.bass as bass
import concourse.mybir as mybir
from concourse.bass_utils import run_bass_kernel_spmd

BFNP = ml_dtypes.bfloat16
F32 = mybir.dt.float32
BF16 = mybir.dt.bfloat16
AF = mybir.ActivationFunctionType
OP = mybir.AluOpType
AX = mybir.AxisListType.X

D = 1024
S = 4096
SQ = 2048
H = 16
HD = 64
R = 257
E = 8
HF = 4096
NQT = SQ // 128
NJT = S // 128
NG = 4


class Ctr:
    def __init__(self, handle):
        self.h = handle
        self.n = 0


class Rec:
    ENG = ("t", "v", "s", "g", "y")

    def __init__(self):
        self.q = {e: [] for e in self.ENG}

    def op(self, e, fn, wait=(), inc=None, inck=1, drain=False):
        w = tuple((c.h if isinstance(c, Ctr) else c, int(v)) for c, v in wait)
        i = None
        if inc is not None:
            inc.n += inck
            i = (inc.h, inck)
        self.q[e].append((w, fn, i, drain))
        return inc.n if inc is not None else None

    def wait(self, e, c, v):
        self.op(e, None, wait=[(c, v)])

    def drain(self, e):
        self.op(e, None, drain=True)

    def replay(self, nc, barrier=True):
        with nc.Block() as block:
            dec = dict(t=block.tensor, v=block.vector, s=block.scalar,
                       g=block.gpsimd, y=block.sync)
            for e in self.ENG:
                items = self.q[e]
                if not items:
                    continue

                def body(engine, items=items):
                    for w, fn, i, dr in items:
                        if dr:
                            engine.drain()
                        for (h, val) in w:
                            engine.wait_ge(h, val)
                        inst = fn(engine) if fn is not None else None
                        if i is not None:
                            inst.then_inc(i[0], i[1])
                dec[e](body)
            self.q = {e: [] for e in self.ENG}
        if barrier:
            nc.all_engine_barrier()


class Ring:
    def __init__(self, nslots):
        self.free_at = [None] * nslots
        self.i = 0

    def acquire(self):
        s = self.i % len(self.free_at)
        self.i += 1
        return s, self.free_at[s]

    def mark(self, slot, ctr, val):
        self.free_at[slot] = (ctr, int(val))


def build_program(dbg=()):
    nc = bass.Bass("TRN2", target_bir_lowering=False,
                   detect_race_conditions=False)
    dbg = set(dbg)

    def dram(name, shape, dt, kind):
        if kind == "Internal" and name in dbg:
            kind = "ExternalOutput"
        return nc.dram_tensor(name, shape, dt, kind=kind).ap()

    C = {}
    C["xb"] = dram("xb", [S, D], F32, "ExternalInput")
    C["wqkv"] = dram("wqkv", [3 * D, D], F32, "ExternalInput")
    C["bqkv"] = dram("bqkv", [1, 3 * D], F32, "ExternalInput")
    C["wr"] = dram("wr", [R, HD], F32, "ExternalInput")
    C["br"] = dram("br", [1, R], F32, "ExternalInput")
    C["w1"] = dram("w1", [E, D, HF], F32, "ExternalInput")
    C["w2"] = dram("w2", [E, HF, D], F32, "ExternalInput")
    C["wg"] = dram("wg", [D, E], F32, "ExternalInput")
    C["bg"] = dram("bg", [E, 1], F32, "ExternalInput")
    C["crow"] = dram("crow", [1, D], F32, "ExternalInput")
    C["wmod"] = dram("wmod", [D, 6 * D], F32, "ExternalInput")
    C["eye_bh"] = dram("eye_bh", [128, 128], BF16, "ExternalInput")
    C["eye_fh"] = dram("eye_fh", [128, 128], F32, "ExternalInput")
    C["tri_geh"] = dram("tri_geh", [128, 128], F32, "ExternalInput")
    C["tri_lth"] = dram("tri_lth", [128, 128], F32, "ExternalInput")
    C["tri_leh"] = dram("tri_leh", [128, 128], F32, "ExternalInput")
    C["tri_gth"] = dram("tri_gth", [128, 128], F32, "ExternalInput")
    C["ones_h"] = dram("ones_h", [128, 1], BF16, "ExternalInput")
    C["ones_fh"] = dram("ones_fh", [128, 1], F32, "ExternalInput")
    C["iota_h"] = dram("iota_h", [128, 16], mybir.dt.int32, "ExternalInput")
    C["dump_h"] = dram("dump_h", [128, 1], mybir.dt.int32, "ExternalInput")
    C["eoff_h"] = dram("eoff_h", [8, 1], F32, "ExternalInput")
    C["out"] = dram("out", [SQ, D], F32, "ExternalOutput")
    C["rp_hbm"] = dram("rp_hbm", [8, 128, R], F32, "Internal")
    C["attn_hbm"] = dram("attn_hbm", [SQ, D], BF16, "Internal")
    C["x2_hbm"] = dram("x2_hbm", [SQ, D], F32, "Internal")
    C["mod_hbm"] = dram("mod_hbm", [1, 6 * D], F32, "Internal")
    C["h2_hbm"] = dram("h2_hbm", [SQ + 1, D], BF16, "Internal")
    C["y_hbm"] = dram("y_hbm", [SQ + 1, D], BF16, "Internal")
    C["idx_hbm"] = dram("idx_hbm", [4096, 1], mybir.dt.int32, "Internal")

    with ExitStack() as _stk:
        eye_b = _stk.enter_context(nc.sbuf_tensor("eye_b", [128, 128], BF16))
        eye_f = _stk.enter_context(nc.sbuf_tensor("eye_f", [128, 128], F32))
        tri_ge = _stk.enter_context(nc.sbuf_tensor("tri_ge", [128, 128], F32))
        tri_lt = _stk.enter_context(nc.sbuf_tensor("tri_lt", [128, 128], F32))
        tri_le = _stk.enter_context(nc.sbuf_tensor("tri_le", [128, 128], F32))
        tri_gt = _stk.enter_context(nc.sbuf_tensor("tri_gt", [128, 128], F32))
        ones_b = _stk.enter_context(nc.sbuf_tensor("ones_b", [128, 1], BF16))
        wrt8 = _stk.enter_context(nc.sbuf_tensor("wrt8", [HD, R], BF16))
        br_b = _stk.enter_context(nc.sbuf_tensor("br_b", [128, R], F32))
        ca1b = _stk.enter_context(nc.sbuf_tensor("ca1b", [128, D], F32))
        sab = _stk.enter_context(nc.sbuf_tensor("sab", [128, D], F32))
        gab = _stk.enter_context(nc.sbuf_tensor("gab", [128, D], F32))
        cb1b = _stk.enter_context(nc.sbuf_tensor("cb1b", [128, D], F32))
        sbb = _stk.enter_context(nc.sbuf_tensor("sbb", [128, D], F32))
        gbt = _stk.enter_context(nc.sbuf_tensor("gbt", [128, E], F32))
        _tm = _stk.enter_context(nc.semaphore("t_mm"))
        _ve = _stk.enter_context(nc.semaphore("v_ev"))
        _sg = _stk.enter_context(nc.semaphore("s_ge"))
        SM = {}
        for name in ("c", "wr", "wm0", "wm1", "xt0", "xt1", "xt2",
                     "at0", "at1", "sp0", "sp1", "sp2", "wst", "bias",
                     "rpw", "sk0", "sk1", "sk2", "sk3", "ao0", "ao1", "gt", "mask",
                     "w1a", "w1b", "w2s", "h1d", "cc", "mb",
                     "h2s", "rt0", "rt1", "rt2", "rt3", "gx"):
            SM[name] = Ctr(_stk.enter_context(nc.semaphore("sm_" + name)))
        C.update(eye_b=eye_b, eye_f=eye_f, tri_ge=tri_ge, tri_lt=tri_lt, tri_le=tri_le,
                 tri_gt=tri_gt, ones_b=ones_b, wrt8=wrt8, br_b=br_b,
                 ca1b=ca1b, sab=sab, gab=gab, cb1b=cb1b, sbb=sbb, gbt=gbt,
                 tm=Ctr(_tm), ve=Ctr(_ve), sg=Ctr(_sg), SM=SM)

        if "DUMP_PT" in dbg:
            C["DUMP_PT"] = True
        if "NO_POS" in dbg:
            C["NO_POS"] = True
        with nc.Block() as _blk0:
            @_blk0.gpsimd
            def _(g):
                for s in list(SM.values()):
                    g.sem_clear(s.h)
                g.sem_clear(_tm)
                g.sem_clear(_ve)
                g.sem_clear(_sg)
        nc.all_engine_barrier()
        emit_p0(nc, C)
        with nc.sbuf_tensor("h1T", [128, 8 * S], BF16) as h1T, \
             nc.sbuf_tensor("stats_sb", [128, 32 * 8], F32) as _ssb:
            C["h1T"] = h1T
            if "DUMP_H1T" in dbg:
                C["stats_dump"] = _ssb
            emit_ln(nc, C, first=True)
            if "DUMP_H1T" in dbg:
                dmp = nc.dram_tensor("h1T_dump", [128, 8 * S], BF16,
                                     kind="ExternalOutput").ap()
                sdmp_h = nc.dram_tensor("stats_dump", [128, 32 * 8], F32,
                                        kind="ExternalOutput").ap()
                dmp2 = nc.dram_tensor("bc_dump", [128, 3 * D], F32,
                                      kind="ExternalOutput").ap()
                Pd = Rec()
                Pd.wait("y", C["ve"], C["p1_done"])
                Pd.op("y", lambda y: y.dma_start(dmp, h1T[:, :]),
                      inc=C["SM"]["h1d"], inck=16)
                Pd.op("y", lambda y: y.dma_start(dmp2[:, 0:D], C["ca1b"][:, :]),
                      inc=C["SM"]["h1d"], inck=16)
                Pd.op("y", lambda y: y.dma_start(dmp2[:, D:2 * D], C["sab"][:, :]),
                      inc=C["SM"]["h1d"], inck=16)
                Pd.op("y", lambda y: y.dma_start(dmp2[:, 2 * D:3 * D], C["gab"][:, :]),
                      inc=C["SM"]["h1d"], inck=16)
                Pd.op("y", lambda y: y.dma_start(sdmp_h, C["stats_dump"][:, :]),
                      inc=C["SM"]["h1d"], inck=16)
                Pd.wait("y", C["SM"]["h1d"], C["SM"]["h1d"].n)
                Pd.replay(nc)
            if "STOP_P1" not in dbg:
                emit_attn(nc, C)
                if "STOP_P2" in dbg:
                    Pd = Rec()
                    Pd.wait("y", C["SM"]["ao0"], C["SM"]["ao0"].n)
                    Pd.wait("y", C["SM"]["ao1"], C["SM"]["ao1"].n)
                    Pd.replay(nc)
        if "STOP_P1" in dbg or "STOP_P2" in dbg:
            return nc
        with nc.sbuf_tensor("h2T", [128, 8 * SQ], BF16) as h2T, \
                nc.sbuf_tensor("h2loT", [128, 8 * SQ], BF16) as h2loT:
            C["h2T"] = h2T
            C["h2loT"] = h2loT
            emit_ln(nc, C, first=False)
            emit_gate(nc, C)
            if "STOP_P3" in dbg:
                Pd = Rec()
                for s in C["SM"].values():
                    Pd.wait("y", s, s.n)
                Pd.replay(nc)
                return nc
        with nc.sbuf_tensor("ysb", [128, 8 * SQ], BF16) as ysb:
            C["ysb"] = ysb
            emit_ffn(nc, C)
            emit_out(nc, C)
    return nc


def emit_p0(nc, C):
    P = Rec()
    tm, ve, sg = C["tm"], C["ve"], C["sg"]
    sm = C["SM"]
    smc, smwr, wm = sm["c"], sm["wr"], (sm["wm0"], sm["wm1"])
    smcc, smmb = sm["cc"], sm["mb"]

    for dst, src in [(C["eye_b"], C["eye_bh"]), (C["eye_f"], C["eye_fh"]),
                     (C["tri_ge"], C["tri_geh"]),
                     (C["tri_lt"], C["tri_lth"]), (C["tri_le"], C["tri_leh"]),
                     (C["tri_gt"], C["tri_gth"]), (C["ones_b"], C["ones_h"])]:
        P.op("g", lambda g, d=dst, s=src: g.dma_start(d[:, :], s), inc=smc, inck=16)
    P.op("g", lambda g: g.dma_start(C["br_b"][:, :],
                                    C["br"].partition_broadcast(128)),
         inc=smc, inck=16)
    ld_consts = smc.n

    with ExitStack() as _stk:
        wr_st = _stk.enter_context(nc.sbuf_tensor("wr_st", [128, 128], BF16))
        c_sb = _stk.enter_context(nc.sbuf_tensor("c_sb", [1, D], F32))
        cs_b = _stk.enter_context(nc.sbuf_tensor("cs_b", [1, D], BF16))
        csT = _stk.enter_context(nc.sbuf_tensor("csT", [128, 8], BF16))
        wmod_sb = _stk.enter_context(nc.sbuf_tensor("wmod_sb", [128, 2 * 1536], BF16))
        modrow = _stk.enter_context(nc.sbuf_tensor("modrow", [1, 6 * D], F32))
        g_row = _stk.enter_context(nc.sbuf_tensor("g_row", [1, D], F32))
        pP0f = _stk.enter_context(nc.psum_tensor("pP0f", [128, 512], F32))
        pP0b = _stk.enter_context(nc.psum_tensor("pP0b", [128, 1024], BF16))
        pmod = _stk.enter_context(nc.psum_tensor("pmod", [1, 1536], F32))
        P.wait("t", smc, ld_consts)
        last_tp = None
        last_evw = None
        for ch in range(3):
            r0, r1 = [(0, 128), (128, 256), (256, 257)][ch]
            n = r1 - r0
            w = [(tm, last_tp)] if last_tp is not None else []
            ld = P.op("g", lambda g, r0=r0, r1=r1, n=n: g.dma_start(
                wr_st[0:n, 0:HD], C["wr"][r0:r1, :]), wait=w, inc=smwr, inck=16)
            wtp = [(smwr, ld)]
            if last_evw is not None:
                wtp.append((ve, last_evw))
            tp = P.op("t", lambda t, n=n, ch=ch: t.transpose(
                pP0b[0:HD, ch * 128:ch * 128 + n], wr_st[0:n, 0:HD],
                C["eye_b"][0:n, 0:n]), wait=wtp, inc=tm)
            last_evw = P.op("v", lambda v, ch=ch, n=n: v.tensor_scalar(
                C["wrt8"][:, ch * 128:ch * 128 + n],
                pP0b[0:HD, ch * 128:ch * 128 + n], 8.0, None, OP.mult),
                wait=[(tm, tp)], inc=ve)
            last_tp = tp

        ldc = P.op("g", lambda g: g.dma_start(c_sb[:, :], C["crow"]),
                   inc=smcc, inck=16)
        sil = P.op("s", lambda s: s.activation(cs_b[:, :], c_sb[:, :], AF.Silu),
                   wait=[(smcc, ldc)], inc=sg)
        tpc = None
        for dc in range(8):
            wtc = [(sg, sil), (ve, last_evw)] if dc == 0 else []
            tpc = P.op("t", lambda t, dc=dc: t.transpose(
                pP0b[:, 384 + 4 * dc:385 + 4 * dc],
                cs_b[:, dc * 128:(dc + 1) * 128],
                C["eye_b"][0:1, 0:1]), wait=wtc, inc=tm)
        evc = P.op("v", lambda v: v.tensor_copy(csT[:, :], pP0b[:, 384:416:4]),
                   wait=[(tm, tpc)], inc=ve)

        mm_done = {}
        ev_prev = None
        for q4 in range(4):
            for dc in range(8):
                k = q4 * 8 + dc
                bo = (k % 2) * 1536
                w = [(tm, mm_done[k - 2])] if k >= 2 else []
                ld = P.op("g", lambda g, q4=q4, dc=dc, bo=bo: g.dma_start(
                    wmod_sb[:, bo:bo + 1536],
                    C["wmod"][dc * 128:(dc + 1) * 128,
                              q4 * 1536:(q4 + 1) * 1536]),
                    wait=w, inc=wm[k % 2], inck=16)
                mm = None
                for ns in range(3):
                    wmm = []
                    if ns == 0:
                        wmm.append((wm[k % 2], ld))
                        if dc == 0:
                            wmm.append((ve, evc if ev_prev is None else ev_prev))
                    mm = P.op("t", lambda t, dc=dc, bo=bo, ns=ns: t.matmul(
                        pmod[:, ns * 512:(ns + 1) * 512], csT[:, dc:dc + 1],
                        wmod_sb[:, bo + ns * 512:bo + (ns + 1) * 512],
                        start=(dc == 0), stop=(dc == 7)),
                        wait=wmm, inc=(tm if ns == 2 else None))
                mm_done[k] = tm.n
            ev_prev = P.op("v", lambda v, q4=q4: v.tensor_copy(
                modrow[:, q4 * 1536:(q4 + 1) * 1536], pmod[:, :]),
                wait=[(tm, mm_done[q4 * 8 + 7])], inc=ve)

        def seg(i):
            return modrow[:, i * D:(i + 1) * D]
        P.op("v", lambda v: v.tensor_scalar(seg(1), seg(1), 1.0, None, OP.add),
             wait=[(ve, ev_prev)], inc=ve)
        P.op("v", lambda v: v.tensor_scalar(seg(4), seg(4), 1.0, None, OP.add),
             inc=ve)
        tlast = None
        for i in (2, 5):
            ab = P.op("s", lambda s, i=i: s.activation(
                g_row[:, :], seg(i), AF.Abs), wait=[(ve, ve.n)], inc=sg)
            P.op("v", lambda v, i=i: v.tensor_scalar(
                g_row[:, :], g_row[:, :], 0.25, None, OP.max),
                wait=[(sg, ab)], inc=ve)
            P.op("v", lambda v, i=i: v.tensor_scalar(
                seg(i), seg(i), 0.0, 2.0, OP.is_ge, OP.mult), inc=ve)
            P.op("v", lambda v, i=i: v.tensor_scalar(
                seg(i), seg(i), -1.0, None, OP.add), inc=ve)
            tlast = P.op("v", lambda v, i=i: v.tensor_tensor(
                seg(i), seg(i), g_row[:, :], OP.mult), inc=ve)
        st = P.op("g", lambda g: g.dma_start(C["mod_hbm"], modrow[:, :]),
                  wait=[(ve, tlast)], inc=smmb, inck=16)
        first_b = True
        for dst, i in [(C["sab"], 0), (C["ca1b"], 1), (C["gab"], 2),
                       (C["sbb"], 3), (C["cb1b"], 4)]:
            P.op("g", lambda g, d=dst, i=i: g.dma_start(
                d[:, :], C["mod_hbm"][:, i * D:(i + 1) * D].partition_broadcast(128)),
                wait=([(smmb, st)] if first_b else []), inc=smmb, inck=16)
            first_b = False
        C["p0_bcast_c"] = smmb.n
        tb = None
        for dc in range(8):
            tb = P.op("t", lambda t, dc=dc: t.transpose(
                pP0f[:, dc:dc + 1], seg(5)[:, dc * 128:(dc + 1) * 128],
                C["eye_f"][0:1, 0:1]),
                wait=[(ve, tlast)], inc=tm)
        P.op("v", lambda v: v.tensor_copy(C["gbt"][:, :], pP0f[:, 0:8]),
             wait=[(tm, tb)], inc=ve)
        P.wait("v", smc, ld_consts)
        P.replay(nc)


def emit_ln(nc, C, first):
    P = Rec()
    tm, ve, sg = C["tm"], C["ve"], C["sg"]
    sm = C["SM"]
    xts = (sm["xt0"], sm["xt1"], sm["xt2"])
    ats = (sm["at0"], sm["at1"])
    sps = (sm["sp0"], sm["sp1"], sm["sp2"])
    ntt = (S if first else SQ) // 128
    dstT = C["h1T"] if first else C["h2T"]
    seqw = S if first else SQ
    cab = C["ca1b"] if first else C["cb1b"]
    sab = C["sab"] if first else C["sbb"]
    stats_dmp = C.get("stats_dump") if first else None
    if first:
        P.wait("v", sm["mb"], C["p0_bcast_c"])
    else:
        P.wait("y", sm["ao0"], sm["ao0"].n)
        P.wait("y", sm["ao1"], sm["ao1"].n)

    sfx = "1" if first else "2"
    with ExitStack() as _stk:
        xt = _stk.enter_context(nc.sbuf_tensor("xt" + sfx, [128, 3 * D], F32))
        at = _stk.enter_context(nc.sbuf_tensor("at" + sfx, [128, 2 * D], BF16))
        xsq = _stk.enter_context(nc.sbuf_tensor("xsq" + sfx, [128, D], F32))
        stats = _stk.enter_context(nc.sbuf_tensor("stats" + sfx, [128, 8], F32))
        hst = _stk.enter_context(nc.sbuf_tensor("hst" + sfx, [128, 2 * D], BF16))
        hstF = _stk.enter_context(nc.sbuf_tensor("hstF" + sfx, [128, D], F32))
        hstLo = _stk.enter_context(nc.sbuf_tensor("hstLo" + sfx, [128, 2 * D], BF16))
        pL0 = _stk.enter_context(nc.psum_tensor("pL0" + sfx, [128, 1024], BF16))
        pL1 = _stk.enter_context(nc.psum_tensor("pL1" + sfx, [128, 1024], BF16))
        pL2 = _stk.enter_context(nc.psum_tensor("pL2" + sfx, [128, 1024], BF16))
        pL3 = _stk.enter_context(nc.psum_tensor("pL3" + sfx, [128, 1024], BF16))
        ev_done = {}
        tp_done = {}
        tp_last = {}
        spill_ds = {}
        sq2_prev = [None]
        h2sp = {}
        for tt in range(ntt):
            sl = tt % 3
            w = [(ve, ev_done[tt - 3])] if tt >= 3 else []
            if (not first) and tt >= 3:
                w.append((sps[sl], spill_ds[tt - 3]))
            ld = P.op("y", lambda y, tt=tt, sl=sl: y.dma_start(
                xt[:, sl * D:(sl + 1) * D], C["xb"][tt * 128:(tt + 1) * 128, :]),
                wait=w, inc=xts[sl], inck=16)
            if not first:
                w2 = [(ve, ev_done[tt - 2])] if tt >= 2 else []
                lda = P.op("y", lambda y, tt=tt: y.dma_start(
                    at[:, (tt % 2) * D:(tt % 2 + 1) * D],
                    C["attn_hbm"][tt * 128:(tt + 1) * 128, :]),
                    wait=w2, inc=ats[tt % 2], inck=16)
                add = P.op("v", lambda v, sl=sl, tt=tt: v.tensor_tensor(
                    xt[:, sl * D:(sl + 1) * D], xt[:, sl * D:(sl + 1) * D],
                    at[:, (tt % 2) * D:(tt % 2 + 1) * D], OP.add),
                    wait=[(xts[sl], ld), (ats[tt % 2], lda)], inc=ve)
                spill_ds[tt] = P.op("y", lambda y, tt=tt, sl=sl: y.dma_start(
                    C["x2_hbm"][tt * 128:(tt + 1) * 128, :],
                    xt[:, sl * D:(sl + 1) * D]),
                    wait=[(ve, add)], inc=sps[sl], inck=16)
                ready_v = [(ve, add)]
                ready_s = [(ve, add)]
            else:
                ready_v = [(xts[sl], ld)]
                ready_s = [(xts[sl], ld)]
            P.op("v", lambda v, sl=sl: v.tensor_reduce(
                stats[:, 0:1], xt[:, sl * D:(sl + 1) * D], AX, OP.add),
                wait=ready_v, inc=ve)
            sq = P.op("s", lambda s, sl=sl: s.activation(
                xsq[:, :], xt[:, sl * D:(sl + 1) * D], AF.Square),
                wait=ready_s + ([(ve, ev_done[tt - 1])] if tt >= 1 else []),
                inc=sg)
            ws2 = [(sg, sq)]
            if tt >= 1:
                ws2.append((sg, sq2_prev[0]))
            P.op("v", lambda v: v.tensor_reduce(
                stats[:, 1:2], xsq[:, :], AX, OP.add), wait=ws2, inc=ve)
            P.op("v", lambda v: v.tensor_scalar(
                stats[:, 0:1], stats[:, 0:1], 1.0 / D, None, OP.mult), inc=ve,
                drain=True)
            P.op("v", lambda v: v.tensor_scalar(
                stats[:, 1:2], stats[:, 1:2], 1.0 / D, None, OP.mult), inc=ve,
                drain=True)
            P.op("v", lambda v: v.tensor_tensor(
                stats[:, 2:3], stats[:, 0:1], stats[:, 0:1], OP.mult), inc=ve,
                drain=True)
            v5 = P.op("v", lambda v: v.tensor_scalar(
                stats[:, 1:2], stats[:, 1:2], stats[:, 2:3], 1e-5,
                OP.subtract, OP.add), inc=ve, drain=True)
            sq2 = P.op("s", lambda s: s.activation(
                stats[:, 3:4], stats[:, 1:2], AF.Sqrt), wait=[(ve, v5)], inc=sg)
            sq2_prev[0] = sq2
            v6 = P.op("v", lambda v: v.reciprocal(
                stats[:, 4:5], stats[:, 3:4]), wait=[(sg, sq2)], inc=ve)
            if stats_dmp is not None:
                P.op("v", lambda v, tt=tt: v.tensor_copy(
                    stats_dmp[:, tt * 8:tt * 8 + 8], stats[:, :]), inc=ve)
            P.op("v", lambda v, sl=sl: v.tensor_scalar(
                xsq[:, :], xt[:, sl * D:(sl + 1) * D], stats[:, 0:1],
                stats[:, 4:5], OP.subtract, OP.mult), wait=[(ve, v6)], inc=ve,
                drain=True)
            P.op("v", lambda v: v.tensor_tensor(
                xsq[:, :], xsq[:, :], cab[:, :], OP.mult), inc=ve)
            hsl = (tt % 2) * D
            wa3 = [(tm, tp_last[tt - 2])] if tt >= 2 else []
            if (not first) and tt >= 2:
                wa3.append((sm["h2s"], h2sp[tt - 2]))
            if first:
                a3 = P.op("v", lambda v, hsl=hsl: v.tensor_tensor(
                    hst[:, hsl:hsl + D], xsq[:, :], sab[:, :], OP.add),
                    wait=wa3, inc=ve)
            else:
                P.op("v", lambda v: v.tensor_tensor(
                    hstF[:, :], xsq[:, :], sab[:, :], OP.add),
                    wait=wa3, inc=ve)
                P.op("v", lambda v, hsl=hsl: v.tensor_copy(
                    hst[:, hsl:hsl + D], hstF[:, :]), inc=ve, drain=True)
                a3 = P.op("v", lambda v, hsl=hsl: v.scalar_tensor_tensor(
                    hstLo[:, hsl:hsl + D], hst[:, hsl:hsl + D], -1.0,
                    hstF[:, :], OP.mult, OP.add), inc=ve, drain=True)
            ev_done[tt] = a3
            if not first:
                h2sp[tt] = P.op("y", lambda y, tt=tt, hsl=hsl: y.dma_start(
                    C["h2_hbm"][tt * 128:(tt + 1) * 128, :],
                    hst[:, hsl:hsl + D]),
                    wait=[(ve, a3)], inc=sm["h2s"], inck=16)
            pps = [(pL0, pL1)[tt % 2], (pL2, pL3)[tt % 2]]
            tps = []
            srcs = [(hst, dstT)] if first else [(hst, dstT), (hstLo, C["h2loT"])]
            for si, (sbuf_src, _dst) in enumerate(srcs):
                for dc in range(8):
                    wt = [(ve, a3)] if (dc == 0 and si == 0) else []
                    if dc == 0 and si == 0 and tt >= 2:
                        wt.append((ve, tp_done[tt - 2]))
                    tp = P.op("t", lambda t, dc=dc, pp=pps[si], hsl=hsl,
                              s=sbuf_src: t.transpose(
                                  pp[:, dc * 128:(dc + 1) * 128],
                                  s[:, hsl + dc * 128:hsl + (dc + 1) * 128],
                                  C["eye_b"][:, :]),
                              wait=wt, inc=tm)
                    tps.append(tp)
            tp_last[tt] = tps[-1]
            e2 = None
            for si, (sbuf_src, dst_) in enumerate(srcs):
                for dc in range(8):
                    e2 = P.op("v", lambda v, dc=dc, tt=tt, pp=pps[si], d=dst_:
                              v.tensor_copy(
                                  d[:, dc * seqw + tt * 128:
                                    dc * seqw + (tt + 1) * 128],
                                  pp[:, dc * 128:(dc + 1) * 128]),
                              wait=[(tm, tps[(si + 1) * 8 - 1])], inc=ve)
            tp_done[tt] = e2
        C["p1_done" if first else "p3_ln_done"] = ve.n
        if not first:
            zr = P.op("v", lambda v: v.memset(hst[0:1, 0:D], 0.0),
                      wait=[(sm["h2s"], h2sp[ntt - 1]),
                            (sm["h2s"], h2sp[ntt - 2])], inc=ve)
            P.op("y", lambda y: y.dma_start(C["h2_hbm"][SQ:SQ + 1, :],
                                            hst[0:1, 0:D]),
                 wait=[(ve, zr)], inc=sm["h2s"], inck=16)
            C["h2s_all"] = sm["h2s"].n
        P.replay(nc)


def emit_attn(nc, C):
    P = Rec()
    tm, ve, sg = C["tm"], C["ve"], C["sg"]
    sm = C["SM"]
    smb, smw, smr = sm["bias"], sm["wst"], sm["rpw"]
    sks = (sm["sk0"], sm["sk1"], sm["sk2"], sm["sk3"])
    aos = (sm["ao0"], sm["ao1"])
    h1T, eye_b = C["h1T"], C["eye_b"]
    if "DUMP_PT" in C:
        C["pt_dump"] = nc.dram_tensor("pt_dump", [NJT, 128, 512], BF16,
                                      kind="ExternalOutput").ap()

    with ExitStack() as _stk:
        w_st = _stk.enter_context(nc.sbuf_tensor("w_st", [128, D], BF16))
        wqT = _stk.enter_context(nc.sbuf_tensor("wqT", [128, D], BF16))
        wkT = _stk.enter_context(nc.sbuf_tensor("wkT", [128, D], BF16))
        wvT = _stk.enter_context(nc.sbuf_tensor("wvT", [128, D], BF16))
        qb2 = _stk.enter_context(nc.sbuf_tensor("qb2", [128, 1], F32))
        kb2 = _stk.enter_context(nc.sbuf_tensor("kb2", [128, 1], F32))
        bvb = _stk.enter_context(nc.sbuf_tensor("bvb", [128, 128], F32))
        qN = _stk.enter_context(nc.sbuf_tensor("qN", [65, 2 * SQ], BF16))
        qA = _stk.enter_context(nc.sbuf_tensor("qA", [65, 2 * SQ], BF16))
        qB = _stk.enter_context(nc.sbuf_tensor("qB", [65, 2 * SQ], BF16))
        kt = _stk.enter_context(nc.sbuf_tensor("kt", [65, 2 * S], BF16))
        vsb = _stk.enter_context(nc.sbuf_tensor("vsb", [128, 2 * NJT * 65], BF16))
        qsq = _stk.enter_context(nc.sbuf_tensor("qsq", [64, SQ], BF16))
        mrow = _stk.enter_context(nc.sbuf_tensor("mrow", [1, 2 * SQ], BF16))
        nkrow = _stk.enter_context(nc.sbuf_tensor("nkrow", [1, 512], F32))
        maxk = _stk.enter_context(nc.sbuf_tensor("maxk", [1, 16], F32))
        skews = _stk.enter_context(nc.sbuf_tensor("skews", [128, 4 * 384], F32))
        poss = _stk.enter_context(nc.sbuf_tensor("poss", [128, 4 * 384], F32))
        posb = _stk.enter_context(nc.sbuf_tensor("posb", [128, 8 * 384], BF16))
        rp01 = _stk.enter_context(nc.sbuf_tensor("rp01", [128, 16], BF16))
        epT = _stk.enter_context(nc.sbuf_tensor("epT", [128, 4 * 12 * 128], BF16))
        rp_sb = _stk.enter_context(nc.sbuf_tensor("rp_sb", [128, 8 * R], F32))
        PT = _stk.enter_context(nc.sbuf_tensor("PT", [128, 6 * 512], BF16))
        avsb = _stk.enter_context(nc.sbuf_tensor("avsb", [65, 1024], BF16))
        avrc = _stk.enter_context(nc.sbuf_tensor("avrc", [128, 1], F32))
        avout = _stk.enter_context(nc.sbuf_tensor("avout", [128, 2 * 64], BF16))
        pS0 = _stk.enter_context(nc.psum_tensor("pS0", [128, 512], F32))
        pS1 = _stk.enter_context(nc.psum_tensor("pS1", [128, 512], F32))
        pS2 = _stk.enter_context(nc.psum_tensor("pS2", [128, 512], F32))
        pS3 = _stk.enter_context(nc.psum_tensor("pS3", [128, 512], F32))
        pAVa = _stk.enter_context(nc.psum_tensor("pAVa", [128, 512], F32))
        pAVb = _stk.enter_context(nc.psum_tensor("pAVb", [128, 512], F32))
        pBF0 = _stk.enter_context(nc.psum_tensor("pBF0", [128, 1024], BF16))
        pBF1 = _stk.enter_context(nc.psum_tensor("pBF1", [128, 1024], BF16))
        pS = [pS0, pS1, pS2, pS3]
        pBF = [pBF0, pBF1]
        rS = Ring(4)
        rBF = Ring(2)
        rRP = Ring(8)
        rPT = Ring(6)
        rSK = Ring(4)
        rAO = Ring(2)
        rHB = Ring(8)
        pAV = {0: pAVa, 1: pAVb}

        posT_last = [None] * 8
        posex_last = [None, None, None, None]
        tpr_last = [None] * 8
        prologue_mm = [None]
        P.wait("t", ve, C["p1_done"])
        P.op("v", lambda v: v.memset(kt[64:65, :], 1.0), inc=ve)
        ones_ev = P.op("v", lambda v: v.memset(
            vsb[:, 64:64 + 65 * (2 * NJT - 1) + 1:65], 1.0), inc=ve)
        C["p2_ones_ev"] = ones_ev

        def pair_setup(p):
            h0, h1 = 2 * p, 2 * p + 1
            bq = C["bqkv"]
            if p > 0:
                P.wait("g", tm, C[f"proj_mm_{p - 1}"])
            for dst, off0, off1 in ((qb2, 192 * h0, 192 * h1),
                                    (kb2, 192 * h0 + 64, 192 * h1 + 64)):
                P.op("g", lambda g, d=dst, o=off0: g.dma_start(
                    d[0:64, :], bq[:, o:o + 64].transpose((1, 0))),
                    inc=smb, inck=16)
                P.op("g", lambda g, d=dst, o=off1: g.dma_start(
                    d[64:128, :], bq[:, o:o + 64].transpose((1, 0))),
                    inc=smb, inck=16)
            P.op("g", lambda g: g.dma_start(
                bvb[:, 0:64],
                bq[:, 192 * h0 + 128:192 * h0 + 192].partition_broadcast(128)),
                inc=smb, inck=16)
            P.op("g", lambda g: g.dma_start(
                bvb[:, 64:128],
                bq[:, 192 * h1 + 128:192 * h1 + 192].partition_broadcast(128)),
                inc=smb, inck=16)
            bias_ld = smb.n
            P.op("v", lambda v: v.tensor_scalar(
                qb2[:, :], qb2[:, :], 0.125, None, OP.mult),
                wait=[(smb, bias_ld)], inc=ve)

            for kind, dstW, scale in (("q", wqT, 0.125), ("k", wkT, None),
                                      ("v", wvT, None)):
                off = {"q": 0, "k": 64, "v": 128}[kind]
                P.op("g", lambda g, o=off: g.dma_start(
                    w_st[0:64, :], C["wqkv"][192 * h0 + o:192 * h0 + o + 64, :]),
                    inc=smw, inck=16)
                lb = P.op("g", lambda g, o=off: g.dma_start(
                    w_st[64:128, :], C["wqkv"][192 * h1 + o:192 * h1 + o + 64, :]),
                    inc=smw, inck=16)
                sbf, wbf = rBF.acquire()
                tps = []
                for dc in range(8):
                    wt = []
                    if dc == 0:
                        wt.append((smw, lb))
                        if wbf is not None:
                            wt.append(wbf)
                    tp = P.op("t", lambda t, dc=dc, s=sbf: t.transpose(
                        pBF[s][:, dc * 128:(dc + 1) * 128],
                        w_st[:, dc * 128:(dc + 1) * 128], eye_b[:, :]),
                        wait=wt, inc=tm)
                    tps.append(tp)
                evl = None
                for dc in range(8):
                    if scale is None:
                        evl = P.op("v", lambda v, dc=dc, s=sbf, dW=dstW:
                                   v.tensor_copy(dW[:, dc * 128:(dc + 1) * 128],
                                                 pBF[s][:, dc * 128:(dc + 1) * 128]),
                                   wait=[(tm, tps[-1])], inc=ve)
                    else:
                        evl = P.op("v", lambda v, dc=dc, s=sbf, dW=dstW, sc=scale:
                                   v.tensor_scalar(dW[:, dc * 128:(dc + 1) * 128],
                                                   pBF[s][:, dc * 128:(dc + 1) * 128],
                                                   sc, None, OP.mult),
                                   wait=[(tm, tps[-1])], inc=ve)
                rBF.mark(sbf, ve, evl)
                P.wait("g", tm, tps[-1])
            wT_ev = ve.n

            for ns in range(4):
                sl, wfree = rS.acquire()
                wt = [(ve, wT_ev), (smb, bias_ld)] + ([wfree] if wfree else [])
                mm = None
                for dc in range(8):
                    mm = P.op("t", lambda t, dc=dc, ns=ns, sl=sl: t.matmul(
                        pS[sl][:, :], wqT[:, dc * 128:(dc + 1) * 128],
                        h1T[:, dc * S + ns * 512: dc * S + (ns + 1) * 512],
                        start=(dc == 0), stop=(dc == 7)),
                        wait=(wt if dc == 0 else []),
                        inc=(tm if dc == 7 else None))
                ev = None
                for dstq in (qN, qA, qB):
                    for pi in range(2):
                        ev = P.op("v", lambda v, d=dstq, pi=pi, ns=ns, sl=sl:
                                  v.tensor_scalar(
                                      d[0:64, pi * SQ + ns * 512:
                                        pi * SQ + (ns + 1) * 512],
                                      pS[sl][pi * 64:(pi + 1) * 64, :],
                                      qb2[pi * 64:(pi + 1) * 64, :],
                                      None, OP.add),
                                  wait=[(tm, mm)], inc=ve)
                rS.mark(sl, ve, ev)
            C[f"proj_q_{p}"] = ve.n
            for ns in range(8):
                sl, wfree = rS.acquire()
                wt = [(ve, wT_ev), (smb, bias_ld)] + ([wfree] if wfree else [])
                mm = None
                for dc in range(8):
                    mm = P.op("t", lambda t, dc=dc, ns=ns, sl=sl: t.matmul(
                        pS[sl][:, :], wkT[:, dc * 128:(dc + 1) * 128],
                        h1T[:, dc * S + ns * 512: dc * S + (ns + 1) * 512],
                        start=(dc == 0), stop=(dc == 7)),
                        wait=(wt if dc == 0 else []),
                        inc=(tm if dc == 7 else None))
                ev = None
                for pi in range(2):
                    ev = P.op("v", lambda v, pi=pi, ns=ns, sl=sl:
                              v.tensor_scalar(
                                  kt[0:64, pi * S + ns * 512:
                                     pi * S + (ns + 1) * 512],
                                  pS[sl][pi * 64:(pi + 1) * 64, :],
                                  kb2[pi * 64:(pi + 1) * 64, :], None, OP.add),
                              wait=[(tm, mm)], inc=ve)
                rS.mark(sl, ve, ev)
            C[f"proj_k_{p}"] = ve.n
            for jt in range(NJT):
                sl, wfree = rS.acquire()
                wt = [(ve, wT_ev), (smb, bias_ld)] + ([wfree] if wfree else [])
                mm = None
                for dc in range(8):
                    mm = P.op("t", lambda t, dc=dc, jt=jt, sl=sl: t.matmul(
                        pS[sl][:, 0:128],
                        h1T[:, dc * S + jt * 128: dc * S + (jt + 1) * 128],
                        wvT[:, dc * 128:(dc + 1) * 128],
                        start=(dc == 0), stop=(dc == 7)),
                        wait=(wt if dc == 0 else []),
                        inc=(tm if dc == 7 else None))
                ev = None
                for pi in range(2):
                    ev = P.op("v", lambda v, pi=pi, jt=jt, sl=sl:
                              v.tensor_tensor(
                                  vsb[:, pi * NJT * 65 + jt * 65:
                                      pi * NJT * 65 + jt * 65 + 64],
                                  pS[sl][:, pi * 64:(pi + 1) * 64],
                                  bvb[:, pi * 64:(pi + 1) * 64], OP.add),
                              wait=[(tm, mm)], inc=ve)
                rS.mark(sl, ve, ev)
            C[f"proj_ev_{p}"] = ve.n
            C[f"proj_mm_{p}"] = tm.n

        def prologue(h):
            pi = h % 2
            wsq1 = [(ve, C[f"proj_q_{h // 2}"])]
            if prologue_mm[0] is not None:
                wsq1.append((tm, prologue_mm[0]))
            sq1 = P.op("s", lambda s, pi=pi: s.activation(
                qsq[:, :], qN[0:64, pi * SQ:(pi + 1) * SQ], AF.Square),
                wait=wsq1, inc=sg)
            for ns in range(4):
                sl, wfree = rS.acquire()
                wt = [(sg, sq1)] if ns == 0 else []
                if wfree is not None:
                    wt.append(wfree)
                mm = P.op("t", lambda t, ns=ns, sl=sl: t.matmul(
                    pS[sl][0:1, :], C["ones_b"][0:64, :],
                    qsq[0:64, ns * 512:(ns + 1) * 512], start=True, stop=True),
                    wait=wt, inc=tm)
                sv = P.op("s", lambda s, pi=pi, ns=ns, sl=sl: s.activation(
                    mrow[0:1, pi * SQ + ns * 512:pi * SQ + (ns + 1) * 512],
                    pS[sl][0:1, :], AF.Sqrt), wait=[(tm, mm)], inc=sg)
                rS.mark(sl, sg, sv)
            prev_mm = None
            for ns in range(8):
                wsq = [(ve, C[f"proj_k_{h // 2}"])] if ns == 0 else [(tm, prev_mm)]
                sq2 = P.op("s", lambda s, pi=pi, ns=ns: s.activation(
                    qsq[:, 0:512], kt[0:64, pi * S + ns * 512:
                                      pi * S + (ns + 1) * 512], AF.Square),
                    wait=wsq, inc=sg)
                sl, wfree = rS.acquire()
                mm = P.op("t", lambda t, sl=sl: t.matmul(
                    pS[sl][0:1, :], C["ones_b"][0:64, :], qsq[0:64, 0:512],
                    start=True, stop=True),
                    wait=[(sg, sq2)] + ([wfree] if wfree else []), inc=tm)
                rv = P.op("v", lambda v, ns=ns, sl=sl: v.tensor_reduce(
                    maxk[:, ns:ns + 1], pS[sl][0:1, :], AX, OP.max),
                    wait=[(tm, mm)], inc=ve)
                rS.mark(sl, ve, rv)
                prev_mm = mm
            prologue_mm[0] = prev_mm
            red = P.op("v", lambda v: v.tensor_reduce(
                maxk[:, 8:9], maxk[:, 0:8], AX, OP.max), inc=ve, drain=True)
            sqm = P.op("s", lambda s: s.activation(
                maxk[:, 9:10], maxk[:, 8:9], AF.Sqrt), wait=[(ve, red)], inc=sg)
            m1 = P.op("v", lambda v, pi=pi: v.tensor_scalar(
                mrow[0:1, pi * SQ:(pi + 1) * SQ],
                mrow[0:1, pi * SQ:(pi + 1) * SQ], maxk[0:1, 9:10], -1.0,
                OP.mult, OP.mult), wait=[(sg, sqm)], inc=ve)
            r64 = P.op("v", lambda v, pi=pi: v.tensor_copy(
                qN[64:65, pi * SQ:(pi + 1) * SQ],
                mrow[0:1, pi * SQ:(pi + 1) * SQ]), wait=[(ve, m1)], inc=ve)
            C[f"row64_{h}"] = r64

        def rp_stage_a1(h, qt):
            pi = h % 2
            slr, wr_ = rRP.acquire()
            slh, wh_ = rHB.acquire()
            sl, wfree = rS.acquire()
            wt = [(ve, C[f"row64_{h}"])] + ([wfree] if wfree else [])
            mm = P.op("t", lambda t, pi=pi, qt=qt, sl=sl: t.matmul(
                pS[sl][:, 0:R], qN[0:64, pi * SQ + qt * 128:
                                   pi * SQ + (qt + 1) * 128],
                C["wrt8"][:, :], start=True, stop=True), wait=wt, inc=tm)
            evr = P.op("v", lambda v, slr=slr, sl=sl: v.tensor_tensor(
                rp_sb[:, slr * R:slr * R + R], pS[sl][:, 0:R], br_b_ap(C),
                OP.add), wait=[(tm, mm)] + ([wr_] if wr_ else []), inc=ve)
            rS.mark(sl, ve, evr)
            rslot = (qt % 4) + 4 * (h % 2)
            wcc = [(ve, evr)]
            if tpr_last[rslot] is not None:
                wcc.append((tm, tpr_last[rslot]))
            cc = P.op("v", lambda v, slr=slr, rslot=rslot: v.tensor_copy(
                rp01[:, 2 * rslot:2 * rslot + 2],
                rp_sb[:, slr * R:slr * R + 257:256]),
                wait=wcc, inc=ve, drain=True)
            wst = [(ve, evr)] + ([wh_] if wh_ else [])
            st = P.op("y", lambda y, slr=slr, slh=slh: y.dma_start(
                C["rp_hbm"][slh, :, :], rp_sb[:, slr * R:slr * R + R]),
                wait=wst, inc=smr, inck=16)
            return slr, slh, st, cc

        def rp_stage_a2(h, qt, slp_shared, cc, prev_e65):
            pi = h % 2
            rslot = (qt % 4) + 4 * (h % 2)
            tpr = None
            for j in range(2):
                wt = []
                if j == 0:
                    wt.append((ve, cc))
                    if prev_e65 is not None:
                        wt.append((ve, prev_e65))
                tpr = P.op("t", lambda t, rslot=rslot, j=j, slp=slp_shared:
                           t.transpose(
                               pBF[slp][0:1, 512 + j * 128:512 + (j + 1) * 128],
                               rp01[:, 2 * rslot + j:2 * rslot + j + 1],
                               eye_b[:, :]),
                           wait=wt, inc=tm)
            tpr_last[rslot] = tpr
            e65 = None
            for dstq, j in ((qA, 0), (qB, 1)):
                e65 = P.op("v", lambda v, d=dstq, j=j, pi=pi, qt=qt,
                           slp=slp_shared: v.tensor_tensor(
                               d[64:65, pi * SQ + qt * 128:pi * SQ + (qt + 1) * 128],
                               pBF[slp][0:1, 512 + j * 128:512 + (j + 1) * 128],
                               mrow[0:1, pi * SQ + qt * 128:pi * SQ + (qt + 1) * 128],
                               OP.add),
                           wait=[(tm, tpr), (ve, C[f"row64_{h}"])], inc=ve)
            C[f"row65_{h}_{qt}"] = e65
            return e65

        def rp_stage_b1(h, qt, slr, slh, st):
            ms = [0, 1, 2] if qt >= 1 else [1, 2]
            ssk, wsk = rSK.acquire()
            base = ssk * 384
            if posex_last[qt % 4] is not None:
                P.wait("v", sg, posex_last[qt % 4])
            m0 = ms[0]
            lds = P.op("y", lambda y, m0=m0, slh=slh, base=base: y.dma_start(
                skews[:, base + m0 * 128:base + 384],
                bass.AP(C["rp_hbm"].tensor, slh * 128 * R + m0 * 128,
                        [[256, 128], [1, 384 - m0 * 128]])),
                wait=[(smr, st)] + ([wsk] if wsk else []),
                inc=sks[ssk], inck=16)
            po = (qt % 4) * 384
            pb = ((qt % 4) + 4 * (h % 2)) * 384
            pslot = (qt % 4) + 4 * (h % 2)
            bl = []
            for m in ms:
                if m == 1:
                    bl.append(P.op("v", lambda v, base=base, po=po: v.tensor_copy(
                        poss[:, po + 128:po + 256],
                        skews[:, base + 128:base + 256]),
                        wait=[(sks[ssk], lds)], inc=ve))
                elif m == 0:
                    P.op("v", lambda v, base=base, po=po: v.tensor_tensor(
                        poss[:, po:po + 128], skews[:, base:base + 128],
                        C["tri_ge"][:, :], OP.mult), wait=[(sks[ssk], lds)], inc=ve)
                    bl.append(P.op("v", lambda v, slr=slr, po=po:
                                   v.scalar_tensor_tensor(
                                       poss[:, po:po + 128], C["tri_lt"][:, :],
                                       rp_sb[:, slr * R:slr * R + 1],
                                       poss[:, po:po + 128], OP.mult, OP.add),
                                   inc=ve))
                else:
                    P.op("v", lambda v, base=base, po=po: v.tensor_tensor(
                        poss[:, po + 256:po + 384],
                        skews[:, base + 256:base + 384],
                        C["tri_le"][:, :], OP.mult), wait=[(sks[ssk], lds)], inc=ve)
                    bl.append(P.op("v", lambda v, slr=slr, po=po:
                                   v.scalar_tensor_tensor(
                                       poss[:, po + 256:po + 384],
                                       C["tri_gt"][:, :],
                                       rp_sb[:, slr * R + 256:slr * R + 257],
                                       poss[:, po + 256:po + 384],
                                       OP.mult, OP.add), inc=ve))
            rSK.mark(ssk, ve, bl[-1])
            rRP.mark(slr, ve, bl[-1])
            lo = 0 if qt >= 1 else 128
            wex = [(ve, bl[-1])]
            if posT_last[pslot] is not None:
                wex.append((tm, posT_last[pslot]))
            ex = P.op("s", lambda s, pb=pb, po=po, lo=lo: s.activation(
                posb[:, pb + lo:pb + 384], poss[:, po + lo:po + 384], AF.Exp),
                wait=wex, inc=sg)
            posex_last[qt % 4] = ex
            rHB.mark(slh, sks[ssk], lds)
            return ex

        def rp_stage_b2(h, qt, g_, side, ex):
            ms = [0, 1, 2] if qt >= 1 else [1, 2]
            pb = ((qt % 4) + 4 * (h % 2)) * 384
            pslot = (qt % 4) + 4 * (h % 2)
            slot = (g_ % 2) * 2 + side
            sbf, wbf = rBF.acquire()
            tps = []
            for m in ms:
                wt = [(sg, ex)] if m == ms[0] else []
                if m == ms[0] and wbf is not None:
                    wt.append(wbf)
                tps.append(P.op("t", lambda t, m=m, pb=pb, s=sbf: t.transpose(
                    pBF[s][:, m * 128:(m + 1) * 128],
                    posb[:, pb + m * 128:pb + (m + 1) * 128], eye_b[:, :]),
                    wait=wt, inc=tm))
            posT_last[pslot] = tps[-1]
            eve = None
            for i, m in enumerate(ms):
                eve = P.op("v", lambda v, m=m, s=sbf, slot=slot, qt=qt: v.tensor_copy(
                    epT[:, slot * 1536 + (qt % 4) * 384 + m * 128:
                        slot * 1536 + (qt % 4) * 384 + (m + 1) * 128],
                    pBF[s][:, m * 128:(m + 1) * 128]),
                    wait=[(tm, tps[-1])], inc=ve)
            rBF.mark(sbf, ve, eve)
            C[f"ep_{h}_{qt}"] = eve

        def segments(g_, jt):
            its = [4 * g_ + l for l in range(4)]
            cls = []
            for it in its:
                if abs(jt - it) <= 1:
                    cls.append("N")
                elif it <= jt - 2:
                    cls.append("R")
                else:
                    cls.append("L")
            segs = []
            s0 = 0
            for i in range(1, 5):
                if i == 4 or cls[i] != cls[s0]:
                    segs.append((cls[s0], s0, i))
                    s0 = i
            return segs

        def body2(hA, hB, g_, rp_hooks):
            near = [jt for jt in range(max(0, 4 * g_ - 1), min(NJT, 4 * g_ + 5))]
            far = [jt for jt in range(NJT) if jt not in near]
            order = far + near
            heads = (hA, hB)
            nch = {0: 0, 1: 0}
            epslot = (g_ % 2) * 2
            units = [(jt, s) for jt in order for s in (0, 1)]
            pend_av = []
            LAG = 4
            for ui, (jt, side) in enumerate(units):
                if ui in rp_hooks:
                    rp_hooks[ui]()
                h = heads[side]
                pi = h % 2
                segs = segments(g_, jt)
                sl, wfree = rS.acquire()
                slt, wpt = rPT.acquire()
                mm = None
                for si, (cls, l0, l1) in enumerate(segs):
                    qsrc = {"N": qN, "L": qA, "R": qB}[cls]
                    wt = []
                    if si == 0 and wfree:
                        wt.append(wfree)
                    if cls != "N":
                        wt.append((ve, C[f"row65_{h}_{4 * g_ + l1 - 1}"]))
                    elif si == 0:
                        wt.append((ve, C[f"row64_{h}"]))
                    mm = P.op("t", lambda t, qsrc=qsrc, l0=l0, l1=l1, jt=jt,
                              sl=sl, si=si, nseg=len(segs), pi=pi, g_=g_:
                              t.matmul(
                                  pS[sl][:, l0 * 128:l1 * 128],
                                  kt[0:65, pi * S + jt * 128:
                                     pi * S + (jt + 1) * 128],
                                  qsrc[0:65, pi * SQ + g_ * 512 + l0 * 128:
                                       pi * SQ + g_ * 512 + l1 * 128],
                                  start=(si == 0), stop=(si == nseg - 1)),
                              wait=wt, inc=(tm if si == len(segs) - 1 else None))
                ex = P.op("s", lambda s, sl=sl, slt=slt: s.activation(
                    PT[:, slt * 512:(slt + 1) * 512], pS[sl][:, :], AF.Exp),
                    wait=[(tm, mm)] + ([wpt] if wpt else []), inc=sg)
                rS.mark(sl, sg, ex)
                av_wait = [(sg, ex)]
                for (cls, l0, l1) in (segs if not C.get("NO_POS") else []):
                    if cls != "N":
                        continue
                    for l in range(l0, l1):
                        it = 4 * g_ + l
                        m = jt - it + 1
                        mu = P.op("v", lambda v, slt=slt, l=l, it=it, m=m,
                                  side=side, epslot=epslot:
                                  v.tensor_tensor(
                                      PT[:, slt * 512 + l * 128:
                                         slt * 512 + (l + 1) * 128],
                                      PT[:, slt * 512 + l * 128:
                                         slt * 512 + (l + 1) * 128],
                                      epT[:, (epslot + side) * 1536 +
                                          (it % 4) * 384 + m * 128:
                                          (epslot + side) * 1536 +
                                          (it % 4) * 384 + (m + 1) * 128],
                                      OP.mult),
                                  wait=[(sg, ex), (ve, C[f"ep_{h}_{it}"])],
                                  inc=ve)
                        av_wait = [(ve, mu)]
                wt = list(av_wait)
                if nch[side] == 0:
                    aw = C.get(f"av_free_{side}")
                    if aw is not None:
                        wt.append(aw)
                    wt.append((ve, C["p2_ones_ev"]))

                def emit_av(jt=jt, slt=slt, pi=pi, side=side, wt=tuple(wt),
                            first=(nch[side] == 0), last=(nch[side] == NJT - 1)):
                    av = P.op("t", lambda t, jt=jt, slt=slt, pi=pi, side=side,
                              first=first, last=last: t.matmul(
                                  pAV[side][0:65, :],
                                  vsb[:, pi * NJT * 65 + jt * 65:
                                      pi * NJT * 65 + (jt + 1) * 65],
                                  PT[:, slt * 512:(slt + 1) * 512],
                                  start=first, stop=last),
                              wait=list(wt), inc=tm)
                    rPT.mark(slt, tm, av)
                    C[f"av_done_{side}"] = av
                pend_av.append(emit_av)
                nch[side] += 1
                if len(pend_av) > LAG:
                    pend_av.pop(0)()
            for f in pend_av:
                f()
            cps = {}
            for side in (0, 1):
                cp = P.op("v", lambda v, side=side: v.tensor_copy(
                    avsb[:, side * 512:(side + 1) * 512], pAV[side][0:65, :]),
                    wait=[(tm, C[f"av_done_{side}"])], inc=ve)
                C[f"av_free_{side}"] = (ve, cp)
                cps[side] = cp

            def finish(g_=g_, cps=cps, heads=heads):
                for side in (0, 1):
                    h = heads[side]
                    cp = cps[side]
                    sbf, wbf = rBF.acquire()
                    tps = []
                    for l in range(4):
                        wt = [(ve, cp)] if l == 0 else []
                        if l == 0 and wbf is not None:
                            wt.append(wbf)
                        tps.append(P.op("t", lambda t, l=l, s=sbf, side=side:
                                        t.transpose(
                                            pBF[s][:, l * 256:l * 256 + 65],
                                            avsb[:, side * 512 + l * 128:
                                                 side * 512 + (l + 1) * 128],
                                            eye_b[0:65, 0:65]), wait=wt, inc=tm))
                    evf = None
                    for l in range(4):
                        qt = 4 * g_ + l
                        so, wao = rAO.acquire()
                        rc = P.op("v", lambda v, l=l, s=sbf: v.reciprocal(
                            avrc[:, :], pBF[s][:, l * 256 + 64:l * 256 + 65]),
                            wait=[(tm, tps[-1])] + ([wao] if wao else []), inc=ve)
                        evf = P.op("v", lambda v, l=l, s=sbf, so=so, h=h:
                                   v.scalar_tensor_tensor(
                                       avout[:, so * 64:(so + 1) * 64],
                                       pBF[s][:, l * 256:l * 256 + 64], avrc[:, :],
                                       gab_slice(C, h), OP.mult, OP.mult),
                                   wait=[(ve, rc)], inc=ve, drain=True)
                        st = P.op("y", lambda y, so=so, qt=qt, h=h: y.dma_start(
                            C["attn_hbm"][qt * 128:(qt + 1) * 128,
                                          h * 64:(h + 1) * 64],
                            avout[:, so * 64:(so + 1) * 64]),
                            wait=[(ve, evf)], inc=aos[so], inck=16)
                        rAO.mark(so, aos[so], st)
                    rBF.mark(sbf, ve, evf)
            return finish

        def rp_a1_round(heads, g_):
            out = []
            for h in heads:
                for qt in range(4 * g_, 4 * g_ + 4):
                    out.append((h, qt) + rp_stage_a1(h, qt))
            return out

        def rp_a2_round(out):
            slp, wp_ = rBF.acquire()
            if wp_ is not None:
                P.wait("t", wp_[0], wp_[1])
            e65 = None
            for (h, qt, slr, slh, st, cc) in out:
                e65 = rp_stage_a2(h, qt, slp, cc, e65)
            rBF.mark(slp, ve, e65)
            return [(h, qt, slr, slh, st) for (h, qt, slr, slh, st, cc) in out]

        def rp_b1_round(infos):
            return [(h, qt, rp_stage_b1(h, qt, slr, slh, st))
                    for (h, qt, slr, slh, st) in infos]

        def rp_b2_round(infos, heads, g_):
            for (h, qt, ex) in infos:
                rp_stage_b2(h, qt, g_, 0 if h == heads[0] else 1, ex)

        for p in range(H // 2):
            heads = (2 * p, 2 * p + 1)
            pair_setup(p)
            prologue(2 * p)
            prologue(2 * p + 1)
            ia = rp_a1_round(heads, 0)
            ia = rp_a2_round(ia)
            ib = rp_b1_round(ia)
            rp_b2_round(ib, heads, 0)
            fin = C.pop("p2_fin", None)
            for g_ in range(NG):
                hooks = {}
                if fin is not None:
                    hooks[6] = fin
                if g_ < NG - 1:
                    state = {}

                    def hk1(g_=g_, state=state):
                        state["ia"] = rp_a1_round(heads, g_ + 1)

                    def hk1b(state=state):
                        ia = rp_a2_round(state["ia"])
                        state["ib"] = rp_b1_round(ia)

                    def hk2(g_=g_, state=state):
                        rp_b2_round(state["ib"], heads, g_ + 1)
                    hooks[12] = hk1
                    hooks[24] = hk1b
                    hooks[42] = hk2
                fin = body2(heads[0], heads[1], g_, hooks)
            C["p2_fin"] = fin
        C.pop("p2_fin")()
        P.replay(nc)


def br_b_ap(C):
    return C["br_b"][:, :]


def gab_slice(C, h):
    return C["gab"][:, h * 64:(h + 1) * 64]


def emit_gate(nc, C):
    P = Rec()
    tm, ve, sg = C["tm"], C["ve"], C["sg"]
    sm = C["SM"]
    smg = sm["gt"]
    h2T = C["h2T"]
    CAP = 512
    I32 = mybir.dt.int32
    with ExitStack() as _stk:
        wg_sb = _stk.enter_context(nc.sbuf_tensor("wg_sb", [128, 64], BF16))
        wg_lo = _stk.enter_context(nc.sbuf_tensor("wg_lo", [128, 64], BF16))
        wgf = _stk.enter_context(nc.sbuf_tensor("wgf", [128, 64], F32))
        bgc = _stk.enter_context(nc.sbuf_tensor("bgc", [E, 1], F32))
        lgT = _stk.enter_context(nc.sbuf_tensor("lgT", [E, SQ], F32))
        lgtok = _stk.enter_context(nc.sbuf_tensor("lgtok", [128, 128], F32))
        mx16 = _stk.enter_context(nc.sbuf_tensor("mx16", [128, 16], F32))
        onehot = _stk.enter_context(nc.sbuf_tensor("onehot", [128, 128], F32))
        onehT = _stk.enter_context(nc.sbuf_tensor("onehT", [E, SQ], F32))
        cum0 = _stk.enter_context(nc.sbuf_tensor("cum0", [E, SQ], F32))
        cum1 = _stk.enter_context(nc.sbuf_tensor("cum1", [E, SQ], F32))
        eoffc = _stk.enter_context(nc.sbuf_tensor("eoffc", [E, 1], F32))
        onesf = _stk.enter_context(nc.sbuf_tensor("onesf", [128, 1], F32))
        slotr = _stk.enter_context(nc.sbuf_tensor("slotr", [1, SQ], F32))
        slotf = _stk.enter_context(nc.sbuf_tensor("slotf", [128, 16], F32))
        sloti = _stk.enter_context(nc.sbuf_tensor("sloti", [128, 16], I32))
        iotas = _stk.enter_context(nc.sbuf_tensor("iotas", [128, 16], I32))
        dumps = _stk.enter_context(nc.sbuf_tensor("dumps", [128, 1], I32))
        _ps1 = ExitStack()
        pG = _ps1.enter_context(nc.psum_tensor("pG", [8, 2048], F32))
        pGT = _ps1.enter_context(nc.psum_tensor("pGT", [128, 512], F32))

        ldw = P.op("g", lambda g: g.dma_start(
            wgf[:, :], bass.AP(C["wg"].tensor, 0, [[8, 128], [1024, 8], [1, 8]])),
            inc=smg, inck=16)
        ldb = P.op("g", lambda g: g.dma_start(bgc[:, :], C["bg"]), inc=smg, inck=16)
        lde = P.op("g", lambda g: g.dma_start(eoffc[:, :], C["eoff_h"]),
                   inc=smg, inck=16)
        ldo = P.op("g", lambda g: g.dma_start(onesf[:, :], C["ones_fh"]),
                   inc=smg, inck=16)
        ldi = P.op("g", lambda g: g.dma_start(iotas[:, :], C["iota_h"]),
                   inc=smg, inck=16)
        ldd = P.op("g", lambda g: g.dma_start(dumps[:, :], C["dump_h"]),
                   inc=smg, inck=16)
        ld_all = smg.n
        lnv = C["p3_ln_done"]
        P.op("v", lambda v: v.tensor_copy(wg_sb[:, :], wgf[:, :]),
             wait=[(smg, ld_all)], inc=ve, drain=True)
        wglo_ev = P.op("v", lambda v: v.scalar_tensor_tensor(
            wg_lo[:, :], wg_sb[:, :], -1.0, wgf[:, :], OP.mult, OP.add),
            inc=ve, drain=True)
        h2lT = C["h2loT"]
        mm = None
        for ns in range(4):
            ops = [(wg_sb, h2T), (wg_lo, h2T), (wg_sb, h2lT)]
            nmm = len(ops) * 8
            i = 0
            for wop, hop in ops:
                for dc in range(8):
                    wt = [(smg, ld_all), (ve, lnv), (ve, wglo_ev)]                         if (ns == 0 and i == 0) else []
                    mm = P.op("t", lambda t, ns=ns, dc=dc, wop=wop, hop=hop,
                              st=(i == 0), sp=(i == nmm - 1): t.matmul(
                        pG[:, ns * 512:(ns + 1) * 512], wop[:, dc * 8:(dc + 1) * 8],
                        hop[:, dc * SQ + ns * 512: dc * SQ + (ns + 1) * 512],
                        start=st, stop=sp),
                        wait=wt, inc=(tm if i == nmm - 1 else None))
                    i += 1
        evg = P.op("v", lambda v: v.tensor_scalar(
            lgT[:, :], pG[:, :], bgc[:, :], None, OP.add),
            wait=[(tm, mm)], inc=ve)
        tps = []
        for tt in range(16):
            tp = P.op("t", lambda t, tt=tt: t.transpose(
                pGT[:, tt * 8:(tt + 1) * 8], lgT[:, tt * 128:(tt + 1) * 128],
                C["eye_f"][0:8, 0:8]), wait=([(ve, evg)] if tt == 0 else []),
                inc=tm)
            tps.append(tp)
        ev2 = P.op("v", lambda v: v.tensor_copy(lgtok[:, :], pGT[:, 0:128]),
                   wait=[(tm, tps[-1])], inc=ve)
        for tt in range(16):
            P.op("v", lambda v, tt=tt: v.tensor_reduce(
                mx16[:, tt:tt + 1], lgtok[:, tt * 8:(tt + 1) * 8], AX, OP.max),
                wait=([(ve, ev2)] if tt == 0 else []), inc=ve, drain=(tt == 0))
        P.drain("v")
        oh = None
        for tt in range(16):
            oh = P.op("v", lambda v, tt=tt: v.tensor_tensor(
                onehot[:, tt * 8:(tt + 1) * 8], lgtok[:, tt * 8:(tt + 1) * 8],
                mx16[:, tt:tt + 1].to_broadcast([128, 8]), OP.is_equal), inc=ve)
        _ps1.close()
        _ps2 = ExitStack()
        pOT = _ps2.enter_context(nc.psum_tensor("pOT", [8, 2048], F32))
        tps2 = []
        for tt in range(16):
            tp = P.op("t", lambda t, tt=tt: t.transpose(
                pOT[:, tt * 128:(tt + 1) * 128], onehot[:, tt * 8:(tt + 1) * 8],
                C["eye_f"][:, :]), wait=([(ve, oh)] if tt == 0 else []), inc=tm)
            tps2.append(tp)
        ev3 = P.op("v", lambda v: v.tensor_copy(onehT[:, :], pOT[:, :]),
                   wait=[(tm, tps2[-1])], inc=ve)
        a, b = onehT, cum0
        prev = ev3
        k = 1
        while k < SQ:
            P.op("v", lambda v, a=a, b=b, k=k: v.tensor_copy(
                b[:, 0:k], a[:, 0:k]), wait=[(ve, prev)], inc=ve)
            prev = P.op("v", lambda v, a=a, b=b, k=k: v.tensor_tensor(
                b[:, k:SQ], a[:, k:SQ], a[:, 0:SQ - k], OP.add), inc=ve)
            a, b = b, (cum1 if b is cum0 else cum0)
            if a is onehT:
                a = cum1
            k *= 2
        cum = a
        c1 = P.op("v", lambda v, cum=cum: v.tensor_scalar(
            cum[:, :], cum[:, :], float(CAP), eoffc[:, :], OP.min, OP.add),
            wait=[(ve, prev), (smg, ld_all)], inc=ve, drain=True)
        c2 = P.op("v", lambda v, cum=cum: v.tensor_tensor(
            cum[:, :], cum[:, :], onehT[:, :], OP.mult), inc=ve, drain=True)
        _ps2.close()
        _ps3 = ExitStack()
        pSL = _ps3.enter_context(nc.psum_tensor("pSL", [1, 2048], F32))
        pST = _ps3.enter_context(nc.psum_tensor("pST", [128, 512], F32))
        mms = None
        for ns in range(4):
            mms = P.op("t", lambda t, ns=ns, cum=cum: t.matmul(
                pSL[:, ns * 512:(ns + 1) * 512], onesf[0:8, :],
                cum[0:8, ns * 512:(ns + 1) * 512], start=True, stop=True),
                wait=([(ve, c2)] if ns == 0 else []), inc=tm)
        evs = P.op("v", lambda v: v.tensor_copy(slotr[:, :], pSL[:, :]),
                   wait=[(tm, mms)], inc=ve)
        tps3 = []
        for tt in range(16):
            tp = P.op("t", lambda t, tt=tt: t.transpose(
                pST[:, tt * 4:tt * 4 + 1] if False else pST[:, tt:tt + 1],
                slotr[:, tt * 128:(tt + 1) * 128], C["eye_f"][0:1, 0:1]),
                wait=([(ve, evs)] if tt == 0 else []), inc=tm)
            tps3.append(tp)
        cl = P.op("v", lambda v: v.tensor_scalar(
            slotf[:, :], pST[:, 0:16], 4095.0, 0.0, OP.min, OP.max),
            wait=[(tm, tps3[-1])], inc=ve)
        ci = P.op("v", lambda v: v.tensor_copy(sloti[:, :], slotf[:, :]),
                  wait=[(ve, cl)], inc=ve, drain=True)
        pf = None
        for k2 in range(32):
            pf = P.op("g", lambda g, k2=k2: g.dma_start(
                C["idx_hbm"][k2 * 128:(k2 + 1) * 128, :], dumps[:, :]),
                wait=([(smg, ld_all)] if k2 == 0 else []),
                inc=sm["rt0"], inck=16)
        P.wait("g", sm["rt0"], pf)
        sc = None
        for tt in range(16):
            sc = P.op("g", lambda g, tt=tt: g.indirect_dma_start(
                C["idx_hbm"],
                bass.IndirectOffsetOnAxis(ap=sloti[:, tt:tt + 1], axis=0),
                iotas[:, tt:tt + 1], None),
                wait=([(ve, ci)] if tt == 0 else []), inc=sm["rt1"], inck=16)
        C["idx_ready"] = sc
        _ps3.close()
        P.replay(nc)


def emit_ffn(nc, C):
    P = Rec()
    tm, ve, sg = C["tm"], C["ve"], C["sg"]
    sm = C["SM"]
    w1sm = (sm["w1a"], sm["w1b"])
    w2sm = sm["w2s"]
    smgx = sm["gx"]
    I32 = mybir.dt.int32
    CAP = 512
    NSL = 8 * CAP
    HTILES = NSL // 128
    with ExitStack() as _stk:
        w1s = _stk.enter_context(nc.sbuf_tensor("w1s", [128, 2 * 8192], BF16))
        w2s = _stk.enter_context(nc.sbuf_tensor("w2s", [128, 8192], BF16))
        h2pT = _stk.enter_context(nc.sbuf_tensor("h2pT", [128, 8 * 2048], BF16))
        hid = _stk.enter_context(nc.sbuf_tensor("hid", [128, 8 * 512], BF16))
        gst = _stk.enter_context(nc.sbuf_tensor("gst", [128, 2 * D], BF16))
        yst = _stk.enter_context(nc.sbuf_tensor("yst", [128, 2 * D], BF16))
        idxs = _stk.enter_context(nc.sbuf_tensor("idxs", [128, 32], I32))
        zt = _stk.enter_context(nc.sbuf_tensor("zt", [128, D], BF16))
        ysb = C["ysb"]
        ph0 = _stk.enter_context(nc.psum_tensor("ph0", [128, 512], F32))
        ph1 = _stk.enter_context(nc.psum_tensor("ph1", [128, 512], F32))
        po0 = _stk.enter_context(nc.psum_tensor("po0", [128, 512], F32))
        po1 = _stk.enter_context(nc.psum_tensor("po1", [128, 512], F32))
        pT0 = _stk.enter_context(nc.psum_tensor("pT0", [128, 1024], BF16))
        pT1 = _stk.enter_context(nc.psum_tensor("pT1", [128, 1024], BF16))
        pT = (pT0, pT1)

        ldx = None
        for k in range(32):
            ldx = P.op("g", lambda g, k=k: g.dma_start(
                idxs[:, k:k + 1], C["idx_hbm"][k * 128:(k + 1) * 128, :]),
                wait=([(sm["rt1"], C["idx_ready"])] if k == 0 else []),
                inc=smgx, inck=16)
        mz = P.op("v", lambda v: v.memset(zt[:, :], 0.0), inc=ve)
        pz = None
        for k in range(17):
            r0 = min(k * 128, SQ)
            pz = P.op("y", lambda y, r0=r0, k=k: y.dma_start(
                C["y_hbm"][r0:r0 + 128 if k < 16 else SQ + 1, :],
                zt[:, :] if k < 16 else zt[0:1, :]),
                wait=([(ve, mz)] if k == 0 else []), inc=sm["rt3"], inck=16)
        yzero = sm["rt3"].n
        ms = P.op("v", lambda v: v.memset(ysb[:, :], 0.0), inc=ve)

        gi_count = [0]
        w1ld, w2ld = {}, {}

        def stream_w(e, gg):
            gi = gi_count[0]
            sl1 = gi % 2
            w = [(tm, C[f"w1_free_{gi - 2}"])] if gi >= 2 else []
            w1ld[gi] = P.op("g", lambda g, e=e, gg=gg, sl1=sl1: g.dma_start(
                w1s[:, sl1 * 8192:(sl1 + 1) * 8192],
                bass.AP(C["w1"].tensor, e * D * HF + gg * 1024,
                        [[HF, 128], [128 * HF, 8], [1, 1024]])),
                wait=w, inc=w1sm[sl1], inck=16)
            w = [(tm, C[f"w2_free_{gi - 1}"])] if gi >= 1 else []
            w2ld[gi] = P.op("g", lambda g, e=e, gg=gg: g.dma_start(
                w2s[:, :],
                bass.AP(C["w2"].tensor, e * HF * D + gg * 1024 * D,
                        [[D, 128], [128 * D, 8], [1, 1024]])),
                wait=w, inc=w2sm, inck=16)
            gi_count[0] += 1
            return gi, sl1

        last_gelu = {}
        last_evac = {}
        for half in range(2):
            for k in range(16):
                tile = half * 16 + k
                wt = [(smgx, ldx), (sm["h2s"], C["h2s_all"])] if (k == 0 and half == 0) else []
                if tile >= 2:
                    wt.append((tm, C[f"gtp_{tile - 2}"]))
                gx = P.op("g", lambda g, tile=tile, k=k: g.indirect_dma_start(
                    gst[:, (k % 2) * D:(k % 2 + 1) * D], None, C["h2_hbm"],
                    bass.IndirectOffsetOnAxis(ap=idxs[:, tile:tile + 1], axis=0)),
                    wait=wt, inc=smgx, inck=16)
                tps = []
                for dc in range(8):
                    wt2 = [(smgx, gx)] if dc == 0 else []
                    if dc == 0 and tile >= 2:
                        wt2.append((ve, C[f"gx_ev_{tile - 2}"]))
                    tps.append(P.op("t", lambda t, dc=dc, k=k: t.transpose(
                        pT[k % 2][:, dc * 128:(dc + 1) * 128],
                        gst[:, (k % 2) * D + dc * 128:(k % 2) * D + (dc + 1) * 128],
                        C["eye_b"][:, :]), wait=wt2, inc=tm))
                C[f"gtp_{tile}"] = tps[-1]
                e2 = None
                for dc in range(8):
                    e2 = P.op("v", lambda v, dc=dc, k=k: v.tensor_copy(
                        h2pT[:, dc * 2048 + k * 128: dc * 2048 + (k + 1) * 128],
                        pT[k % 2][:, dc * 128:(dc + 1) * 128]),
                        wait=[(tm, tps[-1])], inc=ve)
                C[f"gx_ev_{tile}"] = e2
            gather_ev = ve.n
            for er in range(4):
                e = half * 4 + er
                for gg in range(4):
                    gi, sl1 = stream_w(e, gg)
                    for ht in range(8):
                        kk = gi * 8 + ht
                        wt = []
                        if ht == 0:
                            wt.append((w1sm[sl1], w1ld[gi]))
                            if gg == 0:
                                wt.append((ve, gather_ev))
                        if kk >= 2:
                            wt.append((sg, last_gelu[kk - 2]))
                        mm = None
                        for dc in range(8):
                            mm = P.op("t", lambda t, dc=dc, ht=ht, sl1=sl1,
                                      er=er: t.matmul(
                                          (ph0, ph1)[ht % 2][:, :],
                                          w1s[:, sl1 * 8192 + dc * 1024 +
                                              ht * 128:sl1 * 8192 + dc * 1024 +
                                              ht * 128 + 128],
                                          h2pT[:, dc * 2048 + er * 512:
                                               dc * 2048 + (er + 1) * 512],
                                          start=(dc == 0), stop=(dc == 7)),
                                      wait=(wt if dc == 0 else []),
                                      inc=(tm if dc == 7 else None))
                        if ht == 7:
                            C[f"w1_free_{gi}"] = tm.n
                        gl = P.op("s", lambda s, ht=ht: s.activation(
                            hid[:, ht * 512:(ht + 1) * 512],
                            (ph0, ph1)[ht % 2][:, :], AF.Gelu_apprx_tanh),
                            wait=[(tm, tm.n)], inc=sg)
                        last_gelu[kk] = gl
                    for dt in range(8):
                        kk2 = gi * 8 + dt
                        wt = [(sg, last_gelu[gi * 8 + 7])]
                        if dt == 0:
                            wt.append((w2sm, w2ld[gi]))
                        if kk2 >= 2:
                            wt.append((ve, last_evac[kk2 - 2]))
                        mm = None
                        for ht in range(8):
                            mm = P.op("t", lambda t, ht=ht, dt=dt: t.matmul(
                                (po0, po1)[dt % 2][:, :],
                                w2s[:, ht * 1024 + dt * 128:
                                    ht * 1024 + dt * 128 + 128],
                                hid[:, ht * 512:(ht + 1) * 512],
                                start=(ht == 0), stop=(ht == 7)),
                                wait=(wt if ht == 0 else []),
                                inc=(tm if ht == 7 else None))
                        if dt == 7:
                            C[f"w2_free_{gi}"] = tm.n
                        evv = P.op("v", lambda v, dt=dt, er=er: v.scalar_tensor_tensor(
                            ysb[:, dt * 2048 + er * 512: dt * 2048 + (er + 1) * 512],
                            (po0, po1)[dt % 2][:, :], C["gbt"][:, dt:dt + 1],
                            ysb[:, dt * 2048 + er * 512: dt * 2048 + (er + 1) * 512],
                            OP.mult, OP.add),
                            wait=[(tm, tm.n)] + ([(ve, ms)] if gi % 16 == 0 and dt == 0
                                                 else []), inc=ve)
                        last_evac[kk2] = evv
            ffn_half_ev = ve.n
            sev = {}
            ysc = {}
            last_ytp = None
            for k in range(16):
                tile = half * 16 + k
                tps = []
                for dc in range(8):
                    wt2 = []
                    if dc == 0:
                        wt2.append((ve, ffn_half_ev))
                        if k >= 2:
                            wt2.append((ve, sev[k - 2]))
                    tps.append(P.op("t", lambda t, dc=dc, k=k: t.transpose(
                        pT[k % 2][:, dc * 128:(dc + 1) * 128],
                        ysb[:, dc * 2048 + k * 128: dc * 2048 + (k + 1) * 128],
                        C["eye_b"][:, :]), wait=wt2, inc=tm))
                last_ytp = tps[-1]
                we2 = [(tm, tps[-1])]
                if k >= 2:
                    we2.append((sm["rt2"], ysc[k - 2]))
                e2 = P.op("v", lambda v, k=k: v.tensor_copy(
                    yst[:, (k % 2) * D:(k % 2 + 1) * D], pT[k % 2][:, :]),
                    wait=we2, inc=ve)
                sev[k] = e2
                wt3 = [(ve, e2)]
                if half == 0 and k == 0:
                    wt3.append((sm["rt3"], yzero))
                ysc[k] = P.op("g", lambda g, tile=tile, k=k: g.indirect_dma_start(
                    C["y_hbm"],
                    bass.IndirectOffsetOnAxis(ap=idxs[:, tile:tile + 1], axis=0),
                    yst[:, (k % 2) * D:(k % 2 + 1) * D], None),
                    wait=wt3, inc=sm["rt2"], inck=16)
            if half == 0:
                ms = P.op("v", lambda v: v.memset(ysb[:, :], 0.0),
                          wait=[(tm, last_ytp)], inc=ve)
        C["ysc_all"] = sm["rt2"].n
        P.replay(nc)


def emit_out(nc, C):
    P = Rec()
    tm, ve, sg = C["tm"], C["ve"], C["sg"]
    sm = C["SM"]
    xts = (sm["xt0"], sm["xt1"])
    sps = (sm["sp0"], sm["sp1"])
    yls = (sm["at0"], sm["at1"])
    with ExitStack() as _stk:
        x2t = _stk.enter_context(nc.sbuf_tensor("x2t", [128, 2 * D], F32))
        yt = _stk.enter_context(nc.sbuf_tensor("yt", [128, 2 * D], BF16))
        outt = _stk.enter_context(nc.sbuf_tensor("outt", [128, 2 * D], F32))
        for s in (sm["sp0"], sm["sp1"], sm["sp2"]):
            P.wait("y", s, s.n)
        P.wait("y", sm["rt2"], C["ysc_all"])
        ev_done = {}
        st_done = {}
        for tt in range(SQ // 128):
            sl = tt % 2
            w = [(ve, ev_done[tt - 2])] if tt >= 2 else []
            ld = P.op("y", lambda y, tt=tt, sl=sl: y.dma_start(
                x2t[:, sl * D:(sl + 1) * D],
                C["x2_hbm"][tt * 128:(tt + 1) * 128, :]),
                wait=w, inc=xts[sl], inck=16)
            ldy = P.op("y", lambda y, tt=tt, sl=sl: y.dma_start(
                yt[:, sl * D:(sl + 1) * D],
                C["y_hbm"][tt * 128:(tt + 1) * 128, :]),
                wait=[], inc=yls[sl], inck=16)
            wad = [(xts[sl], ld), (yls[sl], ldy)]
            if tt >= 2:
                wad.append((sps[sl], st_done[tt - 2]))
            ad = P.op("v", lambda v, sl=sl: v.tensor_tensor(
                outt[:, sl * D:(sl + 1) * D], yt[:, sl * D:(sl + 1) * D],
                x2t[:, sl * D:(sl + 1) * D], OP.add), wait=wad, inc=ve)
            ev_done[tt] = ad
            st_done[tt] = P.op("y", lambda y, tt=tt, sl=sl: y.dma_start(
                C["out"][tt * 128:(tt + 1) * 128, :],
                outt[:, sl * D:(sl + 1) * D]),
                wait=[(ve, ad)], inc=sps[sl], inck=16)
        for s in sm.values():
            P.wait("y", s, s.n)
        P.replay(nc)


_CACHED = {}


def kernel(L0_attn_x, L0_attn_Wqkv, L0_attn_bqkv, L0_attn_Wr, L0_attn_br,
           L0_ffn_W1, L0_ffn_W2, L0_ffn_Wg, L0_ffn_bg, L0_c, L0_wmod):
    if "nc" not in _CACHED:
        _CACHED["nc"] = build_program()
    nc = _CACHED["nc"]

    eye = np.eye(128)
    di = np.arange(128)
    tri = dict(
        eye_bh=eye.astype(BFNP),
        eye_fh=eye.astype(np.float32),
        ones_fh=np.ones((128, 1), dtype=np.float32),
        iota_h=np.ascontiguousarray(
            np.arange(SQ, dtype=np.int32).reshape(16, 128).T),
        dump_h=np.full((128, 1), SQ, dtype=np.int32),
        eoff_h=(np.arange(8, dtype=np.float32) * 512 - 1).reshape(8, 1),
        tri_geh=(di[None, :] >= di[:, None]).astype(np.float32),
        tri_lth=(di[None, :] < di[:, None]).astype(np.float32),
        tri_leh=(di[None, :] <= di[:, None]).astype(np.float32),
        tri_gth=(di[None, :] > di[:, None]).astype(np.float32),
        ones_h=np.ones((128, 1), dtype=BFNP),
    )
    wr_f = np.ascontiguousarray(L0_attn_Wr)
    br_f = np.ascontiguousarray(L0_attn_br).reshape(1, R)
    wr_r = np.ascontiguousarray(L0_attn_Wr[::-1])
    br_r = np.ascontiguousarray(L0_attn_br[::-1]).reshape(1, R)
    shared = dict(
        wqkv=np.ascontiguousarray(L0_attn_Wqkv),
        bqkv=np.ascontiguousarray(L0_attn_bqkv).reshape(1, 3 * D),
        w1=np.ascontiguousarray(L0_ffn_W1),
        w2=np.ascontiguousarray(L0_ffn_W2),
        wg=np.ascontiguousarray(L0_ffn_Wg),
        bg=np.ascontiguousarray(L0_ffn_bg).reshape(E, 1),
        wmod=np.ascontiguousarray(L0_wmod),
        **tri,
    )
    in_maps = []
    for c in range(8):
        b = c // 2
        rev = c % 2 == 1
        xbm = L0_attn_x[b] if not rev else L0_attn_x[b][::-1]
        m = dict(shared)
        m["xb"] = np.ascontiguousarray(xbm)
        m["wr"] = wr_r if rev else wr_f
        m["br"] = br_r if rev else br_f
        m["crow"] = np.ascontiguousarray(L0_c[b]).reshape(1, D)
        in_maps.append(m)

    res = run_bass_kernel_spmd(nc, in_maps, core_ids=list(range(8)))
    out = np.empty((4, S, D), dtype=np.float32)
    for c in range(8):
        b = c // 2
        o = np.asarray(res.results[c]["out"])
        if c % 2 == 0:
            out[b, 0:SQ] = o
        else:
            out[b, SQ:S] = o[::-1]
    return out


# version 13
# speedup vs baseline: 1.1100x; 1.0213x over previous
import numpy as np
from contextlib import ExitStack
import ml_dtypes
import concourse.bass as bass
import concourse.mybir as mybir
from concourse.bass_utils import run_bass_kernel_spmd

BFNP = ml_dtypes.bfloat16
F32 = mybir.dt.float32
BF16 = mybir.dt.bfloat16
AF = mybir.ActivationFunctionType
OP = mybir.AluOpType
AX = mybir.AxisListType.X

D = 1024
S = 4096
SQ = 2048
H = 16
HD = 64
R = 257
E = 8
HF = 4096
NQT = SQ // 128
NJT = S // 128
NG = 4


class Ctr:
    def __init__(self, handle):
        self.h = handle
        self.n = 0


class Rec:
    ENG = ("t", "v", "s", "g", "y")

    def __init__(self):
        self.q = {e: [] for e in self.ENG}

    def op(self, e, fn, wait=(), inc=None, inck=1, drain=False):
        w = tuple((c.h if isinstance(c, Ctr) else c, int(v)) for c, v in wait)
        i = None
        if inc is not None:
            inc.n += inck
            i = (inc.h, inck)
        self.q[e].append((w, fn, i, drain))
        return inc.n if inc is not None else None

    def wait(self, e, c, v):
        self.op(e, None, wait=[(c, v)])

    def drain(self, e):
        self.op(e, None, drain=True)

    def replay(self, nc, barrier=True):
        with nc.Block() as block:
            dec = dict(t=block.tensor, v=block.vector, s=block.scalar,
                       g=block.gpsimd, y=block.sync)
            for e in self.ENG:
                items = self.q[e]
                if not items:
                    continue

                def body(engine, items=items):
                    for w, fn, i, dr in items:
                        if dr:
                            engine.drain()
                        for (h, val) in w:
                            engine.wait_ge(h, val)
                        inst = fn(engine) if fn is not None else None
                        if i is not None:
                            inst.then_inc(i[0], i[1])
                dec[e](body)
            self.q = {e: [] for e in self.ENG}
        if barrier:
            nc.all_engine_barrier()


class Ring:
    def __init__(self, nslots):
        self.free_at = [None] * nslots
        self.i = 0

    def acquire(self):
        s = self.i % len(self.free_at)
        self.i += 1
        return s, self.free_at[s]

    def mark(self, slot, ctr, val):
        self.free_at[slot] = (ctr, int(val))


def build_program(dbg=()):
    nc = bass.Bass("TRN2", target_bir_lowering=False,
                   detect_race_conditions=False)
    dbg = set(dbg)

    def dram(name, shape, dt, kind):
        if kind == "Internal" and name in dbg:
            kind = "ExternalOutput"
        return nc.dram_tensor(name, shape, dt, kind=kind).ap()

    C = {}
    C["xb"] = dram("xb", [S, D], F32, "ExternalInput")
    C["wqkv"] = dram("wqkv", [3 * D, D], F32, "ExternalInput")
    C["bqkv"] = dram("bqkv", [1, 3 * D], F32, "ExternalInput")
    C["wr"] = dram("wr", [R, HD], F32, "ExternalInput")
    C["br"] = dram("br", [1, R], F32, "ExternalInput")
    C["w1"] = dram("w1", [E, D, HF], F32, "ExternalInput")
    C["w2"] = dram("w2", [E, HF, D], F32, "ExternalInput")
    C["wg"] = dram("wg", [D, E], F32, "ExternalInput")
    C["bg"] = dram("bg", [E, 1], F32, "ExternalInput")
    C["crow"] = dram("crow", [1, D], F32, "ExternalInput")
    C["wmod"] = dram("wmod", [D, 6 * D], F32, "ExternalInput")
    C["eye_bh"] = dram("eye_bh", [128, 128], BF16, "ExternalInput")
    C["eye_fh"] = dram("eye_fh", [128, 128], F32, "ExternalInput")
    C["tri_geh"] = dram("tri_geh", [128, 128], F32, "ExternalInput")
    C["tri_lth"] = dram("tri_lth", [128, 128], F32, "ExternalInput")
    C["tri_leh"] = dram("tri_leh", [128, 128], F32, "ExternalInput")
    C["tri_gth"] = dram("tri_gth", [128, 128], F32, "ExternalInput")
    C["ones_h"] = dram("ones_h", [128, 1], BF16, "ExternalInput")
    C["ones_fh"] = dram("ones_fh", [128, 1], F32, "ExternalInput")
    C["iota_h"] = dram("iota_h", [128, 16], mybir.dt.int32, "ExternalInput")
    C["dump_h"] = dram("dump_h", [128, 1], mybir.dt.int32, "ExternalInput")
    C["eoff_h"] = dram("eoff_h", [8, 1], F32, "ExternalInput")
    C["out"] = dram("out", [SQ, D], F32, "ExternalOutput")
    C["rp_hbm"] = dram("rp_hbm", [8, 128, R], F32, "Internal")
    C["attn_hbm"] = dram("attn_hbm", [SQ, D], BF16, "Internal")
    C["x2_hbm"] = dram("x2_hbm", [SQ, D], F32, "Internal")
    C["mod_hbm"] = dram("mod_hbm", [1, 6 * D], F32, "Internal")
    C["h2_hbm"] = dram("h2_hbm", [SQ + 1, D], BF16, "Internal")
    C["y_hbm"] = dram("y_hbm", [SQ + 1, D], BF16, "Internal")
    C["idx_hbm"] = dram("idx_hbm", [4096, 1], mybir.dt.int32, "Internal")

    with ExitStack() as _stk:
        eye_b = _stk.enter_context(nc.sbuf_tensor("eye_b", [128, 128], BF16))
        eye_f = _stk.enter_context(nc.sbuf_tensor("eye_f", [128, 128], F32))
        tri_ge = _stk.enter_context(nc.sbuf_tensor("tri_ge", [128, 128], F32))
        tri_lt = _stk.enter_context(nc.sbuf_tensor("tri_lt", [128, 128], F32))
        tri_le = _stk.enter_context(nc.sbuf_tensor("tri_le", [128, 128], F32))
        tri_gt = _stk.enter_context(nc.sbuf_tensor("tri_gt", [128, 128], F32))
        ones_b = _stk.enter_context(nc.sbuf_tensor("ones_b", [128, 1], BF16))
        wrt8 = _stk.enter_context(nc.sbuf_tensor("wrt8", [HD, R], BF16))
        br_b = _stk.enter_context(nc.sbuf_tensor("br_b", [128, R], F32))
        ca1b = _stk.enter_context(nc.sbuf_tensor("ca1b", [128, D], F32))
        sab = _stk.enter_context(nc.sbuf_tensor("sab", [128, D], F32))
        gab = _stk.enter_context(nc.sbuf_tensor("gab", [128, D], F32))
        cb1b = _stk.enter_context(nc.sbuf_tensor("cb1b", [128, D], F32))
        sbb = _stk.enter_context(nc.sbuf_tensor("sbb", [128, D], F32))
        gbt = _stk.enter_context(nc.sbuf_tensor("gbt", [128, E], F32))
        _tm = _stk.enter_context(nc.semaphore("t_mm"))
        _ve = _stk.enter_context(nc.semaphore("v_ev"))
        _sg = _stk.enter_context(nc.semaphore("s_ge"))
        SM = {}
        for name in ("c", "wr", "wm0", "wm1", "xt0", "xt1", "xt2",
                     "at0", "at1", "sp0", "sp1", "sp2", "wst", "bias",
                     "rpw", "sk0", "sk1", "sk2", "sk3", "ao0", "ao1", "gt", "mask",
                     "w1a", "w1b", "w2s", "h1d", "cc", "mb",
                     "h2s", "rt0", "rt1", "rt2", "rt3", "gx"):
            SM[name] = Ctr(_stk.enter_context(nc.semaphore("sm_" + name)))
        C.update(eye_b=eye_b, eye_f=eye_f, tri_ge=tri_ge, tri_lt=tri_lt, tri_le=tri_le,
                 tri_gt=tri_gt, ones_b=ones_b, wrt8=wrt8, br_b=br_b,
                 ca1b=ca1b, sab=sab, gab=gab, cb1b=cb1b, sbb=sbb, gbt=gbt,
                 tm=Ctr(_tm), ve=Ctr(_ve), sg=Ctr(_sg), SM=SM)

        if "DUMP_PT" in dbg:
            C["DUMP_PT"] = True
        if "NO_POS" in dbg:
            C["NO_POS"] = True
        with nc.Block() as _blk0:
            @_blk0.gpsimd
            def _(g):
                for s in list(SM.values()):
                    g.sem_clear(s.h)
                g.sem_clear(_tm)
                g.sem_clear(_ve)
                g.sem_clear(_sg)
        nc.all_engine_barrier()
        emit_p0(nc, C)
        with nc.sbuf_tensor("h1T", [128, 8 * S], BF16) as h1T, \
             nc.sbuf_tensor("stats_sb", [128, 32 * 8], F32) as _ssb:
            C["h1T"] = h1T
            if "DUMP_H1T" in dbg:
                C["stats_dump"] = _ssb
            emit_ln(nc, C, first=True)
            if "DUMP_H1T" in dbg:
                dmp = nc.dram_tensor("h1T_dump", [128, 8 * S], BF16,
                                     kind="ExternalOutput").ap()
                sdmp_h = nc.dram_tensor("stats_dump", [128, 32 * 8], F32,
                                        kind="ExternalOutput").ap()
                dmp2 = nc.dram_tensor("bc_dump", [128, 3 * D], F32,
                                      kind="ExternalOutput").ap()
                Pd = Rec()
                Pd.wait("y", C["ve"], C["p1_done"])
                Pd.op("y", lambda y: y.dma_start(dmp, h1T[:, :]),
                      inc=C["SM"]["h1d"], inck=16)
                Pd.op("y", lambda y: y.dma_start(dmp2[:, 0:D], C["ca1b"][:, :]),
                      inc=C["SM"]["h1d"], inck=16)
                Pd.op("y", lambda y: y.dma_start(dmp2[:, D:2 * D], C["sab"][:, :]),
                      inc=C["SM"]["h1d"], inck=16)
                Pd.op("y", lambda y: y.dma_start(dmp2[:, 2 * D:3 * D], C["gab"][:, :]),
                      inc=C["SM"]["h1d"], inck=16)
                Pd.op("y", lambda y: y.dma_start(sdmp_h, C["stats_dump"][:, :]),
                      inc=C["SM"]["h1d"], inck=16)
                Pd.wait("y", C["SM"]["h1d"], C["SM"]["h1d"].n)
                Pd.replay(nc)
            if "STOP_P1" not in dbg:
                emit_attn(nc, C)
                if "STOP_P2" in dbg:
                    Pd = Rec()
                    Pd.wait("y", C["SM"]["ao0"], C["SM"]["ao0"].n)
                    Pd.wait("y", C["SM"]["ao1"], C["SM"]["ao1"].n)
                    Pd.replay(nc)
        if "STOP_P1" in dbg or "STOP_P2" in dbg:
            return nc
        with nc.sbuf_tensor("h2T", [128, 8 * SQ], BF16) as h2T, \
                nc.sbuf_tensor("h2loT", [128, 8 * SQ], BF16) as h2loT:
            C["h2T"] = h2T
            C["h2loT"] = h2loT
            emit_ln(nc, C, first=False)
            emit_gate(nc, C)
            if "STOP_P3" in dbg:
                Pd = Rec()
                for s in C["SM"].values():
                    Pd.wait("y", s, s.n)
                Pd.replay(nc)
                return nc
        with nc.sbuf_tensor("ysb", [128, 8 * SQ], BF16) as ysb:
            C["ysb"] = ysb
            emit_ffn(nc, C)
            emit_out(nc, C)
    return nc


def emit_p0(nc, C):
    P = Rec()
    tm, ve, sg = C["tm"], C["ve"], C["sg"]
    sm = C["SM"]
    smc, smwr, wm = sm["c"], sm["wr"], (sm["wm0"], sm["wm1"])
    smcc, smmb = sm["cc"], sm["mb"]

    for dst, src in [(C["eye_b"], C["eye_bh"]), (C["eye_f"], C["eye_fh"]),
                     (C["tri_ge"], C["tri_geh"]),
                     (C["tri_lt"], C["tri_lth"]), (C["tri_le"], C["tri_leh"]),
                     (C["tri_gt"], C["tri_gth"]), (C["ones_b"], C["ones_h"])]:
        P.op("g", lambda g, d=dst, s=src: g.dma_start(d[:, :], s), inc=smc, inck=16)
    P.op("g", lambda g: g.dma_start(C["br_b"][:, :],
                                    C["br"].partition_broadcast(128)),
         inc=smc, inck=16)
    ld_consts = smc.n

    with ExitStack() as _stk:
        wr_st = _stk.enter_context(nc.sbuf_tensor("wr_st", [128, 128], BF16))
        c_sb = _stk.enter_context(nc.sbuf_tensor("c_sb", [1, D], F32))
        cs_b = _stk.enter_context(nc.sbuf_tensor("cs_b", [1, D], BF16))
        csT = _stk.enter_context(nc.sbuf_tensor("csT", [128, 8], BF16))
        wmod_sb = _stk.enter_context(nc.sbuf_tensor("wmod_sb", [128, 2 * 1536], BF16))
        modrow = _stk.enter_context(nc.sbuf_tensor("modrow", [1, 6 * D], F32))
        g_row = _stk.enter_context(nc.sbuf_tensor("g_row", [1, D], F32))
        pP0f = _stk.enter_context(nc.psum_tensor("pP0f", [128, 512], F32))
        pP0b = _stk.enter_context(nc.psum_tensor("pP0b", [128, 1024], BF16))
        pmod = _stk.enter_context(nc.psum_tensor("pmod", [1, 1536], F32))
        P.wait("t", smc, ld_consts)
        last_tp = None
        last_evw = None
        for ch in range(3):
            r0, r1 = [(0, 128), (128, 256), (256, 257)][ch]
            n = r1 - r0
            w = [(tm, last_tp)] if last_tp is not None else []
            ld = P.op("g", lambda g, r0=r0, r1=r1, n=n: g.dma_start(
                wr_st[0:n, 0:HD], C["wr"][r0:r1, :]), wait=w, inc=smwr, inck=16)
            wtp = [(smwr, ld)]
            if last_evw is not None:
                wtp.append((ve, last_evw))
            tp = P.op("t", lambda t, n=n, ch=ch: t.transpose(
                pP0b[0:HD, ch * 128:ch * 128 + n], wr_st[0:n, 0:HD],
                C["eye_b"][0:n, 0:n]), wait=wtp, inc=tm)
            last_evw = P.op("v", lambda v, ch=ch, n=n: v.tensor_scalar(
                C["wrt8"][:, ch * 128:ch * 128 + n],
                pP0b[0:HD, ch * 128:ch * 128 + n], 8.0, None, OP.mult),
                wait=[(tm, tp)], inc=ve)
            last_tp = tp

        ldc = P.op("g", lambda g: g.dma_start(c_sb[:, :], C["crow"]),
                   inc=smcc, inck=16)
        sil = P.op("s", lambda s: s.activation(cs_b[:, :], c_sb[:, :], AF.Silu),
                   wait=[(smcc, ldc)], inc=sg)
        tpc = None
        for dc in range(8):
            wtc = [(sg, sil), (ve, last_evw)] if dc == 0 else []
            tpc = P.op("t", lambda t, dc=dc: t.transpose(
                pP0b[:, 384 + 4 * dc:385 + 4 * dc],
                cs_b[:, dc * 128:(dc + 1) * 128],
                C["eye_b"][0:1, 0:1]), wait=wtc, inc=tm)
        evc = P.op("v", lambda v: v.tensor_copy(csT[:, :], pP0b[:, 384:416:4]),
                   wait=[(tm, tpc)], inc=ve)

        mm_done = {}
        ev_prev = None
        for q4 in range(4):
            for dc in range(8):
                k = q4 * 8 + dc
                bo = (k % 2) * 1536
                w = [(tm, mm_done[k - 2])] if k >= 2 else []
                ld = P.op("g", lambda g, q4=q4, dc=dc, bo=bo: g.dma_start(
                    wmod_sb[:, bo:bo + 1536],
                    C["wmod"][dc * 128:(dc + 1) * 128,
                              q4 * 1536:(q4 + 1) * 1536]),
                    wait=w, inc=wm[k % 2], inck=16)
                mm = None
                for ns in range(3):
                    wmm = []
                    if ns == 0:
                        wmm.append((wm[k % 2], ld))
                        if dc == 0:
                            wmm.append((ve, evc if ev_prev is None else ev_prev))
                    mm = P.op("t", lambda t, dc=dc, bo=bo, ns=ns: t.matmul(
                        pmod[:, ns * 512:(ns + 1) * 512], csT[:, dc:dc + 1],
                        wmod_sb[:, bo + ns * 512:bo + (ns + 1) * 512],
                        start=(dc == 0), stop=(dc == 7)),
                        wait=wmm, inc=(tm if ns == 2 else None))
                mm_done[k] = tm.n
            ev_prev = P.op("v", lambda v, q4=q4: v.tensor_copy(
                modrow[:, q4 * 1536:(q4 + 1) * 1536], pmod[:, :]),
                wait=[(tm, mm_done[q4 * 8 + 7])], inc=ve)

        def seg(i):
            return modrow[:, i * D:(i + 1) * D]
        P.op("v", lambda v: v.tensor_scalar(seg(1), seg(1), 1.0, None, OP.add),
             wait=[(ve, ev_prev)], inc=ve)
        P.op("v", lambda v: v.tensor_scalar(seg(4), seg(4), 1.0, None, OP.add),
             inc=ve)
        tlast = None
        for i in (2, 5):
            ab = P.op("s", lambda s, i=i: s.activation(
                g_row[:, :], seg(i), AF.Abs), wait=[(ve, ve.n)], inc=sg)
            P.op("v", lambda v, i=i: v.tensor_scalar(
                g_row[:, :], g_row[:, :], 0.25, None, OP.max),
                wait=[(sg, ab)], inc=ve)
            P.op("v", lambda v, i=i: v.tensor_scalar(
                seg(i), seg(i), 0.0, 2.0, OP.is_ge, OP.mult), inc=ve)
            P.op("v", lambda v, i=i: v.tensor_scalar(
                seg(i), seg(i), -1.0, None, OP.add), inc=ve)
            tlast = P.op("v", lambda v, i=i: v.tensor_tensor(
                seg(i), seg(i), g_row[:, :], OP.mult), inc=ve)
        st = P.op("g", lambda g: g.dma_start(C["mod_hbm"], modrow[:, :]),
                  wait=[(ve, tlast)], inc=smmb, inck=16)
        first_b = True
        for dst, i in [(C["sab"], 0), (C["ca1b"], 1), (C["gab"], 2),
                       (C["sbb"], 3), (C["cb1b"], 4)]:
            P.op("g", lambda g, d=dst, i=i: g.dma_start(
                d[:, :], C["mod_hbm"][:, i * D:(i + 1) * D].partition_broadcast(128)),
                wait=([(smmb, st)] if first_b else []), inc=smmb, inck=16)
            first_b = False
        C["p0_bcast_c"] = smmb.n
        tb = None
        for dc in range(8):
            tb = P.op("t", lambda t, dc=dc: t.transpose(
                pP0f[:, dc:dc + 1], seg(5)[:, dc * 128:(dc + 1) * 128],
                C["eye_f"][0:1, 0:1]),
                wait=[(ve, tlast)], inc=tm)
        P.op("v", lambda v: v.tensor_copy(C["gbt"][:, :], pP0f[:, 0:8]),
             wait=[(tm, tb)], inc=ve)
        P.wait("v", smc, ld_consts)
        P.replay(nc)


def emit_ln(nc, C, first):
    P = Rec()
    tm, ve, sg = C["tm"], C["ve"], C["sg"]
    sm = C["SM"]
    xts = (sm["xt0"], sm["xt1"], sm["xt2"])
    ats = (sm["at0"], sm["at1"])
    sps = (sm["sp0"], sm["sp1"], sm["sp2"])
    ntt = (S if first else SQ) // 128
    dstT = C["h1T"] if first else C["h2T"]
    seqw = S if first else SQ
    cab = C["ca1b"] if first else C["cb1b"]
    sab = C["sab"] if first else C["sbb"]
    stats_dmp = C.get("stats_dump") if first else None
    if first:
        P.wait("v", sm["mb"], C["p0_bcast_c"])
    else:
        P.wait("y", sm["ao0"], sm["ao0"].n)
        P.wait("y", sm["ao1"], sm["ao1"].n)

    sfx = "1" if first else "2"
    with ExitStack() as _stk:
        xt = _stk.enter_context(nc.sbuf_tensor("xt" + sfx, [128, 3 * D], F32))
        at = _stk.enter_context(nc.sbuf_tensor("at" + sfx, [128, 2 * D], BF16))
        xsq = _stk.enter_context(nc.sbuf_tensor("xsq" + sfx, [128, D], F32))
        stats = _stk.enter_context(nc.sbuf_tensor("stats" + sfx, [128, 8], F32))
        hst = _stk.enter_context(nc.sbuf_tensor("hst" + sfx, [128, 2 * D], BF16))
        hstF = _stk.enter_context(nc.sbuf_tensor("hstF" + sfx, [128, D], F32))
        hstLo = _stk.enter_context(nc.sbuf_tensor("hstLo" + sfx, [128, 2 * D], BF16))
        pL0 = _stk.enter_context(nc.psum_tensor("pL0" + sfx, [128, 1024], BF16))
        pL1 = _stk.enter_context(nc.psum_tensor("pL1" + sfx, [128, 1024], BF16))
        pL2 = _stk.enter_context(nc.psum_tensor("pL2" + sfx, [128, 1024], BF16))
        pL3 = _stk.enter_context(nc.psum_tensor("pL3" + sfx, [128, 1024], BF16))
        ev_done = {}
        tp_done = {}
        tp_last = {}
        spill_ds = {}
        sq2_prev = [None]
        h2sp = {}
        for tt in range(ntt):
            sl = tt % 3
            w = [(ve, ev_done[tt - 3])] if tt >= 3 else []
            if (not first) and tt >= 3:
                w.append((sps[sl], spill_ds[tt - 3]))
            ld = P.op("y", lambda y, tt=tt, sl=sl: y.dma_start(
                xt[:, sl * D:(sl + 1) * D], C["xb"][tt * 128:(tt + 1) * 128, :]),
                wait=w, inc=xts[sl], inck=16)
            if not first:
                w2 = [(ve, ev_done[tt - 2])] if tt >= 2 else []
                lda = P.op("y", lambda y, tt=tt: y.dma_start(
                    at[:, (tt % 2) * D:(tt % 2 + 1) * D],
                    C["attn_hbm"][tt * 128:(tt + 1) * 128, :]),
                    wait=w2, inc=ats[tt % 2], inck=16)
                add = P.op("v", lambda v, sl=sl, tt=tt: v.tensor_tensor(
                    xt[:, sl * D:(sl + 1) * D], xt[:, sl * D:(sl + 1) * D],
                    at[:, (tt % 2) * D:(tt % 2 + 1) * D], OP.add),
                    wait=[(xts[sl], ld), (ats[tt % 2], lda)], inc=ve)
                spill_ds[tt] = P.op("y", lambda y, tt=tt, sl=sl: y.dma_start(
                    C["x2_hbm"][tt * 128:(tt + 1) * 128, :],
                    xt[:, sl * D:(sl + 1) * D]),
                    wait=[(ve, add)], inc=sps[sl], inck=16)
                ready_v = [(ve, add)]
                ready_s = [(ve, add)]
            else:
                ready_v = [(xts[sl], ld)]
                ready_s = [(xts[sl], ld)]
            P.op("v", lambda v, sl=sl: v.tensor_reduce(
                stats[:, 0:1], xt[:, sl * D:(sl + 1) * D], AX, OP.add),
                wait=ready_v, inc=ve)
            sq = P.op("s", lambda s, sl=sl: s.activation(
                xsq[:, :], xt[:, sl * D:(sl + 1) * D], AF.Square),
                wait=ready_s + ([(ve, ev_done[tt - 1])] if tt >= 1 else []),
                inc=sg)
            ws2 = [(sg, sq)]
            if tt >= 1:
                ws2.append((sg, sq2_prev[0]))
            P.op("v", lambda v: v.tensor_reduce(
                stats[:, 1:2], xsq[:, :], AX, OP.add), wait=ws2, inc=ve)
            P.op("v", lambda v: v.tensor_scalar(
                stats[:, 0:1], stats[:, 0:1], 1.0 / D, None, OP.mult), inc=ve,
                drain=True)
            P.op("v", lambda v: v.tensor_scalar(
                stats[:, 1:2], stats[:, 1:2], 1.0 / D, None, OP.mult), inc=ve,
                drain=True)
            P.op("v", lambda v: v.tensor_tensor(
                stats[:, 2:3], stats[:, 0:1], stats[:, 0:1], OP.mult), inc=ve,
                drain=True)
            v5 = P.op("v", lambda v: v.tensor_scalar(
                stats[:, 1:2], stats[:, 1:2], stats[:, 2:3], 1e-5,
                OP.subtract, OP.add), inc=ve, drain=True)
            sq2 = P.op("s", lambda s: s.activation(
                stats[:, 3:4], stats[:, 1:2], AF.Sqrt), wait=[(ve, v5)], inc=sg)
            sq2_prev[0] = sq2
            v6 = P.op("v", lambda v: v.reciprocal(
                stats[:, 4:5], stats[:, 3:4]), wait=[(sg, sq2)], inc=ve)
            if stats_dmp is not None:
                P.op("v", lambda v, tt=tt: v.tensor_copy(
                    stats_dmp[:, tt * 8:tt * 8 + 8], stats[:, :]), inc=ve)
            P.op("v", lambda v, sl=sl: v.tensor_scalar(
                xsq[:, :], xt[:, sl * D:(sl + 1) * D], stats[:, 0:1],
                stats[:, 4:5], OP.subtract, OP.mult), wait=[(ve, v6)], inc=ve,
                drain=True)
            P.op("v", lambda v: v.tensor_tensor(
                xsq[:, :], xsq[:, :], cab[:, :], OP.mult), inc=ve)
            hsl = (tt % 2) * D
            wa3 = [(tm, tp_last[tt - 2])] if tt >= 2 else []
            if (not first) and tt >= 2:
                wa3.append((sm["h2s"], h2sp[tt - 2]))
            if first:
                a3 = P.op("v", lambda v, hsl=hsl: v.tensor_tensor(
                    hst[:, hsl:hsl + D], xsq[:, :], sab[:, :], OP.add),
                    wait=wa3, inc=ve)
            else:
                P.op("v", lambda v: v.tensor_tensor(
                    hstF[:, :], xsq[:, :], sab[:, :], OP.add),
                    wait=wa3, inc=ve)
                P.op("v", lambda v, hsl=hsl: v.tensor_copy(
                    hst[:, hsl:hsl + D], hstF[:, :]), inc=ve, drain=True)
                a3 = P.op("v", lambda v, hsl=hsl: v.scalar_tensor_tensor(
                    hstLo[:, hsl:hsl + D], hst[:, hsl:hsl + D], -1.0,
                    hstF[:, :], OP.mult, OP.add), inc=ve, drain=True)
            ev_done[tt] = a3
            if not first:
                h2sp[tt] = P.op("y", lambda y, tt=tt, hsl=hsl: y.dma_start(
                    C["h2_hbm"][tt * 128:(tt + 1) * 128, :],
                    hst[:, hsl:hsl + D]),
                    wait=[(ve, a3)], inc=sm["h2s"], inck=16)
            pps = [(pL0, pL1)[tt % 2], (pL2, pL3)[tt % 2]]
            tps = []
            srcs = [(hst, dstT)] if first else [(hst, dstT), (hstLo, C["h2loT"])]
            for si, (sbuf_src, _dst) in enumerate(srcs):
                for dc in range(8):
                    wt = [(ve, a3)] if (dc == 0 and si == 0) else []
                    if dc == 0 and si == 0 and tt >= 2:
                        wt.append((ve, tp_done[tt - 2]))
                    tp = P.op("t", lambda t, dc=dc, pp=pps[si], hsl=hsl,
                              s=sbuf_src: t.transpose(
                                  pp[:, dc * 128:(dc + 1) * 128],
                                  s[:, hsl + dc * 128:hsl + (dc + 1) * 128],
                                  C["eye_b"][:, :]),
                              wait=wt, inc=tm)
                    tps.append(tp)
            tp_last[tt] = tps[-1]
            e2 = None
            for si, (sbuf_src, dst_) in enumerate(srcs):
                for dc in range(8):
                    e2 = P.op("v", lambda v, dc=dc, tt=tt, pp=pps[si], d=dst_:
                              v.tensor_copy(
                                  d[:, dc * seqw + tt * 128:
                                    dc * seqw + (tt + 1) * 128],
                                  pp[:, dc * 128:(dc + 1) * 128]),
                              wait=[(tm, tps[(si + 1) * 8 - 1])], inc=ve)
            tp_done[tt] = e2
        C["p1_done" if first else "p3_ln_done"] = ve.n
        if not first:
            zr = P.op("v", lambda v: v.memset(hst[0:1, 0:D], 0.0),
                      wait=[(sm["h2s"], h2sp[ntt - 1]),
                            (sm["h2s"], h2sp[ntt - 2])], inc=ve)
            P.op("y", lambda y: y.dma_start(C["h2_hbm"][SQ:SQ + 1, :],
                                            hst[0:1, 0:D]),
                 wait=[(ve, zr)], inc=sm["h2s"], inck=16)
            C["h2s_all"] = sm["h2s"].n
        P.replay(nc)


def emit_attn(nc, C):
    P = Rec()
    tm, ve, sg = C["tm"], C["ve"], C["sg"]
    sm = C["SM"]
    smb, smw, smr = sm["bias"], sm["wst"], sm["rpw"]
    sks = (sm["sk0"], sm["sk1"], sm["sk2"], sm["sk3"])
    aos = (sm["ao0"], sm["ao1"])
    h1T, eye_b = C["h1T"], C["eye_b"]
    if "DUMP_PT" in C:
        C["pt_dump"] = nc.dram_tensor("pt_dump", [NJT, 128, 512], BF16,
                                      kind="ExternalOutput").ap()

    with ExitStack() as _stk:
        w_st = _stk.enter_context(nc.sbuf_tensor("w_st", [128, D], BF16))
        wqT = _stk.enter_context(nc.sbuf_tensor("wqT", [128, D], BF16))
        wkT = _stk.enter_context(nc.sbuf_tensor("wkT", [128, D], BF16))
        wvT = _stk.enter_context(nc.sbuf_tensor("wvT", [128, D], BF16))
        qb2 = _stk.enter_context(nc.sbuf_tensor("qb2", [128, 1], F32))
        kb2 = _stk.enter_context(nc.sbuf_tensor("kb2", [128, 1], F32))
        bvb = _stk.enter_context(nc.sbuf_tensor("bvb", [128, 128], F32))
        qN = _stk.enter_context(nc.sbuf_tensor("qN", [65, 2 * SQ], BF16))
        qA = _stk.enter_context(nc.sbuf_tensor("qA", [65, 2 * SQ], BF16))
        qB = _stk.enter_context(nc.sbuf_tensor("qB", [65, 2 * SQ], BF16))
        kt = _stk.enter_context(nc.sbuf_tensor("kt", [65, 2 * S], BF16))
        vsb = _stk.enter_context(nc.sbuf_tensor("vsb", [128, 2 * NJT * 65], BF16))
        qsq = _stk.enter_context(nc.sbuf_tensor("qsq", [64, SQ], BF16))
        mrow = _stk.enter_context(nc.sbuf_tensor("mrow", [1, 2 * SQ], BF16))
        nkrow = _stk.enter_context(nc.sbuf_tensor("nkrow", [1, 512], F32))
        maxk = _stk.enter_context(nc.sbuf_tensor("maxk", [1, 16], F32))
        skews = _stk.enter_context(nc.sbuf_tensor("skews", [128, 4 * 384], F32))
        poss = _stk.enter_context(nc.sbuf_tensor("poss", [128, 4 * 384], F32))
        posb = _stk.enter_context(nc.sbuf_tensor("posb", [128, 8 * 384], BF16))
        rp01 = _stk.enter_context(nc.sbuf_tensor("rp01", [128, 16], BF16))
        epT = _stk.enter_context(nc.sbuf_tensor("epT", [128, 4 * 12 * 128], BF16))
        rp_sb = _stk.enter_context(nc.sbuf_tensor("rp_sb", [128, 8 * R], F32))
        PT = _stk.enter_context(nc.sbuf_tensor("PT", [128, 6 * 512], BF16))
        avsb = _stk.enter_context(nc.sbuf_tensor("avsb", [65, 1024], BF16))
        avrc = _stk.enter_context(nc.sbuf_tensor("avrc", [128, 1], F32))
        avout = _stk.enter_context(nc.sbuf_tensor("avout", [128, 2 * 64], BF16))
        pS0 = _stk.enter_context(nc.psum_tensor("pS0", [128, 512], F32))
        pS1 = _stk.enter_context(nc.psum_tensor("pS1", [128, 512], F32))
        pS2 = _stk.enter_context(nc.psum_tensor("pS2", [128, 512], F32))
        pS3 = _stk.enter_context(nc.psum_tensor("pS3", [128, 512], F32))
        pAVa = _stk.enter_context(nc.psum_tensor("pAVa", [128, 512], F32))
        pAVb = _stk.enter_context(nc.psum_tensor("pAVb", [128, 512], F32))
        pBF0 = _stk.enter_context(nc.psum_tensor("pBF0", [128, 1024], BF16))
        pBF1 = _stk.enter_context(nc.psum_tensor("pBF1", [128, 1024], BF16))
        pS = [pS0, pS1, pS2, pS3]
        pBF = [pBF0, pBF1]
        rS = Ring(4)
        rBF = Ring(2)
        rRP = Ring(8)
        rPT = Ring(6)
        rSK = Ring(4)
        rAO = Ring(2)
        rHB = Ring(8)
        pAV = {0: pAVa, 1: pAVb}

        posT_last = [None] * 8
        posex_last = [None, None, None, None]
        tpr_last = [None] * 8
        prologue_mm = [None]
        P.wait("t", ve, C["p1_done"])
        P.op("v", lambda v: v.memset(kt[64:65, :], 1.0), inc=ve)
        ones_ev = P.op("v", lambda v: v.memset(
            vsb[:, 64:64 + 65 * (2 * NJT - 1) + 1:65], 1.0), inc=ve)
        C["p2_ones_ev"] = ones_ev

        def pair_setup(p):
            h0, h1 = 2 * p, 2 * p + 1
            bq = C["bqkv"]
            if p > 0:
                P.wait("g", tm, C[f"proj_mm_{p - 1}"])
            for dst, off0, off1 in ((qb2, 192 * h0, 192 * h1),
                                    (kb2, 192 * h0 + 64, 192 * h1 + 64)):
                P.op("g", lambda g, d=dst, o=off0: g.dma_start(
                    d[0:64, :], bq[:, o:o + 64].transpose((1, 0))),
                    inc=smb, inck=16)
                P.op("g", lambda g, d=dst, o=off1: g.dma_start(
                    d[64:128, :], bq[:, o:o + 64].transpose((1, 0))),
                    inc=smb, inck=16)
            P.op("g", lambda g: g.dma_start(
                bvb[:, 0:64],
                bq[:, 192 * h0 + 128:192 * h0 + 192].partition_broadcast(128)),
                inc=smb, inck=16)
            P.op("g", lambda g: g.dma_start(
                bvb[:, 64:128],
                bq[:, 192 * h1 + 128:192 * h1 + 192].partition_broadcast(128)),
                inc=smb, inck=16)
            bias_ld = smb.n
            P.op("v", lambda v: v.tensor_scalar(
                qb2[:, :], qb2[:, :], 0.125, None, OP.mult),
                wait=[(smb, bias_ld)], inc=ve)

            for kind, dstW, scale in (("q", wqT, 0.125), ("k", wkT, None),
                                      ("v", wvT, None)):
                off = {"q": 0, "k": 64, "v": 128}[kind]
                P.op("g", lambda g, o=off: g.dma_start(
                    w_st[0:64, :], C["wqkv"][192 * h0 + o:192 * h0 + o + 64, :]),
                    inc=smw, inck=16)
                lb = P.op("g", lambda g, o=off: g.dma_start(
                    w_st[64:128, :], C["wqkv"][192 * h1 + o:192 * h1 + o + 64, :]),
                    inc=smw, inck=16)
                sbf, wbf = rBF.acquire()
                tps = []
                for dc in range(8):
                    wt = []
                    if dc == 0:
                        wt.append((smw, lb))
                        if wbf is not None:
                            wt.append(wbf)
                    tp = P.op("t", lambda t, dc=dc, s=sbf: t.transpose(
                        pBF[s][:, dc * 128:(dc + 1) * 128],
                        w_st[:, dc * 128:(dc + 1) * 128], eye_b[:, :]),
                        wait=wt, inc=tm)
                    tps.append(tp)
                evl = None
                for dc in range(8):
                    if scale is None:
                        evl = P.op("v", lambda v, dc=dc, s=sbf, dW=dstW:
                                   v.tensor_copy(dW[:, dc * 128:(dc + 1) * 128],
                                                 pBF[s][:, dc * 128:(dc + 1) * 128]),
                                   wait=[(tm, tps[-1])], inc=ve)
                    else:
                        evl = P.op("v", lambda v, dc=dc, s=sbf, dW=dstW, sc=scale:
                                   v.tensor_scalar(dW[:, dc * 128:(dc + 1) * 128],
                                                   pBF[s][:, dc * 128:(dc + 1) * 128],
                                                   sc, None, OP.mult),
                                   wait=[(tm, tps[-1])], inc=ve)
                rBF.mark(sbf, ve, evl)
                P.wait("g", tm, tps[-1])
            wT_ev = ve.n

            for ns in range(4):
                sl, wfree = rS.acquire()
                wt = [(ve, wT_ev), (smb, bias_ld)] + ([wfree] if wfree else [])
                mm = None
                for dc in range(8):
                    mm = P.op("t", lambda t, dc=dc, ns=ns, sl=sl: t.matmul(
                        pS[sl][:, :], wqT[:, dc * 128:(dc + 1) * 128],
                        h1T[:, dc * S + ns * 512: dc * S + (ns + 1) * 512],
                        start=(dc == 0), stop=(dc == 7)),
                        wait=(wt if dc == 0 else []),
                        inc=(tm if dc == 7 else None))
                ev = None
                for dstq in (qN, qA, qB):
                    for pi in range(2):
                        ev = P.op("v", lambda v, d=dstq, pi=pi, ns=ns, sl=sl:
                                  v.tensor_scalar(
                                      d[0:64, pi * SQ + ns * 512:
                                        pi * SQ + (ns + 1) * 512],
                                      pS[sl][pi * 64:(pi + 1) * 64, :],
                                      qb2[pi * 64:(pi + 1) * 64, :],
                                      None, OP.add),
                                  wait=[(tm, mm)], inc=ve)
                rS.mark(sl, ve, ev)
            C[f"proj_q_{p}"] = ve.n
            for ns in range(8):
                sl, wfree = rS.acquire()
                wt = [(ve, wT_ev), (smb, bias_ld)] + ([wfree] if wfree else [])
                mm = None
                for dc in range(8):
                    mm = P.op("t", lambda t, dc=dc, ns=ns, sl=sl: t.matmul(
                        pS[sl][:, :], wkT[:, dc * 128:(dc + 1) * 128],
                        h1T[:, dc * S + ns * 512: dc * S + (ns + 1) * 512],
                        start=(dc == 0), stop=(dc == 7)),
                        wait=(wt if dc == 0 else []),
                        inc=(tm if dc == 7 else None))
                ev = None
                for pi in range(2):
                    ev = P.op("v", lambda v, pi=pi, ns=ns, sl=sl:
                              v.tensor_scalar(
                                  kt[0:64, pi * S + ns * 512:
                                     pi * S + (ns + 1) * 512],
                                  pS[sl][pi * 64:(pi + 1) * 64, :],
                                  kb2[pi * 64:(pi + 1) * 64, :], None, OP.add),
                              wait=[(tm, mm)], inc=ve)
                rS.mark(sl, ve, ev)
            C[f"proj_k_{p}"] = ve.n
            for jt in range(NJT):
                sl, wfree = rS.acquire()
                wt = [(ve, wT_ev), (smb, bias_ld)] + ([wfree] if wfree else [])
                mm = None
                for dc in range(8):
                    mm = P.op("t", lambda t, dc=dc, jt=jt, sl=sl: t.matmul(
                        pS[sl][:, 0:128],
                        h1T[:, dc * S + jt * 128: dc * S + (jt + 1) * 128],
                        wvT[:, dc * 128:(dc + 1) * 128],
                        start=(dc == 0), stop=(dc == 7)),
                        wait=(wt if dc == 0 else []),
                        inc=(tm if dc == 7 else None))
                ev = None
                for pi in range(2):
                    ev = P.op("v", lambda v, pi=pi, jt=jt, sl=sl:
                              v.tensor_tensor(
                                  vsb[:, pi * NJT * 65 + jt * 65:
                                      pi * NJT * 65 + jt * 65 + 64],
                                  pS[sl][:, pi * 64:(pi + 1) * 64],
                                  bvb[:, pi * 64:(pi + 1) * 64], OP.add),
                              wait=[(tm, mm)], inc=ve)
                rS.mark(sl, ve, ev)
            C[f"proj_ev_{p}"] = ve.n
            C[f"proj_mm_{p}"] = tm.n

        def prologue(h):
            pi = h % 2
            wsq1 = [(ve, C[f"proj_q_{h // 2}"])]
            if prologue_mm[0] is not None:
                wsq1.append((tm, prologue_mm[0]))
            sq1 = P.op("s", lambda s, pi=pi: s.activation(
                qsq[:, :], qN[0:64, pi * SQ:(pi + 1) * SQ], AF.Square),
                wait=wsq1, inc=sg)
            for ns in range(4):
                sl, wfree = rS.acquire()
                wt = [(sg, sq1)] if ns == 0 else []
                if wfree is not None:
                    wt.append(wfree)
                mm = P.op("t", lambda t, ns=ns, sl=sl: t.matmul(
                    pS[sl][0:1, :], C["ones_b"][0:64, :],
                    qsq[0:64, ns * 512:(ns + 1) * 512], start=True, stop=True),
                    wait=wt, inc=tm)
                sv = P.op("s", lambda s, pi=pi, ns=ns, sl=sl: s.activation(
                    mrow[0:1, pi * SQ + ns * 512:pi * SQ + (ns + 1) * 512],
                    pS[sl][0:1, :], AF.Sqrt), wait=[(tm, mm)], inc=sg)
                rS.mark(sl, sg, sv)
            mm_hist = {}
            for ns in range(8):
                ks = (ns % 4) * 512
                if ns == 0:
                    wsq = [(ve, C[f"proj_k_{h // 2}"])]
                elif ns >= 4:
                    wsq = [(tm, mm_hist[ns - 4])]
                else:
                    wsq = []
                sq2 = P.op("s", lambda s, pi=pi, ns=ns, ks=ks: s.activation(
                    qsq[:, ks:ks + 512], kt[0:64, pi * S + ns * 512:
                                            pi * S + (ns + 1) * 512], AF.Square),
                    wait=wsq, inc=sg)
                sl, wfree = rS.acquire()
                mm = P.op("t", lambda t, sl=sl, ks=ks: t.matmul(
                    pS[sl][0:1, :], C["ones_b"][0:64, :], qsq[0:64, ks:ks + 512],
                    start=True, stop=True),
                    wait=[(sg, sq2)] + ([wfree] if wfree else []), inc=tm)
                rv = P.op("v", lambda v, ns=ns, sl=sl: v.tensor_reduce(
                    maxk[:, ns:ns + 1], pS[sl][0:1, :], AX, OP.max),
                    wait=[(tm, mm)], inc=ve)
                rS.mark(sl, ve, rv)
                mm_hist[ns] = mm
            prologue_mm[0] = mm_hist[7]
            red = P.op("v", lambda v: v.tensor_reduce(
                maxk[:, 8:9], maxk[:, 0:8], AX, OP.max), inc=ve, drain=True)
            sqm = P.op("s", lambda s: s.activation(
                maxk[:, 9:10], maxk[:, 8:9], AF.Sqrt), wait=[(ve, red)], inc=sg)
            m1 = P.op("v", lambda v, pi=pi: v.tensor_scalar(
                mrow[0:1, pi * SQ:(pi + 1) * SQ],
                mrow[0:1, pi * SQ:(pi + 1) * SQ], maxk[0:1, 9:10], -1.0,
                OP.mult, OP.mult), wait=[(sg, sqm)], inc=ve)
            r64 = P.op("v", lambda v, pi=pi: v.tensor_copy(
                qN[64:65, pi * SQ:(pi + 1) * SQ],
                mrow[0:1, pi * SQ:(pi + 1) * SQ]), wait=[(ve, m1)], inc=ve)
            C[f"row64_{h}"] = r64

        def rp_stage_a1(h, qt):
            pi = h % 2
            slr, wr_ = rRP.acquire()
            slh, wh_ = rHB.acquire()
            sl, wfree = rS.acquire()
            wt = [(ve, C[f"proj_ev_{h // 2}"])] + ([wfree] if wfree else [])
            mm = P.op("t", lambda t, pi=pi, qt=qt, sl=sl: t.matmul(
                pS[sl][:, 0:R], qN[0:64, pi * SQ + qt * 128:
                                   pi * SQ + (qt + 1) * 128],
                C["wrt8"][:, :], start=True, stop=True), wait=wt, inc=tm)
            evr = P.op("v", lambda v, slr=slr, sl=sl: v.tensor_tensor(
                rp_sb[:, slr * R:slr * R + R], pS[sl][:, 0:R], br_b_ap(C),
                OP.add), wait=[(tm, mm)] + ([wr_] if wr_ else []), inc=ve)
            rS.mark(sl, ve, evr)
            rslot = (qt % 4) + 4 * (h % 2)
            wcc = [(ve, evr)]
            if tpr_last[rslot] is not None:
                wcc.append((tm, tpr_last[rslot]))
            cc = P.op("v", lambda v, slr=slr, rslot=rslot: v.tensor_copy(
                rp01[:, 2 * rslot:2 * rslot + 2],
                rp_sb[:, slr * R:slr * R + 257:256]),
                wait=wcc, inc=ve, drain=True)
            wst = [(ve, evr)] + ([wh_] if wh_ else [])
            st = P.op("y", lambda y, slr=slr, slh=slh: y.dma_start(
                C["rp_hbm"][slh, :, :], rp_sb[:, slr * R:slr * R + R]),
                wait=wst, inc=smr, inck=16)
            return slr, slh, st, cc

        def rp_stage_a2(h, qt, slp_shared, cc, prev_e65):
            pi = h % 2
            rslot = (qt % 4) + 4 * (h % 2)
            tpr = None
            for j in range(2):
                wt = []
                if j == 0:
                    wt.append((ve, cc))
                    if prev_e65 is not None:
                        wt.append((ve, prev_e65))
                tpr = P.op("t", lambda t, rslot=rslot, j=j, slp=slp_shared:
                           t.transpose(
                               pBF[slp][0:1, 512 + j * 128:512 + (j + 1) * 128],
                               rp01[:, 2 * rslot + j:2 * rslot + j + 1],
                               eye_b[:, :]),
                           wait=wt, inc=tm)
            tpr_last[rslot] = tpr
            e65 = None
            for dstq, j in ((qA, 0), (qB, 1)):
                e65 = P.op("v", lambda v, d=dstq, j=j, pi=pi, qt=qt,
                           slp=slp_shared: v.tensor_tensor(
                               d[64:65, pi * SQ + qt * 128:pi * SQ + (qt + 1) * 128],
                               pBF[slp][0:1, 512 + j * 128:512 + (j + 1) * 128],
                               mrow[0:1, pi * SQ + qt * 128:pi * SQ + (qt + 1) * 128],
                               OP.add),
                           wait=[(tm, tpr), (ve, C[f"row64_{h}"])], inc=ve)
            C[f"row65_{h}_{qt}"] = e65
            return e65

        def rp_stage_b1(h, qt, slr, slh, st):
            ms = [0, 1, 2] if qt >= 1 else [1, 2]
            ssk, wsk = rSK.acquire()
            base = ssk * 384
            if posex_last[qt % 4] is not None:
                P.wait("v", sg, posex_last[qt % 4])
            m0 = ms[0]
            lds = P.op("y", lambda y, m0=m0, slh=slh, base=base: y.dma_start(
                skews[:, base + m0 * 128:base + 384],
                bass.AP(C["rp_hbm"].tensor, slh * 128 * R + m0 * 128,
                        [[256, 128], [1, 384 - m0 * 128]])),
                wait=[(smr, st)] + ([wsk] if wsk else []),
                inc=sks[ssk], inck=16)
            po = (qt % 4) * 384
            pb = ((qt % 4) + 4 * (h % 2)) * 384
            pslot = (qt % 4) + 4 * (h % 2)
            bl = []
            for m in ms:
                if m == 1:
                    bl.append(P.op("v", lambda v, base=base, po=po: v.tensor_copy(
                        poss[:, po + 128:po + 256],
                        skews[:, base + 128:base + 256]),
                        wait=[(sks[ssk], lds)], inc=ve))
                elif m == 0:
                    P.op("v", lambda v, base=base, po=po: v.tensor_tensor(
                        poss[:, po:po + 128], skews[:, base:base + 128],
                        C["tri_ge"][:, :], OP.mult), wait=[(sks[ssk], lds)], inc=ve)
                    bl.append(P.op("v", lambda v, slr=slr, po=po:
                                   v.scalar_tensor_tensor(
                                       poss[:, po:po + 128], C["tri_lt"][:, :],
                                       rp_sb[:, slr * R:slr * R + 1],
                                       poss[:, po:po + 128], OP.mult, OP.add),
                                   inc=ve))
                else:
                    P.op("v", lambda v, base=base, po=po: v.tensor_tensor(
                        poss[:, po + 256:po + 384],
                        skews[:, base + 256:base + 384],
                        C["tri_le"][:, :], OP.mult), wait=[(sks[ssk], lds)], inc=ve)
                    bl.append(P.op("v", lambda v, slr=slr, po=po:
                                   v.scalar_tensor_tensor(
                                       poss[:, po + 256:po + 384],
                                       C["tri_gt"][:, :],
                                       rp_sb[:, slr * R + 256:slr * R + 257],
                                       poss[:, po + 256:po + 384],
                                       OP.mult, OP.add), inc=ve))
            rSK.mark(ssk, ve, bl[-1])
            rRP.mark(slr, ve, bl[-1])
            lo = 0 if qt >= 1 else 128
            wex = [(ve, bl[-1])]
            if posT_last[pslot] is not None:
                wex.append((tm, posT_last[pslot]))
            ex = P.op("s", lambda s, pb=pb, po=po, lo=lo: s.activation(
                posb[:, pb + lo:pb + 384], poss[:, po + lo:po + 384], AF.Exp),
                wait=wex, inc=sg)
            posex_last[qt % 4] = ex
            rHB.mark(slh, sks[ssk], lds)
            return ex

        def rp_stage_b2(h, qt, g_, side, ex):
            ms = [0, 1, 2] if qt >= 1 else [1, 2]
            pb = ((qt % 4) + 4 * (h % 2)) * 384
            pslot = (qt % 4) + 4 * (h % 2)
            slot = (g_ % 2) * 2 + side
            sbf, wbf = rBF.acquire()
            tps = []
            for m in ms:
                wt = [(sg, ex)] if m == ms[0] else []
                if m == ms[0] and wbf is not None:
                    wt.append(wbf)
                tps.append(P.op("t", lambda t, m=m, pb=pb, s=sbf: t.transpose(
                    pBF[s][:, m * 128:(m + 1) * 128],
                    posb[:, pb + m * 128:pb + (m + 1) * 128], eye_b[:, :]),
                    wait=wt, inc=tm))
            posT_last[pslot] = tps[-1]
            eve = None
            for i, m in enumerate(ms):
                eve = P.op("v", lambda v, m=m, s=sbf, slot=slot, qt=qt: v.tensor_copy(
                    epT[:, slot * 1536 + (qt % 4) * 384 + m * 128:
                        slot * 1536 + (qt % 4) * 384 + (m + 1) * 128],
                    pBF[s][:, m * 128:(m + 1) * 128]),
                    wait=[(tm, tps[-1])], inc=ve)
            rBF.mark(sbf, ve, eve)
            C[f"ep_{h}_{qt}"] = eve

        def segments(g_, jt):
            its = [4 * g_ + l for l in range(4)]
            cls = []
            for it in its:
                if abs(jt - it) <= 1:
                    cls.append("N")
                elif it <= jt - 2:
                    cls.append("R")
                else:
                    cls.append("L")
            segs = []
            s0 = 0
            for i in range(1, 5):
                if i == 4 or cls[i] != cls[s0]:
                    segs.append((cls[s0], s0, i))
                    s0 = i
            return segs

        def body2(hA, hB, g_, rp_hooks):
            near = [jt for jt in range(max(0, 4 * g_ - 1), min(NJT, 4 * g_ + 5))]
            far = [jt for jt in range(NJT) if jt not in near]
            order = far + near
            heads = (hA, hB)
            nch = {0: 0, 1: 0}
            epslot = (g_ % 2) * 2
            units = [(jt, s) for jt in order for s in (0, 1)]
            pend_av = []
            LAG = 4
            for ui, (jt, side) in enumerate(units):
                if ui in rp_hooks:
                    rp_hooks[ui]()
                h = heads[side]
                pi = h % 2
                segs = segments(g_, jt)
                sl, wfree = rS.acquire()
                slt, wpt = rPT.acquire()
                mm = None
                for si, (cls, l0, l1) in enumerate(segs):
                    qsrc = {"N": qN, "L": qA, "R": qB}[cls]
                    wt = []
                    if si == 0 and wfree:
                        wt.append(wfree)
                    if cls != "N":
                        wt.append((ve, C[f"row65_{h}_{4 * g_ + l1 - 1}"]))
                    elif si == 0:
                        wt.append((ve, C[f"row64_{h}"]))
                    mm = P.op("t", lambda t, qsrc=qsrc, l0=l0, l1=l1, jt=jt,
                              sl=sl, si=si, nseg=len(segs), pi=pi, g_=g_:
                              t.matmul(
                                  pS[sl][:, l0 * 128:l1 * 128],
                                  kt[0:65, pi * S + jt * 128:
                                     pi * S + (jt + 1) * 128],
                                  qsrc[0:65, pi * SQ + g_ * 512 + l0 * 128:
                                       pi * SQ + g_ * 512 + l1 * 128],
                                  start=(si == 0), stop=(si == nseg - 1)),
                              wait=wt, inc=(tm if si == len(segs) - 1 else None))
                ex = P.op("s", lambda s, sl=sl, slt=slt: s.activation(
                    PT[:, slt * 512:(slt + 1) * 512], pS[sl][:, :], AF.Exp),
                    wait=[(tm, mm)] + ([wpt] if wpt else []), inc=sg)
                rS.mark(sl, sg, ex)
                av_wait = [(sg, ex)]
                for (cls, l0, l1) in (segs if not C.get("NO_POS") else []):
                    if cls != "N":
                        continue
                    for l in range(l0, l1):
                        it = 4 * g_ + l
                        m = jt - it + 1
                        mu = P.op("v", lambda v, slt=slt, l=l, it=it, m=m,
                                  side=side, epslot=epslot:
                                  v.tensor_tensor(
                                      PT[:, slt * 512 + l * 128:
                                         slt * 512 + (l + 1) * 128],
                                      PT[:, slt * 512 + l * 128:
                                         slt * 512 + (l + 1) * 128],
                                      epT[:, (epslot + side) * 1536 +
                                          (it % 4) * 384 + m * 128:
                                          (epslot + side) * 1536 +
                                          (it % 4) * 384 + (m + 1) * 128],
                                      OP.mult),
                                  wait=[(sg, ex), (ve, C[f"ep_{h}_{it}"])],
                                  inc=ve)
                        av_wait = [(ve, mu)]
                wt = list(av_wait)
                if nch[side] == 0:
                    aw = C.get(f"av_free_{side}")
                    if aw is not None:
                        wt.append(aw)
                    wt.append((ve, C["p2_ones_ev"]))

                def emit_av(jt=jt, slt=slt, pi=pi, side=side, wt=tuple(wt),
                            first=(nch[side] == 0), last=(nch[side] == NJT - 1)):
                    av = P.op("t", lambda t, jt=jt, slt=slt, pi=pi, side=side,
                              first=first, last=last: t.matmul(
                                  pAV[side][0:65, :],
                                  vsb[:, pi * NJT * 65 + jt * 65:
                                      pi * NJT * 65 + (jt + 1) * 65],
                                  PT[:, slt * 512:(slt + 1) * 512],
                                  start=first, stop=last),
                              wait=list(wt), inc=tm)
                    rPT.mark(slt, tm, av)
                    C[f"av_done_{side}"] = av
                pend_av.append(emit_av)
                nch[side] += 1
                if len(pend_av) > LAG:
                    pend_av.pop(0)()
            for f in pend_av:
                f()
            cps = {}
            for side in (0, 1):
                cp = P.op("v", lambda v, side=side: v.tensor_copy(
                    avsb[:, side * 512:(side + 1) * 512], pAV[side][0:65, :]),
                    wait=[(tm, C[f"av_done_{side}"])], inc=ve)
                C[f"av_free_{side}"] = (ve, cp)
                cps[side] = cp

            def finish(g_=g_, cps=cps, heads=heads):
                for side in (0, 1):
                    h = heads[side]
                    cp = cps[side]
                    sbf, wbf = rBF.acquire()
                    tps = []
                    for l in range(4):
                        wt = [(ve, cp)] if l == 0 else []
                        if l == 0 and wbf is not None:
                            wt.append(wbf)
                        tps.append(P.op("t", lambda t, l=l, s=sbf, side=side:
                                        t.transpose(
                                            pBF[s][:, l * 256:l * 256 + 65],
                                            avsb[:, side * 512 + l * 128:
                                                 side * 512 + (l + 1) * 128],
                                            eye_b[0:65, 0:65]), wait=wt, inc=tm))
                    evf = None
                    for l in range(4):
                        qt = 4 * g_ + l
                        so, wao = rAO.acquire()
                        rc = P.op("v", lambda v, l=l, s=sbf: v.reciprocal(
                            avrc[:, :], pBF[s][:, l * 256 + 64:l * 256 + 65]),
                            wait=[(tm, tps[-1])] + ([wao] if wao else []), inc=ve)
                        evf = P.op("v", lambda v, l=l, s=sbf, so=so, h=h:
                                   v.scalar_tensor_tensor(
                                       avout[:, so * 64:(so + 1) * 64],
                                       pBF[s][:, l * 256:l * 256 + 64], avrc[:, :],
                                       gab_slice(C, h), OP.mult, OP.mult),
                                   wait=[(ve, rc)], inc=ve, drain=True)
                        st = P.op("y", lambda y, so=so, qt=qt, h=h: y.dma_start(
                            C["attn_hbm"][qt * 128:(qt + 1) * 128,
                                          h * 64:(h + 1) * 64],
                            avout[:, so * 64:(so + 1) * 64]),
                            wait=[(ve, evf)], inc=aos[so], inck=16)
                        rAO.mark(so, aos[so], st)
                    rBF.mark(sbf, ve, evf)
            return finish

        def rp_a1_round(heads, g_):
            out = []
            for h in heads:
                for qt in range(4 * g_, 4 * g_ + 4):
                    out.append((h, qt) + rp_stage_a1(h, qt))
            return out

        def rp_a2_round(out):
            slp, wp_ = rBF.acquire()
            if wp_ is not None:
                P.wait("t", wp_[0], wp_[1])
            e65 = None
            for (h, qt, slr, slh, st, cc) in out:
                e65 = rp_stage_a2(h, qt, slp, cc, e65)
            rBF.mark(slp, ve, e65)
            return [(h, qt, slr, slh, st) for (h, qt, slr, slh, st, cc) in out]

        def rp_b1_round(infos):
            return [(h, qt, rp_stage_b1(h, qt, slr, slh, st))
                    for (h, qt, slr, slh, st) in infos]

        def rp_b2_round(infos, heads, g_):
            for (h, qt, ex) in infos:
                rp_stage_b2(h, qt, g_, 0 if h == heads[0] else 1, ex)

        for p in range(H // 2):
            heads = (2 * p, 2 * p + 1)
            pair_setup(p)
            prologue(2 * p)
            prologue(2 * p + 1)
            ia = rp_a1_round(heads, 0)
            ia = rp_a2_round(ia)
            ib = rp_b1_round(ia)
            rp_b2_round(ib, heads, 0)
            fin = C.pop("p2_fin", None)
            for g_ in range(NG):
                hooks = {}
                if fin is not None:
                    hooks[6] = fin
                if g_ < NG - 1:
                    state = {}

                    def hk1(g_=g_, state=state):
                        state["ia"] = rp_a1_round(heads, g_ + 1)

                    def hk1b(state=state):
                        ia = rp_a2_round(state["ia"])
                        state["ib"] = rp_b1_round(ia)

                    def hk2(g_=g_, state=state):
                        rp_b2_round(state["ib"], heads, g_ + 1)
                    hooks[12] = hk1
                    hooks[24] = hk1b
                    hooks[42] = hk2
                fin = body2(heads[0], heads[1], g_, hooks)
            C["p2_fin"] = fin
        C.pop("p2_fin")()
        P.replay(nc)


def br_b_ap(C):
    return C["br_b"][:, :]


def gab_slice(C, h):
    return C["gab"][:, h * 64:(h + 1) * 64]


def emit_gate(nc, C):
    P = Rec()
    tm, ve, sg = C["tm"], C["ve"], C["sg"]
    sm = C["SM"]
    smg = sm["gt"]
    h2T = C["h2T"]
    CAP = 512
    I32 = mybir.dt.int32
    with ExitStack() as _stk:
        wg_sb = _stk.enter_context(nc.sbuf_tensor("wg_sb", [128, 64], BF16))
        wg_lo = _stk.enter_context(nc.sbuf_tensor("wg_lo", [128, 64], BF16))
        wgf = _stk.enter_context(nc.sbuf_tensor("wgf", [128, 64], F32))
        bgc = _stk.enter_context(nc.sbuf_tensor("bgc", [E, 1], F32))
        lgT = _stk.enter_context(nc.sbuf_tensor("lgT", [E, SQ], F32))
        lgtok = _stk.enter_context(nc.sbuf_tensor("lgtok", [128, 128], F32))
        mx16 = _stk.enter_context(nc.sbuf_tensor("mx16", [128, 16], F32))
        onehot = _stk.enter_context(nc.sbuf_tensor("onehot", [128, 128], F32))
        onehT = _stk.enter_context(nc.sbuf_tensor("onehT", [E, SQ], F32))
        cum0 = _stk.enter_context(nc.sbuf_tensor("cum0", [E, SQ], F32))
        cum1 = _stk.enter_context(nc.sbuf_tensor("cum1", [E, SQ], F32))
        eoffc = _stk.enter_context(nc.sbuf_tensor("eoffc", [E, 1], F32))
        onesf = _stk.enter_context(nc.sbuf_tensor("onesf", [128, 1], F32))
        slotr = _stk.enter_context(nc.sbuf_tensor("slotr", [1, SQ], F32))
        slotf = _stk.enter_context(nc.sbuf_tensor("slotf", [128, 16], F32))
        sloti = _stk.enter_context(nc.sbuf_tensor("sloti", [128, 16], I32))
        iotas = _stk.enter_context(nc.sbuf_tensor("iotas", [128, 16], I32))
        dumps = _stk.enter_context(nc.sbuf_tensor("dumps", [128, 1], I32))
        _ps1 = ExitStack()
        pG = _ps1.enter_context(nc.psum_tensor("pG", [8, 2048], F32))
        pGT = _ps1.enter_context(nc.psum_tensor("pGT", [128, 512], F32))

        ldw = P.op("g", lambda g: g.dma_start(
            wgf[:, :], bass.AP(C["wg"].tensor, 0, [[8, 128], [1024, 8], [1, 8]])),
            inc=smg, inck=16)
        ldb = P.op("g", lambda g: g.dma_start(bgc[:, :], C["bg"]), inc=smg, inck=16)
        lde = P.op("g", lambda g: g.dma_start(eoffc[:, :], C["eoff_h"]),
                   inc=smg, inck=16)
        ldo = P.op("g", lambda g: g.dma_start(onesf[:, :], C["ones_fh"]),
                   inc=smg, inck=16)
        ldi = P.op("g", lambda g: g.dma_start(iotas[:, :], C["iota_h"]),
                   inc=smg, inck=16)
        ldd = P.op("g", lambda g: g.dma_start(dumps[:, :], C["dump_h"]),
                   inc=smg, inck=16)
        ld_all = smg.n
        lnv = C["p3_ln_done"]
        P.op("v", lambda v: v.tensor_copy(wg_sb[:, :], wgf[:, :]),
             wait=[(smg, ld_all)], inc=ve, drain=True)
        wglo_ev = P.op("v", lambda v: v.scalar_tensor_tensor(
            wg_lo[:, :], wg_sb[:, :], -1.0, wgf[:, :], OP.mult, OP.add),
            inc=ve, drain=True)
        h2lT = C["h2loT"]
        mm = None
        for ns in range(4):
            ops = [(wg_sb, h2T), (wg_lo, h2T), (wg_sb, h2lT)]
            nmm = len(ops) * 8
            i = 0
            for wop, hop in ops:
                for dc in range(8):
                    wt = [(smg, ld_all), (ve, lnv), (ve, wglo_ev)]                         if (ns == 0 and i == 0) else []
                    mm = P.op("t", lambda t, ns=ns, dc=dc, wop=wop, hop=hop,
                              st=(i == 0), sp=(i == nmm - 1): t.matmul(
                        pG[:, ns * 512:(ns + 1) * 512], wop[:, dc * 8:(dc + 1) * 8],
                        hop[:, dc * SQ + ns * 512: dc * SQ + (ns + 1) * 512],
                        start=st, stop=sp),
                        wait=wt, inc=(tm if i == nmm - 1 else None))
                    i += 1
        evg = P.op("v", lambda v: v.tensor_scalar(
            lgT[:, :], pG[:, :], bgc[:, :], None, OP.add),
            wait=[(tm, mm)], inc=ve)
        tps = []
        for tt in range(16):
            tp = P.op("t", lambda t, tt=tt: t.transpose(
                pGT[:, tt * 8:(tt + 1) * 8], lgT[:, tt * 128:(tt + 1) * 128],
                C["eye_f"][0:8, 0:8]), wait=([(ve, evg)] if tt == 0 else []),
                inc=tm)
            tps.append(tp)
        ev2 = P.op("v", lambda v: v.tensor_copy(lgtok[:, :], pGT[:, 0:128]),
                   wait=[(tm, tps[-1])], inc=ve)
        for tt in range(16):
            P.op("v", lambda v, tt=tt: v.tensor_reduce(
                mx16[:, tt:tt + 1], lgtok[:, tt * 8:(tt + 1) * 8], AX, OP.max),
                wait=([(ve, ev2)] if tt == 0 else []), inc=ve, drain=(tt == 0))
        P.drain("v")
        oh = None
        for tt in range(16):
            oh = P.op("v", lambda v, tt=tt: v.tensor_tensor(
                onehot[:, tt * 8:(tt + 1) * 8], lgtok[:, tt * 8:(tt + 1) * 8],
                mx16[:, tt:tt + 1].to_broadcast([128, 8]), OP.is_equal), inc=ve)
        _ps1.close()
        _ps2 = ExitStack()
        pOT = _ps2.enter_context(nc.psum_tensor("pOT", [8, 2048], F32))
        tps2 = []
        for tt in range(16):
            tp = P.op("t", lambda t, tt=tt: t.transpose(
                pOT[:, tt * 128:(tt + 1) * 128], onehot[:, tt * 8:(tt + 1) * 8],
                C["eye_f"][:, :]), wait=([(ve, oh)] if tt == 0 else []), inc=tm)
            tps2.append(tp)
        ev3 = P.op("v", lambda v: v.tensor_copy(onehT[:, :], pOT[:, :]),
                   wait=[(tm, tps2[-1])], inc=ve)
        a, b = onehT, cum0
        prev = ev3
        k = 1
        while k < SQ:
            P.op("v", lambda v, a=a, b=b, k=k: v.tensor_copy(
                b[:, 0:k], a[:, 0:k]), wait=[(ve, prev)], inc=ve)
            prev = P.op("v", lambda v, a=a, b=b, k=k: v.tensor_tensor(
                b[:, k:SQ], a[:, k:SQ], a[:, 0:SQ - k], OP.add), inc=ve)
            a, b = b, (cum1 if b is cum0 else cum0)
            if a is onehT:
                a = cum1
            k *= 2
        cum = a
        c1 = P.op("v", lambda v, cum=cum: v.tensor_scalar(
            cum[:, :], cum[:, :], float(CAP), eoffc[:, :], OP.min, OP.add),
            wait=[(ve, prev), (smg, ld_all)], inc=ve, drain=True)
        c2 = P.op("v", lambda v, cum=cum: v.tensor_tensor(
            cum[:, :], cum[:, :], onehT[:, :], OP.mult), inc=ve, drain=True)
        _ps2.close()
        _ps3 = ExitStack()
        pSL = _ps3.enter_context(nc.psum_tensor("pSL", [1, 2048], F32))
        pST = _ps3.enter_context(nc.psum_tensor("pST", [128, 512], F32))
        mms = None
        for ns in range(4):
            mms = P.op("t", lambda t, ns=ns, cum=cum: t.matmul(
                pSL[:, ns * 512:(ns + 1) * 512], onesf[0:8, :],
                cum[0:8, ns * 512:(ns + 1) * 512], start=True, stop=True),
                wait=([(ve, c2)] if ns == 0 else []), inc=tm)
        evs = P.op("v", lambda v: v.tensor_copy(slotr[:, :], pSL[:, :]),
                   wait=[(tm, mms)], inc=ve)
        tps3 = []
        for tt in range(16):
            tp = P.op("t", lambda t, tt=tt: t.transpose(
                pST[:, tt * 4:tt * 4 + 1] if False else pST[:, tt:tt + 1],
                slotr[:, tt * 128:(tt + 1) * 128], C["eye_f"][0:1, 0:1]),
                wait=([(ve, evs)] if tt == 0 else []), inc=tm)
            tps3.append(tp)
        cl = P.op("v", lambda v: v.tensor_scalar(
            slotf[:, :], pST[:, 0:16], 4095.0, 0.0, OP.min, OP.max),
            wait=[(tm, tps3[-1])], inc=ve)
        ci = P.op("v", lambda v: v.tensor_copy(sloti[:, :], slotf[:, :]),
                  wait=[(ve, cl)], inc=ve, drain=True)
        pf = None
        for k2 in range(32):
            pf = P.op("g", lambda g, k2=k2: g.dma_start(
                C["idx_hbm"][k2 * 128:(k2 + 1) * 128, :], dumps[:, :]),
                wait=([(smg, ld_all)] if k2 == 0 else []),
                inc=sm["rt0"], inck=16)
        P.wait("g", sm["rt0"], pf)
        sc = None
        for tt in range(16):
            sc = P.op("g", lambda g, tt=tt: g.indirect_dma_start(
                C["idx_hbm"],
                bass.IndirectOffsetOnAxis(ap=sloti[:, tt:tt + 1], axis=0),
                iotas[:, tt:tt + 1], None),
                wait=([(ve, ci)] if tt == 0 else []), inc=sm["rt1"], inck=16)
        C["idx_ready"] = sc
        _ps3.close()
        P.replay(nc)


def emit_ffn(nc, C):
    P = Rec()
    tm, ve, sg = C["tm"], C["ve"], C["sg"]
    sm = C["SM"]
    w1sm = (sm["w1a"], sm["w1b"])
    w2sm = sm["w2s"]
    smgx = sm["gx"]
    I32 = mybir.dt.int32
    CAP = 512
    NSL = 8 * CAP
    HTILES = NSL // 128
    with ExitStack() as _stk:
        w1s = _stk.enter_context(nc.sbuf_tensor("w1s", [128, 2 * 8192], BF16))
        w2s = _stk.enter_context(nc.sbuf_tensor("w2s", [128, 8192], BF16))
        h2pT = _stk.enter_context(nc.sbuf_tensor("h2pT", [128, 8 * 2048], BF16))
        hid = _stk.enter_context(nc.sbuf_tensor("hid", [128, 8 * 512], BF16))
        gst = _stk.enter_context(nc.sbuf_tensor("gst", [128, 2 * D], BF16))
        yst = _stk.enter_context(nc.sbuf_tensor("yst", [128, 2 * D], BF16))
        idxs = _stk.enter_context(nc.sbuf_tensor("idxs", [128, 32], I32))
        zt = _stk.enter_context(nc.sbuf_tensor("zt", [128, D], BF16))
        ysb = C["ysb"]
        ph0 = _stk.enter_context(nc.psum_tensor("ph0", [128, 512], F32))
        ph1 = _stk.enter_context(nc.psum_tensor("ph1", [128, 512], F32))
        po0 = _stk.enter_context(nc.psum_tensor("po0", [128, 512], F32))
        po1 = _stk.enter_context(nc.psum_tensor("po1", [128, 512], F32))
        pT0 = _stk.enter_context(nc.psum_tensor("pT0", [128, 1024], BF16))
        pT1 = _stk.enter_context(nc.psum_tensor("pT1", [128, 1024], BF16))
        pT = (pT0, pT1)

        ldx = None
        for k in range(32):
            ldx = P.op("g", lambda g, k=k: g.dma_start(
                idxs[:, k:k + 1], C["idx_hbm"][k * 128:(k + 1) * 128, :]),
                wait=([(sm["rt1"], C["idx_ready"])] if k == 0 else []),
                inc=smgx, inck=16)
        mz = P.op("v", lambda v: v.memset(zt[:, :], 0.0), inc=ve)
        pz = None
        for k in range(17):
            r0 = min(k * 128, SQ)
            pz = P.op("y", lambda y, r0=r0, k=k: y.dma_start(
                C["y_hbm"][r0:r0 + 128 if k < 16 else SQ + 1, :],
                zt[:, :] if k < 16 else zt[0:1, :]),
                wait=([(ve, mz)] if k == 0 else []), inc=sm["rt3"], inck=16)
        yzero = sm["rt3"].n
        ms = P.op("v", lambda v: v.memset(ysb[:, :], 0.0), inc=ve)

        gi_count = [0]
        w1ld, w2ld = {}, {}

        def stream_w(e, gg):
            gi = gi_count[0]
            sl1 = gi % 2
            w = [(tm, C[f"w1_free_{gi - 2}"])] if gi >= 2 else []
            w1ld[gi] = P.op("g", lambda g, e=e, gg=gg, sl1=sl1: g.dma_start(
                w1s[:, sl1 * 8192:(sl1 + 1) * 8192],
                bass.AP(C["w1"].tensor, e * D * HF + gg * 1024,
                        [[HF, 128], [128 * HF, 8], [1, 1024]])),
                wait=w, inc=w1sm[sl1], inck=16)
            w = [(tm, C[f"w2_free_{gi - 1}"])] if gi >= 1 else []
            w2ld[gi] = P.op("g", lambda g, e=e, gg=gg: g.dma_start(
                w2s[:, :],
                bass.AP(C["w2"].tensor, e * HF * D + gg * 1024 * D,
                        [[D, 128], [128 * D, 8], [1, 1024]])),
                wait=w, inc=w2sm, inck=16)
            gi_count[0] += 1
            return gi, sl1

        last_gelu = {}
        last_evac = {}
        for half in range(2):
            for k in range(16):
                tile = half * 16 + k
                wt = [(smgx, ldx), (sm["h2s"], C["h2s_all"])] if (k == 0 and half == 0) else []
                if tile >= 2:
                    wt.append((tm, C[f"gtp_{tile - 2}"]))
                gx = P.op("g", lambda g, tile=tile, k=k: g.indirect_dma_start(
                    gst[:, (k % 2) * D:(k % 2 + 1) * D], None, C["h2_hbm"],
                    bass.IndirectOffsetOnAxis(ap=idxs[:, tile:tile + 1], axis=0)),
                    wait=wt, inc=smgx, inck=16)
                tps = []
                for dc in range(8):
                    wt2 = [(smgx, gx)] if dc == 0 else []
                    if dc == 0 and tile >= 2:
                        wt2.append((ve, C[f"gx_ev_{tile - 2}"]))
                    tps.append(P.op("t", lambda t, dc=dc, k=k: t.transpose(
                        pT[k % 2][:, dc * 128:(dc + 1) * 128],
                        gst[:, (k % 2) * D + dc * 128:(k % 2) * D + (dc + 1) * 128],
                        C["eye_b"][:, :]), wait=wt2, inc=tm))
                C[f"gtp_{tile}"] = tps[-1]
                e2 = None
                for dc in range(8):
                    e2 = P.op("v", lambda v, dc=dc, k=k: v.tensor_copy(
                        h2pT[:, dc * 2048 + k * 128: dc * 2048 + (k + 1) * 128],
                        pT[k % 2][:, dc * 128:(dc + 1) * 128]),
                        wait=[(tm, tps[-1])], inc=ve)
                C[f"gx_ev_{tile}"] = e2
            gather_ev = ve.n
            for er in range(4):
                e = half * 4 + er
                for gg in range(4):
                    gi, sl1 = stream_w(e, gg)
                    for ht in range(8):
                        kk = gi * 8 + ht
                        wt = []
                        if ht == 0:
                            wt.append((w1sm[sl1], w1ld[gi]))
                            if gg == 0:
                                wt.append((ve, gather_ev))
                        if kk >= 2:
                            wt.append((sg, last_gelu[kk - 2]))
                        mm = None
                        for dc in range(8):
                            mm = P.op("t", lambda t, dc=dc, ht=ht, sl1=sl1,
                                      er=er: t.matmul(
                                          (ph0, ph1)[ht % 2][:, :],
                                          w1s[:, sl1 * 8192 + dc * 1024 +
                                              ht * 128:sl1 * 8192 + dc * 1024 +
                                              ht * 128 + 128],
                                          h2pT[:, dc * 2048 + er * 512:
                                               dc * 2048 + (er + 1) * 512],
                                          start=(dc == 0), stop=(dc == 7)),
                                      wait=(wt if dc == 0 else []),
                                      inc=(tm if dc == 7 else None))
                        if ht == 7:
                            C[f"w1_free_{gi}"] = tm.n
                        gl = P.op("s", lambda s, ht=ht: s.activation(
                            hid[:, ht * 512:(ht + 1) * 512],
                            (ph0, ph1)[ht % 2][:, :], AF.Gelu_apprx_tanh),
                            wait=[(tm, tm.n)], inc=sg)
                        last_gelu[kk] = gl
                    for dt in range(8):
                        kk2 = gi * 8 + dt
                        wt = [(sg, last_gelu[gi * 8 + 7])]
                        if dt == 0:
                            wt.append((w2sm, w2ld[gi]))
                        if kk2 >= 2:
                            wt.append((ve, last_evac[kk2 - 2]))
                        mm = None
                        for ht in range(8):
                            mm = P.op("t", lambda t, ht=ht, dt=dt: t.matmul(
                                (po0, po1)[dt % 2][:, :],
                                w2s[:, ht * 1024 + dt * 128:
                                    ht * 1024 + dt * 128 + 128],
                                hid[:, ht * 512:(ht + 1) * 512],
                                start=(ht == 0), stop=(ht == 7)),
                                wait=(wt if ht == 0 else []),
                                inc=(tm if ht == 7 else None))
                        if dt == 7:
                            C[f"w2_free_{gi}"] = tm.n
                        evv = P.op("v", lambda v, dt=dt, er=er: v.scalar_tensor_tensor(
                            ysb[:, dt * 2048 + er * 512: dt * 2048 + (er + 1) * 512],
                            (po0, po1)[dt % 2][:, :], C["gbt"][:, dt:dt + 1],
                            ysb[:, dt * 2048 + er * 512: dt * 2048 + (er + 1) * 512],
                            OP.mult, OP.add),
                            wait=[(tm, tm.n)] + ([(ve, ms)] if gi % 16 == 0 and dt == 0
                                                 else []), inc=ve)
                        last_evac[kk2] = evv
            ffn_half_ev = ve.n
            sev = {}
            ysc = {}
            last_ytp = None
            for k in range(16):
                tile = half * 16 + k
                tps = []
                for dc in range(8):
                    wt2 = []
                    if dc == 0:
                        wt2.append((ve, ffn_half_ev))
                        if k >= 2:
                            wt2.append((ve, sev[k - 2]))
                    tps.append(P.op("t", lambda t, dc=dc, k=k: t.transpose(
                        pT[k % 2][:, dc * 128:(dc + 1) * 128],
                        ysb[:, dc * 2048 + k * 128: dc * 2048 + (k + 1) * 128],
                        C["eye_b"][:, :]), wait=wt2, inc=tm))
                last_ytp = tps[-1]
                we2 = [(tm, tps[-1])]
                if k >= 2:
                    we2.append((sm["rt2"], ysc[k - 2]))
                e2 = P.op("v", lambda v, k=k: v.tensor_copy(
                    yst[:, (k % 2) * D:(k % 2 + 1) * D], pT[k % 2][:, :]),
                    wait=we2, inc=ve)
                sev[k] = e2
                wt3 = [(ve, e2)]
                if half == 0 and k == 0:
                    wt3.append((sm["rt3"], yzero))
                ysc[k] = P.op("g", lambda g, tile=tile, k=k: g.indirect_dma_start(
                    C["y_hbm"],
                    bass.IndirectOffsetOnAxis(ap=idxs[:, tile:tile + 1], axis=0),
                    yst[:, (k % 2) * D:(k % 2 + 1) * D], None),
                    wait=wt3, inc=sm["rt2"], inck=16)
            if half == 0:
                ms = P.op("v", lambda v: v.memset(ysb[:, :], 0.0),
                          wait=[(tm, last_ytp)], inc=ve)
        C["ysc_all"] = sm["rt2"].n
        P.replay(nc)


def emit_out(nc, C):
    P = Rec()
    tm, ve, sg = C["tm"], C["ve"], C["sg"]
    sm = C["SM"]
    xts = (sm["xt0"], sm["xt1"])
    sps = (sm["sp0"], sm["sp1"])
    yls = (sm["at0"], sm["at1"])
    with ExitStack() as _stk:
        x2t = _stk.enter_context(nc.sbuf_tensor("x2t", [128, 2 * D], F32))
        yt = _stk.enter_context(nc.sbuf_tensor("yt", [128, 2 * D], BF16))
        outt = _stk.enter_context(nc.sbuf_tensor("outt", [128, 2 * D], F32))
        for s in (sm["sp0"], sm["sp1"], sm["sp2"]):
            P.wait("y", s, s.n)
        P.wait("y", sm["rt2"], C["ysc_all"])
        ev_done = {}
        st_done = {}
        for tt in range(SQ // 128):
            sl = tt % 2
            w = [(ve, ev_done[tt - 2])] if tt >= 2 else []
            ld = P.op("y", lambda y, tt=tt, sl=sl: y.dma_start(
                x2t[:, sl * D:(sl + 1) * D],
                C["x2_hbm"][tt * 128:(tt + 1) * 128, :]),
                wait=w, inc=xts[sl], inck=16)
            ldy = P.op("y", lambda y, tt=tt, sl=sl: y.dma_start(
                yt[:, sl * D:(sl + 1) * D],
                C["y_hbm"][tt * 128:(tt + 1) * 128, :]),
                wait=[], inc=yls[sl], inck=16)
            wad = [(xts[sl], ld), (yls[sl], ldy)]
            if tt >= 2:
                wad.append((sps[sl], st_done[tt - 2]))
            ad = P.op("v", lambda v, sl=sl: v.tensor_tensor(
                outt[:, sl * D:(sl + 1) * D], yt[:, sl * D:(sl + 1) * D],
                x2t[:, sl * D:(sl + 1) * D], OP.add), wait=wad, inc=ve)
            ev_done[tt] = ad
            st_done[tt] = P.op("y", lambda y, tt=tt, sl=sl: y.dma_start(
                C["out"][tt * 128:(tt + 1) * 128, :],
                outt[:, sl * D:(sl + 1) * D]),
                wait=[(ve, ad)], inc=sps[sl], inck=16)
        for s in sm.values():
            P.wait("y", s, s.n)
        P.replay(nc)


_CACHED = {}


def kernel(L0_attn_x, L0_attn_Wqkv, L0_attn_bqkv, L0_attn_Wr, L0_attn_br,
           L0_ffn_W1, L0_ffn_W2, L0_ffn_Wg, L0_ffn_bg, L0_c, L0_wmod):
    if "nc" not in _CACHED:
        _CACHED["nc"] = build_program()
    nc = _CACHED["nc"]

    eye = np.eye(128)
    di = np.arange(128)
    tri = dict(
        eye_bh=eye.astype(BFNP),
        eye_fh=eye.astype(np.float32),
        ones_fh=np.ones((128, 1), dtype=np.float32),
        iota_h=np.ascontiguousarray(
            np.arange(SQ, dtype=np.int32).reshape(16, 128).T),
        dump_h=np.full((128, 1), SQ, dtype=np.int32),
        eoff_h=(np.arange(8, dtype=np.float32) * 512 - 1).reshape(8, 1),
        tri_geh=(di[None, :] >= di[:, None]).astype(np.float32),
        tri_lth=(di[None, :] < di[:, None]).astype(np.float32),
        tri_leh=(di[None, :] <= di[:, None]).astype(np.float32),
        tri_gth=(di[None, :] > di[:, None]).astype(np.float32),
        ones_h=np.ones((128, 1), dtype=BFNP),
    )
    wr_f = np.ascontiguousarray(L0_attn_Wr)
    br_f = np.ascontiguousarray(L0_attn_br).reshape(1, R)
    wr_r = np.ascontiguousarray(L0_attn_Wr[::-1])
    br_r = np.ascontiguousarray(L0_attn_br[::-1]).reshape(1, R)
    shared = dict(
        wqkv=np.ascontiguousarray(L0_attn_Wqkv),
        bqkv=np.ascontiguousarray(L0_attn_bqkv).reshape(1, 3 * D),
        w1=np.ascontiguousarray(L0_ffn_W1),
        w2=np.ascontiguousarray(L0_ffn_W2),
        wg=np.ascontiguousarray(L0_ffn_Wg),
        bg=np.ascontiguousarray(L0_ffn_bg).reshape(E, 1),
        wmod=np.ascontiguousarray(L0_wmod),
        **tri,
    )
    in_maps = []
    for c in range(8):
        b = c // 2
        rev = c % 2 == 1
        xbm = L0_attn_x[b] if not rev else L0_attn_x[b][::-1]
        m = dict(shared)
        m["xb"] = np.ascontiguousarray(xbm)
        m["wr"] = wr_r if rev else wr_f
        m["br"] = br_r if rev else br_f
        m["crow"] = np.ascontiguousarray(L0_c[b]).reshape(1, D)
        in_maps.append(m)

    res = run_bass_kernel_spmd(nc, in_maps, core_ids=list(range(8)))
    out = np.empty((4, S, D), dtype=np.float32)
    for c in range(8):
        b = c // 2
        o = np.asarray(res.results[c]["out"])
        if c % 2 == 0:
            out[b, 0:SQ] = o
        else:
            out[b, SQ:S] = o[::-1]
    return out


# version 14 (final)
# speedup vs baseline: 1.1135x; 1.0032x over previous
import numpy as np
from contextlib import ExitStack
import ml_dtypes
import concourse.bass as bass
import concourse.mybir as mybir
from concourse.bass_utils import run_bass_kernel_spmd

BFNP = ml_dtypes.bfloat16
F32 = mybir.dt.float32
BF16 = mybir.dt.bfloat16
AF = mybir.ActivationFunctionType
OP = mybir.AluOpType
AX = mybir.AxisListType.X

D = 1024
S = 4096
SQ = 2048
H = 16
HD = 64
R = 257
E = 8
HF = 4096
NQT = SQ // 128
NJT = S // 128
NG = 4


class Ctr:
    def __init__(self, handle):
        self.h = handle
        self.n = 0


class Rec:
    ENG = ("t", "v", "s", "g", "y")

    def __init__(self):
        self.q = {e: [] for e in self.ENG}

    def op(self, e, fn, wait=(), inc=None, inck=1, drain=False):
        w = tuple((c.h if isinstance(c, Ctr) else c, int(v)) for c, v in wait)
        i = None
        if inc is not None:
            inc.n += inck
            i = (inc.h, inck)
        self.q[e].append((w, fn, i, drain))
        return inc.n if inc is not None else None

    def wait(self, e, c, v):
        self.op(e, None, wait=[(c, v)])

    def drain(self, e):
        self.op(e, None, drain=True)

    def replay(self, nc, barrier=True):
        with nc.Block() as block:
            dec = dict(t=block.tensor, v=block.vector, s=block.scalar,
                       g=block.gpsimd, y=block.sync)
            for e in self.ENG:
                items = self.q[e]
                if not items:
                    continue

                def body(engine, items=items):
                    for w, fn, i, dr in items:
                        if dr:
                            engine.drain()
                        for (h, val) in w:
                            engine.wait_ge(h, val)
                        inst = fn(engine) if fn is not None else None
                        if i is not None:
                            inst.then_inc(i[0], i[1])
                dec[e](body)
            self.q = {e: [] for e in self.ENG}
        if barrier:
            nc.all_engine_barrier()


class Ring:
    def __init__(self, nslots):
        self.free_at = [None] * nslots
        self.i = 0

    def acquire(self):
        s = self.i % len(self.free_at)
        self.i += 1
        return s, self.free_at[s]

    def mark(self, slot, ctr, val):
        self.free_at[slot] = (ctr, int(val))


def build_program(dbg=()):
    nc = bass.Bass("TRN2", target_bir_lowering=False,
                   detect_race_conditions=False)
    dbg = set(dbg)

    def dram(name, shape, dt, kind):
        if kind == "Internal" and name in dbg:
            kind = "ExternalOutput"
        return nc.dram_tensor(name, shape, dt, kind=kind).ap()

    C = {}
    C["xb"] = dram("xb", [S, D], F32, "ExternalInput")
    C["wqkv"] = dram("wqkv", [3 * D, D], F32, "ExternalInput")
    C["bqkv"] = dram("bqkv", [1, 3 * D], F32, "ExternalInput")
    C["wr"] = dram("wr", [R, HD], F32, "ExternalInput")
    C["br"] = dram("br", [1, R], F32, "ExternalInput")
    C["w1"] = dram("w1", [E, D, HF], F32, "ExternalInput")
    C["w2"] = dram("w2", [E, HF, D], F32, "ExternalInput")
    C["wg"] = dram("wg", [D, E], F32, "ExternalInput")
    C["bg"] = dram("bg", [E, 1], F32, "ExternalInput")
    C["crow"] = dram("crow", [1, D], F32, "ExternalInput")
    C["wmod"] = dram("wmod", [D, 6 * D], F32, "ExternalInput")
    C["eye_bh"] = dram("eye_bh", [128, 128], BF16, "ExternalInput")
    C["eye_fh"] = dram("eye_fh", [128, 128], F32, "ExternalInput")
    C["tri_geh"] = dram("tri_geh", [128, 128], F32, "ExternalInput")
    C["tri_lth"] = dram("tri_lth", [128, 128], F32, "ExternalInput")
    C["tri_leh"] = dram("tri_leh", [128, 128], F32, "ExternalInput")
    C["tri_gth"] = dram("tri_gth", [128, 128], F32, "ExternalInput")
    C["ones_h"] = dram("ones_h", [128, 1], BF16, "ExternalInput")
    C["ones_fh"] = dram("ones_fh", [128, 1], F32, "ExternalInput")
    C["iota_h"] = dram("iota_h", [128, 16], mybir.dt.int32, "ExternalInput")
    C["dump_h"] = dram("dump_h", [128, 1], mybir.dt.int32, "ExternalInput")
    C["eoff_h"] = dram("eoff_h", [8, 1], F32, "ExternalInput")
    C["out"] = dram("out", [SQ, D], F32, "ExternalOutput")
    C["rp_hbm"] = dram("rp_hbm", [8, 128, R], F32, "Internal")
    C["attn_hbm"] = dram("attn_hbm", [SQ, D], BF16, "Internal")
    C["x2_hbm"] = dram("x2_hbm", [SQ, D], F32, "Internal")
    C["mod_hbm"] = dram("mod_hbm", [1, 6 * D], F32, "Internal")
    C["h2_hbm"] = dram("h2_hbm", [SQ + 1, D], BF16, "Internal")
    C["y_hbm"] = dram("y_hbm", [SQ + 1, D], BF16, "Internal")
    C["idx_hbm"] = dram("idx_hbm", [4096, 1], mybir.dt.int32, "Internal")

    with ExitStack() as _stk:
        eye_b = _stk.enter_context(nc.sbuf_tensor("eye_b", [128, 128], BF16))
        eye_f = _stk.enter_context(nc.sbuf_tensor("eye_f", [128, 128], F32))
        tri_ge = _stk.enter_context(nc.sbuf_tensor("tri_ge", [128, 128], F32))
        tri_lt = _stk.enter_context(nc.sbuf_tensor("tri_lt", [128, 128], F32))
        tri_le = _stk.enter_context(nc.sbuf_tensor("tri_le", [128, 128], F32))
        tri_gt = _stk.enter_context(nc.sbuf_tensor("tri_gt", [128, 128], F32))
        ones_b = _stk.enter_context(nc.sbuf_tensor("ones_b", [128, 1], BF16))
        wrt8 = _stk.enter_context(nc.sbuf_tensor("wrt8", [HD, R], BF16))
        br_b = _stk.enter_context(nc.sbuf_tensor("br_b", [128, R], F32))
        ca1b = _stk.enter_context(nc.sbuf_tensor("ca1b", [128, D], F32))
        sab = _stk.enter_context(nc.sbuf_tensor("sab", [128, D], F32))
        gab = _stk.enter_context(nc.sbuf_tensor("gab", [128, D], F32))
        cb1b = _stk.enter_context(nc.sbuf_tensor("cb1b", [128, D], F32))
        sbb = _stk.enter_context(nc.sbuf_tensor("sbb", [128, D], F32))
        gbt = _stk.enter_context(nc.sbuf_tensor("gbt", [128, E], F32))
        _tm = _stk.enter_context(nc.semaphore("t_mm"))
        _ve = _stk.enter_context(nc.semaphore("v_ev"))
        _sg = _stk.enter_context(nc.semaphore("s_ge"))
        SM = {}
        for name in ("c", "wr", "wm0", "wm1", "xt0", "xt1", "xt2",
                     "at0", "at1", "sp0", "sp1", "sp2", "wst", "bias",
                     "rpw", "sk0", "sk1", "sk2", "sk3", "ao0", "ao1", "gt", "mask",
                     "w1a", "w1b", "w2s", "h1d", "cc", "mb",
                     "h2s", "rt0", "rt1", "rt2", "rt3", "gx"):
            SM[name] = Ctr(_stk.enter_context(nc.semaphore("sm_" + name)))
        C.update(eye_b=eye_b, eye_f=eye_f, tri_ge=tri_ge, tri_lt=tri_lt, tri_le=tri_le,
                 tri_gt=tri_gt, ones_b=ones_b, wrt8=wrt8, br_b=br_b,
                 ca1b=ca1b, sab=sab, gab=gab, cb1b=cb1b, sbb=sbb, gbt=gbt,
                 tm=Ctr(_tm), ve=Ctr(_ve), sg=Ctr(_sg), SM=SM)

        if "DUMP_PT" in dbg:
            C["DUMP_PT"] = True
        if "NO_POS" in dbg:
            C["NO_POS"] = True
        with nc.Block() as _blk0:
            @_blk0.gpsimd
            def _(g):
                for s in list(SM.values()):
                    g.sem_clear(s.h)
                g.sem_clear(_tm)
                g.sem_clear(_ve)
                g.sem_clear(_sg)
        nc.all_engine_barrier()
        with nc.sbuf_tensor("h1T", [128, 8 * S], BF16) as h1T, \
             nc.sbuf_tensor("stats_sb", [128, 32 * 8], F32) as _ssb:
            C["h1T"] = h1T
            if "DUMP_H1T" in dbg:
                C["stats_dump"] = _ssb
            with ExitStack() as _p0stk:
                C["p0stk"] = _p0stk
                emit_p0(nc, C)
                emit_ln(nc, C, first=True)
            if "DUMP_H1T" in dbg:
                dmp = nc.dram_tensor("h1T_dump", [128, 8 * S], BF16,
                                     kind="ExternalOutput").ap()
                sdmp_h = nc.dram_tensor("stats_dump", [128, 32 * 8], F32,
                                        kind="ExternalOutput").ap()
                dmp2 = nc.dram_tensor("bc_dump", [128, 3 * D], F32,
                                      kind="ExternalOutput").ap()
                Pd = Rec()
                Pd.wait("y", C["ve"], C["p1_done"])
                Pd.op("y", lambda y: y.dma_start(dmp, h1T[:, :]),
                      inc=C["SM"]["h1d"], inck=16)
                Pd.op("y", lambda y: y.dma_start(dmp2[:, 0:D], C["ca1b"][:, :]),
                      inc=C["SM"]["h1d"], inck=16)
                Pd.op("y", lambda y: y.dma_start(dmp2[:, D:2 * D], C["sab"][:, :]),
                      inc=C["SM"]["h1d"], inck=16)
                Pd.op("y", lambda y: y.dma_start(dmp2[:, 2 * D:3 * D], C["gab"][:, :]),
                      inc=C["SM"]["h1d"], inck=16)
                Pd.op("y", lambda y: y.dma_start(sdmp_h, C["stats_dump"][:, :]),
                      inc=C["SM"]["h1d"], inck=16)
                Pd.wait("y", C["SM"]["h1d"], C["SM"]["h1d"].n)
                Pd.replay(nc)
            if "STOP_P1" not in dbg:
                emit_attn(nc, C)
                if "STOP_P2" in dbg:
                    Pd = Rec()
                    Pd.wait("y", C["SM"]["ao0"], C["SM"]["ao0"].n)
                    Pd.wait("y", C["SM"]["ao1"], C["SM"]["ao1"].n)
                    Pd.replay(nc)
        if "STOP_P1" in dbg or "STOP_P2" in dbg:
            return nc
        with nc.sbuf_tensor("h2T", [128, 8 * SQ], BF16) as h2T, \
                nc.sbuf_tensor("h2loT", [128, 8 * SQ], BF16) as h2loT:
            C["h2T"] = h2T
            C["h2loT"] = h2loT
            emit_ln(nc, C, first=False)
            emit_gate(nc, C)
            if "STOP_P3" in dbg:
                Pd = Rec()
                for s in C["SM"].values():
                    Pd.wait("y", s, s.n)
                Pd.replay(nc)
                return nc
        with nc.sbuf_tensor("ysb", [128, 8 * SQ], BF16) as ysb:
            C["ysb"] = ysb
            emit_ffn(nc, C)
            emit_out(nc, C)
    return nc


def emit_p0(nc, C):
    P = Rec()
    tm, ve, sg = C["tm"], C["ve"], C["sg"]
    sm = C["SM"]
    smc, smwr, wm = sm["c"], sm["wr"], (sm["wm0"], sm["wm1"])
    smcc, smmb = sm["cc"], sm["mb"]

    for dst, src in [(C["eye_b"], C["eye_bh"]), (C["eye_f"], C["eye_fh"]),
                     (C["tri_ge"], C["tri_geh"]),
                     (C["tri_lt"], C["tri_lth"]), (C["tri_le"], C["tri_leh"]),
                     (C["tri_gt"], C["tri_gth"]), (C["ones_b"], C["ones_h"])]:
        P.op("g", lambda g, d=dst, s=src: g.dma_start(d[:, :], s), inc=smc, inck=16)
    P.op("g", lambda g: g.dma_start(C["br_b"][:, :],
                                    C["br"].partition_broadcast(128)),
         inc=smc, inck=16)
    ld_consts = smc.n

    if True:
        _stk = C["p0stk"]
        wr_st = _stk.enter_context(nc.sbuf_tensor("wr_st", [128, 128], BF16))
        c_sb = _stk.enter_context(nc.sbuf_tensor("c_sb", [1, D], F32))
        cs_b = _stk.enter_context(nc.sbuf_tensor("cs_b", [1, D], BF16))
        csT = _stk.enter_context(nc.sbuf_tensor("csT", [128, 8], BF16))
        wmod_sb = _stk.enter_context(nc.sbuf_tensor("wmod_sb", [128, 2 * 1536], BF16))
        modrow = _stk.enter_context(nc.sbuf_tensor("modrow", [1, 6 * D], F32))
        g_row = _stk.enter_context(nc.sbuf_tensor("g_row", [1, D], F32))
        pP0f = _stk.enter_context(nc.psum_tensor("pP0f", [128, 512], F32))
        pP0b = _stk.enter_context(nc.psum_tensor("pP0b", [128, 1024], BF16))
        pmod = _stk.enter_context(nc.psum_tensor("pmod", [1, 1536], F32))
        P.wait("t", smc, ld_consts)
        last_tp = None
        last_evw = None
        for ch in range(3):
            r0, r1 = [(0, 128), (128, 256), (256, 257)][ch]
            n = r1 - r0
            w = [(tm, last_tp)] if last_tp is not None else []
            ld = P.op("g", lambda g, r0=r0, r1=r1, n=n: g.dma_start(
                wr_st[0:n, 0:HD], C["wr"][r0:r1, :]), wait=w, inc=smwr, inck=16)
            wtp = [(smwr, ld)]
            if last_evw is not None:
                wtp.append((ve, last_evw))
            tp = P.op("t", lambda t, n=n, ch=ch: t.transpose(
                pP0b[0:HD, ch * 128:ch * 128 + n], wr_st[0:n, 0:HD],
                C["eye_b"][0:n, 0:n]), wait=wtp, inc=tm)
            last_evw = P.op("v", lambda v, ch=ch, n=n: v.tensor_scalar(
                C["wrt8"][:, ch * 128:ch * 128 + n],
                pP0b[0:HD, ch * 128:ch * 128 + n], 8.0, None, OP.mult),
                wait=[(tm, tp)], inc=ve)
            last_tp = tp

        ldc = P.op("g", lambda g: g.dma_start(c_sb[:, :], C["crow"]),
                   inc=smcc, inck=16)
        sil = P.op("s", lambda s: s.activation(cs_b[:, :], c_sb[:, :], AF.Silu),
                   wait=[(smcc, ldc)], inc=sg)
        tpc = None
        for dc in range(8):
            wtc = [(sg, sil), (ve, last_evw)] if dc == 0 else []
            tpc = P.op("t", lambda t, dc=dc: t.transpose(
                pP0b[:, 384 + 4 * dc:385 + 4 * dc],
                cs_b[:, dc * 128:(dc + 1) * 128],
                C["eye_b"][0:1, 0:1]), wait=wtc, inc=tm)
        evc = P.op("v", lambda v: v.tensor_copy(csT[:, :], pP0b[:, 384:416:4]),
                   wait=[(tm, tpc)], inc=ve)

        mm_done = {}
        ev_prev = None
        for q4 in range(4):
            for dc in range(8):
                k = q4 * 8 + dc
                bo = (k % 2) * 1536
                w = [(tm, mm_done[k - 2])] if k >= 2 else []
                ld = P.op("g", lambda g, q4=q4, dc=dc, bo=bo: g.dma_start(
                    wmod_sb[:, bo:bo + 1536],
                    C["wmod"][dc * 128:(dc + 1) * 128,
                              q4 * 1536:(q4 + 1) * 1536]),
                    wait=w, inc=wm[k % 2], inck=16)
                mm = None
                for ns in range(3):
                    wmm = []
                    if ns == 0:
                        wmm.append((wm[k % 2], ld))
                        if dc == 0:
                            wmm.append((ve, evc if ev_prev is None else ev_prev))
                    mm = P.op("t", lambda t, dc=dc, bo=bo, ns=ns: t.matmul(
                        pmod[:, ns * 512:(ns + 1) * 512], csT[:, dc:dc + 1],
                        wmod_sb[:, bo + ns * 512:bo + (ns + 1) * 512],
                        start=(dc == 0), stop=(dc == 7)),
                        wait=wmm, inc=(tm if ns == 2 else None))
                mm_done[k] = tm.n
            ev_prev = P.op("v", lambda v, q4=q4: v.tensor_copy(
                modrow[:, q4 * 1536:(q4 + 1) * 1536], pmod[:, :]),
                wait=[(tm, mm_done[q4 * 8 + 7])], inc=ve)

        def seg(i):
            return modrow[:, i * D:(i + 1) * D]
        P.op("v", lambda v: v.tensor_scalar(seg(1), seg(1), 1.0, None, OP.add),
             wait=[(ve, ev_prev)], inc=ve)
        P.op("v", lambda v: v.tensor_scalar(seg(4), seg(4), 1.0, None, OP.add),
             inc=ve)
        tlast = None
        for i in (2, 5):
            ab = P.op("s", lambda s, i=i: s.activation(
                g_row[:, :], seg(i), AF.Abs), wait=[(ve, ve.n)], inc=sg)
            P.op("v", lambda v, i=i: v.tensor_scalar(
                g_row[:, :], g_row[:, :], 0.25, None, OP.max),
                wait=[(sg, ab)], inc=ve)
            P.op("v", lambda v, i=i: v.tensor_scalar(
                seg(i), seg(i), 0.0, 2.0, OP.is_ge, OP.mult), inc=ve)
            P.op("v", lambda v, i=i: v.tensor_scalar(
                seg(i), seg(i), -1.0, None, OP.add), inc=ve)
            tlast = P.op("v", lambda v, i=i: v.tensor_tensor(
                seg(i), seg(i), g_row[:, :], OP.mult), inc=ve)
        st = P.op("g", lambda g: g.dma_start(C["mod_hbm"], modrow[:, :]),
                  wait=[(ve, tlast)], inc=smmb, inck=16)
        first_b = True
        for dst, i in [(C["sab"], 0), (C["ca1b"], 1), (C["gab"], 2),
                       (C["sbb"], 3), (C["cb1b"], 4)]:
            P.op("g", lambda g, d=dst, i=i: g.dma_start(
                d[:, :], C["mod_hbm"][:, i * D:(i + 1) * D].partition_broadcast(128)),
                wait=([(smmb, st)] if first_b else []), inc=smmb, inck=16)
            first_b = False
        C["p0_bcast_c"] = smmb.n
        tb = None
        for dc in range(8):
            tb = P.op("t", lambda t, dc=dc: t.transpose(
                pP0f[:, dc:dc + 1], seg(5)[:, dc * 128:(dc + 1) * 128],
                C["eye_f"][0:1, 0:1]),
                wait=[(ve, tlast)], inc=tm)
        P.op("v", lambda v: v.tensor_copy(C["gbt"][:, :], pP0f[:, 0:8]),
             wait=[(tm, tb)], inc=ve)
        P.wait("v", smc, ld_consts)
        P.replay(nc, barrier=False)


def emit_ln(nc, C, first):
    P = Rec()
    tm, ve, sg = C["tm"], C["ve"], C["sg"]
    sm = C["SM"]
    xts = (sm["xt0"], sm["xt1"], sm["xt2"])
    ats = (sm["at0"], sm["at1"])
    sps = (sm["sp0"], sm["sp1"], sm["sp2"])
    ntt = (S if first else SQ) // 128
    dstT = C["h1T"] if first else C["h2T"]
    seqw = S if first else SQ
    cab = C["ca1b"] if first else C["cb1b"]
    sab = C["sab"] if first else C["sbb"]
    stats_dmp = C.get("stats_dump") if first else None
    if first:
        P.wait("v", sm["mb"], C["p0_bcast_c"])
    else:
        P.wait("y", sm["ao0"], sm["ao0"].n)
        P.wait("y", sm["ao1"], sm["ao1"].n)

    sfx = "1" if first else "2"
    with ExitStack() as _stk:
        xt = _stk.enter_context(nc.sbuf_tensor("xt" + sfx, [128, 3 * D], F32))
        at = _stk.enter_context(nc.sbuf_tensor("at" + sfx, [128, 2 * D], BF16))
        xsq = _stk.enter_context(nc.sbuf_tensor("xsq" + sfx, [128, D], F32))
        stats = _stk.enter_context(nc.sbuf_tensor("stats" + sfx, [128, 8], F32))
        hst = _stk.enter_context(nc.sbuf_tensor("hst" + sfx, [128, 2 * D], BF16))
        hstF = _stk.enter_context(nc.sbuf_tensor("hstF" + sfx, [128, D], F32))
        hstLo = _stk.enter_context(nc.sbuf_tensor("hstLo" + sfx, [128, 2 * D], BF16))
        pL0 = _stk.enter_context(nc.psum_tensor("pL0" + sfx, [128, 1024], BF16))
        pL1 = _stk.enter_context(nc.psum_tensor("pL1" + sfx, [128, 1024], BF16))
        if not first:
            pL2 = _stk.enter_context(nc.psum_tensor("pL2" + sfx, [128, 1024], BF16))
            pL3 = _stk.enter_context(nc.psum_tensor("pL3" + sfx, [128, 1024], BF16))
        else:
            pL2 = pL3 = None
        ev_done = {}
        tp_done = {}
        tp_last = {}
        spill_ds = {}
        sq2_prev = [None]
        h2sp = {}
        for tt in range(ntt):
            sl = tt % 3
            w = [(ve, ev_done[tt - 3])] if tt >= 3 else []
            if (not first) and tt >= 3:
                w.append((sps[sl], spill_ds[tt - 3]))
            ld = P.op("y", lambda y, tt=tt, sl=sl: y.dma_start(
                xt[:, sl * D:(sl + 1) * D], C["xb"][tt * 128:(tt + 1) * 128, :]),
                wait=w, inc=xts[sl], inck=16)
            if not first:
                w2 = [(ve, ev_done[tt - 2])] if tt >= 2 else []
                lda = P.op("y", lambda y, tt=tt: y.dma_start(
                    at[:, (tt % 2) * D:(tt % 2 + 1) * D],
                    C["attn_hbm"][tt * 128:(tt + 1) * 128, :]),
                    wait=w2, inc=ats[tt % 2], inck=16)
                add = P.op("v", lambda v, sl=sl, tt=tt: v.tensor_tensor(
                    xt[:, sl * D:(sl + 1) * D], xt[:, sl * D:(sl + 1) * D],
                    at[:, (tt % 2) * D:(tt % 2 + 1) * D], OP.add),
                    wait=[(xts[sl], ld), (ats[tt % 2], lda)], inc=ve)
                spill_ds[tt] = P.op("y", lambda y, tt=tt, sl=sl: y.dma_start(
                    C["x2_hbm"][tt * 128:(tt + 1) * 128, :],
                    xt[:, sl * D:(sl + 1) * D]),
                    wait=[(ve, add)], inc=sps[sl], inck=16)
                ready_v = [(ve, add)]
                ready_s = [(ve, add)]
            else:
                ready_v = [(xts[sl], ld)]
                ready_s = [(xts[sl], ld)]
            P.op("v", lambda v, sl=sl: v.tensor_reduce(
                stats[:, 0:1], xt[:, sl * D:(sl + 1) * D], AX, OP.add),
                wait=ready_v, inc=ve)
            sq = P.op("s", lambda s, sl=sl: s.activation(
                xsq[:, :], xt[:, sl * D:(sl + 1) * D], AF.Square),
                wait=ready_s + ([(ve, ev_done[tt - 1])] if tt >= 1 else []),
                inc=sg)
            ws2 = [(sg, sq)]
            if tt >= 1:
                ws2.append((sg, sq2_prev[0]))
            P.op("v", lambda v: v.tensor_reduce(
                stats[:, 1:2], xsq[:, :], AX, OP.add), wait=ws2, inc=ve)
            P.op("v", lambda v: v.tensor_scalar(
                stats[:, 0:1], stats[:, 0:1], 1.0 / D, None, OP.mult), inc=ve,
                drain=True)
            P.op("v", lambda v: v.tensor_scalar(
                stats[:, 1:2], stats[:, 1:2], 1.0 / D, None, OP.mult), inc=ve,
                drain=True)
            P.op("v", lambda v: v.tensor_tensor(
                stats[:, 2:3], stats[:, 0:1], stats[:, 0:1], OP.mult), inc=ve,
                drain=True)
            v5 = P.op("v", lambda v: v.tensor_scalar(
                stats[:, 1:2], stats[:, 1:2], stats[:, 2:3], 1e-5,
                OP.subtract, OP.add), inc=ve, drain=True)
            sq2 = P.op("s", lambda s: s.activation(
                stats[:, 3:4], stats[:, 1:2], AF.Sqrt), wait=[(ve, v5)], inc=sg)
            sq2_prev[0] = sq2
            v6 = P.op("v", lambda v: v.reciprocal(
                stats[:, 4:5], stats[:, 3:4]), wait=[(sg, sq2)], inc=ve)
            if stats_dmp is not None:
                P.op("v", lambda v, tt=tt: v.tensor_copy(
                    stats_dmp[:, tt * 8:tt * 8 + 8], stats[:, :]), inc=ve)
            P.op("v", lambda v, sl=sl: v.tensor_scalar(
                xsq[:, :], xt[:, sl * D:(sl + 1) * D], stats[:, 0:1],
                stats[:, 4:5], OP.subtract, OP.mult), wait=[(ve, v6)], inc=ve,
                drain=True)
            P.op("v", lambda v: v.tensor_tensor(
                xsq[:, :], xsq[:, :], cab[:, :], OP.mult), inc=ve)
            hsl = (tt % 2) * D
            wa3 = [(tm, tp_last[tt - 2])] if tt >= 2 else []
            if (not first) and tt >= 2:
                wa3.append((sm["h2s"], h2sp[tt - 2]))
            if first:
                a3 = P.op("v", lambda v, hsl=hsl: v.tensor_tensor(
                    hst[:, hsl:hsl + D], xsq[:, :], sab[:, :], OP.add),
                    wait=wa3, inc=ve)
            else:
                P.op("v", lambda v: v.tensor_tensor(
                    hstF[:, :], xsq[:, :], sab[:, :], OP.add),
                    wait=wa3, inc=ve)
                P.op("v", lambda v, hsl=hsl: v.tensor_copy(
                    hst[:, hsl:hsl + D], hstF[:, :]), inc=ve, drain=True)
                a3 = P.op("v", lambda v, hsl=hsl: v.scalar_tensor_tensor(
                    hstLo[:, hsl:hsl + D], hst[:, hsl:hsl + D], -1.0,
                    hstF[:, :], OP.mult, OP.add), inc=ve, drain=True)
            ev_done[tt] = a3
            if not first:
                h2sp[tt] = P.op("y", lambda y, tt=tt, hsl=hsl: y.dma_start(
                    C["h2_hbm"][tt * 128:(tt + 1) * 128, :],
                    hst[:, hsl:hsl + D]),
                    wait=[(ve, a3)], inc=sm["h2s"], inck=16)
            pps = [(pL0, pL1)[tt % 2], (pL2, pL3)[tt % 2]]
            tps = []
            srcs = [(hst, dstT)] if first else [(hst, dstT), (hstLo, C["h2loT"])]
            for si, (sbuf_src, _dst) in enumerate(srcs):
                for dc in range(8):
                    wt = [(ve, a3)] if (dc == 0 and si == 0) else []
                    if dc == 0 and si == 0 and tt >= 2:
                        wt.append((ve, tp_done[tt - 2]))
                    tp = P.op("t", lambda t, dc=dc, pp=pps[si], hsl=hsl,
                              s=sbuf_src: t.transpose(
                                  pp[:, dc * 128:(dc + 1) * 128],
                                  s[:, hsl + dc * 128:hsl + (dc + 1) * 128],
                                  C["eye_b"][:, :]),
                              wait=wt, inc=tm)
                    tps.append(tp)
            tp_last[tt] = tps[-1]
            e2 = None
            for si, (sbuf_src, dst_) in enumerate(srcs):
                for dc in range(8):
                    e2 = P.op("v", lambda v, dc=dc, tt=tt, pp=pps[si], d=dst_:
                              v.tensor_copy(
                                  d[:, dc * seqw + tt * 128:
                                    dc * seqw + (tt + 1) * 128],
                                  pp[:, dc * 128:(dc + 1) * 128]),
                              wait=[(tm, tps[(si + 1) * 8 - 1])], inc=ve)
            tp_done[tt] = e2
        C["p1_done" if first else "p3_ln_done"] = ve.n
        if not first:
            zr = P.op("v", lambda v: v.memset(hst[0:1, 0:D], 0.0),
                      wait=[(sm["h2s"], h2sp[ntt - 1]),
                            (sm["h2s"], h2sp[ntt - 2])], inc=ve)
            P.op("y", lambda y: y.dma_start(C["h2_hbm"][SQ:SQ + 1, :],
                                            hst[0:1, 0:D]),
                 wait=[(ve, zr)], inc=sm["h2s"], inck=16)
            C["h2s_all"] = sm["h2s"].n
        P.replay(nc)


def emit_attn(nc, C):
    P = Rec()
    tm, ve, sg = C["tm"], C["ve"], C["sg"]
    sm = C["SM"]
    smb, smw, smr = sm["bias"], sm["wst"], sm["rpw"]
    sks = (sm["sk0"], sm["sk1"], sm["sk2"], sm["sk3"])
    aos = (sm["ao0"], sm["ao1"])
    h1T, eye_b = C["h1T"], C["eye_b"]
    if "DUMP_PT" in C:
        C["pt_dump"] = nc.dram_tensor("pt_dump", [NJT, 128, 512], BF16,
                                      kind="ExternalOutput").ap()

    with ExitStack() as _stk:
        w_st = _stk.enter_context(nc.sbuf_tensor("w_st", [128, D], BF16))
        wqT = _stk.enter_context(nc.sbuf_tensor("wqT", [128, D], BF16))
        wkT = _stk.enter_context(nc.sbuf_tensor("wkT", [128, D], BF16))
        wvT = _stk.enter_context(nc.sbuf_tensor("wvT", [128, D], BF16))
        qb2 = _stk.enter_context(nc.sbuf_tensor("qb2", [128, 1], F32))
        kb2 = _stk.enter_context(nc.sbuf_tensor("kb2", [128, 1], F32))
        bvb = _stk.enter_context(nc.sbuf_tensor("bvb", [128, 128], F32))
        qN = _stk.enter_context(nc.sbuf_tensor("qN", [65, 2 * SQ], BF16))
        qA = _stk.enter_context(nc.sbuf_tensor("qA", [65, 2 * SQ], BF16))
        qB = _stk.enter_context(nc.sbuf_tensor("qB", [65, 2 * SQ], BF16))
        kt = _stk.enter_context(nc.sbuf_tensor("kt", [65, 2 * S], BF16))
        vsb = _stk.enter_context(nc.sbuf_tensor("vsb", [128, 2 * NJT * 65], BF16))
        qsq = _stk.enter_context(nc.sbuf_tensor("qsq", [64, SQ], BF16))
        mrow = _stk.enter_context(nc.sbuf_tensor("mrow", [1, 2 * SQ], BF16))
        nkrow = _stk.enter_context(nc.sbuf_tensor("nkrow", [1, 512], F32))
        maxk = _stk.enter_context(nc.sbuf_tensor("maxk", [1, 16], F32))
        skews = _stk.enter_context(nc.sbuf_tensor("skews", [128, 4 * 384], F32))
        poss = _stk.enter_context(nc.sbuf_tensor("poss", [128, 4 * 384], F32))
        posb = _stk.enter_context(nc.sbuf_tensor("posb", [128, 8 * 384], BF16))
        rp01 = _stk.enter_context(nc.sbuf_tensor("rp01", [128, 16], BF16))
        epT = _stk.enter_context(nc.sbuf_tensor("epT", [128, 4 * 12 * 128], BF16))
        rp_sb = _stk.enter_context(nc.sbuf_tensor("rp_sb", [128, 8 * R], F32))
        PT = _stk.enter_context(nc.sbuf_tensor("PT", [128, 6 * 512], BF16))
        avsb = _stk.enter_context(nc.sbuf_tensor("avsb", [65, 1024], BF16))
        avrc = _stk.enter_context(nc.sbuf_tensor("avrc", [128, 1], F32))
        avout = _stk.enter_context(nc.sbuf_tensor("avout", [128, 2 * 64], BF16))
        pS0 = _stk.enter_context(nc.psum_tensor("pS0", [128, 512], F32))
        pS1 = _stk.enter_context(nc.psum_tensor("pS1", [128, 512], F32))
        pS2 = _stk.enter_context(nc.psum_tensor("pS2", [128, 512], F32))
        pS3 = _stk.enter_context(nc.psum_tensor("pS3", [128, 512], F32))
        pAVa = _stk.enter_context(nc.psum_tensor("pAVa", [128, 512], F32))
        pAVb = _stk.enter_context(nc.psum_tensor("pAVb", [128, 512], F32))
        pBF0 = _stk.enter_context(nc.psum_tensor("pBF0", [128, 1024], BF16))
        pBF1 = _stk.enter_context(nc.psum_tensor("pBF1", [128, 1024], BF16))
        pS = [pS0, pS1, pS2, pS3]
        pBF = [pBF0, pBF1]
        rS = Ring(4)
        rBF = Ring(2)
        rRP = Ring(8)
        rPT = Ring(6)
        rSK = Ring(4)
        rAO = Ring(2)
        rHB = Ring(8)
        pAV = {0: pAVa, 1: pAVb}

        posT_last = [None] * 8
        posex_last = [None, None, None, None]
        tpr_last = [None] * 8
        prologue_mm = [None]
        P.wait("t", ve, C["p1_done"])
        P.op("v", lambda v: v.memset(kt[64:65, :], 1.0), inc=ve)
        ones_ev = P.op("v", lambda v: v.memset(
            vsb[:, 64:64 + 65 * (2 * NJT - 1) + 1:65], 1.0), inc=ve)
        C["p2_ones_ev"] = ones_ev

        def pair_setup(p):
            h0, h1 = 2 * p, 2 * p + 1
            bq = C["bqkv"]
            if p > 0:
                P.wait("g", tm, C[f"proj_mm_{p - 1}"])
            for dst, off0, off1 in ((qb2, 192 * h0, 192 * h1),
                                    (kb2, 192 * h0 + 64, 192 * h1 + 64)):
                P.op("g", lambda g, d=dst, o=off0: g.dma_start(
                    d[0:64, :], bq[:, o:o + 64].transpose((1, 0))),
                    inc=smb, inck=16)
                P.op("g", lambda g, d=dst, o=off1: g.dma_start(
                    d[64:128, :], bq[:, o:o + 64].transpose((1, 0))),
                    inc=smb, inck=16)
            P.op("g", lambda g: g.dma_start(
                bvb[:, 0:64],
                bq[:, 192 * h0 + 128:192 * h0 + 192].partition_broadcast(128)),
                inc=smb, inck=16)
            P.op("g", lambda g: g.dma_start(
                bvb[:, 64:128],
                bq[:, 192 * h1 + 128:192 * h1 + 192].partition_broadcast(128)),
                inc=smb, inck=16)
            bias_ld = smb.n
            P.op("v", lambda v: v.tensor_scalar(
                qb2[:, :], qb2[:, :], 0.125, None, OP.mult),
                wait=[(smb, bias_ld)], inc=ve)

            for kind, dstW, scale in (("q", wqT, 0.125), ("k", wkT, None),
                                      ("v", wvT, None)):
                off = {"q": 0, "k": 64, "v": 128}[kind]
                P.op("g", lambda g, o=off: g.dma_start(
                    w_st[0:64, :], C["wqkv"][192 * h0 + o:192 * h0 + o + 64, :]),
                    inc=smw, inck=16)
                lb = P.op("g", lambda g, o=off: g.dma_start(
                    w_st[64:128, :], C["wqkv"][192 * h1 + o:192 * h1 + o + 64, :]),
                    inc=smw, inck=16)
                sbf, wbf = rBF.acquire()
                tps = []
                for dc in range(8):
                    wt = []
                    if dc == 0:
                        wt.append((smw, lb))
                        if wbf is not None:
                            wt.append(wbf)
                    tp = P.op("t", lambda t, dc=dc, s=sbf: t.transpose(
                        pBF[s][:, dc * 128:(dc + 1) * 128],
                        w_st[:, dc * 128:(dc + 1) * 128], eye_b[:, :]),
                        wait=wt, inc=tm)
                    tps.append(tp)
                evl = None
                for dc in range(8):
                    if scale is None:
                        evl = P.op("v", lambda v, dc=dc, s=sbf, dW=dstW:
                                   v.tensor_copy(dW[:, dc * 128:(dc + 1) * 128],
                                                 pBF[s][:, dc * 128:(dc + 1) * 128]),
                                   wait=[(tm, tps[-1])], inc=ve)
                    else:
                        evl = P.op("v", lambda v, dc=dc, s=sbf, dW=dstW, sc=scale:
                                   v.tensor_scalar(dW[:, dc * 128:(dc + 1) * 128],
                                                   pBF[s][:, dc * 128:(dc + 1) * 128],
                                                   sc, None, OP.mult),
                                   wait=[(tm, tps[-1])], inc=ve)
                rBF.mark(sbf, ve, evl)
                P.wait("g", tm, tps[-1])
            wT_ev = ve.n

            for ns in range(4):
                sl, wfree = rS.acquire()
                wt = [(ve, wT_ev), (smb, bias_ld)] + ([wfree] if wfree else [])
                mm = None
                for dc in range(8):
                    mm = P.op("t", lambda t, dc=dc, ns=ns, sl=sl: t.matmul(
                        pS[sl][:, :], wqT[:, dc * 128:(dc + 1) * 128],
                        h1T[:, dc * S + ns * 512: dc * S + (ns + 1) * 512],
                        start=(dc == 0), stop=(dc == 7)),
                        wait=(wt if dc == 0 else []),
                        inc=(tm if dc == 7 else None))
                ev = None
                for dstq in (qN, qA, qB):
                    for pi in range(2):
                        ev = P.op("v", lambda v, d=dstq, pi=pi, ns=ns, sl=sl:
                                  v.tensor_scalar(
                                      d[0:64, pi * SQ + ns * 512:
                                        pi * SQ + (ns + 1) * 512],
                                      pS[sl][pi * 64:(pi + 1) * 64, :],
                                      qb2[pi * 64:(pi + 1) * 64, :],
                                      None, OP.add),
                                  wait=[(tm, mm)], inc=ve)
                rS.mark(sl, ve, ev)
            C[f"proj_q_{p}"] = ve.n
            for ns in range(8):
                sl, wfree = rS.acquire()
                wt = [(ve, wT_ev), (smb, bias_ld)] + ([wfree] if wfree else [])
                mm = None
                for dc in range(8):
                    mm = P.op("t", lambda t, dc=dc, ns=ns, sl=sl: t.matmul(
                        pS[sl][:, :], wkT[:, dc * 128:(dc + 1) * 128],
                        h1T[:, dc * S + ns * 512: dc * S + (ns + 1) * 512],
                        start=(dc == 0), stop=(dc == 7)),
                        wait=(wt if dc == 0 else []),
                        inc=(tm if dc == 7 else None))
                ev = None
                for pi in range(2):
                    ev = P.op("v", lambda v, pi=pi, ns=ns, sl=sl:
                              v.tensor_scalar(
                                  kt[0:64, pi * S + ns * 512:
                                     pi * S + (ns + 1) * 512],
                                  pS[sl][pi * 64:(pi + 1) * 64, :],
                                  kb2[pi * 64:(pi + 1) * 64, :], None, OP.add),
                              wait=[(tm, mm)], inc=ve)
                rS.mark(sl, ve, ev)
            C[f"proj_k_{p}"] = ve.n
            for jt in range(NJT):
                sl, wfree = rS.acquire()
                wt = [(ve, wT_ev), (smb, bias_ld)] + ([wfree] if wfree else [])
                mm = None
                for dc in range(8):
                    mm = P.op("t", lambda t, dc=dc, jt=jt, sl=sl: t.matmul(
                        pS[sl][:, 0:128],
                        h1T[:, dc * S + jt * 128: dc * S + (jt + 1) * 128],
                        wvT[:, dc * 128:(dc + 1) * 128],
                        start=(dc == 0), stop=(dc == 7)),
                        wait=(wt if dc == 0 else []),
                        inc=(tm if dc == 7 else None))
                ev = None
                for pi in range(2):
                    ev = P.op("v", lambda v, pi=pi, jt=jt, sl=sl:
                              v.tensor_tensor(
                                  vsb[:, pi * NJT * 65 + jt * 65:
                                      pi * NJT * 65 + jt * 65 + 64],
                                  pS[sl][:, pi * 64:(pi + 1) * 64],
                                  bvb[:, pi * 64:(pi + 1) * 64], OP.add),
                              wait=[(tm, mm)], inc=ve)
                rS.mark(sl, ve, ev)
            C[f"proj_ev_{p}"] = ve.n
            C[f"proj_mm_{p}"] = tm.n

        def prologue(h):
            pi = h % 2
            wsq1 = [(ve, C[f"proj_q_{h // 2}"])]
            if prologue_mm[0] is not None:
                wsq1.append((tm, prologue_mm[0]))
            sq1 = P.op("s", lambda s, pi=pi: s.activation(
                qsq[:, :], qN[0:64, pi * SQ:(pi + 1) * SQ], AF.Square),
                wait=wsq1, inc=sg)
            for ns in range(4):
                sl, wfree = rS.acquire()
                wt = [(sg, sq1)] if ns == 0 else []
                if wfree is not None:
                    wt.append(wfree)
                mm = P.op("t", lambda t, ns=ns, sl=sl: t.matmul(
                    pS[sl][0:1, :], C["ones_b"][0:64, :],
                    qsq[0:64, ns * 512:(ns + 1) * 512], start=True, stop=True),
                    wait=wt, inc=tm)
                sv = P.op("s", lambda s, pi=pi, ns=ns, sl=sl: s.activation(
                    mrow[0:1, pi * SQ + ns * 512:pi * SQ + (ns + 1) * 512],
                    pS[sl][0:1, :], AF.Sqrt), wait=[(tm, mm)], inc=sg)
                rS.mark(sl, sg, sv)
            mm_hist = {}
            for ns in range(8):
                ks = (ns % 4) * 512
                if ns == 0:
                    wsq = [(ve, C[f"proj_k_{h // 2}"])]
                elif ns >= 4:
                    wsq = [(tm, mm_hist[ns - 4])]
                else:
                    wsq = []
                sq2 = P.op("s", lambda s, pi=pi, ns=ns, ks=ks: s.activation(
                    qsq[:, ks:ks + 512], kt[0:64, pi * S + ns * 512:
                                            pi * S + (ns + 1) * 512], AF.Square),
                    wait=wsq, inc=sg)
                sl, wfree = rS.acquire()
                mm = P.op("t", lambda t, sl=sl, ks=ks: t.matmul(
                    pS[sl][0:1, :], C["ones_b"][0:64, :], qsq[0:64, ks:ks + 512],
                    start=True, stop=True),
                    wait=[(sg, sq2)] + ([wfree] if wfree else []), inc=tm)
                rv = P.op("v", lambda v, ns=ns, sl=sl: v.tensor_reduce(
                    maxk[:, ns:ns + 1], pS[sl][0:1, :], AX, OP.max),
                    wait=[(tm, mm)], inc=ve)
                rS.mark(sl, ve, rv)
                mm_hist[ns] = mm
            prologue_mm[0] = mm_hist[7]
            red = P.op("v", lambda v: v.tensor_reduce(
                maxk[:, 8:9], maxk[:, 0:8], AX, OP.max), inc=ve, drain=True)
            sqm = P.op("s", lambda s: s.activation(
                maxk[:, 9:10], maxk[:, 8:9], AF.Sqrt), wait=[(ve, red)], inc=sg)
            m1 = P.op("v", lambda v, pi=pi: v.tensor_scalar(
                mrow[0:1, pi * SQ:(pi + 1) * SQ],
                mrow[0:1, pi * SQ:(pi + 1) * SQ], maxk[0:1, 9:10], -1.0,
                OP.mult, OP.mult), wait=[(sg, sqm)], inc=ve)
            r64 = P.op("v", lambda v, pi=pi: v.tensor_copy(
                qN[64:65, pi * SQ:(pi + 1) * SQ],
                mrow[0:1, pi * SQ:(pi + 1) * SQ]), wait=[(ve, m1)], inc=ve)
            C[f"row64_{h}"] = r64

        def rp_stage_a1(h, qt):
            pi = h % 2
            slr, wr_ = rRP.acquire()
            slh, wh_ = rHB.acquire()
            sl, wfree = rS.acquire()
            wt = [(ve, C[f"proj_ev_{h // 2}"])] + ([wfree] if wfree else [])
            mm = P.op("t", lambda t, pi=pi, qt=qt, sl=sl: t.matmul(
                pS[sl][:, 0:R], qN[0:64, pi * SQ + qt * 128:
                                   pi * SQ + (qt + 1) * 128],
                C["wrt8"][:, :], start=True, stop=True), wait=wt, inc=tm)
            evr = P.op("v", lambda v, slr=slr, sl=sl: v.tensor_tensor(
                rp_sb[:, slr * R:slr * R + R], pS[sl][:, 0:R], br_b_ap(C),
                OP.add), wait=[(tm, mm)] + ([wr_] if wr_ else []), inc=ve)
            rS.mark(sl, ve, evr)
            rslot = (qt % 4) + 4 * (h % 2)
            wcc = [(ve, evr)]
            if tpr_last[rslot] is not None:
                wcc.append((tm, tpr_last[rslot]))
            cc = P.op("v", lambda v, slr=slr, rslot=rslot: v.tensor_copy(
                rp01[:, 2 * rslot:2 * rslot + 2],
                rp_sb[:, slr * R:slr * R + 257:256]),
                wait=wcc, inc=ve, drain=True)
            wst = [(ve, evr)] + ([wh_] if wh_ else [])
            st = P.op("y", lambda y, slr=slr, slh=slh: y.dma_start(
                C["rp_hbm"][slh, :, :], rp_sb[:, slr * R:slr * R + R]),
                wait=wst, inc=smr, inck=16)
            return slr, slh, st, cc

        def rp_stage_a2(h, qt, slp_shared, cc, prev_e65):
            pi = h % 2
            rslot = (qt % 4) + 4 * (h % 2)
            tpr = None
            for j in range(2):
                wt = []
                if j == 0:
                    wt.append((ve, cc))
                    if prev_e65 is not None:
                        wt.append((ve, prev_e65))
                tpr = P.op("t", lambda t, rslot=rslot, j=j, slp=slp_shared:
                           t.transpose(
                               pBF[slp][0:1, 512 + j * 128:512 + (j + 1) * 128],
                               rp01[:, 2 * rslot + j:2 * rslot + j + 1],
                               eye_b[:, :]),
                           wait=wt, inc=tm)
            tpr_last[rslot] = tpr
            e65 = None
            for dstq, j in ((qA, 0), (qB, 1)):
                e65 = P.op("v", lambda v, d=dstq, j=j, pi=pi, qt=qt,
                           slp=slp_shared: v.tensor_tensor(
                               d[64:65, pi * SQ + qt * 128:pi * SQ + (qt + 1) * 128],
                               pBF[slp][0:1, 512 + j * 128:512 + (j + 1) * 128],
                               mrow[0:1, pi * SQ + qt * 128:pi * SQ + (qt + 1) * 128],
                               OP.add),
                           wait=[(tm, tpr), (ve, C[f"row64_{h}"])], inc=ve)
            C[f"row65_{h}_{qt}"] = e65
            return e65

        def rp_stage_b1(h, qt, slr, slh, st):
            ms = [0, 1, 2] if qt >= 1 else [1, 2]
            ssk, wsk = rSK.acquire()
            base = ssk * 384
            if posex_last[qt % 4] is not None:
                P.wait("v", sg, posex_last[qt % 4])
            m0 = ms[0]
            lds = P.op("y", lambda y, m0=m0, slh=slh, base=base: y.dma_start(
                skews[:, base + m0 * 128:base + 384],
                bass.AP(C["rp_hbm"].tensor, slh * 128 * R + m0 * 128,
                        [[256, 128], [1, 384 - m0 * 128]])),
                wait=[(smr, st)] + ([wsk] if wsk else []),
                inc=sks[ssk], inck=16)
            po = (qt % 4) * 384
            pb = ((qt % 4) + 4 * (h % 2)) * 384
            pslot = (qt % 4) + 4 * (h % 2)
            bl = []
            for m in ms:
                if m == 1:
                    bl.append(P.op("v", lambda v, base=base, po=po: v.tensor_copy(
                        poss[:, po + 128:po + 256],
                        skews[:, base + 128:base + 256]),
                        wait=[(sks[ssk], lds)], inc=ve))
                elif m == 0:
                    P.op("v", lambda v, base=base, po=po: v.tensor_tensor(
                        poss[:, po:po + 128], skews[:, base:base + 128],
                        C["tri_ge"][:, :], OP.mult), wait=[(sks[ssk], lds)], inc=ve)
                    bl.append(P.op("v", lambda v, slr=slr, po=po:
                                   v.scalar_tensor_tensor(
                                       poss[:, po:po + 128], C["tri_lt"][:, :],
                                       rp_sb[:, slr * R:slr * R + 1],
                                       poss[:, po:po + 128], OP.mult, OP.add),
                                   inc=ve))
                else:
                    P.op("v", lambda v, base=base, po=po: v.tensor_tensor(
                        poss[:, po + 256:po + 384],
                        skews[:, base + 256:base + 384],
                        C["tri_le"][:, :], OP.mult), wait=[(sks[ssk], lds)], inc=ve)
                    bl.append(P.op("v", lambda v, slr=slr, po=po:
                                   v.scalar_tensor_tensor(
                                       poss[:, po + 256:po + 384],
                                       C["tri_gt"][:, :],
                                       rp_sb[:, slr * R + 256:slr * R + 257],
                                       poss[:, po + 256:po + 384],
                                       OP.mult, OP.add), inc=ve))
            rSK.mark(ssk, ve, bl[-1])
            rRP.mark(slr, ve, bl[-1])
            lo = 0 if qt >= 1 else 128
            wex = [(ve, bl[-1])]
            if posT_last[pslot] is not None:
                wex.append((tm, posT_last[pslot]))
            ex = P.op("s", lambda s, pb=pb, po=po, lo=lo: s.activation(
                posb[:, pb + lo:pb + 384], poss[:, po + lo:po + 384], AF.Exp),
                wait=wex, inc=sg)
            posex_last[qt % 4] = ex
            rHB.mark(slh, sks[ssk], lds)
            return ex

        def rp_stage_b2(h, qt, g_, side, ex):
            ms = [0, 1, 2] if qt >= 1 else [1, 2]
            pb = ((qt % 4) + 4 * (h % 2)) * 384
            pslot = (qt % 4) + 4 * (h % 2)
            slot = (g_ % 2) * 2 + side
            sbf, wbf = rBF.acquire()
            tps = []
            for m in ms:
                wt = [(sg, ex)] if m == ms[0] else []
                if m == ms[0] and wbf is not None:
                    wt.append(wbf)
                tps.append(P.op("t", lambda t, m=m, pb=pb, s=sbf: t.transpose(
                    pBF[s][:, m * 128:(m + 1) * 128],
                    posb[:, pb + m * 128:pb + (m + 1) * 128], eye_b[:, :]),
                    wait=wt, inc=tm))
            posT_last[pslot] = tps[-1]
            eve = None
            for i, m in enumerate(ms):
                eve = P.op("v", lambda v, m=m, s=sbf, slot=slot, qt=qt: v.tensor_copy(
                    epT[:, slot * 1536 + (qt % 4) * 384 + m * 128:
                        slot * 1536 + (qt % 4) * 384 + (m + 1) * 128],
                    pBF[s][:, m * 128:(m + 1) * 128]),
                    wait=[(tm, tps[-1])], inc=ve)
            rBF.mark(sbf, ve, eve)
            C[f"ep_{h}_{qt}"] = eve

        def segments(g_, jt):
            its = [4 * g_ + l for l in range(4)]
            cls = []
            for it in its:
                if abs(jt - it) <= 1:
                    cls.append("N")
                elif it <= jt - 2:
                    cls.append("R")
                else:
                    cls.append("L")
            segs = []
            s0 = 0
            for i in range(1, 5):
                if i == 4 or cls[i] != cls[s0]:
                    segs.append((cls[s0], s0, i))
                    s0 = i
            return segs

        def body2(hA, hB, g_, rp_hooks):
            near = [jt for jt in range(max(0, 4 * g_ - 1), min(NJT, 4 * g_ + 5))]
            far = [jt for jt in range(NJT) if jt not in near]
            order = far + near
            heads = (hA, hB)
            nch = {0: 0, 1: 0}
            epslot = (g_ % 2) * 2
            units = [(jt, s) for jt in order for s in (0, 1)]
            pend_av = []
            LAG = 4
            for ui, (jt, side) in enumerate(units):
                if ui in rp_hooks:
                    rp_hooks[ui]()
                h = heads[side]
                pi = h % 2
                segs = segments(g_, jt)
                sl, wfree = rS.acquire()
                slt, wpt = rPT.acquire()
                mm = None
                for si, (cls, l0, l1) in enumerate(segs):
                    qsrc = {"N": qN, "L": qA, "R": qB}[cls]
                    wt = []
                    if si == 0 and wfree:
                        wt.append(wfree)
                    if cls != "N":
                        wt.append((ve, C[f"row65_{h}_{4 * g_ + l1 - 1}"]))
                    elif si == 0:
                        wt.append((ve, C[f"row64_{h}"]))
                    mm = P.op("t", lambda t, qsrc=qsrc, l0=l0, l1=l1, jt=jt,
                              sl=sl, si=si, nseg=len(segs), pi=pi, g_=g_:
                              t.matmul(
                                  pS[sl][:, l0 * 128:l1 * 128],
                                  kt[0:65, pi * S + jt * 128:
                                     pi * S + (jt + 1) * 128],
                                  qsrc[0:65, pi * SQ + g_ * 512 + l0 * 128:
                                       pi * SQ + g_ * 512 + l1 * 128],
                                  start=(si == 0), stop=(si == nseg - 1)),
                              wait=wt, inc=(tm if si == len(segs) - 1 else None))
                ex = P.op("s", lambda s, sl=sl, slt=slt: s.activation(
                    PT[:, slt * 512:(slt + 1) * 512], pS[sl][:, :], AF.Exp),
                    wait=[(tm, mm)] + ([wpt] if wpt else []), inc=sg)
                rS.mark(sl, sg, ex)
                av_wait = [(sg, ex)]
                for (cls, l0, l1) in (segs if not C.get("NO_POS") else []):
                    if cls != "N":
                        continue
                    for l in range(l0, l1):
                        it = 4 * g_ + l
                        m = jt - it + 1
                        mu = P.op("v", lambda v, slt=slt, l=l, it=it, m=m,
                                  side=side, epslot=epslot:
                                  v.tensor_tensor(
                                      PT[:, slt * 512 + l * 128:
                                         slt * 512 + (l + 1) * 128],
                                      PT[:, slt * 512 + l * 128:
                                         slt * 512 + (l + 1) * 128],
                                      epT[:, (epslot + side) * 1536 +
                                          (it % 4) * 384 + m * 128:
                                          (epslot + side) * 1536 +
                                          (it % 4) * 384 + (m + 1) * 128],
                                      OP.mult),
                                  wait=[(sg, ex), (ve, C[f"ep_{h}_{it}"])],
                                  inc=ve)
                        av_wait = [(ve, mu)]
                wt = list(av_wait)
                if nch[side] == 0:
                    aw = C.get(f"av_free_{side}")
                    if aw is not None:
                        wt.append(aw)
                    wt.append((ve, C["p2_ones_ev"]))

                def emit_av(jt=jt, slt=slt, pi=pi, side=side, wt=tuple(wt),
                            first=(nch[side] == 0), last=(nch[side] == NJT - 1)):
                    av = P.op("t", lambda t, jt=jt, slt=slt, pi=pi, side=side,
                              first=first, last=last: t.matmul(
                                  pAV[side][0:65, :],
                                  vsb[:, pi * NJT * 65 + jt * 65:
                                      pi * NJT * 65 + (jt + 1) * 65],
                                  PT[:, slt * 512:(slt + 1) * 512],
                                  start=first, stop=last),
                              wait=list(wt), inc=tm)
                    rPT.mark(slt, tm, av)
                    C[f"av_done_{side}"] = av
                pend_av.append(emit_av)
                nch[side] += 1
                if len(pend_av) > LAG:
                    pend_av.pop(0)()
            for f in pend_av:
                f()
            cps = {}
            for side in (0, 1):
                cp = P.op("v", lambda v, side=side: v.tensor_copy(
                    avsb[:, side * 512:(side + 1) * 512], pAV[side][0:65, :]),
                    wait=[(tm, C[f"av_done_{side}"])], inc=ve)
                C[f"av_free_{side}"] = (ve, cp)
                cps[side] = cp

            def finish(g_=g_, cps=cps, heads=heads):
                for side in (0, 1):
                    h = heads[side]
                    cp = cps[side]
                    sbf, wbf = rBF.acquire()
                    tps = []
                    for l in range(4):
                        wt = [(ve, cp)] if l == 0 else []
                        if l == 0 and wbf is not None:
                            wt.append(wbf)
                        tps.append(P.op("t", lambda t, l=l, s=sbf, side=side:
                                        t.transpose(
                                            pBF[s][:, l * 256:l * 256 + 65],
                                            avsb[:, side * 512 + l * 128:
                                                 side * 512 + (l + 1) * 128],
                                            eye_b[0:65, 0:65]), wait=wt, inc=tm))
                    evf = None
                    for l in range(4):
                        qt = 4 * g_ + l
                        so, wao = rAO.acquire()
                        rc = P.op("v", lambda v, l=l, s=sbf: v.reciprocal(
                            avrc[:, :], pBF[s][:, l * 256 + 64:l * 256 + 65]),
                            wait=[(tm, tps[-1])] + ([wao] if wao else []), inc=ve)
                        evf = P.op("v", lambda v, l=l, s=sbf, so=so, h=h:
                                   v.scalar_tensor_tensor(
                                       avout[:, so * 64:(so + 1) * 64],
                                       pBF[s][:, l * 256:l * 256 + 64], avrc[:, :],
                                       gab_slice(C, h), OP.mult, OP.mult),
                                   wait=[(ve, rc)], inc=ve, drain=True)
                        st = P.op("y", lambda y, so=so, qt=qt, h=h: y.dma_start(
                            C["attn_hbm"][qt * 128:(qt + 1) * 128,
                                          h * 64:(h + 1) * 64],
                            avout[:, so * 64:(so + 1) * 64]),
                            wait=[(ve, evf)], inc=aos[so], inck=16)
                        rAO.mark(so, aos[so], st)
                    rBF.mark(sbf, ve, evf)
            return finish

        def rp_a1_round(heads, g_):
            out = []
            for h in heads:
                for qt in range(4 * g_, 4 * g_ + 4):
                    out.append((h, qt) + rp_stage_a1(h, qt))
            return out

        def rp_a2_round(out):
            slp, wp_ = rBF.acquire()
            if wp_ is not None:
                P.wait("t", wp_[0], wp_[1])
            e65 = None
            for (h, qt, slr, slh, st, cc) in out:
                e65 = rp_stage_a2(h, qt, slp, cc, e65)
            rBF.mark(slp, ve, e65)
            return [(h, qt, slr, slh, st) for (h, qt, slr, slh, st, cc) in out]

        def rp_b1_round(infos):
            return [(h, qt, rp_stage_b1(h, qt, slr, slh, st))
                    for (h, qt, slr, slh, st) in infos]

        def rp_b2_round(infos, heads, g_):
            for (h, qt, ex) in infos:
                rp_stage_b2(h, qt, g_, 0 if h == heads[0] else 1, ex)

        for p in range(H // 2):
            heads = (2 * p, 2 * p + 1)
            pair_setup(p)
            prologue(2 * p)
            prologue(2 * p + 1)
            ia = rp_a1_round(heads, 0)
            ia = rp_a2_round(ia)
            ib = rp_b1_round(ia)
            rp_b2_round(ib, heads, 0)
            fin = C.pop("p2_fin", None)
            for g_ in range(NG):
                hooks = {}
                if fin is not None:
                    hooks[6] = fin
                if g_ < NG - 1:
                    state = {}

                    def hk1(g_=g_, state=state):
                        state["ia"] = rp_a1_round(heads, g_ + 1)

                    def hk1b(state=state):
                        ia = rp_a2_round(state["ia"])
                        state["ib"] = rp_b1_round(ia)

                    def hk2(g_=g_, state=state):
                        rp_b2_round(state["ib"], heads, g_ + 1)
                    hooks[12] = hk1
                    hooks[24] = hk1b
                    hooks[42] = hk2
                fin = body2(heads[0], heads[1], g_, hooks)
            C["p2_fin"] = fin
        C.pop("p2_fin")()
        P.replay(nc)


def br_b_ap(C):
    return C["br_b"][:, :]


def gab_slice(C, h):
    return C["gab"][:, h * 64:(h + 1) * 64]


def emit_gate(nc, C):
    P = Rec()
    tm, ve, sg = C["tm"], C["ve"], C["sg"]
    sm = C["SM"]
    smg = sm["gt"]
    h2T = C["h2T"]
    CAP = 512
    I32 = mybir.dt.int32
    with ExitStack() as _stk:
        wg_sb = _stk.enter_context(nc.sbuf_tensor("wg_sb", [128, 64], BF16))
        wg_lo = _stk.enter_context(nc.sbuf_tensor("wg_lo", [128, 64], BF16))
        wgf = _stk.enter_context(nc.sbuf_tensor("wgf", [128, 64], F32))
        bgc = _stk.enter_context(nc.sbuf_tensor("bgc", [E, 1], F32))
        lgT = _stk.enter_context(nc.sbuf_tensor("lgT", [E, SQ], F32))
        lgtok = _stk.enter_context(nc.sbuf_tensor("lgtok", [128, 128], F32))
        mx16 = _stk.enter_context(nc.sbuf_tensor("mx16", [128, 16], F32))
        onehot = _stk.enter_context(nc.sbuf_tensor("onehot", [128, 128], F32))
        onehT = _stk.enter_context(nc.sbuf_tensor("onehT", [E, SQ], F32))
        cum0 = _stk.enter_context(nc.sbuf_tensor("cum0", [E, SQ], F32))
        cum1 = _stk.enter_context(nc.sbuf_tensor("cum1", [E, SQ], F32))
        eoffc = _stk.enter_context(nc.sbuf_tensor("eoffc", [E, 1], F32))
        onesf = _stk.enter_context(nc.sbuf_tensor("onesf", [128, 1], F32))
        slotr = _stk.enter_context(nc.sbuf_tensor("slotr", [1, SQ], F32))
        slotf = _stk.enter_context(nc.sbuf_tensor("slotf", [128, 16], F32))
        sloti = _stk.enter_context(nc.sbuf_tensor("sloti", [128, 16], I32))
        iotas = _stk.enter_context(nc.sbuf_tensor("iotas", [128, 16], I32))
        dumps = _stk.enter_context(nc.sbuf_tensor("dumps", [128, 1], I32))
        _ps1 = ExitStack()
        pG = _ps1.enter_context(nc.psum_tensor("pG", [8, 2048], F32))
        pGT = _ps1.enter_context(nc.psum_tensor("pGT", [128, 512], F32))

        ldw = P.op("g", lambda g: g.dma_start(
            wgf[:, :], bass.AP(C["wg"].tensor, 0, [[8, 128], [1024, 8], [1, 8]])),
            inc=smg, inck=16)
        ldb = P.op("g", lambda g: g.dma_start(bgc[:, :], C["bg"]), inc=smg, inck=16)
        lde = P.op("g", lambda g: g.dma_start(eoffc[:, :], C["eoff_h"]),
                   inc=smg, inck=16)
        ldo = P.op("g", lambda g: g.dma_start(onesf[:, :], C["ones_fh"]),
                   inc=smg, inck=16)
        ldi = P.op("g", lambda g: g.dma_start(iotas[:, :], C["iota_h"]),
                   inc=smg, inck=16)
        ldd = P.op("g", lambda g: g.dma_start(dumps[:, :], C["dump_h"]),
                   inc=smg, inck=16)
        ld_all = smg.n
        lnv = C["p3_ln_done"]
        P.op("v", lambda v: v.tensor_copy(wg_sb[:, :], wgf[:, :]),
             wait=[(smg, ld_all)], inc=ve, drain=True)
        wglo_ev = P.op("v", lambda v: v.scalar_tensor_tensor(
            wg_lo[:, :], wg_sb[:, :], -1.0, wgf[:, :], OP.mult, OP.add),
            inc=ve, drain=True)
        h2lT = C["h2loT"]
        mm = None
        for ns in range(4):
            ops = [(wg_sb, h2T), (wg_lo, h2T), (wg_sb, h2lT)]
            nmm = len(ops) * 8
            i = 0
            for wop, hop in ops:
                for dc in range(8):
                    wt = [(smg, ld_all), (ve, lnv), (ve, wglo_ev)]                         if (ns == 0 and i == 0) else []
                    mm = P.op("t", lambda t, ns=ns, dc=dc, wop=wop, hop=hop,
                              st=(i == 0), sp=(i == nmm - 1): t.matmul(
                        pG[:, ns * 512:(ns + 1) * 512], wop[:, dc * 8:(dc + 1) * 8],
                        hop[:, dc * SQ + ns * 512: dc * SQ + (ns + 1) * 512],
                        start=st, stop=sp),
                        wait=wt, inc=(tm if i == nmm - 1 else None))
                    i += 1
        evg = P.op("v", lambda v: v.tensor_scalar(
            lgT[:, :], pG[:, :], bgc[:, :], None, OP.add),
            wait=[(tm, mm)], inc=ve)
        tps = []
        for tt in range(16):
            tp = P.op("t", lambda t, tt=tt: t.transpose(
                pGT[:, tt * 8:(tt + 1) * 8], lgT[:, tt * 128:(tt + 1) * 128],
                C["eye_f"][0:8, 0:8]), wait=([(ve, evg)] if tt == 0 else []),
                inc=tm)
            tps.append(tp)
        ev2 = P.op("v", lambda v: v.tensor_copy(lgtok[:, :], pGT[:, 0:128]),
                   wait=[(tm, tps[-1])], inc=ve)
        for tt in range(16):
            P.op("v", lambda v, tt=tt: v.tensor_reduce(
                mx16[:, tt:tt + 1], lgtok[:, tt * 8:(tt + 1) * 8], AX, OP.max),
                wait=([(ve, ev2)] if tt == 0 else []), inc=ve, drain=(tt == 0))
        P.drain("v")
        oh = None
        for tt in range(16):
            oh = P.op("v", lambda v, tt=tt: v.tensor_tensor(
                onehot[:, tt * 8:(tt + 1) * 8], lgtok[:, tt * 8:(tt + 1) * 8],
                mx16[:, tt:tt + 1].to_broadcast([128, 8]), OP.is_equal), inc=ve)
        _ps1.close()
        _ps2 = ExitStack()
        pOT = _ps2.enter_context(nc.psum_tensor("pOT", [8, 2048], F32))
        tps2 = []
        for tt in range(16):
            tp = P.op("t", lambda t, tt=tt: t.transpose(
                pOT[:, tt * 128:(tt + 1) * 128], onehot[:, tt * 8:(tt + 1) * 8],
                C["eye_f"][:, :]), wait=([(ve, oh)] if tt == 0 else []), inc=tm)
            tps2.append(tp)
        ev3 = P.op("v", lambda v: v.tensor_copy(onehT[:, :], pOT[:, :]),
                   wait=[(tm, tps2[-1])], inc=ve)
        a, b = onehT, cum0
        prev = ev3
        k = 1
        while k < SQ:
            P.op("v", lambda v, a=a, b=b, k=k: v.tensor_copy(
                b[:, 0:k], a[:, 0:k]), wait=[(ve, prev)], inc=ve)
            prev = P.op("v", lambda v, a=a, b=b, k=k: v.tensor_tensor(
                b[:, k:SQ], a[:, k:SQ], a[:, 0:SQ - k], OP.add), inc=ve)
            a, b = b, (cum1 if b is cum0 else cum0)
            if a is onehT:
                a = cum1
            k *= 2
        cum = a
        c1 = P.op("v", lambda v, cum=cum: v.tensor_scalar(
            cum[:, :], cum[:, :], float(CAP), eoffc[:, :], OP.min, OP.add),
            wait=[(ve, prev), (smg, ld_all)], inc=ve, drain=True)
        c2 = P.op("v", lambda v, cum=cum: v.tensor_tensor(
            cum[:, :], cum[:, :], onehT[:, :], OP.mult), inc=ve, drain=True)
        _ps2.close()
        _ps3 = ExitStack()
        pSL = _ps3.enter_context(nc.psum_tensor("pSL", [1, 2048], F32))
        pST = _ps3.enter_context(nc.psum_tensor("pST", [128, 512], F32))
        mms = None
        for ns in range(4):
            mms = P.op("t", lambda t, ns=ns, cum=cum: t.matmul(
                pSL[:, ns * 512:(ns + 1) * 512], onesf[0:8, :],
                cum[0:8, ns * 512:(ns + 1) * 512], start=True, stop=True),
                wait=([(ve, c2)] if ns == 0 else []), inc=tm)
        evs = P.op("v", lambda v: v.tensor_copy(slotr[:, :], pSL[:, :]),
                   wait=[(tm, mms)], inc=ve)
        tps3 = []
        for tt in range(16):
            tp = P.op("t", lambda t, tt=tt: t.transpose(
                pST[:, tt * 4:tt * 4 + 1] if False else pST[:, tt:tt + 1],
                slotr[:, tt * 128:(tt + 1) * 128], C["eye_f"][0:1, 0:1]),
                wait=([(ve, evs)] if tt == 0 else []), inc=tm)
            tps3.append(tp)
        cl = P.op("v", lambda v: v.tensor_scalar(
            slotf[:, :], pST[:, 0:16], 4095.0, 0.0, OP.min, OP.max),
            wait=[(tm, tps3[-1])], inc=ve)
        ci = P.op("v", lambda v: v.tensor_copy(sloti[:, :], slotf[:, :]),
                  wait=[(ve, cl)], inc=ve, drain=True)
        pf = None
        for k2 in range(32):
            pf = P.op("g", lambda g, k2=k2: g.dma_start(
                C["idx_hbm"][k2 * 128:(k2 + 1) * 128, :], dumps[:, :]),
                wait=([(smg, ld_all)] if k2 == 0 else []),
                inc=sm["rt0"], inck=16)
        P.wait("g", sm["rt0"], pf)
        sc = None
        for tt in range(16):
            sc = P.op("g", lambda g, tt=tt: g.indirect_dma_start(
                C["idx_hbm"],
                bass.IndirectOffsetOnAxis(ap=sloti[:, tt:tt + 1], axis=0),
                iotas[:, tt:tt + 1], None),
                wait=([(ve, ci)] if tt == 0 else []), inc=sm["rt1"], inck=16)
        C["idx_ready"] = sc
        _ps3.close()
        P.replay(nc)


def emit_ffn(nc, C):
    P = Rec()
    tm, ve, sg = C["tm"], C["ve"], C["sg"]
    sm = C["SM"]
    w1sm = (sm["w1a"], sm["w1b"])
    w2sm = sm["w2s"]
    smgx = sm["gx"]
    I32 = mybir.dt.int32
    CAP = 512
    NSL = 8 * CAP
    HTILES = NSL // 128
    with ExitStack() as _stk:
        w1s = _stk.enter_context(nc.sbuf_tensor("w1s", [128, 2 * 8192], BF16))
        w2s = _stk.enter_context(nc.sbuf_tensor("w2s", [128, 8192], BF16))
        h2pT = _stk.enter_context(nc.sbuf_tensor("h2pT", [128, 8 * 2048], BF16))
        hid = _stk.enter_context(nc.sbuf_tensor("hid", [128, 8 * 512], BF16))
        gst = _stk.enter_context(nc.sbuf_tensor("gst", [128, 2 * D], BF16))
        yst = _stk.enter_context(nc.sbuf_tensor("yst", [128, 2 * D], BF16))
        idxs = _stk.enter_context(nc.sbuf_tensor("idxs", [128, 32], I32))
        zt = _stk.enter_context(nc.sbuf_tensor("zt", [128, D], BF16))
        ysb = C["ysb"]
        ph0 = _stk.enter_context(nc.psum_tensor("ph0", [128, 512], F32))
        ph1 = _stk.enter_context(nc.psum_tensor("ph1", [128, 512], F32))
        po0 = _stk.enter_context(nc.psum_tensor("po0", [128, 512], F32))
        po1 = _stk.enter_context(nc.psum_tensor("po1", [128, 512], F32))
        pT0 = _stk.enter_context(nc.psum_tensor("pT0", [128, 1024], BF16))
        pT1 = _stk.enter_context(nc.psum_tensor("pT1", [128, 1024], BF16))
        pT = (pT0, pT1)

        ldx = None
        for k in range(32):
            ldx = P.op("g", lambda g, k=k: g.dma_start(
                idxs[:, k:k + 1], C["idx_hbm"][k * 128:(k + 1) * 128, :]),
                wait=([(sm["rt1"], C["idx_ready"])] if k == 0 else []),
                inc=smgx, inck=16)
        mz = P.op("v", lambda v: v.memset(zt[:, :], 0.0), inc=ve)
        pz = None
        for k in range(17):
            r0 = min(k * 128, SQ)
            pz = P.op("y", lambda y, r0=r0, k=k: y.dma_start(
                C["y_hbm"][r0:r0 + 128 if k < 16 else SQ + 1, :],
                zt[:, :] if k < 16 else zt[0:1, :]),
                wait=([(ve, mz)] if k == 0 else []), inc=sm["rt3"], inck=16)
        yzero = sm["rt3"].n
        ms = P.op("v", lambda v: v.memset(ysb[:, :], 0.0), inc=ve)

        gi_count = [0]
        w1ld, w2ld = {}, {}

        def stream_w(e, gg):
            gi = gi_count[0]
            sl1 = gi % 2
            w = [(tm, C[f"w1_free_{gi - 2}"])] if gi >= 2 else []
            w1ld[gi] = P.op("g", lambda g, e=e, gg=gg, sl1=sl1: g.dma_start(
                w1s[:, sl1 * 8192:(sl1 + 1) * 8192],
                bass.AP(C["w1"].tensor, e * D * HF + gg * 1024,
                        [[HF, 128], [128 * HF, 8], [1, 1024]])),
                wait=w, inc=w1sm[sl1], inck=16)
            w = [(tm, C[f"w2_free_{gi - 1}"])] if gi >= 1 else []
            w2ld[gi] = P.op("g", lambda g, e=e, gg=gg: g.dma_start(
                w2s[:, :],
                bass.AP(C["w2"].tensor, e * HF * D + gg * 1024 * D,
                        [[D, 128], [128 * D, 8], [1, 1024]])),
                wait=w, inc=w2sm, inck=16)
            gi_count[0] += 1
            return gi, sl1

        last_gelu = {}
        last_evac = {}
        for half in range(2):
            for k in range(16):
                tile = half * 16 + k
                wt = [(smgx, ldx), (sm["h2s"], C["h2s_all"])] if (k == 0 and half == 0) else []
                if tile >= 2:
                    wt.append((tm, C[f"gtp_{tile - 2}"]))
                gx = P.op("g", lambda g, tile=tile, k=k: g.indirect_dma_start(
                    gst[:, (k % 2) * D:(k % 2 + 1) * D], None, C["h2_hbm"],
                    bass.IndirectOffsetOnAxis(ap=idxs[:, tile:tile + 1], axis=0)),
                    wait=wt, inc=smgx, inck=16)
                tps = []
                for dc in range(8):
                    wt2 = [(smgx, gx)] if dc == 0 else []
                    if dc == 0 and tile >= 2:
                        wt2.append((ve, C[f"gx_ev_{tile - 2}"]))
                    tps.append(P.op("t", lambda t, dc=dc, k=k: t.transpose(
                        pT[k % 2][:, dc * 128:(dc + 1) * 128],
                        gst[:, (k % 2) * D + dc * 128:(k % 2) * D + (dc + 1) * 128],
                        C["eye_b"][:, :]), wait=wt2, inc=tm))
                C[f"gtp_{tile}"] = tps[-1]
                e2 = None
                for dc in range(8):
                    e2 = P.op("v", lambda v, dc=dc, k=k: v.tensor_copy(
                        h2pT[:, dc * 2048 + k * 128: dc * 2048 + (k + 1) * 128],
                        pT[k % 2][:, dc * 128:(dc + 1) * 128]),
                        wait=[(tm, tps[-1])], inc=ve)
                C[f"gx_ev_{tile}"] = e2
            gather_ev = ve.n
            for er in range(4):
                e = half * 4 + er
                for gg in range(4):
                    gi, sl1 = stream_w(e, gg)
                    for ht in range(8):
                        kk = gi * 8 + ht
                        wt = []
                        if ht == 0:
                            wt.append((w1sm[sl1], w1ld[gi]))
                            if gg == 0:
                                wt.append((ve, gather_ev))
                        if kk >= 2:
                            wt.append((sg, last_gelu[kk - 2]))
                        mm = None
                        for dc in range(8):
                            mm = P.op("t", lambda t, dc=dc, ht=ht, sl1=sl1,
                                      er=er: t.matmul(
                                          (ph0, ph1)[ht % 2][:, :],
                                          w1s[:, sl1 * 8192 + dc * 1024 +
                                              ht * 128:sl1 * 8192 + dc * 1024 +
                                              ht * 128 + 128],
                                          h2pT[:, dc * 2048 + er * 512:
                                               dc * 2048 + (er + 1) * 512],
                                          start=(dc == 0), stop=(dc == 7)),
                                      wait=(wt if dc == 0 else []),
                                      inc=(tm if dc == 7 else None))
                        if ht == 7:
                            C[f"w1_free_{gi}"] = tm.n
                        gl = P.op("s", lambda s, ht=ht: s.activation(
                            hid[:, ht * 512:(ht + 1) * 512],
                            (ph0, ph1)[ht % 2][:, :], AF.Gelu_apprx_tanh),
                            wait=[(tm, tm.n)], inc=sg)
                        last_gelu[kk] = gl
                    for dt in range(8):
                        kk2 = gi * 8 + dt
                        wt = [(sg, last_gelu[gi * 8 + 7])]
                        if dt == 0:
                            wt.append((w2sm, w2ld[gi]))
                        if kk2 >= 2:
                            wt.append((ve, last_evac[kk2 - 2]))
                        mm = None
                        for ht in range(8):
                            mm = P.op("t", lambda t, ht=ht, dt=dt: t.matmul(
                                (po0, po1)[dt % 2][:, :],
                                w2s[:, ht * 1024 + dt * 128:
                                    ht * 1024 + dt * 128 + 128],
                                hid[:, ht * 512:(ht + 1) * 512],
                                start=(ht == 0), stop=(ht == 7)),
                                wait=(wt if ht == 0 else []),
                                inc=(tm if ht == 7 else None))
                        if dt == 7:
                            C[f"w2_free_{gi}"] = tm.n
                        evv = P.op("v", lambda v, dt=dt, er=er: v.scalar_tensor_tensor(
                            ysb[:, dt * 2048 + er * 512: dt * 2048 + (er + 1) * 512],
                            (po0, po1)[dt % 2][:, :], C["gbt"][:, dt:dt + 1],
                            ysb[:, dt * 2048 + er * 512: dt * 2048 + (er + 1) * 512],
                            OP.mult, OP.add),
                            wait=[(tm, tm.n)] + ([(ve, ms)] if gi % 16 == 0 and dt == 0
                                                 else []), inc=ve)
                        last_evac[kk2] = evv
            ffn_half_ev = ve.n
            sev = {}
            ysc = {}
            last_ytp = None
            for k in range(16):
                tile = half * 16 + k
                tps = []
                for dc in range(8):
                    wt2 = []
                    if dc == 0:
                        wt2.append((ve, ffn_half_ev))
                        if k >= 2:
                            wt2.append((ve, sev[k - 2]))
                    tps.append(P.op("t", lambda t, dc=dc, k=k: t.transpose(
                        pT[k % 2][:, dc * 128:(dc + 1) * 128],
                        ysb[:, dc * 2048 + k * 128: dc * 2048 + (k + 1) * 128],
                        C["eye_b"][:, :]), wait=wt2, inc=tm))
                last_ytp = tps[-1]
                we2 = [(tm, tps[-1])]
                if k >= 2:
                    we2.append((sm["rt2"], ysc[k - 2]))
                e2 = P.op("v", lambda v, k=k: v.tensor_copy(
                    yst[:, (k % 2) * D:(k % 2 + 1) * D], pT[k % 2][:, :]),
                    wait=we2, inc=ve)
                sev[k] = e2
                wt3 = [(ve, e2)]
                if half == 0 and k == 0:
                    wt3.append((sm["rt3"], yzero))
                ysc[k] = P.op("g", lambda g, tile=tile, k=k: g.indirect_dma_start(
                    C["y_hbm"],
                    bass.IndirectOffsetOnAxis(ap=idxs[:, tile:tile + 1], axis=0),
                    yst[:, (k % 2) * D:(k % 2 + 1) * D], None),
                    wait=wt3, inc=sm["rt2"], inck=16)
            if half == 0:
                ms = P.op("v", lambda v: v.memset(ysb[:, :], 0.0),
                          wait=[(tm, last_ytp)], inc=ve)
        C["ysc_all"] = sm["rt2"].n
        P.replay(nc)


def emit_out(nc, C):
    P = Rec()
    tm, ve, sg = C["tm"], C["ve"], C["sg"]
    sm = C["SM"]
    xts = (sm["xt0"], sm["xt1"])
    sps = (sm["sp0"], sm["sp1"])
    yls = (sm["at0"], sm["at1"])
    with ExitStack() as _stk:
        x2t = _stk.enter_context(nc.sbuf_tensor("x2t", [128, 2 * D], F32))
        yt = _stk.enter_context(nc.sbuf_tensor("yt", [128, 2 * D], BF16))
        outt = _stk.enter_context(nc.sbuf_tensor("outt", [128, 2 * D], F32))
        for s in (sm["sp0"], sm["sp1"], sm["sp2"]):
            P.wait("y", s, s.n)
        P.wait("y", sm["rt2"], C["ysc_all"])
        ev_done = {}
        st_done = {}
        for tt in range(SQ // 128):
            sl = tt % 2
            w = [(ve, ev_done[tt - 2])] if tt >= 2 else []
            ld = P.op("y", lambda y, tt=tt, sl=sl: y.dma_start(
                x2t[:, sl * D:(sl + 1) * D],
                C["x2_hbm"][tt * 128:(tt + 1) * 128, :]),
                wait=w, inc=xts[sl], inck=16)
            ldy = P.op("y", lambda y, tt=tt, sl=sl: y.dma_start(
                yt[:, sl * D:(sl + 1) * D],
                C["y_hbm"][tt * 128:(tt + 1) * 128, :]),
                wait=[], inc=yls[sl], inck=16)
            wad = [(xts[sl], ld), (yls[sl], ldy)]
            if tt >= 2:
                wad.append((sps[sl], st_done[tt - 2]))
            ad = P.op("v", lambda v, sl=sl: v.tensor_tensor(
                outt[:, sl * D:(sl + 1) * D], yt[:, sl * D:(sl + 1) * D],
                x2t[:, sl * D:(sl + 1) * D], OP.add), wait=wad, inc=ve)
            ev_done[tt] = ad
            st_done[tt] = P.op("y", lambda y, tt=tt, sl=sl: y.dma_start(
                C["out"][tt * 128:(tt + 1) * 128, :],
                outt[:, sl * D:(sl + 1) * D]),
                wait=[(ve, ad)], inc=sps[sl], inck=16)
        for s in sm.values():
            P.wait("y", s, s.n)
        P.replay(nc)


_CACHED = {}


def kernel(L0_attn_x, L0_attn_Wqkv, L0_attn_bqkv, L0_attn_Wr, L0_attn_br,
           L0_ffn_W1, L0_ffn_W2, L0_ffn_Wg, L0_ffn_bg, L0_c, L0_wmod):
    if "nc" not in _CACHED:
        _CACHED["nc"] = build_program()
    nc = _CACHED["nc"]

    eye = np.eye(128)
    di = np.arange(128)
    tri = dict(
        eye_bh=eye.astype(BFNP),
        eye_fh=eye.astype(np.float32),
        ones_fh=np.ones((128, 1), dtype=np.float32),
        iota_h=np.ascontiguousarray(
            np.arange(SQ, dtype=np.int32).reshape(16, 128).T),
        dump_h=np.full((128, 1), SQ, dtype=np.int32),
        eoff_h=(np.arange(8, dtype=np.float32) * 512 - 1).reshape(8, 1),
        tri_geh=(di[None, :] >= di[:, None]).astype(np.float32),
        tri_lth=(di[None, :] < di[:, None]).astype(np.float32),
        tri_leh=(di[None, :] <= di[:, None]).astype(np.float32),
        tri_gth=(di[None, :] > di[:, None]).astype(np.float32),
        ones_h=np.ones((128, 1), dtype=BFNP),
    )
    wr_f = np.ascontiguousarray(L0_attn_Wr)
    br_f = np.ascontiguousarray(L0_attn_br).reshape(1, R)
    wr_r = np.ascontiguousarray(L0_attn_Wr[::-1])
    br_r = np.ascontiguousarray(L0_attn_br[::-1]).reshape(1, R)
    shared = dict(
        wqkv=np.ascontiguousarray(L0_attn_Wqkv),
        bqkv=np.ascontiguousarray(L0_attn_bqkv).reshape(1, 3 * D),
        w1=np.ascontiguousarray(L0_ffn_W1),
        w2=np.ascontiguousarray(L0_ffn_W2),
        wg=np.ascontiguousarray(L0_ffn_Wg),
        bg=np.ascontiguousarray(L0_ffn_bg).reshape(E, 1),
        wmod=np.ascontiguousarray(L0_wmod),
        **tri,
    )
    in_maps = []
    for c in range(8):
        b = c // 2
        rev = c % 2 == 1
        xbm = L0_attn_x[b] if not rev else L0_attn_x[b][::-1]
        m = dict(shared)
        m["xb"] = np.ascontiguousarray(xbm)
        m["wr"] = wr_r if rev else wr_f
        m["br"] = br_r if rev else br_f
        m["crow"] = np.ascontiguousarray(L0_c[b]).reshape(1, D)
        in_maps.append(m)

    res = run_bass_kernel_spmd(nc, in_maps, core_ids=list(range(8)))
    out = np.empty((4, S, D), dtype=np.float32)
    for c in range(8):
        b = c // 2
        o = np.asarray(res.results[c]["out"])
        if c % 2 == 0:
            out[b, 0:SQ] = o
        else:
            out[b, SQ:S] = o[::-1]
    return out


# version 16
# speedup vs baseline: 1.1323x; 1.0168x over previous
import numpy as np
from contextlib import ExitStack
import ml_dtypes
import concourse.bass as bass
import concourse.mybir as mybir
from concourse.bass_utils import run_bass_kernel_spmd

BFNP = ml_dtypes.bfloat16
F32 = mybir.dt.float32
BF16 = mybir.dt.bfloat16
AF = mybir.ActivationFunctionType
OP = mybir.AluOpType
AX = mybir.AxisListType.X

D = 1024
S = 4096
SQ = 2048
H = 16
HD = 64
R = 257
E = 8
HF = 4096
NQT = SQ // 128
NJT = S // 128
NG = 4


class Ctr:
    def __init__(self, handle):
        self.h = handle
        self.n = 0


class Rec:
    ENG = ("t", "v", "s", "g", "y")

    def __init__(self):
        self.q = {e: [] for e in self.ENG}

    def op(self, e, fn, wait=(), inc=None, inck=1, drain=False):
        w = tuple((c.h if isinstance(c, Ctr) else c, int(v)) for c, v in wait)
        i = None
        if inc is not None:
            inc.n += inck
            i = (inc.h, inck)
        self.q[e].append((w, fn, i, drain))
        return inc.n if inc is not None else None

    def wait(self, e, c, v):
        self.op(e, None, wait=[(c, v)])

    def drain(self, e):
        self.op(e, None, drain=True)

    def replay(self, nc, barrier=True):
        with nc.Block() as block:
            dec = dict(t=block.tensor, v=block.vector, s=block.scalar,
                       g=block.gpsimd, y=block.sync)
            for e in self.ENG:
                items = self.q[e]
                if not items:
                    continue

                def body(engine, items=items):
                    for w, fn, i, dr in items:
                        if dr:
                            engine.drain()
                        for (h, val) in w:
                            engine.wait_ge(h, val)
                        inst = fn(engine) if fn is not None else None
                        if i is not None:
                            inst.then_inc(i[0], i[1])
                dec[e](body)
            self.q = {e: [] for e in self.ENG}
        if barrier:
            nc.all_engine_barrier()


class Ring:
    def __init__(self, nslots):
        self.free_at = [None] * nslots
        self.i = 0

    def acquire(self):
        s = self.i % len(self.free_at)
        self.i += 1
        return s, self.free_at[s]

    def mark(self, slot, ctr, val):
        self.free_at[slot] = (ctr, int(val))


def build_program(dbg=()):
    nc = bass.Bass("TRN2", target_bir_lowering=False,
                   detect_race_conditions=False)
    dbg = set(dbg)

    def dram(name, shape, dt, kind):
        if kind == "Internal" and name in dbg:
            kind = "ExternalOutput"
        return nc.dram_tensor(name, shape, dt, kind=kind).ap()

    C = {}
    C["xb"] = dram("xb", [S, D], F32, "ExternalInput")
    C["wqkv"] = dram("wqkv", [3 * D, D], F32, "ExternalInput")
    C["bqkv"] = dram("bqkv", [1, 3 * D], F32, "ExternalInput")
    C["wr"] = dram("wr", [R, HD], F32, "ExternalInput")
    C["br"] = dram("br", [1, R], F32, "ExternalInput")
    C["w1"] = dram("w1", [E, D, HF], F32, "ExternalInput")
    C["w2"] = dram("w2", [E, HF, D], F32, "ExternalInput")
    C["wg"] = dram("wg", [D, E], F32, "ExternalInput")
    C["bg"] = dram("bg", [E, 1], F32, "ExternalInput")
    C["crow"] = dram("crow", [1, D], F32, "ExternalInput")
    C["wmod"] = dram("wmod", [D, 6 * D], F32, "ExternalInput")
    C["eye_bh"] = dram("eye_bh", [128, 128], BF16, "ExternalInput")
    C["eye_fh"] = dram("eye_fh", [128, 128], F32, "ExternalInput")
    C["tri_geh"] = dram("tri_geh", [128, 128], F32, "ExternalInput")
    C["tri_lth"] = dram("tri_lth", [128, 128], F32, "ExternalInput")
    C["tri_leh"] = dram("tri_leh", [128, 128], F32, "ExternalInput")
    C["tri_gth"] = dram("tri_gth", [128, 128], F32, "ExternalInput")
    C["ones_h"] = dram("ones_h", [128, 1], BF16, "ExternalInput")
    C["ones_fh"] = dram("ones_fh", [128, 1], F32, "ExternalInput")
    C["iota_h"] = dram("iota_h", [128, 16], mybir.dt.int32, "ExternalInput")
    C["dump_h"] = dram("dump_h", [128, 1], mybir.dt.int32, "ExternalInput")
    C["eoff_h"] = dram("eoff_h", [8, 1], F32, "ExternalInput")
    C["out"] = dram("out", [SQ, D], F32, "ExternalOutput")
    C["rp_hbm"] = dram("rp_hbm", [8, 128, R], F32, "Internal")
    C["attn_hbm"] = dram("attn_hbm", [SQ, D], BF16, "Internal")
    C["x2_hbm"] = dram("x2_hbm", [SQ, D], F32, "Internal")
    C["mod_hbm"] = dram("mod_hbm", [1, 6 * D], F32, "Internal")
    C["h2_hbm"] = dram("h2_hbm", [SQ + 1, D], BF16, "Internal")
    C["y_hbm"] = dram("y_hbm", [SQ + 1, D], BF16, "Internal")
    C["idx_hbm"] = dram("idx_hbm", [4096, 1], mybir.dt.int32, "Internal")

    with ExitStack() as _stk:
        eye_b = _stk.enter_context(nc.sbuf_tensor("eye_b", [128, 128], BF16))
        eye_f = _stk.enter_context(nc.sbuf_tensor("eye_f", [128, 128], F32))
        tri_ge = _stk.enter_context(nc.sbuf_tensor("tri_ge", [128, 128], F32))
        tri_lt = _stk.enter_context(nc.sbuf_tensor("tri_lt", [128, 128], F32))
        tri_le = _stk.enter_context(nc.sbuf_tensor("tri_le", [128, 128], F32))
        tri_gt = _stk.enter_context(nc.sbuf_tensor("tri_gt", [128, 128], F32))
        ones_b = _stk.enter_context(nc.sbuf_tensor("ones_b", [128, 1], BF16))
        wrt8 = _stk.enter_context(nc.sbuf_tensor("wrt8", [HD, R], BF16))
        br_b = _stk.enter_context(nc.sbuf_tensor("br_b", [128, R], F32))
        ca1b = _stk.enter_context(nc.sbuf_tensor("ca1b", [128, D], F32))
        sab = _stk.enter_context(nc.sbuf_tensor("sab", [128, D], F32))
        gab = _stk.enter_context(nc.sbuf_tensor("gab", [128, D], F32))
        cb1b = _stk.enter_context(nc.sbuf_tensor("cb1b", [128, D], F32))
        sbb = _stk.enter_context(nc.sbuf_tensor("sbb", [128, D], F32))
        gbt = _stk.enter_context(nc.sbuf_tensor("gbt", [128, E], F32))
        _tm = _stk.enter_context(nc.semaphore("t_mm"))
        _ve = _stk.enter_context(nc.semaphore("v_ev"))
        _sg = _stk.enter_context(nc.semaphore("s_ge"))
        SM = {}
        for name in ("c", "wr", "wm0", "wm1", "xt0", "xt1", "xt2",
                     "at0", "at1", "sp0", "sp1", "sp2", "wst", "bias",
                     "rpw", "sk0", "sk1", "sk2", "sk3", "ao0", "ao1", "gt", "mask",
                     "xg0", "xg1", "os0", "os1",
                     "w1a", "w1b", "w2s", "h1d", "cc", "mb",
                     "h2s", "rt0", "rt1", "rt2", "rt3", "gx"):
            SM[name] = Ctr(_stk.enter_context(nc.semaphore("sm_" + name)))
        C.update(eye_b=eye_b, eye_f=eye_f, tri_ge=tri_ge, tri_lt=tri_lt, tri_le=tri_le,
                 tri_gt=tri_gt, ones_b=ones_b, wrt8=wrt8, br_b=br_b,
                 ca1b=ca1b, sab=sab, gab=gab, cb1b=cb1b, sbb=sbb, gbt=gbt,
                 tm=Ctr(_tm), ve=Ctr(_ve), sg=Ctr(_sg), SM=SM)

        if "DUMP_PT" in dbg:
            C["DUMP_PT"] = True
        if "NO_POS" in dbg:
            C["NO_POS"] = True
        with nc.Block() as _blk0:
            @_blk0.gpsimd
            def _(g):
                for s in list(SM.values()):
                    g.sem_clear(s.h)
                g.sem_clear(_tm)
                g.sem_clear(_ve)
                g.sem_clear(_sg)
        nc.all_engine_barrier()
        with nc.sbuf_tensor("h1T", [128, 8 * S], BF16) as h1T, \
             nc.sbuf_tensor("stats_sb", [128, 32 * 8], F32) as _ssb:
            C["h1T"] = h1T
            if "DUMP_H1T" in dbg:
                C["stats_dump"] = _ssb
            with ExitStack() as _p0stk:
                C["p0stk"] = _p0stk
                emit_p0(nc, C)
                emit_ln(nc, C, first=True)
            if "DUMP_H1T" in dbg:
                dmp = nc.dram_tensor("h1T_dump", [128, 8 * S], BF16,
                                     kind="ExternalOutput").ap()
                sdmp_h = nc.dram_tensor("stats_dump", [128, 32 * 8], F32,
                                        kind="ExternalOutput").ap()
                dmp2 = nc.dram_tensor("bc_dump", [128, 3 * D], F32,
                                      kind="ExternalOutput").ap()
                Pd = Rec()
                Pd.wait("y", C["ve"], C["p1_done"])
                Pd.op("y", lambda y: y.dma_start(dmp, h1T[:, :]),
                      inc=C["SM"]["h1d"], inck=16)
                Pd.op("y", lambda y: y.dma_start(dmp2[:, 0:D], C["ca1b"][:, :]),
                      inc=C["SM"]["h1d"], inck=16)
                Pd.op("y", lambda y: y.dma_start(dmp2[:, D:2 * D], C["sab"][:, :]),
                      inc=C["SM"]["h1d"], inck=16)
                Pd.op("y", lambda y: y.dma_start(dmp2[:, 2 * D:3 * D], C["gab"][:, :]),
                      inc=C["SM"]["h1d"], inck=16)
                Pd.op("y", lambda y: y.dma_start(sdmp_h, C["stats_dump"][:, :]),
                      inc=C["SM"]["h1d"], inck=16)
                Pd.wait("y", C["SM"]["h1d"], C["SM"]["h1d"].n)
                Pd.replay(nc)
            if "STOP_P1" not in dbg:
                emit_attn(nc, C)
                if "STOP_P2" in dbg:
                    Pd = Rec()
                    Pd.wait("y", C["SM"]["ao0"], C["SM"]["ao0"].n)
                    Pd.wait("y", C["SM"]["ao1"], C["SM"]["ao1"].n)
                    Pd.replay(nc)
        if "STOP_P1" in dbg or "STOP_P2" in dbg:
            return nc
        with nc.sbuf_tensor("h2T", [128, 8 * SQ], BF16) as h2T, \
                nc.sbuf_tensor("h2loT", [128, 8 * SQ], BF16) as h2loT:
            C["h2T"] = h2T
            C["h2loT"] = h2loT
            emit_ln(nc, C, first=False)
            emit_gate(nc, C)
            if "STOP_P3" in dbg:
                Pd = Rec()
                for s in C["SM"].values():
                    Pd.wait("y", s, s.n)
                Pd.replay(nc)
                return nc
        with nc.sbuf_tensor("ysb", [128, 8 * SQ], BF16) as ysb:
            C["ysb"] = ysb
            emit_ffn(nc, C)
            emit_out(nc, C)
    return nc


def emit_p0(nc, C):
    P = Rec()
    tm, ve, sg = C["tm"], C["ve"], C["sg"]
    sm = C["SM"]
    smc, smwr, wm = sm["c"], sm["wr"], (sm["wm0"], sm["wm1"])
    smcc, smmb = sm["cc"], sm["mb"]

    for dst, src in [(C["eye_b"], C["eye_bh"]), (C["eye_f"], C["eye_fh"]),
                     (C["tri_ge"], C["tri_geh"]),
                     (C["tri_lt"], C["tri_lth"]), (C["tri_le"], C["tri_leh"]),
                     (C["tri_gt"], C["tri_gth"]), (C["ones_b"], C["ones_h"])]:
        P.op("g", lambda g, d=dst, s=src: g.dma_start(d[:, :], s), inc=smc, inck=16)
    P.op("g", lambda g: g.dma_start(C["br_b"][:, :],
                                    C["br"].partition_broadcast(128)),
         inc=smc, inck=16)
    ld_consts = smc.n

    if True:
        _stk = C["p0stk"]
        wr_st = _stk.enter_context(nc.sbuf_tensor("wr_st", [128, 128], BF16))
        c_sb = _stk.enter_context(nc.sbuf_tensor("c_sb", [1, D], F32))
        cs_b = _stk.enter_context(nc.sbuf_tensor("cs_b", [1, D], BF16))
        csT = _stk.enter_context(nc.sbuf_tensor("csT", [128, 8], BF16))
        wmod_sb = _stk.enter_context(nc.sbuf_tensor("wmod_sb", [128, 2 * 1536], BF16))
        modrow = _stk.enter_context(nc.sbuf_tensor("modrow", [1, 6 * D], F32))
        g_row = _stk.enter_context(nc.sbuf_tensor("g_row", [1, D], F32))
        pP0f = _stk.enter_context(nc.psum_tensor("pP0f", [128, 512], F32))
        pP0b = _stk.enter_context(nc.psum_tensor("pP0b", [128, 1024], BF16))
        pmod = _stk.enter_context(nc.psum_tensor("pmod", [1, 1536], F32))
        P.wait("t", smc, ld_consts)
        last_tp = None
        last_evw = None
        for ch in range(3):
            r0, r1 = [(0, 128), (128, 256), (256, 257)][ch]
            n = r1 - r0
            w = [(tm, last_tp)] if last_tp is not None else []
            ld = P.op("g", lambda g, r0=r0, r1=r1, n=n: g.dma_start(
                wr_st[0:n, 0:HD], C["wr"][r0:r1, :]), wait=w, inc=smwr, inck=16)
            wtp = [(smwr, ld)]
            if last_evw is not None:
                wtp.append((ve, last_evw))
            tp = P.op("t", lambda t, n=n, ch=ch: t.transpose(
                pP0b[0:HD, ch * 128:ch * 128 + n], wr_st[0:n, 0:HD],
                C["eye_b"][0:n, 0:n]), wait=wtp, inc=tm)
            last_evw = P.op("v", lambda v, ch=ch, n=n: v.tensor_scalar(
                C["wrt8"][:, ch * 128:ch * 128 + n],
                pP0b[0:HD, ch * 128:ch * 128 + n], 8.0, None, OP.mult),
                wait=[(tm, tp)], inc=ve)
            last_tp = tp

        ldc = P.op("g", lambda g: g.dma_start(c_sb[:, :], C["crow"]),
                   inc=smcc, inck=16)
        sil = P.op("s", lambda s: s.activation(cs_b[:, :], c_sb[:, :], AF.Silu),
                   wait=[(smcc, ldc)], inc=sg)
        tpc = None
        for dc in range(8):
            wtc = [(sg, sil), (ve, last_evw)] if dc == 0 else []
            tpc = P.op("t", lambda t, dc=dc: t.transpose(
                pP0b[:, 384 + 4 * dc:385 + 4 * dc],
                cs_b[:, dc * 128:(dc + 1) * 128],
                C["eye_b"][0:1, 0:1]), wait=wtc, inc=tm)
        evc = P.op("v", lambda v: v.tensor_copy(csT[:, :], pP0b[:, 384:416:4]),
                   wait=[(tm, tpc)], inc=ve)

        mm_done = {}
        ev_prev = None
        for q4 in range(4):
            for dc in range(8):
                k = q4 * 8 + dc
                bo = (k % 2) * 1536
                w = [(tm, mm_done[k - 2])] if k >= 2 else []
                ld = P.op("g", lambda g, q4=q4, dc=dc, bo=bo: g.dma_start(
                    wmod_sb[:, bo:bo + 1536],
                    C["wmod"][dc * 128:(dc + 1) * 128,
                              q4 * 1536:(q4 + 1) * 1536]),
                    wait=w, inc=wm[k % 2], inck=16)
                mm = None
                for ns in range(3):
                    wmm = []
                    if ns == 0:
                        wmm.append((wm[k % 2], ld))
                        if dc == 0:
                            wmm.append((ve, evc if ev_prev is None else ev_prev))
                    mm = P.op("t", lambda t, dc=dc, bo=bo, ns=ns: t.matmul(
                        pmod[:, ns * 512:(ns + 1) * 512], csT[:, dc:dc + 1],
                        wmod_sb[:, bo + ns * 512:bo + (ns + 1) * 512],
                        start=(dc == 0), stop=(dc == 7)),
                        wait=wmm, inc=(tm if ns == 2 else None))
                mm_done[k] = tm.n
            ev_prev = P.op("v", lambda v, q4=q4: v.tensor_copy(
                modrow[:, q4 * 1536:(q4 + 1) * 1536], pmod[:, :]),
                wait=[(tm, mm_done[q4 * 8 + 7])], inc=ve)

        def seg(i):
            return modrow[:, i * D:(i + 1) * D]
        P.op("v", lambda v: v.tensor_scalar(seg(1), seg(1), 1.0, None, OP.add),
             wait=[(ve, ev_prev)], inc=ve)
        P.op("v", lambda v: v.tensor_scalar(seg(4), seg(4), 1.0, None, OP.add),
             inc=ve)
        tlast = None
        for i in (2, 5):
            ab = P.op("s", lambda s, i=i: s.activation(
                g_row[:, :], seg(i), AF.Abs), wait=[(ve, ve.n)], inc=sg)
            P.op("v", lambda v, i=i: v.tensor_scalar(
                g_row[:, :], g_row[:, :], 0.25, None, OP.max),
                wait=[(sg, ab)], inc=ve)
            P.op("v", lambda v, i=i: v.tensor_scalar(
                seg(i), seg(i), 0.0, 2.0, OP.is_ge, OP.mult), inc=ve)
            P.op("v", lambda v, i=i: v.tensor_scalar(
                seg(i), seg(i), -1.0, None, OP.add), inc=ve)
            tlast = P.op("v", lambda v, i=i: v.tensor_tensor(
                seg(i), seg(i), g_row[:, :], OP.mult), inc=ve)
        st = P.op("g", lambda g: g.dma_start(C["mod_hbm"], modrow[:, :]),
                  wait=[(ve, tlast)], inc=smmb, inck=16)
        first_b = True
        for dst, i in [(C["sab"], 0), (C["ca1b"], 1), (C["gab"], 2),
                       (C["sbb"], 3), (C["cb1b"], 4)]:
            P.op("g", lambda g, d=dst, i=i: g.dma_start(
                d[:, :], C["mod_hbm"][:, i * D:(i + 1) * D].partition_broadcast(128)),
                wait=([(smmb, st)] if first_b else []), inc=smmb, inck=16)
            first_b = False
        C["p0_bcast_c"] = smmb.n
        tb = None
        for dc in range(8):
            tb = P.op("t", lambda t, dc=dc: t.transpose(
                pP0f[:, dc:dc + 1], seg(5)[:, dc * 128:(dc + 1) * 128],
                C["eye_f"][0:1, 0:1]),
                wait=[(ve, tlast)], inc=tm)
        P.op("v", lambda v: v.tensor_copy(C["gbt"][:, :], pP0f[:, 0:8]),
             wait=[(tm, tb)], inc=ve)
        P.wait("v", smc, ld_consts)
        P.replay(nc, barrier=False)


def emit_ln(nc, C, first):
    P = Rec()
    tm, ve, sg = C["tm"], C["ve"], C["sg"]
    sm = C["SM"]
    xts = (sm["xt0"], sm["xt1"], sm["xt2"])
    ats = (sm["at0"], sm["at1"])
    sps = (sm["sp0"], sm["sp1"], sm["sp2"])
    ntt = (S if first else SQ) // 128
    dstT = C["h1T"] if first else C["h2T"]
    seqw = S if first else SQ
    cab = C["ca1b"] if first else C["cb1b"]
    sab = C["sab"] if first else C["sbb"]
    stats_dmp = C.get("stats_dump") if first else None
    if first:
        P.wait("v", sm["mb"], C["p0_bcast_c"])
    else:
        P.wait("y", sm["ao0"], sm["ao0"].n)
        P.wait("y", sm["ao1"], sm["ao1"].n)

    sfx = "1" if first else "2"
    with ExitStack() as _stk:
        xt = _stk.enter_context(nc.sbuf_tensor("xt" + sfx, [128, 3 * D], F32))
        at = _stk.enter_context(nc.sbuf_tensor("at" + sfx, [128, 2 * D], BF16))
        xsq = _stk.enter_context(nc.sbuf_tensor("xsq" + sfx, [128, D], F32))
        stats = _stk.enter_context(nc.sbuf_tensor("stats" + sfx, [128, 8], F32))
        hst = _stk.enter_context(nc.sbuf_tensor("hst" + sfx, [128, 2 * D], BF16))
        hstF = _stk.enter_context(nc.sbuf_tensor("hstF" + sfx, [128, D], F32))
        hstLo = _stk.enter_context(nc.sbuf_tensor("hstLo" + sfx, [128, 2 * D], BF16))
        pL0 = _stk.enter_context(nc.psum_tensor("pL0" + sfx, [128, 1024], BF16))
        pL1 = _stk.enter_context(nc.psum_tensor("pL1" + sfx, [128, 1024], BF16))
        if not first:
            pL2 = _stk.enter_context(nc.psum_tensor("pL2" + sfx, [128, 1024], BF16))
            pL3 = _stk.enter_context(nc.psum_tensor("pL3" + sfx, [128, 1024], BF16))
        else:
            pL2 = pL3 = None
        ev_done = {}
        tp_done = {}
        tp_last = {}
        spill_ds = {}
        sq2_prev = [None]
        h2sp = {}
        for tt in range(ntt):
            sl = tt % 3
            w = [(ve, ev_done[tt - 3])] if tt >= 3 else []
            if (not first) and tt >= 3:
                w.append((sps[sl], spill_ds[tt - 3]))
            ld = P.op("y", lambda y, tt=tt, sl=sl: y.dma_start(
                xt[:, sl * D:(sl + 1) * D], C["xb"][tt * 128:(tt + 1) * 128, :]),
                wait=w, inc=xts[sl], inck=16)
            if not first:
                w2 = [(ve, ev_done[tt - 2])] if tt >= 2 else []
                lda = P.op("y", lambda y, tt=tt: y.dma_start(
                    at[:, (tt % 2) * D:(tt % 2 + 1) * D],
                    C["attn_hbm"][tt * 128:(tt + 1) * 128, :]),
                    wait=w2, inc=ats[tt % 2], inck=16)
                add = P.op("v", lambda v, sl=sl, tt=tt: v.tensor_tensor(
                    xt[:, sl * D:(sl + 1) * D], xt[:, sl * D:(sl + 1) * D],
                    at[:, (tt % 2) * D:(tt % 2 + 1) * D], OP.add),
                    wait=[(xts[sl], ld), (ats[tt % 2], lda)], inc=ve)
                spill_ds[tt] = P.op("y", lambda y, tt=tt, sl=sl: y.dma_start(
                    C["x2_hbm"][tt * 128:(tt + 1) * 128, :],
                    xt[:, sl * D:(sl + 1) * D]),
                    wait=[(ve, add)], inc=sps[sl], inck=16)
                ready_v = [(ve, add)]
                ready_s = [(ve, add)]
            else:
                ready_v = [(xts[sl], ld)]
                ready_s = [(xts[sl], ld)]
            P.op("v", lambda v, sl=sl: v.tensor_reduce(
                stats[:, 0:1], xt[:, sl * D:(sl + 1) * D], AX, OP.add),
                wait=ready_v, inc=ve)
            sq = P.op("s", lambda s, sl=sl: s.activation(
                xsq[:, :], xt[:, sl * D:(sl + 1) * D], AF.Square),
                wait=ready_s + ([(ve, ev_done[tt - 1])] if tt >= 1 else []),
                inc=sg)
            ws2 = [(sg, sq)]
            if tt >= 1:
                ws2.append((sg, sq2_prev[0]))
            P.op("v", lambda v: v.tensor_reduce(
                stats[:, 1:2], xsq[:, :], AX, OP.add), wait=ws2, inc=ve)
            P.op("v", lambda v: v.tensor_scalar(
                stats[:, 0:1], stats[:, 0:1], 1.0 / D, None, OP.mult), inc=ve,
                drain=True)
            P.op("v", lambda v: v.tensor_scalar(
                stats[:, 1:2], stats[:, 1:2], 1.0 / D, None, OP.mult), inc=ve,
                drain=True)
            P.op("v", lambda v: v.tensor_tensor(
                stats[:, 2:3], stats[:, 0:1], stats[:, 0:1], OP.mult), inc=ve,
                drain=True)
            v5 = P.op("v", lambda v: v.tensor_scalar(
                stats[:, 1:2], stats[:, 1:2], stats[:, 2:3], 1e-5,
                OP.subtract, OP.add), inc=ve, drain=True)
            sq2 = P.op("s", lambda s: s.activation(
                stats[:, 3:4], stats[:, 1:2], AF.Sqrt), wait=[(ve, v5)], inc=sg)
            sq2_prev[0] = sq2
            v6 = P.op("v", lambda v: v.reciprocal(
                stats[:, 4:5], stats[:, 3:4]), wait=[(sg, sq2)], inc=ve)
            if stats_dmp is not None:
                P.op("v", lambda v, tt=tt: v.tensor_copy(
                    stats_dmp[:, tt * 8:tt * 8 + 8], stats[:, :]), inc=ve)
            P.op("v", lambda v, sl=sl: v.tensor_scalar(
                xsq[:, :], xt[:, sl * D:(sl + 1) * D], stats[:, 0:1],
                stats[:, 4:5], OP.subtract, OP.mult), wait=[(ve, v6)], inc=ve,
                drain=True)
            P.op("v", lambda v: v.tensor_tensor(
                xsq[:, :], xsq[:, :], cab[:, :], OP.mult), inc=ve)
            hsl = (tt % 2) * D
            wa3 = [(tm, tp_last[tt - 2])] if tt >= 2 else []
            if (not first) and tt >= 2:
                wa3.append((sm["h2s"], h2sp[tt - 2]))
            if first:
                a3 = P.op("v", lambda v, hsl=hsl: v.tensor_tensor(
                    hst[:, hsl:hsl + D], xsq[:, :], sab[:, :], OP.add),
                    wait=wa3, inc=ve)
            else:
                P.op("v", lambda v: v.tensor_tensor(
                    hstF[:, :], xsq[:, :], sab[:, :], OP.add),
                    wait=wa3, inc=ve)
                P.op("v", lambda v, hsl=hsl: v.tensor_copy(
                    hst[:, hsl:hsl + D], hstF[:, :]), inc=ve, drain=True)
                a3 = P.op("v", lambda v, hsl=hsl: v.scalar_tensor_tensor(
                    hstLo[:, hsl:hsl + D], hst[:, hsl:hsl + D], -1.0,
                    hstF[:, :], OP.mult, OP.add), inc=ve, drain=True)
            ev_done[tt] = a3
            if not first:
                h2sp[tt] = P.op("y", lambda y, tt=tt, hsl=hsl: y.dma_start(
                    C["h2_hbm"][tt * 128:(tt + 1) * 128, :],
                    hst[:, hsl:hsl + D]),
                    wait=[(ve, a3)], inc=sm["h2s"], inck=16)
            pps = [(pL0, pL1)[tt % 2], (pL2, pL3)[tt % 2]]
            tps = []
            srcs = [(hst, dstT)] if first else [(hst, dstT), (hstLo, C["h2loT"])]
            for si, (sbuf_src, _dst) in enumerate(srcs):
                for dc in range(8):
                    wt = [(ve, a3)] if (dc == 0 and si == 0) else []
                    if dc == 0 and si == 0 and tt >= 2:
                        wt.append((ve, tp_done[tt - 2]))
                    tp = P.op("t", lambda t, dc=dc, pp=pps[si], hsl=hsl,
                              s=sbuf_src: t.transpose(
                                  pp[:, dc * 128:(dc + 1) * 128],
                                  s[:, hsl + dc * 128:hsl + (dc + 1) * 128],
                                  C["eye_b"][:, :]),
                              wait=wt, inc=tm)
                    tps.append(tp)
            tp_last[tt] = tps[-1]
            e2 = None
            for si, (sbuf_src, dst_) in enumerate(srcs):
                for dc in range(8):
                    e2 = P.op("v", lambda v, dc=dc, tt=tt, pp=pps[si], d=dst_:
                              v.tensor_copy(
                                  d[:, dc * seqw + tt * 128:
                                    dc * seqw + (tt + 1) * 128],
                                  pp[:, dc * 128:(dc + 1) * 128]),
                              wait=[(tm, tps[(si + 1) * 8 - 1])], inc=ve)
            tp_done[tt] = e2
        C["p1_done" if first else "p3_ln_done"] = ve.n
        if not first:
            zr = P.op("v", lambda v: v.memset(hst[0:1, 0:D], 0.0),
                      wait=[(sm["h2s"], h2sp[ntt - 1]),
                            (sm["h2s"], h2sp[ntt - 2])], inc=ve)
            P.op("y", lambda y: y.dma_start(C["h2_hbm"][SQ:SQ + 1, :],
                                            hst[0:1, 0:D]),
                 wait=[(ve, zr)], inc=sm["h2s"], inck=16)
            C["h2s_all"] = sm["h2s"].n
        P.replay(nc)


def emit_attn(nc, C):
    P = Rec()
    tm, ve, sg = C["tm"], C["ve"], C["sg"]
    sm = C["SM"]
    smb, smw, smr = sm["bias"], sm["wst"], sm["rpw"]
    sks = (sm["sk0"], sm["sk1"], sm["sk2"], sm["sk3"])
    aos = (sm["ao0"], sm["ao1"])
    h1T, eye_b = C["h1T"], C["eye_b"]
    if "DUMP_PT" in C:
        C["pt_dump"] = nc.dram_tensor("pt_dump", [NJT, 128, 512], BF16,
                                      kind="ExternalOutput").ap()

    with ExitStack() as _stk:
        w_st = _stk.enter_context(nc.sbuf_tensor("w_st", [128, D], BF16))
        wqT = _stk.enter_context(nc.sbuf_tensor("wqT", [128, D], BF16))
        wkT = _stk.enter_context(nc.sbuf_tensor("wkT", [128, D], BF16))
        wvT = _stk.enter_context(nc.sbuf_tensor("wvT", [128, D], BF16))
        qb2 = _stk.enter_context(nc.sbuf_tensor("qb2", [128, 1], F32))
        kb2 = _stk.enter_context(nc.sbuf_tensor("kb2", [128, 1], F32))
        bvb = _stk.enter_context(nc.sbuf_tensor("bvb", [128, 128], F32))
        qN = _stk.enter_context(nc.sbuf_tensor("qN", [65, 2 * SQ], BF16))
        qA = _stk.enter_context(nc.sbuf_tensor("qA", [65, 2 * SQ], BF16))
        qB = _stk.enter_context(nc.sbuf_tensor("qB", [65, 2 * SQ], BF16))
        kt = _stk.enter_context(nc.sbuf_tensor("kt", [65, 2 * S], BF16))
        vsb = _stk.enter_context(nc.sbuf_tensor("vsb", [128, 2 * NJT * 65], BF16))
        qsq = _stk.enter_context(nc.sbuf_tensor("qsq", [64, SQ], BF16))
        mrow = _stk.enter_context(nc.sbuf_tensor("mrow", [1, 2 * SQ], BF16))
        nkrow = _stk.enter_context(nc.sbuf_tensor("nkrow", [1, 512], F32))
        maxk = _stk.enter_context(nc.sbuf_tensor("maxk", [1, 16], F32))
        skews = _stk.enter_context(nc.sbuf_tensor("skews", [128, 4 * 384], F32))
        poss = _stk.enter_context(nc.sbuf_tensor("poss", [128, 4 * 384], F32))
        posb = _stk.enter_context(nc.sbuf_tensor("posb", [128, 8 * 384], BF16))
        rp01 = _stk.enter_context(nc.sbuf_tensor("rp01", [128, 16], BF16))
        epT = _stk.enter_context(nc.sbuf_tensor("epT", [128, 4 * 12 * 128], BF16))
        rp_sb = _stk.enter_context(nc.sbuf_tensor("rp_sb", [128, 8 * R], F32))
        PT = _stk.enter_context(nc.sbuf_tensor("PT", [128, 6 * 512], BF16))
        avsb = _stk.enter_context(nc.sbuf_tensor("avsb", [65, 1024], BF16))
        avrc = _stk.enter_context(nc.sbuf_tensor("avrc", [128, 1], F32))
        avout = _stk.enter_context(nc.sbuf_tensor("avout", [128, 2 * 64], BF16))
        pS0 = _stk.enter_context(nc.psum_tensor("pS0", [128, 512], F32))
        pS1 = _stk.enter_context(nc.psum_tensor("pS1", [128, 512], F32))
        pS2 = _stk.enter_context(nc.psum_tensor("pS2", [128, 512], F32))
        pS3 = _stk.enter_context(nc.psum_tensor("pS3", [128, 512], F32))
        pAVa = _stk.enter_context(nc.psum_tensor("pAVa", [128, 512], F32))
        pAVb = _stk.enter_context(nc.psum_tensor("pAVb", [128, 512], F32))
        pBF0 = _stk.enter_context(nc.psum_tensor("pBF0", [128, 1024], BF16))
        pBF1 = _stk.enter_context(nc.psum_tensor("pBF1", [128, 1024], BF16))
        pS = [pS0, pS1, pS2, pS3]
        pBF = [pBF0, pBF1]
        rS = Ring(4)
        rBF = Ring(2)
        rRP = Ring(8)
        rPT = Ring(6)
        rSK = Ring(4)
        rAO = Ring(2)
        rHB = Ring(8)
        pAV = {0: pAVa, 1: pAVb}

        posT_last = [None] * 8
        posex_last = [None, None, None, None]
        tpr_last = [None] * 8
        prologue_mm = [None]
        P.wait("t", ve, C["p1_done"])
        P.op("v", lambda v: v.memset(kt[64:65, :], 1.0), inc=ve)
        ones_ev = P.op("v", lambda v: v.memset(
            vsb[:, 64:64 + 65 * (2 * NJT - 1) + 1:65], 1.0), inc=ve)
        C["p2_ones_ev"] = ones_ev

        def pair_setup(p):
            h0, h1 = 2 * p, 2 * p + 1
            bq = C["bqkv"]
            if p > 0:
                P.wait("g", tm, C[f"proj_mm_{p - 1}"])
            for dst, off0, off1 in ((qb2, 192 * h0, 192 * h1),
                                    (kb2, 192 * h0 + 64, 192 * h1 + 64)):
                P.op("g", lambda g, d=dst, o=off0: g.dma_start(
                    d[0:64, :], bq[:, o:o + 64].transpose((1, 0))),
                    inc=smb, inck=16)
                P.op("g", lambda g, d=dst, o=off1: g.dma_start(
                    d[64:128, :], bq[:, o:o + 64].transpose((1, 0))),
                    inc=smb, inck=16)
            P.op("g", lambda g: g.dma_start(
                bvb[:, 0:64],
                bq[:, 192 * h0 + 128:192 * h0 + 192].partition_broadcast(128)),
                inc=smb, inck=16)
            P.op("g", lambda g: g.dma_start(
                bvb[:, 64:128],
                bq[:, 192 * h1 + 128:192 * h1 + 192].partition_broadcast(128)),
                inc=smb, inck=16)
            bias_ld = smb.n
            P.op("v", lambda v: v.tensor_scalar(
                qb2[:, :], qb2[:, :], 0.125, None, OP.mult),
                wait=[(smb, bias_ld)], inc=ve)

            for kind, dstW, scale in (("q", wqT, 0.125), ("k", wkT, None),
                                      ("v", wvT, None)):
                off = {"q": 0, "k": 64, "v": 128}[kind]
                P.op("g", lambda g, o=off: g.dma_start(
                    w_st[0:64, :], C["wqkv"][192 * h0 + o:192 * h0 + o + 64, :]),
                    inc=smw, inck=16)
                lb = P.op("g", lambda g, o=off: g.dma_start(
                    w_st[64:128, :], C["wqkv"][192 * h1 + o:192 * h1 + o + 64, :]),
                    inc=smw, inck=16)
                sbf, wbf = rBF.acquire()
                tps = []
                for dc in range(8):
                    wt = []
                    if dc == 0:
                        wt.append((smw, lb))
                        if wbf is not None:
                            wt.append(wbf)
                    tp = P.op("t", lambda t, dc=dc, s=sbf: t.transpose(
                        pBF[s][:, dc * 128:(dc + 1) * 128],
                        w_st[:, dc * 128:(dc + 1) * 128], eye_b[:, :]),
                        wait=wt, inc=tm)
                    tps.append(tp)
                evl = None
                for dc in range(8):
                    if scale is None:
                        evl = P.op("v", lambda v, dc=dc, s=sbf, dW=dstW:
                                   v.tensor_copy(dW[:, dc * 128:(dc + 1) * 128],
                                                 pBF[s][:, dc * 128:(dc + 1) * 128]),
                                   wait=[(tm, tps[-1])], inc=ve)
                    else:
                        evl = P.op("v", lambda v, dc=dc, s=sbf, dW=dstW, sc=scale:
                                   v.tensor_scalar(dW[:, dc * 128:(dc + 1) * 128],
                                                   pBF[s][:, dc * 128:(dc + 1) * 128],
                                                   sc, None, OP.mult),
                                   wait=[(tm, tps[-1])], inc=ve)
                rBF.mark(sbf, ve, evl)
                P.wait("g", tm, tps[-1])
            wT_ev = ve.n

            for ns in range(4):
                sl, wfree = rS.acquire()
                wt = [(ve, wT_ev), (smb, bias_ld)] + ([wfree] if wfree else [])
                mm = None
                for dc in range(8):
                    mm = P.op("t", lambda t, dc=dc, ns=ns, sl=sl: t.matmul(
                        pS[sl][:, :], wqT[:, dc * 128:(dc + 1) * 128],
                        h1T[:, dc * S + ns * 512: dc * S + (ns + 1) * 512],
                        start=(dc == 0), stop=(dc == 7)),
                        wait=(wt if dc == 0 else []),
                        inc=(tm if dc == 7 else None))
                ev = None
                for dstq in (qN, qA, qB):
                    for pi in range(2):
                        ev = P.op("v", lambda v, d=dstq, pi=pi, ns=ns, sl=sl:
                                  v.tensor_scalar(
                                      d[0:64, pi * SQ + ns * 512:
                                        pi * SQ + (ns + 1) * 512],
                                      pS[sl][pi * 64:(pi + 1) * 64, :],
                                      qb2[pi * 64:(pi + 1) * 64, :],
                                      None, OP.add),
                                  wait=[(tm, mm)], inc=ve)
                rS.mark(sl, ve, ev)
            C[f"proj_q_{p}"] = ve.n
            for ns in range(8):
                sl, wfree = rS.acquire()
                wt = [(ve, wT_ev), (smb, bias_ld)] + ([wfree] if wfree else [])
                mm = None
                for dc in range(8):
                    mm = P.op("t", lambda t, dc=dc, ns=ns, sl=sl: t.matmul(
                        pS[sl][:, :], wkT[:, dc * 128:(dc + 1) * 128],
                        h1T[:, dc * S + ns * 512: dc * S + (ns + 1) * 512],
                        start=(dc == 0), stop=(dc == 7)),
                        wait=(wt if dc == 0 else []),
                        inc=(tm if dc == 7 else None))
                ev = None
                for pi in range(2):
                    ev = P.op("v", lambda v, pi=pi, ns=ns, sl=sl:
                              v.tensor_scalar(
                                  kt[0:64, pi * S + ns * 512:
                                     pi * S + (ns + 1) * 512],
                                  pS[sl][pi * 64:(pi + 1) * 64, :],
                                  kb2[pi * 64:(pi + 1) * 64, :], None, OP.add),
                              wait=[(tm, mm)], inc=ve)
                rS.mark(sl, ve, ev)
            C[f"proj_k_{p}"] = ve.n
            for jt in range(NJT):
                sl, wfree = rS.acquire()
                wt = [(ve, wT_ev), (smb, bias_ld)] + ([wfree] if wfree else [])
                mm = None
                for dc in range(8):
                    mm = P.op("t", lambda t, dc=dc, jt=jt, sl=sl: t.matmul(
                        pS[sl][:, 0:128],
                        h1T[:, dc * S + jt * 128: dc * S + (jt + 1) * 128],
                        wvT[:, dc * 128:(dc + 1) * 128],
                        start=(dc == 0), stop=(dc == 7)),
                        wait=(wt if dc == 0 else []),
                        inc=(tm if dc == 7 else None))
                ev = None
                for pi in range(2):
                    ev = P.op("v", lambda v, pi=pi, jt=jt, sl=sl:
                              v.tensor_tensor(
                                  vsb[:, pi * NJT * 65 + jt * 65:
                                      pi * NJT * 65 + jt * 65 + 64],
                                  pS[sl][:, pi * 64:(pi + 1) * 64],
                                  bvb[:, pi * 64:(pi + 1) * 64], OP.add),
                              wait=[(tm, mm)], inc=ve)
                rS.mark(sl, ve, ev)
            C[f"proj_ev_{p}"] = ve.n
            C[f"proj_mm_{p}"] = tm.n

        def prologue(h):
            pi = h % 2
            wsq1 = [(ve, C[f"proj_q_{h // 2}"])]
            if prologue_mm[0] is not None:
                wsq1.append((tm, prologue_mm[0]))
            sq1 = P.op("s", lambda s, pi=pi: s.activation(
                qsq[:, :], qN[0:64, pi * SQ:(pi + 1) * SQ], AF.Square),
                wait=wsq1, inc=sg)
            for ns in range(4):
                sl, wfree = rS.acquire()
                wt = [(sg, sq1)] if ns == 0 else []
                if wfree is not None:
                    wt.append(wfree)
                mm = P.op("t", lambda t, ns=ns, sl=sl: t.matmul(
                    pS[sl][0:1, :], C["ones_b"][0:64, :],
                    qsq[0:64, ns * 512:(ns + 1) * 512], start=True, stop=True),
                    wait=wt, inc=tm)
                sv = P.op("s", lambda s, pi=pi, ns=ns, sl=sl: s.activation(
                    mrow[0:1, pi * SQ + ns * 512:pi * SQ + (ns + 1) * 512],
                    pS[sl][0:1, :], AF.Sqrt), wait=[(tm, mm)], inc=sg)
                rS.mark(sl, sg, sv)
            mm_hist = {}
            for ns in range(8):
                ks = (ns % 4) * 512
                if ns == 0:
                    wsq = [(ve, C[f"proj_k_{h // 2}"])]
                elif ns >= 4:
                    wsq = [(tm, mm_hist[ns - 4])]
                else:
                    wsq = []
                sq2 = P.op("s", lambda s, pi=pi, ns=ns, ks=ks: s.activation(
                    qsq[:, ks:ks + 512], kt[0:64, pi * S + ns * 512:
                                            pi * S + (ns + 1) * 512], AF.Square),
                    wait=wsq, inc=sg)
                sl, wfree = rS.acquire()
                mm = P.op("t", lambda t, sl=sl, ks=ks: t.matmul(
                    pS[sl][0:1, :], C["ones_b"][0:64, :], qsq[0:64, ks:ks + 512],
                    start=True, stop=True),
                    wait=[(sg, sq2)] + ([wfree] if wfree else []), inc=tm)
                rv = P.op("v", lambda v, ns=ns, sl=sl: v.tensor_reduce(
                    maxk[:, ns:ns + 1], pS[sl][0:1, :], AX, OP.max),
                    wait=[(tm, mm)], inc=ve)
                rS.mark(sl, ve, rv)
                mm_hist[ns] = mm
            prologue_mm[0] = mm_hist[7]
            red = P.op("v", lambda v: v.tensor_reduce(
                maxk[:, 8:9], maxk[:, 0:8], AX, OP.max), inc=ve, drain=True)
            sqm = P.op("s", lambda s: s.activation(
                maxk[:, 9:10], maxk[:, 8:9], AF.Sqrt), wait=[(ve, red)], inc=sg)
            m1 = P.op("v", lambda v, pi=pi: v.tensor_scalar(
                mrow[0:1, pi * SQ:(pi + 1) * SQ],
                mrow[0:1, pi * SQ:(pi + 1) * SQ], maxk[0:1, 9:10], -1.0,
                OP.mult, OP.mult), wait=[(sg, sqm)], inc=ve)
            r64 = P.op("v", lambda v, pi=pi: v.tensor_copy(
                qN[64:65, pi * SQ:(pi + 1) * SQ],
                mrow[0:1, pi * SQ:(pi + 1) * SQ]), wait=[(ve, m1)], inc=ve)
            C[f"row64_{h}"] = r64

        def rp_stage_a1(h, qt):
            pi = h % 2
            slr, wr_ = rRP.acquire()
            slh, wh_ = rHB.acquire()
            sl, wfree = rS.acquire()
            wt = [(ve, C[f"proj_ev_{h // 2}"])] + ([wfree] if wfree else [])
            mm = P.op("t", lambda t, pi=pi, qt=qt, sl=sl: t.matmul(
                pS[sl][:, 0:R], qN[0:64, pi * SQ + qt * 128:
                                   pi * SQ + (qt + 1) * 128],
                C["wrt8"][:, :], start=True, stop=True), wait=wt, inc=tm)
            evr = P.op("v", lambda v, slr=slr, sl=sl: v.tensor_tensor(
                rp_sb[:, slr * R:slr * R + R], pS[sl][:, 0:R], br_b_ap(C),
                OP.add), wait=[(tm, mm)] + ([wr_] if wr_ else []), inc=ve)
            rS.mark(sl, ve, evr)
            rslot = (qt % 4) + 4 * (h % 2)
            wcc = [(ve, evr)]
            if tpr_last[rslot] is not None:
                wcc.append((tm, tpr_last[rslot]))
            cc = P.op("v", lambda v, slr=slr, rslot=rslot: v.tensor_copy(
                rp01[:, 2 * rslot:2 * rslot + 2],
                rp_sb[:, slr * R:slr * R + 257:256]),
                wait=wcc, inc=ve, drain=True)
            wst = [(ve, evr)] + ([wh_] if wh_ else [])
            st = P.op("y", lambda y, slr=slr, slh=slh: y.dma_start(
                C["rp_hbm"][slh, :, :], rp_sb[:, slr * R:slr * R + R]),
                wait=wst, inc=smr, inck=16)
            return slr, slh, st, cc

        def rp_stage_a2(h, qt, slp_shared, cc, prev_e65):
            pi = h % 2
            rslot = (qt % 4) + 4 * (h % 2)
            tpr = None
            for j in range(2):
                wt = []
                if j == 0:
                    wt.append((ve, cc))
                    if prev_e65 is not None:
                        wt.append((ve, prev_e65))
                tpr = P.op("t", lambda t, rslot=rslot, j=j, slp=slp_shared:
                           t.transpose(
                               pBF[slp][0:1, 512 + j * 128:512 + (j + 1) * 128],
                               rp01[:, 2 * rslot + j:2 * rslot + j + 1],
                               eye_b[:, :]),
                           wait=wt, inc=tm)
            tpr_last[rslot] = tpr
            e65 = None
            for dstq, j in ((qA, 0), (qB, 1)):
                e65 = P.op("v", lambda v, d=dstq, j=j, pi=pi, qt=qt,
                           slp=slp_shared: v.tensor_tensor(
                               d[64:65, pi * SQ + qt * 128:pi * SQ + (qt + 1) * 128],
                               pBF[slp][0:1, 512 + j * 128:512 + (j + 1) * 128],
                               mrow[0:1, pi * SQ + qt * 128:pi * SQ + (qt + 1) * 128],
                               OP.add),
                           wait=[(tm, tpr), (ve, C[f"row64_{h}"])], inc=ve)
            C[f"row65_{h}_{qt}"] = e65
            return e65

        def rp_stage_b1(h, qt, slr, slh, st):
            ms = [0, 1, 2] if qt >= 1 else [1, 2]
            ssk, wsk = rSK.acquire()
            base = ssk * 384
            if posex_last[qt % 4] is not None:
                P.wait("v", sg, posex_last[qt % 4])
            m0 = ms[0]
            lds = P.op("y", lambda y, m0=m0, slh=slh, base=base: y.dma_start(
                skews[:, base + m0 * 128:base + 384],
                bass.AP(C["rp_hbm"].tensor, slh * 128 * R + m0 * 128,
                        [[256, 128], [1, 384 - m0 * 128]])),
                wait=[(smr, st)] + ([wsk] if wsk else []),
                inc=sks[ssk], inck=16)
            po = (qt % 4) * 384
            pb = ((qt % 4) + 4 * (h % 2)) * 384
            pslot = (qt % 4) + 4 * (h % 2)
            bl = []
            for m in ms:
                if m == 1:
                    bl.append(P.op("v", lambda v, base=base, po=po: v.tensor_copy(
                        poss[:, po + 128:po + 256],
                        skews[:, base + 128:base + 256]),
                        wait=[(sks[ssk], lds)], inc=ve))
                elif m == 0:
                    P.op("v", lambda v, base=base, po=po: v.tensor_tensor(
                        poss[:, po:po + 128], skews[:, base:base + 128],
                        C["tri_ge"][:, :], OP.mult), wait=[(sks[ssk], lds)], inc=ve)
                    bl.append(P.op("v", lambda v, slr=slr, po=po:
                                   v.scalar_tensor_tensor(
                                       poss[:, po:po + 128], C["tri_lt"][:, :],
                                       rp_sb[:, slr * R:slr * R + 1],
                                       poss[:, po:po + 128], OP.mult, OP.add),
                                   inc=ve))
                else:
                    P.op("v", lambda v, base=base, po=po: v.tensor_tensor(
                        poss[:, po + 256:po + 384],
                        skews[:, base + 256:base + 384],
                        C["tri_le"][:, :], OP.mult), wait=[(sks[ssk], lds)], inc=ve)
                    bl.append(P.op("v", lambda v, slr=slr, po=po:
                                   v.scalar_tensor_tensor(
                                       poss[:, po + 256:po + 384],
                                       C["tri_gt"][:, :],
                                       rp_sb[:, slr * R + 256:slr * R + 257],
                                       poss[:, po + 256:po + 384],
                                       OP.mult, OP.add), inc=ve))
            rSK.mark(ssk, ve, bl[-1])
            rRP.mark(slr, ve, bl[-1])
            lo = 0 if qt >= 1 else 128
            wex = [(ve, bl[-1])]
            if posT_last[pslot] is not None:
                wex.append((tm, posT_last[pslot]))
            ex = P.op("s", lambda s, pb=pb, po=po, lo=lo: s.activation(
                posb[:, pb + lo:pb + 384], poss[:, po + lo:po + 384], AF.Exp),
                wait=wex, inc=sg)
            posex_last[qt % 4] = ex
            rHB.mark(slh, sks[ssk], lds)
            return ex

        def rp_stage_b2(h, qt, g_, side, ex):
            ms = [0, 1, 2] if qt >= 1 else [1, 2]
            pb = ((qt % 4) + 4 * (h % 2)) * 384
            pslot = (qt % 4) + 4 * (h % 2)
            slot = (g_ % 2) * 2 + side
            sbf, wbf = rBF.acquire()
            tps = []
            for m in ms:
                wt = [(sg, ex)] if m == ms[0] else []
                if m == ms[0] and wbf is not None:
                    wt.append(wbf)
                tps.append(P.op("t", lambda t, m=m, pb=pb, s=sbf: t.transpose(
                    pBF[s][:, m * 128:(m + 1) * 128],
                    posb[:, pb + m * 128:pb + (m + 1) * 128], eye_b[:, :]),
                    wait=wt, inc=tm))
            posT_last[pslot] = tps[-1]
            eve = None
            for i, m in enumerate(ms):
                eve = P.op("v", lambda v, m=m, s=sbf, slot=slot, qt=qt: v.tensor_copy(
                    epT[:, slot * 1536 + (qt % 4) * 384 + m * 128:
                        slot * 1536 + (qt % 4) * 384 + (m + 1) * 128],
                    pBF[s][:, m * 128:(m + 1) * 128]),
                    wait=[(tm, tps[-1])], inc=ve)
            rBF.mark(sbf, ve, eve)
            C[f"ep_{h}_{qt}"] = eve

        def segments(g_, jt):
            its = [4 * g_ + l for l in range(4)]
            cls = []
            for it in its:
                if abs(jt - it) <= 1:
                    cls.append("N")
                elif it <= jt - 2:
                    cls.append("R")
                else:
                    cls.append("L")
            segs = []
            s0 = 0
            for i in range(1, 5):
                if i == 4 or cls[i] != cls[s0]:
                    segs.append((cls[s0], s0, i))
                    s0 = i
            return segs

        def body2(hA, hB, g_, rp_hooks):
            near = [jt for jt in range(max(0, 4 * g_ - 1), min(NJT, 4 * g_ + 5))]
            far = [jt for jt in range(NJT) if jt not in near]
            order = far + near
            heads = (hA, hB)
            nch = {0: 0, 1: 0}
            epslot = (g_ % 2) * 2
            units = [(jt, s) for jt in order for s in (0, 1)]
            pend_av = []
            LAG = 4
            for ui, (jt, side) in enumerate(units):
                if ui in rp_hooks:
                    rp_hooks[ui]()
                h = heads[side]
                pi = h % 2
                segs = segments(g_, jt)
                sl, wfree = rS.acquire()
                slt, wpt = rPT.acquire()
                mm = None
                for si, (cls, l0, l1) in enumerate(segs):
                    qsrc = {"N": qN, "L": qA, "R": qB}[cls]
                    wt = []
                    if si == 0 and wfree:
                        wt.append(wfree)
                    if cls != "N":
                        wt.append((ve, C[f"row65_{h}_{4 * g_ + l1 - 1}"]))
                    elif si == 0:
                        wt.append((ve, C[f"row64_{h}"]))
                    mm = P.op("t", lambda t, qsrc=qsrc, l0=l0, l1=l1, jt=jt,
                              sl=sl, si=si, nseg=len(segs), pi=pi, g_=g_:
                              t.matmul(
                                  pS[sl][:, l0 * 128:l1 * 128],
                                  kt[0:65, pi * S + jt * 128:
                                     pi * S + (jt + 1) * 128],
                                  qsrc[0:65, pi * SQ + g_ * 512 + l0 * 128:
                                       pi * SQ + g_ * 512 + l1 * 128],
                                  start=(si == 0), stop=(si == nseg - 1)),
                              wait=wt, inc=(tm if si == len(segs) - 1 else None))
                ex = P.op("s", lambda s, sl=sl, slt=slt: s.activation(
                    PT[:, slt * 512:(slt + 1) * 512], pS[sl][:, :], AF.Exp),
                    wait=[(tm, mm)] + ([wpt] if wpt else []), inc=sg)
                rS.mark(sl, sg, ex)
                av_wait = [(sg, ex)]
                for (cls, l0, l1) in (segs if not C.get("NO_POS") else []):
                    if cls != "N":
                        continue
                    for l in range(l0, l1):
                        it = 4 * g_ + l
                        m = jt - it + 1
                        mu = P.op("v", lambda v, slt=slt, l=l, it=it, m=m,
                                  side=side, epslot=epslot:
                                  v.tensor_tensor(
                                      PT[:, slt * 512 + l * 128:
                                         slt * 512 + (l + 1) * 128],
                                      PT[:, slt * 512 + l * 128:
                                         slt * 512 + (l + 1) * 128],
                                      epT[:, (epslot + side) * 1536 +
                                          (it % 4) * 384 + m * 128:
                                          (epslot + side) * 1536 +
                                          (it % 4) * 384 + (m + 1) * 128],
                                      OP.mult),
                                  wait=[(sg, ex), (ve, C[f"ep_{h}_{it}"])],
                                  inc=ve)
                        av_wait = [(ve, mu)]
                wt = list(av_wait)
                if nch[side] == 0:
                    aw = C.get(f"av_free_{side}")
                    if aw is not None:
                        wt.append(aw)
                    wt.append((ve, C["p2_ones_ev"]))

                def emit_av(jt=jt, slt=slt, pi=pi, side=side, wt=tuple(wt),
                            first=(nch[side] == 0), last=(nch[side] == NJT - 1)):
                    av = P.op("t", lambda t, jt=jt, slt=slt, pi=pi, side=side,
                              first=first, last=last: t.matmul(
                                  pAV[side][0:65, :],
                                  vsb[:, pi * NJT * 65 + jt * 65:
                                      pi * NJT * 65 + (jt + 1) * 65],
                                  PT[:, slt * 512:(slt + 1) * 512],
                                  start=first, stop=last),
                              wait=list(wt), inc=tm)
                    rPT.mark(slt, tm, av)
                    C[f"av_done_{side}"] = av
                pend_av.append(emit_av)
                nch[side] += 1
                if len(pend_av) > LAG:
                    pend_av.pop(0)()
            for f in pend_av:
                f()
            cps = {}
            for side in (0, 1):
                cp = P.op("v", lambda v, side=side: v.tensor_copy(
                    avsb[:, side * 512:(side + 1) * 512], pAV[side][0:65, :]),
                    wait=[(tm, C[f"av_done_{side}"])], inc=ve)
                C[f"av_free_{side}"] = (ve, cp)
                cps[side] = cp

            def finish(g_=g_, cps=cps, heads=heads):
                for side in (0, 1):
                    h = heads[side]
                    cp = cps[side]
                    sbf, wbf = rBF.acquire()
                    tps = []
                    for l in range(4):
                        wt = [(ve, cp)] if l == 0 else []
                        if l == 0 and wbf is not None:
                            wt.append(wbf)
                        tps.append(P.op("t", lambda t, l=l, s=sbf, side=side:
                                        t.transpose(
                                            pBF[s][:, l * 256:l * 256 + 65],
                                            avsb[:, side * 512 + l * 128:
                                                 side * 512 + (l + 1) * 128],
                                            eye_b[0:65, 0:65]), wait=wt, inc=tm))
                    evf = None
                    for l in range(4):
                        qt = 4 * g_ + l
                        so, wao = rAO.acquire()
                        rc = P.op("v", lambda v, l=l, s=sbf: v.reciprocal(
                            avrc[:, :], pBF[s][:, l * 256 + 64:l * 256 + 65]),
                            wait=[(tm, tps[-1])] + ([wao] if wao else []), inc=ve)
                        evf = P.op("v", lambda v, l=l, s=sbf, so=so, h=h:
                                   v.scalar_tensor_tensor(
                                       avout[:, so * 64:(so + 1) * 64],
                                       pBF[s][:, l * 256:l * 256 + 64], avrc[:, :],
                                       gab_slice(C, h), OP.mult, OP.mult),
                                   wait=[(ve, rc)], inc=ve, drain=True)
                        st = P.op("y", lambda y, so=so, qt=qt, h=h: y.dma_start(
                            C["attn_hbm"][qt * 128:(qt + 1) * 128,
                                          h * 64:(h + 1) * 64],
                            avout[:, so * 64:(so + 1) * 64]),
                            wait=[(ve, evf)], inc=aos[so], inck=16)
                        rAO.mark(so, aos[so], st)
                    rBF.mark(sbf, ve, evf)
            return finish

        def rp_a1_round(heads, g_):
            out = []
            for h in heads:
                for qt in range(4 * g_, 4 * g_ + 4):
                    out.append((h, qt) + rp_stage_a1(h, qt))
            return out

        def rp_a2_round(out):
            slp, wp_ = rBF.acquire()
            if wp_ is not None:
                P.wait("t", wp_[0], wp_[1])
            e65 = None
            for (h, qt, slr, slh, st, cc) in out:
                e65 = rp_stage_a2(h, qt, slp, cc, e65)
            rBF.mark(slp, ve, e65)
            return [(h, qt, slr, slh, st) for (h, qt, slr, slh, st, cc) in out]

        def rp_b1_round(infos):
            return [(h, qt, rp_stage_b1(h, qt, slr, slh, st))
                    for (h, qt, slr, slh, st) in infos]

        def rp_b2_round(infos, heads, g_):
            for (h, qt, ex) in infos:
                rp_stage_b2(h, qt, g_, 0 if h == heads[0] else 1, ex)

        for p in range(H // 2):
            heads = (2 * p, 2 * p + 1)
            pair_setup(p)
            prologue(2 * p)
            prologue(2 * p + 1)
            ia = rp_a1_round(heads, 0)
            ia = rp_a2_round(ia)
            ib = rp_b1_round(ia)
            rp_b2_round(ib, heads, 0)
            fin = C.pop("p2_fin", None)
            for g_ in range(NG):
                hooks = {}
                if fin is not None:
                    hooks[6] = fin
                if g_ < NG - 1:
                    state = {}

                    def hk1(g_=g_, state=state):
                        state["ia"] = rp_a1_round(heads, g_ + 1)

                    def hk1b(state=state):
                        ia = rp_a2_round(state["ia"])
                        state["ib"] = rp_b1_round(ia)

                    def hk2(g_=g_, state=state):
                        rp_b2_round(state["ib"], heads, g_ + 1)
                    hooks[12] = hk1
                    hooks[24] = hk1b
                    hooks[42] = hk2
                fin = body2(heads[0], heads[1], g_, hooks)
            C["p2_fin"] = fin
        C.pop("p2_fin")()
        P.replay(nc)


def br_b_ap(C):
    return C["br_b"][:, :]


def gab_slice(C, h):
    return C["gab"][:, h * 64:(h + 1) * 64]


def emit_gate(nc, C):
    P = Rec()
    tm, ve, sg = C["tm"], C["ve"], C["sg"]
    sm = C["SM"]
    smg = sm["gt"]
    h2T = C["h2T"]
    CAP = 512
    I32 = mybir.dt.int32
    with ExitStack() as _stk:
        wg_sb = _stk.enter_context(nc.sbuf_tensor("wg_sb", [128, 64], BF16))
        wg_lo = _stk.enter_context(nc.sbuf_tensor("wg_lo", [128, 64], BF16))
        wgf = _stk.enter_context(nc.sbuf_tensor("wgf", [128, 64], F32))
        bgc = _stk.enter_context(nc.sbuf_tensor("bgc", [E, 1], F32))
        lgT = _stk.enter_context(nc.sbuf_tensor("lgT", [E, SQ], F32))
        lgtok = _stk.enter_context(nc.sbuf_tensor("lgtok", [128, 128], F32))
        mx16 = _stk.enter_context(nc.sbuf_tensor("mx16", [128, 16], F32))
        onehot = _stk.enter_context(nc.sbuf_tensor("onehot", [128, 128], F32))
        onehT = _stk.enter_context(nc.sbuf_tensor("onehT", [E, SQ], F32))
        cum0 = _stk.enter_context(nc.sbuf_tensor("cum0", [E, SQ], F32))
        cum1 = _stk.enter_context(nc.sbuf_tensor("cum1", [E, SQ], F32))
        eoffc = _stk.enter_context(nc.sbuf_tensor("eoffc", [E, 1], F32))
        onesf = _stk.enter_context(nc.sbuf_tensor("onesf", [128, 1], F32))
        slotr = _stk.enter_context(nc.sbuf_tensor("slotr", [1, SQ], F32))
        slotf = _stk.enter_context(nc.sbuf_tensor("slotf", [128, 16], F32))
        sloti = _stk.enter_context(nc.sbuf_tensor("sloti", [128, 16], I32))
        iotas = _stk.enter_context(nc.sbuf_tensor("iotas", [128, 16], I32))
        dumps = _stk.enter_context(nc.sbuf_tensor("dumps", [128, 1], I32))
        _ps1 = ExitStack()
        pG = _ps1.enter_context(nc.psum_tensor("pG", [8, 2048], F32))
        pGT = _ps1.enter_context(nc.psum_tensor("pGT", [128, 512], F32))

        ldw = P.op("g", lambda g: g.dma_start(
            wgf[:, :], bass.AP(C["wg"].tensor, 0, [[8, 128], [1024, 8], [1, 8]])),
            inc=smg, inck=16)
        ldb = P.op("g", lambda g: g.dma_start(bgc[:, :], C["bg"]), inc=smg, inck=16)
        lde = P.op("g", lambda g: g.dma_start(eoffc[:, :], C["eoff_h"]),
                   inc=smg, inck=16)
        ldo = P.op("g", lambda g: g.dma_start(onesf[:, :], C["ones_fh"]),
                   inc=smg, inck=16)
        ldi = P.op("g", lambda g: g.dma_start(iotas[:, :], C["iota_h"]),
                   inc=smg, inck=16)
        ldd = P.op("g", lambda g: g.dma_start(dumps[:, :], C["dump_h"]),
                   inc=smg, inck=16)
        ld_all = smg.n
        lnv = C["p3_ln_done"]
        P.op("v", lambda v: v.tensor_copy(wg_sb[:, :], wgf[:, :]),
             wait=[(smg, ld_all)], inc=ve, drain=True)
        wglo_ev = P.op("v", lambda v: v.scalar_tensor_tensor(
            wg_lo[:, :], wg_sb[:, :], -1.0, wgf[:, :], OP.mult, OP.add),
            inc=ve, drain=True)
        h2lT = C["h2loT"]
        mm = None
        for ns in range(4):
            ops = [(wg_sb, h2T), (wg_lo, h2T), (wg_sb, h2lT)]
            nmm = len(ops) * 8
            i = 0
            for wop, hop in ops:
                for dc in range(8):
                    wt = [(smg, ld_all), (ve, lnv), (ve, wglo_ev)]                         if (ns == 0 and i == 0) else []
                    mm = P.op("t", lambda t, ns=ns, dc=dc, wop=wop, hop=hop,
                              st=(i == 0), sp=(i == nmm - 1): t.matmul(
                        pG[:, ns * 512:(ns + 1) * 512], wop[:, dc * 8:(dc + 1) * 8],
                        hop[:, dc * SQ + ns * 512: dc * SQ + (ns + 1) * 512],
                        start=st, stop=sp),
                        wait=wt, inc=(tm if i == nmm - 1 else None))
                    i += 1
        evg = P.op("v", lambda v: v.tensor_scalar(
            lgT[:, :], pG[:, :], bgc[:, :], None, OP.add),
            wait=[(tm, mm)], inc=ve)
        tps = []
        for tt in range(16):
            tp = P.op("t", lambda t, tt=tt: t.transpose(
                pGT[:, tt * 8:(tt + 1) * 8], lgT[:, tt * 128:(tt + 1) * 128],
                C["eye_f"][0:8, 0:8]), wait=([(ve, evg)] if tt == 0 else []),
                inc=tm)
            tps.append(tp)
        ev2 = P.op("v", lambda v: v.tensor_copy(lgtok[:, :], pGT[:, 0:128]),
                   wait=[(tm, tps[-1])], inc=ve)
        for tt in range(16):
            P.op("v", lambda v, tt=tt: v.tensor_reduce(
                mx16[:, tt:tt + 1], lgtok[:, tt * 8:(tt + 1) * 8], AX, OP.max),
                wait=([(ve, ev2)] if tt == 0 else []), inc=ve, drain=(tt == 0))
        P.drain("v")
        oh = None
        for tt in range(16):
            oh = P.op("v", lambda v, tt=tt: v.tensor_tensor(
                onehot[:, tt * 8:(tt + 1) * 8], lgtok[:, tt * 8:(tt + 1) * 8],
                mx16[:, tt:tt + 1].to_broadcast([128, 8]), OP.is_equal), inc=ve)
        _ps1.close()
        _ps2 = ExitStack()
        pOT = _ps2.enter_context(nc.psum_tensor("pOT", [8, 2048], F32))
        tps2 = []
        for tt in range(16):
            tp = P.op("t", lambda t, tt=tt: t.transpose(
                pOT[:, tt * 128:(tt + 1) * 128], onehot[:, tt * 8:(tt + 1) * 8],
                C["eye_f"][:, :]), wait=([(ve, oh)] if tt == 0 else []), inc=tm)
            tps2.append(tp)
        ev3 = P.op("v", lambda v: v.tensor_copy(onehT[:, :], pOT[:, :]),
                   wait=[(tm, tps2[-1])], inc=ve)
        a, b = onehT, cum0
        prev = ev3
        k = 1
        while k < SQ:
            P.op("v", lambda v, a=a, b=b, k=k: v.tensor_copy(
                b[:, 0:k], a[:, 0:k]), wait=[(ve, prev)], inc=ve)
            prev = P.op("v", lambda v, a=a, b=b, k=k: v.tensor_tensor(
                b[:, k:SQ], a[:, k:SQ], a[:, 0:SQ - k], OP.add), inc=ve)
            a, b = b, (cum1 if b is cum0 else cum0)
            if a is onehT:
                a = cum1
            k *= 2
        cum = a
        c1 = P.op("v", lambda v, cum=cum: v.tensor_scalar(
            cum[:, :], cum[:, :], float(CAP), eoffc[:, :], OP.min, OP.add),
            wait=[(ve, prev), (smg, ld_all)], inc=ve, drain=True)
        c2 = P.op("v", lambda v, cum=cum: v.tensor_tensor(
            cum[:, :], cum[:, :], onehT[:, :], OP.mult), inc=ve, drain=True)
        _ps2.close()
        _ps3 = ExitStack()
        pSL = _ps3.enter_context(nc.psum_tensor("pSL", [1, 2048], F32))
        pST = _ps3.enter_context(nc.psum_tensor("pST", [128, 512], F32))
        mms = None
        for ns in range(4):
            mms = P.op("t", lambda t, ns=ns, cum=cum: t.matmul(
                pSL[:, ns * 512:(ns + 1) * 512], onesf[0:8, :],
                cum[0:8, ns * 512:(ns + 1) * 512], start=True, stop=True),
                wait=([(ve, c2)] if ns == 0 else []), inc=tm)
        evs = P.op("v", lambda v: v.tensor_copy(slotr[:, :], pSL[:, :]),
                   wait=[(tm, mms)], inc=ve)
        tps3 = []
        for tt in range(16):
            tp = P.op("t", lambda t, tt=tt: t.transpose(
                pST[:, tt * 4:tt * 4 + 1] if False else pST[:, tt:tt + 1],
                slotr[:, tt * 128:(tt + 1) * 128], C["eye_f"][0:1, 0:1]),
                wait=([(ve, evs)] if tt == 0 else []), inc=tm)
            tps3.append(tp)
        cl = P.op("v", lambda v: v.tensor_scalar(
            slotf[:, :], pST[:, 0:16], 4095.0, 0.0, OP.min, OP.max),
            wait=[(tm, tps3[-1])], inc=ve)
        ci = P.op("v", lambda v: v.tensor_copy(sloti[:, :], slotf[:, :]),
                  wait=[(ve, cl)], inc=ve, drain=True)
        pf = None
        for k2 in range(32):
            pf = P.op("g", lambda g, k2=k2: g.dma_start(
                C["idx_hbm"][k2 * 128:(k2 + 1) * 128, :], dumps[:, :]),
                wait=([(smg, ld_all)] if k2 == 0 else []),
                inc=sm["rt0"], inck=16)
        P.wait("g", sm["rt0"], pf)
        sc = None
        for tt in range(16):
            sc = P.op("g", lambda g, tt=tt: g.indirect_dma_start(
                C["idx_hbm"],
                bass.IndirectOffsetOnAxis(ap=sloti[:, tt:tt + 1], axis=0),
                iotas[:, tt:tt + 1], None),
                wait=([(ve, ci)] if tt == 0 else []), inc=sm["rt1"], inck=16)
        C["idx_ready"] = sc
        _ps3.close()
        P.replay(nc)


def emit_ffn(nc, C):
    P = Rec()
    tm, ve, sg = C["tm"], C["ve"], C["sg"]
    sm = C["SM"]
    w1sm = (sm["w1a"], sm["w1b"])
    w2sm = sm["w2s"]
    smgx = sm["gx"]
    I32 = mybir.dt.int32
    CAP = 512
    NSL = 8 * CAP
    HTILES = NSL // 128
    with ExitStack() as _stk:
        w1s = _stk.enter_context(nc.sbuf_tensor("w1s", [128, 2 * 8192], BF16))
        w2s = _stk.enter_context(nc.sbuf_tensor("w2s", [128, 8192], BF16))
        h2pT = _stk.enter_context(nc.sbuf_tensor("h2pT", [128, 8 * 2048], BF16))
        hid = _stk.enter_context(nc.sbuf_tensor("hid", [128, 8 * 512], BF16))
        gst = _stk.enter_context(nc.sbuf_tensor("gst", [128, 2 * D], BF16))
        yst = _stk.enter_context(nc.sbuf_tensor("yst", [128, 2 * D], BF16))
        idxs = _stk.enter_context(nc.sbuf_tensor("idxs", [128, 32], I32))
        zt = _stk.enter_context(nc.sbuf_tensor("zt", [128, D], BF16))
        ysb = C["ysb"]
        ph0 = _stk.enter_context(nc.psum_tensor("ph0", [128, 512], F32))
        ph1 = _stk.enter_context(nc.psum_tensor("ph1", [128, 512], F32))
        po0 = _stk.enter_context(nc.psum_tensor("po0", [128, 512], F32))
        po1 = _stk.enter_context(nc.psum_tensor("po1", [128, 512], F32))
        pT0 = _stk.enter_context(nc.psum_tensor("pT0", [128, 1024], BF16))
        pT1 = _stk.enter_context(nc.psum_tensor("pT1", [128, 1024], BF16))
        pT = (pT0, pT1)

        ldx = None
        for k in range(32):
            ldx = P.op("g", lambda g, k=k: g.dma_start(
                idxs[:, k:k + 1], C["idx_hbm"][k * 128:(k + 1) * 128, :]),
                wait=([(sm["rt1"], C["idx_ready"])] if k == 0 else []),
                inc=smgx, inck=16)
        mz = P.op("v", lambda v: v.memset(zt[:, :], 0.0), inc=ve)
        pz = None
        for k in range(17):
            r0 = min(k * 128, SQ)
            pz = P.op("y", lambda y, r0=r0, k=k: y.dma_start(
                C["y_hbm"][r0:r0 + 128 if k < 16 else SQ + 1, :],
                zt[:, :] if k < 16 else zt[0:1, :]),
                wait=([(ve, mz)] if k == 0 else []), inc=sm["rt3"], inck=16)
        yzero = sm["rt3"].n
        ms = P.op("v", lambda v: v.memset(ysb[:, :], 0.0), inc=ve)

        gi_count = [0]
        w1ld, w2ld = {}, {}

        def stream_w(e, gg):
            gi = gi_count[0]
            sl1 = gi % 2
            w = [(tm, C[f"w1_free_{gi - 2}"])] if gi >= 2 else []
            w1ld[gi] = P.op("g", lambda g, e=e, gg=gg, sl1=sl1: g.dma_start(
                w1s[:, sl1 * 8192:(sl1 + 1) * 8192],
                bass.AP(C["w1"].tensor, e * D * HF + gg * 1024,
                        [[HF, 128], [128 * HF, 8], [1, 1024]])),
                wait=w, inc=w1sm[sl1], inck=16)
            w = [(tm, C[f"w2_free_{gi - 1}"])] if gi >= 1 else []
            w2ld[gi] = P.op("g", lambda g, e=e, gg=gg: g.dma_start(
                w2s[:, :],
                bass.AP(C["w2"].tensor, e * HF * D + gg * 1024 * D,
                        [[D, 128], [128 * D, 8], [1, 1024]])),
                wait=w, inc=w2sm, inck=16)
            gi_count[0] += 1
            return gi, sl1

        last_gelu = {}
        last_evac = {}
        for half in range(2):
            for k in range(16):
                tile = half * 16 + k
                wt = [(smgx, ldx), (sm["h2s"], C["h2s_all"])] if (k == 0 and half == 0) else []
                if tile >= 2:
                    wt.append((tm, C[f"gtp_{tile - 2}"]))
                gx = P.op("g", lambda g, tile=tile, k=k: g.indirect_dma_start(
                    gst[:, (k % 2) * D:(k % 2 + 1) * D], None, C["h2_hbm"],
                    bass.IndirectOffsetOnAxis(ap=idxs[:, tile:tile + 1], axis=0)),
                    wait=wt, inc=smgx, inck=16)
                tps = []
                for dc in range(8):
                    wt2 = [(smgx, gx)] if dc == 0 else []
                    if dc == 0 and tile >= 2:
                        wt2.append((ve, C[f"gx_ev_{tile - 2}"]))
                    tps.append(P.op("t", lambda t, dc=dc, k=k: t.transpose(
                        pT[k % 2][:, dc * 128:(dc + 1) * 128],
                        gst[:, (k % 2) * D + dc * 128:(k % 2) * D + (dc + 1) * 128],
                        C["eye_b"][:, :]), wait=wt2, inc=tm))
                C[f"gtp_{tile}"] = tps[-1]
                e2 = None
                for dc in range(8):
                    e2 = P.op("v", lambda v, dc=dc, k=k: v.tensor_copy(
                        h2pT[:, dc * 2048 + k * 128: dc * 2048 + (k + 1) * 128],
                        pT[k % 2][:, dc * 128:(dc + 1) * 128]),
                        wait=[(tm, tps[-1])], inc=ve)
                C[f"gx_ev_{tile}"] = e2
            gather_ev = ve.n
            for er in range(4):
                e = half * 4 + er
                for gg in range(4):
                    gi, sl1 = stream_w(e, gg)
                    for ht in range(8):
                        kk = gi * 8 + ht
                        wt = []
                        if ht == 0:
                            wt.append((w1sm[sl1], w1ld[gi]))
                            if gg == 0:
                                wt.append((ve, gather_ev))
                        if kk >= 2:
                            wt.append((sg, last_gelu[kk - 2]))
                        mm = None
                        for dc in range(8):
                            mm = P.op("t", lambda t, dc=dc, ht=ht, sl1=sl1,
                                      er=er: t.matmul(
                                          (ph0, ph1)[ht % 2][:, :],
                                          w1s[:, sl1 * 8192 + dc * 1024 +
                                              ht * 128:sl1 * 8192 + dc * 1024 +
                                              ht * 128 + 128],
                                          h2pT[:, dc * 2048 + er * 512:
                                               dc * 2048 + (er + 1) * 512],
                                          start=(dc == 0), stop=(dc == 7)),
                                      wait=(wt if dc == 0 else []),
                                      inc=(tm if dc == 7 else None))
                        if ht == 7:
                            C[f"w1_free_{gi}"] = tm.n
                        gl = P.op("s", lambda s, ht=ht: s.activation(
                            hid[:, ht * 512:(ht + 1) * 512],
                            (ph0, ph1)[ht % 2][:, :], AF.Gelu_apprx_tanh),
                            wait=[(tm, tm.n)], inc=sg)
                        last_gelu[kk] = gl
                    for dt in range(8):
                        kk2 = gi * 8 + dt
                        wt = [(sg, last_gelu[gi * 8 + 7])]
                        if dt == 0:
                            wt.append((w2sm, w2ld[gi]))
                        if kk2 >= 2:
                            wt.append((ve, last_evac[kk2 - 2]))
                        mm = None
                        for ht in range(8):
                            mm = P.op("t", lambda t, ht=ht, dt=dt: t.matmul(
                                (po0, po1)[dt % 2][:, :],
                                w2s[:, ht * 1024 + dt * 128:
                                    ht * 1024 + dt * 128 + 128],
                                hid[:, ht * 512:(ht + 1) * 512],
                                start=(ht == 0), stop=(ht == 7)),
                                wait=(wt if ht == 0 else []),
                                inc=(tm if ht == 7 else None))
                        if dt == 7:
                            C[f"w2_free_{gi}"] = tm.n
                        evv = P.op("v", lambda v, dt=dt, er=er: v.scalar_tensor_tensor(
                            ysb[:, dt * 2048 + er * 512: dt * 2048 + (er + 1) * 512],
                            (po0, po1)[dt % 2][:, :], C["gbt"][:, dt:dt + 1],
                            ysb[:, dt * 2048 + er * 512: dt * 2048 + (er + 1) * 512],
                            OP.mult, OP.add),
                            wait=[(tm, tm.n)] + ([(ve, ms)] if gi % 16 == 0 and dt == 0
                                                 else []), inc=ve)
                        last_evac[kk2] = evv
            ffn_half_ev = ve.n
            sev = {}
            ysc = {}
            last_ytp = None
            for k in range(16):
                tile = half * 16 + k
                tps = []
                for dc in range(8):
                    wt2 = []
                    if dc == 0:
                        wt2.append((ve, ffn_half_ev))
                        if k >= 2:
                            wt2.append((ve, sev[k - 2]))
                    tps.append(P.op("t", lambda t, dc=dc, k=k: t.transpose(
                        pT[k % 2][:, dc * 128:(dc + 1) * 128],
                        ysb[:, dc * 2048 + k * 128: dc * 2048 + (k + 1) * 128],
                        C["eye_b"][:, :]), wait=wt2, inc=tm))
                last_ytp = tps[-1]
                we2 = [(tm, tps[-1])]
                if k >= 2:
                    we2.append((sm["rt2"], ysc[k - 2]))
                e2 = P.op("v", lambda v, k=k: v.tensor_copy(
                    yst[:, (k % 2) * D:(k % 2 + 1) * D], pT[k % 2][:, :]),
                    wait=we2, inc=ve)
                sev[k] = e2
                wt3 = [(ve, e2)]
                if half == 0 and k == 0:
                    wt3.append((sm["rt3"], yzero))
                ysc[k] = P.op("g", lambda g, tile=tile, k=k: g.indirect_dma_start(
                    C["y_hbm"],
                    bass.IndirectOffsetOnAxis(ap=idxs[:, tile:tile + 1], axis=0),
                    yst[:, (k % 2) * D:(k % 2 + 1) * D], None),
                    wait=wt3, inc=sm["rt2"], inck=16)
            if half == 0:
                ms = P.op("v", lambda v: v.memset(ysb[:, :], 0.0),
                          wait=[(tm, last_ytp)], inc=ve)
        C["ysc_all"] = sm["rt2"].n
        P.replay(nc)


def emit_out(nc, C):
    P = Rec()
    tm, ve, sg = C["tm"], C["ve"], C["sg"]
    sm = C["SM"]
    xgs = (sm["xg0"], sm["xg1"])
    oss = (sm["os0"], sm["os1"])
    yls = (sm["at0"], sm["at1"])
    with ExitStack() as _stk:
        x2t = _stk.enter_context(nc.sbuf_tensor("x2t", [128, 2 * D], F32))
        yt = _stk.enter_context(nc.sbuf_tensor("yt", [128, 2 * D], BF16))
        outt = _stk.enter_context(nc.sbuf_tensor("outt", [128, 2 * D], F32))
        for s in (sm["sp0"], sm["sp1"], sm["sp2"]):
            P.wait("g", s, s.n)
        P.wait("y", sm["rt2"], C["ysc_all"])
        ev_done = {}
        st_done = {}
        for tt in range(SQ // 128):
            sl = tt % 2
            w = [(ve, ev_done[tt - 2])] if tt >= 2 else []
            ld = P.op("g", lambda g, tt=tt, sl=sl: g.dma_start(
                x2t[:, sl * D:(sl + 1) * D],
                C["x2_hbm"][tt * 128:(tt + 1) * 128, :]),
                wait=w, inc=xgs[sl], inck=16)
            ldy = P.op("y", lambda y, tt=tt, sl=sl: y.dma_start(
                yt[:, sl * D:(sl + 1) * D],
                C["y_hbm"][tt * 128:(tt + 1) * 128, :]),
                wait=list(w), inc=yls[sl], inck=16)
            wad = [(xgs[sl], ld), (yls[sl], ldy)]
            if tt >= 2:
                wad.append((oss[sl], st_done[tt - 2]))
            ad = P.op("v", lambda v, sl=sl: v.tensor_tensor(
                outt[:, sl * D:(sl + 1) * D], yt[:, sl * D:(sl + 1) * D],
                x2t[:, sl * D:(sl + 1) * D], OP.add), wait=wad, inc=ve)
            ev_done[tt] = ad
            st_done[tt] = P.op("s", lambda s, tt=tt, sl=sl: s.dma_start(
                C["out"][tt * 128:(tt + 1) * 128, :],
                outt[:, sl * D:(sl + 1) * D]),
                wait=[(ve, ad)], inc=oss[sl], inck=16)
        for s in sm.values():
            P.wait("y", s, s.n)
        P.replay(nc)


_CACHED = {}


def kernel(L0_attn_x, L0_attn_Wqkv, L0_attn_bqkv, L0_attn_Wr, L0_attn_br,
           L0_ffn_W1, L0_ffn_W2, L0_ffn_Wg, L0_ffn_bg, L0_c, L0_wmod):
    if "nc" not in _CACHED:
        _CACHED["nc"] = build_program()
    nc = _CACHED["nc"]

    eye = np.eye(128)
    di = np.arange(128)
    tri = dict(
        eye_bh=eye.astype(BFNP),
        eye_fh=eye.astype(np.float32),
        ones_fh=np.ones((128, 1), dtype=np.float32),
        iota_h=np.ascontiguousarray(
            np.arange(SQ, dtype=np.int32).reshape(16, 128).T),
        dump_h=np.full((128, 1), SQ, dtype=np.int32),
        eoff_h=(np.arange(8, dtype=np.float32) * 512 - 1).reshape(8, 1),
        tri_geh=(di[None, :] >= di[:, None]).astype(np.float32),
        tri_lth=(di[None, :] < di[:, None]).astype(np.float32),
        tri_leh=(di[None, :] <= di[:, None]).astype(np.float32),
        tri_gth=(di[None, :] > di[:, None]).astype(np.float32),
        ones_h=np.ones((128, 1), dtype=BFNP),
    )
    wr_f = np.ascontiguousarray(L0_attn_Wr)
    br_f = np.ascontiguousarray(L0_attn_br).reshape(1, R)
    wr_r = np.ascontiguousarray(L0_attn_Wr[::-1])
    br_r = np.ascontiguousarray(L0_attn_br[::-1]).reshape(1, R)
    shared = dict(
        wqkv=np.ascontiguousarray(L0_attn_Wqkv),
        bqkv=np.ascontiguousarray(L0_attn_bqkv).reshape(1, 3 * D),
        w1=np.ascontiguousarray(L0_ffn_W1),
        w2=np.ascontiguousarray(L0_ffn_W2),
        wg=np.ascontiguousarray(L0_ffn_Wg),
        bg=np.ascontiguousarray(L0_ffn_bg).reshape(E, 1),
        wmod=np.ascontiguousarray(L0_wmod),
        **tri,
    )
    in_maps = []
    for c in range(8):
        b = c // 2
        rev = c % 2 == 1
        xbm = L0_attn_x[b] if not rev else L0_attn_x[b][::-1]
        m = dict(shared)
        m["xb"] = np.ascontiguousarray(xbm)
        m["wr"] = wr_r if rev else wr_f
        m["br"] = br_r if rev else br_f
        m["crow"] = np.ascontiguousarray(L0_c[b]).reshape(1, D)
        in_maps.append(m)

    res = run_bass_kernel_spmd(nc, in_maps, core_ids=list(range(8)))
    out = np.empty((4, S, D), dtype=np.float32)
    for c in range(8):
        b = c // 2
        o = np.asarray(res.results[c]["out"])
        if c % 2 == 0:
            out[b, 0:SQ] = o
        else:
            out[b, SQ:S] = o[::-1]
    return out
